# version 04
# baseline (speedup 1.0000x reference)
import numpy as np
import ml_dtypes
from contextlib import ExitStack
import concourse.bass as bass
import concourse.bacc as bacc
import concourse.mybir as mybir
from concourse.bass_utils import run_bass_kernel_spmd

F32 = mybir.dt.float32
BF16 = mybir.dt.bfloat16
I32 = mybir.dt.int32
AF = mybir.ActivationFunctionType
AL = mybir.AluOpType
AX = mybir.AxisListType.X

NCORE = 8
NT = 2048
C = 1024
D3 = 3072
KSH = C * NT
VB = KSH
SHARD = KSH + NT * C
TT = 512
NTILE = NT // TT
SCALE = 1.0 / 32.0
EPS = 1e-5
NEGBIG = -1.0e9
NTOK = 16384.0


def build_nc():
    nc = bacc.Bacc("TRN2", target_bir_lowering=False, num_devices=NCORE,
                   detect_race_conditions=False)

    x_fm = nc.dram_tensor("x_fm", [C, NT], F32, kind="ExternalInput")
    wattnT = nc.dram_tensor("wattnT", [C, D3], F32, kind="ExternalInput")
    aaT = nc.dram_tensor("aaT", [64, C], F32, kind="ExternalInput")
    ba_in = nc.dram_tensor("ba_in", [64, D3], F32, kind="ExternalInput")
    wprojT = nc.dram_tensor("wprojT", [C, C], F32, kind="ExternalInput")
    apT = nc.dram_tensor("apT", [64, C], F32, kind="ExternalInput")
    bp_in = nc.dram_tensor("bp_in", [64, C], F32, kind="ExternalInput")
    gam_qk = nc.dram_tensor("gam_qk", [128, 16], F32, kind="ExternalInput")
    bet_qk = nc.dram_tensor("bet_qk", [128, 16], F32, kind="ExternalInput")
    gam_vr = nc.dram_tensor("gam_vr", [1, C], F32, kind="ExternalInput")
    bet_vr = nc.dram_tensor("bet_vr", [1, C], F32, kind="ExternalInput")
    gwT = nc.dram_tensor("gwT", [C, 8], F32, kind="ExternalInput")
    gb_row = nc.dram_tensor("gb_row", [1, 8], F32, kind="ExternalInput")
    w1_in = nc.dram_tensor("w1_in", [8, C, 2048], F32, kind="ExternalInput")
    b1_t = nc.dram_tensor("b1_t", [128, 128], F32, kind="ExternalInput")
    w2_in = nc.dram_tensor("w2_in", [8, 2048, C], F32, kind="ExternalInput")
    b2_in = nc.dram_tensor("b2_in", [8, C], F32, kind="ExternalInput")
    sched = nc.dram_tensor("sched", [1, 7 * 20], I32, kind="ExternalInput")

    out_fm = nc.dram_tensor("out_fm", [C, NT], F32, kind="ExternalOutput")

    kv_in = nc.dram_tensor("kv_in", [1, SHARD], BF16)
    kv_all = nc.dram_tensor("kv_all", [NCORE, SHARD], BF16)
    st_in = nc.dram_tensor("st_in", [1, 6144], F32)
    st_out = nc.dram_tensor("st_out", [1, 6144], F32)
    q_dram = nc.dram_tensor("q_dram", [128, 8 * NT], BF16)
    x2_sp = nc.dram_tensor("x2_sp", [C, NT], F32)

    ident_f_d = nc.inline_tensor(np.eye(128, dtype=np.float32), name="identf")
    ident_b_d = nc.inline_tensor(np.eye(128, dtype=ml_dtypes.bfloat16), name="identb")
    tri_np = np.where(np.arange(128)[None, :] <= np.arange(128)[:, None],
                      0.0, NEGBIG).astype(np.float32)
    tri_d = nc.inline_tensor(tri_np, name="trim")
    onescol_b_d = nc.inline_tensor(np.ones((128, 1), dtype=ml_dtypes.bfloat16),
                                   name="onescolb")
    onesrow_f_d = nc.inline_tensor(np.ones((1, 128), dtype=np.float32),
                                   name="onesrowf")
    onesrow_b_d = nc.inline_tensor(np.ones((1, 128), dtype=ml_dtypes.bfloat16),
                                   name="onesrowb")

    psA = nc.alloc_psum_tensor("psA", [128, 1024], F32)
    psB = nc.alloc_psum_tensor("psB", [128, 1024], F32)
    psC = nc.alloc_psum_tensor("psC", [128, 1024], F32)
    psD = nc.alloc_psum_tensor("psD", [128, 1024], F32)

    A = nc.alloc_sbuf_tensor
    idb = A("idb", [128, 128], BF16)
    idf = A("idf", [128, 128], F32)
    trim = A("trim_sb", [128, 128], F32)
    onecb = A("onecb", [128, 1], BF16)
    onerf = A("onerf", [1, 128], F32)
    onerb = A("onerb", [1, 128], BF16)
    stc = A("stc", [128, 32], F32)
    abqk = A("abqk", [128, 32], F32)
    abv = A("abv", [128, 16], F32)
    qab = A("qab", [128, 16], F32)
    gamc = A("gamc", [128, 16], F32)
    betc = A("betc", [128, 16], F32)
    schc = A("schc", [1, 7 * 20], I32)
    m_sb = A("m_sb", [128, 16], F32)
    l_sb = A("l_sb", [128, 16], F32)
    b1c = A("b1c", [128, 128], F32)
    gwb = A("gwb", [128, 64], BF16)
    gbr = A("gbr", [1, 8], F32)
    gbb = A("gbb", [128, 8], F32)
    pbc = A("pbc", [128, 8], F32)
    sm = A("sm", [128, 16], F32)
    tcol = A("tcol", [128, 4], F32)
    dgap = A("dgap", [128, 64], F32)

    def gap(v):
        return v.tensor_copy(dgap[:, 0:1], dgap[:, 1:2])

    class Sem:
        def __init__(self, name):
            self.h = nc.alloc_semaphore(name)
            self.n = 0
    sMM = Sem("sMM")
    sEV = Sem("sEV")
    sSC = Sem("sSC")
    sCC = Sem("sCC")
    sST = Sem("sST")
    sC = Sem("sC")
    sW = Sem("sW")
    sX = Sem("sX")
    sKVk = Sem("sKVk")
    sKVv = Sem("sKVv")
    sVL = Sem("sVL")
    sSTT = Sem("sSTT")
    sSR = Sem("sSR")
    sGV = Sem("sGV")
    sQW = Sem("sQW")
    sGA = Sem("sGA")
    sPL0 = Sem("sPL0")
    sPL1 = Sem("sPL1")
    sXT0 = Sem("sXT0")
    sXT1 = Sem("sXT1")
    sX2a = Sem("sX2a")
    sX2b = Sem("sX2b")
    sOI = Sem("sOI")
    sWR = Sem("sWR")
    sW10 = Sem("sW10")
    sW11 = Sem("sW11")
    sW20 = Sem("sW20")
    sW21 = Sem("sW21")

    es = ExitStack()
    rg1 = es.enter_context(nc.gpsimd.register("rg1"))
    rg2 = es.enter_context(nc.gpsimd.register("rg2"))
    rg3 = es.enter_context(nc.gpsimd.register("rg3"))
    rv1 = es.enter_context(nc.vector.register("rv1"))
    blk = es.enter_context(nc.Block("main"))

    s1 = ExitStack()
    wmT = s1.enter_context(nc.sbuf_tensor("wmT", [128, 8 * D3], BF16))
    wtmp = s1.enter_context(nc.sbuf_tensor("wtmp", [128, D3], F32))
    aatb = s1.enter_context(nc.sbuf_tensor("aatb", [64, C], BF16))
    bab = s1.enter_context(nc.sbuf_tensor("bab", [64, D3], BF16))
    xt = s1.enter_context(nc.sbuf_tensor("xt", [128, 8 * TT], F32))
    xb = s1.enter_context(nc.sbuf_tensor("xb", [128, 8 * TT], BF16))
    xsq = s1.enter_context(nc.sbuf_tensor("xsq", [128, 8 * TT], BF16))
    ln0 = s1.enter_context(nc.sbuf_tensor("ln0", [128, 8 * TT], BF16))
    abc0 = s1.enter_context(nc.sbuf_tensor("abc0", [128, 2 * TT], F32))
    xtmp = s1.enter_context(nc.sbuf_tensor("xtmp", [128, TT], F32))
    qsb = s1.enter_context(nc.sbuf_tensor("qsb", [128, 8 * NT], BF16))
    kslab = s1.enter_context(nc.sbuf_tensor("kslab", [128, 8 * TT], BF16))
    vslab = s1.enter_context(nc.sbuf_tensor("vslab", [128, 4 * 1024], BF16))
    sqs = s1.enter_context(nc.sbuf_tensor("sqs", [128, 4 * TT], BF16))
    vrow = s1.enter_context(nc.sbuf_tensor("vrow", [1, 2 * C], F32))
    rowb = s1.enter_context(nc.sbuf_tensor("rowb", [1, 4096], F32))

    @blk.sync
    def _(s):
        for dst, src in [(idb, ident_b_d), (idf, ident_f_d), (trim, tri_d),
                         (onecb, onescol_b_d), (onerf, onesrow_f_d),
                         (onerb, onesrow_b_d)]:
            s.dma_start(out=dst[:], in_=src.ap()).then_inc(sC.h, 16)
        for dst, src in [(gamc, gam_qk), (betc, bet_qk), (schc, sched),
                         (b1c, b1_t), (gbr, gb_row)]:
            s.dma_start(out=dst[:], in_=src.ap()).then_inc(sC.h, 16)
    sC.n += 176
    LD_CONST = sC.n

    @blk.gpsimd
    def _(g):
        g.dma_start(out=aatb[:], in_=aaT.ap()).then_inc(sGA.h, 16)
        g.dma_start(out=bab[:], in_=ba_in.ap()).then_inc(sGA.h, 16)
        g.dma_start(out=gwb[:].rearrange("p (cc e) -> p cc e", cc=8),
                    in_=bass.AP(gwT, 0, [[8, 128], [1024, 8], [1, 8]])
                    ).then_inc(sGA.h, 16)
    sGA.n += 48
    GD_LORA = sGA.n

    wm_evs = []
    for cc in range(8):
        @blk.sync
        def _(s, cc=cc):
            if cc >= 1:
                s.wait_ge(sEV.h, wm_evs[cc - 1])
            s.dma_start(out=wtmp[:], in_=wattnT[cc * 128:(cc + 1) * 128, :]
                        ).then_inc(sW.h, 16)
        sW.n += 16
        ld_w = sW.n

        @blk.tensor
        def _(t, cc=cc, ld_w=ld_w):
            if cc == 0:
                t.wait_ge(sGA.h, GD_LORA)
            t.wait_ge(sW.h, ld_w)
            if cc >= 1:
                t.wait_ge(sEV.h, wm_evs[cc - 1])
            last = None
            for s3 in range(3):
                ps = (psA, psB, psC)[s3]
                for rb in range(2):
                    last = t.matmul(
                        ps[:, rb * 512:(rb + 1) * 512],
                        bab[:, s3 * 1024 + cc * 128: s3 * 1024 + (cc + 1) * 128],
                        aatb[:, rb * 512:(rb + 1) * 512],
                        start=True, stop=True, skip_group_check=True)
            last.then_inc(sMM.h, 1)
        sMM.n += 1
        mm_w = sMM.n

        @blk.vector
        def _(v, cc=cc, mm_w=mm_w):
            v.wait_ge(sMM.h, mm_w)
            last = None
            for s3 in range(3):
                ps = (psA, psB, psC)[s3]
                src_w = bass.AP(wtmp, s3, [[D3, 128], [3, 1024]])
                dst = bass.AP(wmT, cc * D3 + s3, [[8 * D3, 128], [3, 1024]])
                last = v.tensor_tensor(dst, src_w, ps[:, 0:1024], AL.add)
            last.then_inc(sEV.h, 1)
        sEV.n += 1
        wm_evs.append(sEV.n)
    MM_WM = sMM.n

    @blk.vector
    def _(v):
        v.memzero(stc[:])
        v.memzero(l_sb[:])
        v.memzero(dgap[:])
        v.memset(m_sb[:], -1.0e30).then_inc(sEV.h, 1)
    sEV.n += 1
    EV_INIT = sEV.n

    def emit_ln_tile(ti, src_dram, xt_, xb_, xsq_, abc_, xtmp_, ln_dst, dst_off,
                     prev, first_waits_sync, first_waits_mm, writeln_wait_mm,
                     rowb):
        t0 = ti * TT

        @blk.sync
        def _(s):
            for sem, val in first_waits_sync:
                s.wait_ge(sem.h, val)
            if prev is not None:
                s.wait_ge(sEV.h, prev["ev_ln"])
            s.dma_start(out=xt_[:].rearrange("p (c t) -> p c t", c=8),
                        in_=bass.AP(src_dram, t0, [[NT, 128], [128 * NT, 8],
                                                   [1, TT]])
                        ).then_inc(sX.h, 16)
        sX.n += 16
        ld_x = sX.n

        @blk.vector
        def _(v, ld_x=ld_x):
            v.wait_ge(sX.h, ld_x)
            if prev is not None:
                v.wait_ge(sMM.h, prev["mm_stats"])
            v.tensor_copy(xb_[:], xt_[:]).then_inc(sEV.h, 1)
        sEV.n += 1
        ev_xb = sEV.n

        @blk.scalar
        def _(sc, ev_xb=ev_xb):
            sc.wait_ge(sEV.h, ev_xb)
            sc.square(xsq_[:], xb_[:]).then_inc(sSC.h, 1)
        sSC.n += 1
        sc_sq = sSC.n

        @blk.tensor
        def _(t, sc_sq=sc_sq):
            t.wait_ge(sSC.h, sc_sq)
            for sem, val in first_waits_mm:
                t.wait_ge(sem.h, val)
            for cc in range(8):
                t.matmul(psD[0:1, 0:TT], onecb[:], xb_[:, cc * TT:(cc + 1) * TT],
                         start=(cc == 0), stop=(cc == 7), skip_group_check=True)
            last = None
            for cc in range(8):
                last = t.matmul(psD[0:1, 512:512 + TT], onecb[:],
                                xsq_[:, cc * TT:(cc + 1) * TT],
                                start=(cc == 0), stop=(cc == 7),
                                skip_group_check=True)
            last.then_inc(sMM.h, 1)
        sMM.n += 1
        mm_st = sMM.n

        MU, EX2, TMP, RR, NB = 0, TT, 2 * TT, 3 * TT, 2048 + 1024

        @blk.vector
        def _(v, mm_st=mm_st):
            v.wait_ge(sMM.h, mm_st)
            v.tensor_scalar_mul(rowb[0:1, MU:MU + TT], psD[0:1, 0:TT], 1.0 / C)
            v.tensor_scalar_mul(rowb[0:1, EX2:EX2 + TT], psD[0:1, 512:512 + TT],
                                1.0 / C).then_inc(sEV.h, 1)
        sEV.n += 1
        ev_mu = sEV.n

        @blk.scalar
        def _(sc, ev_mu=ev_mu):
            sc.wait_ge(sEV.h, ev_mu)
            sc.square(rowb[0:1, TMP:TMP + TT], rowb[0:1, MU:MU + TT]
                      ).then_inc(sSC.h, 1)
        sSC.n += 1
        sc_mu2 = sSC.n

        @blk.vector
        def _(v, sc_mu2=sc_mu2):
            v.wait_ge(sSC.h, sc_mu2)
            v.tensor_sub(rowb[0:1, TMP:TMP + TT], rowb[0:1, EX2:EX2 + TT],
                         rowb[0:1, TMP:TMP + TT])
            v.tensor_scalar_add(rowb[0:1, TMP:TMP + TT], rowb[0:1, TMP:TMP + TT],
                                EPS).then_inc(sEV.h, 1)
        sEV.n += 1
        ev_var = sEV.n

        @blk.scalar
        def _(sc, ev_var=ev_var):
            sc.wait_ge(sEV.h, ev_var)
            sc.sqrt(rowb[0:1, TMP:TMP + TT], rowb[0:1, TMP:TMP + TT]
                    ).then_inc(sSC.h, 1)
        sSC.n += 1
        sc_sd = sSC.n

        @blk.vector
        def _(v, sc_sd=sc_sd):
            v.wait_ge(sSC.h, sc_sd)
            v.reciprocal(rowb[0:1, RR:RR + TT], rowb[0:1, TMP:TMP + TT])
            v.tensor_mul(rowb[0:1, NB:NB + TT], rowb[0:1, MU:MU + TT],
                         rowb[0:1, RR:RR + TT])
            v.tensor_scalar_mul(rowb[0:1, NB:NB + TT], rowb[0:1, NB:NB + TT],
                                -1.0).then_inc(sEV.h, 1)
        sEV.n += 1
        ev_r = sEV.n

        @blk.tensor
        def _(t, ev_r=ev_r):
            t.wait_ge(sEV.h, ev_r)
            t.matmul(psD[:, 0:TT], onerf[:], rowb[0:1, RR:RR + TT],
                     start=True, stop=True, skip_group_check=True)
            t.matmul(psD[:, 512:512 + TT], onerf[:], rowb[0:1, NB:NB + TT],
                     start=True, stop=True, skip_group_check=True
                     ).then_inc(sMM.h, 1)
        sMM.n += 1
        mm_bc = sMM.n

        @blk.vector
        def _(v, mm_bc=mm_bc, writeln_wait_mm=writeln_wait_mm):
            v.wait_ge(sMM.h, mm_bc)
            if writeln_wait_mm is not None:
                v.wait_ge(sMM.h, writeln_wait_mm)
            v.tensor_copy(abc_[:, 0:TT], psD[:, 0:TT])
            v.tensor_copy(abc_[:, TT:2 * TT], psD[:, 512:512 + TT])
            last = None
            for cc in range(8):
                v.tensor_mul(xtmp_[:], xt_[:, cc * TT:(cc + 1) * TT], abc_[:, 0:TT])
                o0, o1 = dst_off(cc, ti)
                last = v.tensor_add(ln_dst[:, o0:o1], xtmp_[:], abc_[:, TT:2 * TT])
            last.then_inc(sEV.h, 1)
        sEV.n += 1
        return {"ev_ln": sEV.n, "mm_stats": mm_st}

    prev_ln = None
    qkv_evs = []
    qkv_scs = []
    kst_lds = []
    vst_lds = []
    last_vproj_mm = None
    last_vev_sc = None
    for ti in range(NTILE):
        t0 = ti * TT
        fw_sync = [(sC, LD_CONST)] if ti == 0 else []
        fw_mm = [(sMM, MM_WM)] if ti == 0 else []
        prev_ln = emit_ln_tile(
            ti, x_fm, xt, xb, xsq, abc0, xtmp, ln0,
            lambda cc, ti_: (cc * TT, (cc + 1) * TT),
            prev_ln, fw_sync, fw_mm,
            last_vproj_mm, rowb)
        ev_ln = prev_ln["ev_ln"]

        for half in range(4):
            dcs = [half * 4 + i for i in range(4)]

            @blk.tensor
            def _(t, dcs=dcs, ev_ln=ev_ln):
                t.wait_ge(sEV.h, ev_ln)
                if qkv_scs:
                    t.wait_ge(sSC.h, qkv_scs[-1])
                if qkv_evs:
                    t.wait_ge(sEV.h, qkv_evs[-1])
                last = None
                for i, dc in enumerate(dcs):
                    ps = (psA, psB)[i // 2]
                    off = (i % 2) * 512
                    for cc in range(8):
                        last = t.matmul(
                            ps[:, off:off + TT],
                            wmT[:, cc * D3 + dc * 128:cc * D3 + (dc + 1) * 128],
                            ln0[:, cc * TT:(cc + 1) * TT],
                            start=(cc == 0), stop=(cc == 7), skip_group_check=True)
                last.then_inc(sMM.h, 1)
            sMM.n += 1
            mm_qk = sMM.n

            @blk.scalar
            def _(sc, dcs=dcs, mm_qk=mm_qk, ti=ti, t0=t0):
                sc.wait_ge(sMM.h, mm_qk)
                if dcs[0] >= 8 and kst_lds:
                    sc.wait_ge(sKVk.h, kst_lds[-1])
                for i, dc in enumerate(dcs):
                    ps = (psA, psB)[i // 2]
                    off = (i % 2) * 512
                    if dc < 8:
                        dst = qsb[:, dc * NT + t0: dc * NT + t0 + TT]
                    else:
                        dst = kslab[:, (dc - 8) * TT:(dc - 7) * TT]
                    sc.activation(dst, ps[:, off:off + TT], AF.Copy,
                                  bias=0.0, scale=1.0)
                    sc.square(sqs[:, i * TT:(i + 1) * TT], ps[:, off:off + TT]
                              ).then_inc(sSC.h, 1)
            sSC.n += 4
            sc_e = sSC.n

            @blk.vector
            def _(v, dcs=dcs, sc_e=sc_e, mm_qk=mm_qk, ti=ti, t0=t0):
                last = None
                for i, dc in enumerate(dcs):
                    v.wait_ge(sSC.h, sc_e - 3 + i)
                    if dc < 8:
                        srcv = qsb[:, dc * NT + t0: dc * NT + t0 + TT]
                    else:
                        srcv = kslab[:, (dc - 8) * TT:(dc - 7) * TT]
                    v.reduce_sum(tcol[:, 0:1], srcv, AX)
                    v.reduce_sum(tcol[:, 1:2], sqs[:, i * TT:(i + 1) * TT], AX)
                    v.tensor_add(stc[:, dc:dc + 1], stc[:, dc:dc + 1], tcol[:, 0:1])
                    last = v.tensor_add(stc[:, 16 + dc:17 + dc],
                                        stc[:, 16 + dc:17 + dc], tcol[:, 1:2])
                last.then_inc(sEV.h, 1)
            sEV.n += 1
            qkv_evs.append(sEV.n)
            qkv_scs.append(sc_e)

        @blk.sync
        def _(s, ti=ti, t0=t0):
            s.wait_ge(sSC.h, qkv_scs[-1])
            s.dma_start(out=bass.AP(kv_in, t0, [[NT, 128], [128 * NT, 8], [1, TT]]),
                        in_=kslab[:].rearrange("p (c t) -> p c t", c=8)
                        ).then_inc(sKVk.h, 16)
        sKVk.n += 16
        kst_lds.append(sKVk.n)

        vev_scs = []
        for tc in range(4):
            @blk.tensor
            def _(t, tc=tc, ev_ln=ev_ln):
                if tc == 0:
                    t.wait_ge(sEV.h, ev_ln)
                    if last_vev_sc is not None:
                        t.wait_ge(sSC.h, last_vev_sc)
                else:
                    t.wait_ge(sSC.h, vev_scs[-1])
                last = None
                for cc in range(8):
                    t.matmul(psC[:, 0:512],
                             ln0[:, cc * TT + tc * 128:cc * TT + (tc + 1) * 128],
                             wmT[:, cc * D3 + 2048:cc * D3 + 2560],
                             start=(cc == 0), stop=(cc == 7), skip_group_check=True)
                for cc in range(8):
                    last = t.matmul(
                        psC[:, 512:1024],
                        ln0[:, cc * TT + tc * 128:cc * TT + (tc + 1) * 128],
                        wmT[:, cc * D3 + 2560:cc * D3 + 3072],
                        start=(cc == 0), stop=(cc == 7), skip_group_check=True)
                last.then_inc(sMM.h, 1)
            sMM.n += 1
            mm_v = sMM.n
            last_vproj_mm = mm_v

            @blk.scalar
            def _(sc, tc=tc, mm_v=mm_v):
                sc.wait_ge(sMM.h, mm_v)
                if tc == 0 and vst_lds:
                    sc.wait_ge(sKVv.h, vst_lds[-1])
                sc.activation(vslab[:, tc * 1024:(tc + 1) * 1024], psC[:, 0:1024],
                              AF.Copy, bias=0.0, scale=1.0).then_inc(sSC.h, 1)
            sSC.n += 1
            vev_scs.append(sSC.n)
        last_vev_sc = vev_scs[-1]

        @blk.sync
        def _(s, ti=ti, t0=t0, w=vev_scs[-1]):
            s.wait_ge(sSC.h, w)
            s.dma_start(out=bass.AP(kv_in, VB + t0 * C,
                                    [[C, 128], [128 * C, 4], [1, 1024]]),
                        in_=vslab[:].rearrange("p (tc v) -> p tc v", tc=4)
                        ).then_inc(sKVv.h, 16)
        sKVv.n += 16
        vst_lds.append(sKVv.n)
    KVK_ALL = sKVk.n
    KVV_ALL = sKVv.n

    vst_mms = []
    for ch in range(16):
        @blk.sync
        def _(s, ch=ch):
            if ch == 0:
                s.wait_ge(sKVk.h, KVK_ALL)
                s.wait_ge(sKVv.h, KVV_ALL)
                s.wait_ge(sEV.h, qkv_evs[-1])
            if len(vst_mms) >= 1:
                s.wait_ge(sMM.h, vst_mms[-1])
            s.dma_start(out=sqs[:, 0:1024],
                        in_=bass.AP(kv_in, VB + ch * 128 * C, [[C, 128], [1, C]])
                        ).then_inc(sVL.h, 16)
        sVL.n += 16
        ld_v = sVL.n

        @blk.scalar
        def _(sc, ch=ch, ld_v=ld_v):
            sc.wait_ge(sVL.h, ld_v)
            sc.square(xsq[:, 0:1024], sqs[:, 0:1024]).then_inc(sSC.h, 1)
        sSC.n += 1
        sc_v2 = sSC.n

        @blk.tensor
        def _(t, ch=ch, sc_v2=sc_v2):
            t.wait_ge(sSC.h, sc_v2)
            last = None
            for hh in range(2):
                t.matmul(psD[0:1, hh * 512:(hh + 1) * 512], onecb[:],
                         sqs[:, hh * 512:(hh + 1) * 512],
                         start=(ch == 0), stop=(ch == 15), skip_group_check=True)
                last = t.matmul(psC[0:1, hh * 512:(hh + 1) * 512], onecb[:],
                                xsq[:, hh * 512:(hh + 1) * 512],
                                start=(ch == 0), stop=(ch == 15),
                                skip_group_check=True)
            last.then_inc(sMM.h, 1)
        sMM.n += 1
        vst_mms.append(sMM.n)
    MM_VST = sMM.n

    @blk.vector
    def _(v):
        v.wait_ge(sMM.h, MM_VST)
        v.tensor_copy(vrow[0:1, 0:C], psD[0:1, 0:C])
        v.tensor_copy(vrow[0:1, C:2 * C], psC[0:1, 0:C]).then_inc(sEV.h, 1)
    sEV.n += 1
    EV_VROW = sEV.n

    @blk.sync
    def _(s):
        s.wait_ge(sEV.h, EV_VROW)
        s.dma_start(out=bass.AP(st_in, 0, [[32, 128], [1, 32]]), in_=stc[:]
                    ).then_inc(sSTT.h, 16)
        s.dma_start(out=st_in[0:1, 4096:6144], in_=vrow[0:1, :]).then_inc(sSTT.h, 16)
    sSTT.n += 32
    LD_STW = sSTT.n

    @blk.gpsimd
    def _(g):
        g.wait_ge(sSTT.h, LD_STW)
        g.collective_compute("AllReduce", AL.add, replica_groups=[list(range(NCORE))],
                             ins=[st_in.ap().opt()], outs=[st_out.ap().opt()]
                             ).then_inc(sCC.h, 1)
        g.wait_ge(sKVk.h, KVK_ALL)
        g.wait_ge(sKVv.h, KVV_ALL)
        g.collective_compute("AllGather", AL.bypass,
                             replica_groups=[list(range(NCORE))],
                             ins=[kv_in.ap().opt()], outs=[kv_all.ap().opt()]
                             ).then_inc(sCC.h, 1)
    sCC.n += 2

    @blk.gpsimd
    def _(g):
        g.wait_ge(sCC.h, 1)
        g.dma_start(out=stc[:], in_=bass.AP(st_out, 0, [[32, 128], [1, 32]])
                    ).then_inc(sSR.h, 16)
        g.dma_start(out=vrow[0:1, :], in_=st_out[0:1, 4096:6144]).then_inc(sSR.h, 16)
    sSR.n += 32
    GD_STR = sSR.n

    @blk.sync
    def _(s):
        s.wait_ge(sCC.h, 1)
        s.dma_start(out=rowb[0:1, 2048:3072], in_=gam_vr.ap()).then_inc(sGV.h, 16)
        s.dma_start(out=rowb[0:1, 3072:4096], in_=bet_vr.ap()).then_inc(sGV.h, 16)
    sGV.n += 32
    ld_gv = sGV.n

    @blk.vector
    def _(v):
        v.wait_ge(sSR.h, GD_STR)
        v.tensor_scalar_mul(stc[:, 0:16], stc[:, 0:16], 1.0 / NTOK)
        v.tensor_scalar_mul(stc[:, 16:32], stc[:, 16:32], 1.0 / NTOK)
        v.tensor_scalar_mul(vrow[0:1, 0:C], vrow[0:1, 0:C], 1.0 / NTOK)
        v.tensor_mul(sm[:, 0:16], stc[:, 0:16], stc[:, 0:16])
        v.tensor_scalar_mul(vrow[0:1, C:2 * C], vrow[0:1, C:2 * C], 1.0 / NTOK)
        v.tensor_sub(abqk[:, 0:16], stc[:, 16:32], sm[:, 0:16])
        gap(v)
        v.tensor_scalar_add(abqk[:, 0:16], abqk[:, 0:16], EPS)
        gap(v).then_inc(sEV.h, 1)
    sEV.n += 1
    ev1 = sEV.n

    @blk.scalar
    def _(sc, ev1=ev1):
        sc.wait_ge(sEV.h, ev1)
        sc.sqrt(abqk[:, 0:16], abqk[:, 0:16])
        sc.square(rowb[0:1, 0:C], vrow[0:1, 0:C]).then_inc(sSC.h, 1)
    sSC.n += 1
    sc1 = sSC.n

    @blk.vector
    def _(v, sc1=sc1):
        v.wait_ge(sSC.h, sc1)
        v.reciprocal(abqk[:, 0:16], abqk[:, 0:16])
        v.tensor_sub(rowb[0:1, 0:C], vrow[0:1, C:2 * C], rowb[0:1, 0:C])
        v.tensor_mul(abqk[:, 0:16], abqk[:, 0:16], gamc[:])
        v.tensor_scalar_add(rowb[0:1, 0:C], rowb[0:1, 0:C], EPS)
        v.tensor_mul(abqk[:, 16:32], stc[:, 0:16], abqk[:, 0:16])
        gap(v)
        v.tensor_sub(abqk[:, 16:32], betc[:], abqk[:, 16:32])
        gap(v).then_inc(sEV.h, 1)
    sEV.n += 1
    ev2 = sEV.n

    @blk.scalar
    def _(sc, ev2=ev2):
        sc.wait_ge(sEV.h, ev2)
        sc.sqrt(rowb[0:1, 0:C], rowb[0:1, 0:C]).then_inc(sSC.h, 1)
    sSC.n += 1
    sc2 = sSC.n

    @blk.vector
    def _(v, sc2=sc2, ld_gv=ld_gv):
        v.wait_ge(sSC.h, sc2)
        v.wait_ge(sGV.h, ld_gv)
        v.reciprocal(rowb[0:1, 0:C], rowb[0:1, 0:C])
        v.tensor_mul(rowb[0:1, 0:C], rowb[0:1, 0:C], rowb[0:1, 2048:3072])
        v.tensor_mul(rowb[0:1, 1024:2048], vrow[0:1, 0:C], rowb[0:1, 0:C])
        v.tensor_sub(rowb[0:1, 1024:2048], rowb[0:1, 3072:4096],
                     rowb[0:1, 1024:2048])
        v.tensor_mul(qab[:, 0:8], abqk[:, 0:8], abqk[:, 8:16])
        v.tensor_mul(qab[:, 8:16], abqk[:, 16:24], abqk[:, 8:16]
                     ).then_inc(sEV.h, 1)
    sEV.n += 1
    EV_STATS = sEV.n

    @blk.tensor
    def _(t):
        t.wait_ge(sEV.h, EV_STATS)
        last = None
        for vcc in range(8):
            t.matmul(psD[:, vcc:vcc + 1], rowb[0:1, vcc * 128:(vcc + 1) * 128],
                     onerf[0:1, 0:1], start=True, stop=True, skip_group_check=True)
            last = t.matmul(psD[:, 8 + vcc:9 + vcc],
                            rowb[0:1, 1024 + vcc * 128:1024 + (vcc + 1) * 128],
                            onerf[0:1, 0:1], start=True, stop=True,
                            skip_group_check=True)
        last = t.matmul(psD[:, 16:24], onerf[:], gbr[0:1, 0:8],
                        start=True, stop=True, skip_group_check=True)
        last.then_inc(sMM.h, 1)
    sMM.n += 1
    mm_col = sMM.n

    @blk.vector
    def _(v, mm_col=mm_col):
        v.wait_ge(sMM.h, mm_col)
        v.tensor_copy(abv[:], psD[:, 0:16])
        v.tensor_copy(gbb[:], psD[:, 16:24])
        last = None
        for cc in range(8):
            v.tensor_scalar_mul(qsb[:, cc * NT:(cc + 1) * NT],
                                qsb[:, cc * NT:(cc + 1) * NT], qab[:, cc:cc + 1])
            last = v.tensor_scalar_add(qsb[:, cc * NT:(cc + 1) * NT],
                                       qsb[:, cc * NT:(cc + 1) * NT],
                                       qab[:, 8 + cc:9 + cc])
        last.then_inc(sEV.h, 1)
    sEV.n += 1
    EV_QAFF = sEV.n

    @blk.sync
    def _(s):
        s.wait_ge(sEV.h, EV_QAFF)
        s.dma_start(out=q_dram.ap(), in_=qsb[:]).then_inc(sQW.h, 16)
    sQW.n += 16
    LD_QW = sQW.n

    s1.close()

    s2 = ExitStack()
    y_sb = s2.enter_context(nc.sbuf_tensor("y_sb", [128, 16 * 1024], F32))
    p_sb0 = s2.enter_context(nc.sbuf_tensor("p_sb0", [128, 1024], BF16))
    p_sb1 = s2.enter_context(nc.sbuf_tensor("p_sb1", [128, 1024], BF16))
    pT0 = s2.enter_context(nc.sbuf_tensor("pT0", [128, 1024], BF16))
    pT1 = s2.enter_context(nc.sbuf_tensor("pT1", [128, 1024], BF16))
    pvsb = s2.enter_context(nc.sbuf_tensor("pvsb", [128, 1024], F32))
    y_st = s2.enter_context(nc.sbuf_tensor("y_st", [128, 1024], F32))
    fc0 = s2.enter_context(nc.sbuf_tensor("fc0", [128, 8], F32))
    fc1 = s2.enter_context(nc.sbuf_tensor("fc1", [128, 8], F32))

    s2a = ExitStack()
    qcur0 = s2a.enter_context(nc.sbuf_tensor("qcur0", [128, 8192], BF16))
    qcur1 = s2a.enter_context(nc.sbuf_tensor("qcur1", [128, 8192], BF16))
    kb0 = s2a.enter_context(nc.sbuf_tensor("kb0", [128, 8192], BF16))
    kb1 = s2a.enter_context(nc.sbuf_tensor("kb1", [128, 8192], BF16))
    vb0 = s2a.enter_context(nc.sbuf_tensor("vb0", [128, 8192], BF16))
    vb1 = s2a.enter_context(nc.sbuf_tensor("vb1", [128, 8192], BF16))

    @blk.vector
    def _(v):
        v.wait_ge(sEV.h, EV_QAFF)
        v.memzero(y_sb[:]).then_inc(sEV.h, 1)
    sEV.n += 1
    EV_YZERO = sEV.n

    pairs = [("diagA",), ("diagB",)] + [("full", gi) for gi in range(7)]
    pair_done_mm = []
    pair_ld_gd = []
    sPL = (sPL0, sPL1)
    last_exp_sc = {}
    last_tr_mm = {}
    last_pTe_ev = [None]
    last_pve_sc = [None]

    for ip, pdesc in enumerate(pairs):
        par = ip % 2
        qcur = (qcur0, qcur1)[par]
        kb = (kb0, kb1)[par]
        vb = (vb0, vb1)[par]
        is_diag = pdesc[0] in ("diagA", "diagB")
        sbase = {"diagA": 0, "diagB": 8}.get(pdesc[0])

        @blk.gpsimd
        def _(g, ip=ip, pdesc=pdesc, qcur=qcur, kb=kb, vb=vb, is_diag=is_diag):
            if ip >= 2:
                g.wait_ge(sMM.h, pair_done_mm[ip - 2])
            if is_diag:
                g.wait_ge(sQW.h, LD_QW)
                g.wait_ge(sKVk.h, KVK_ALL)
                g.wait_ge(sKVv.h, KVV_ALL)
                qoff = 0 if pdesc[0] == "diagA" else 1024
                g.dma_start(out=qcur[:].rearrange("p (c t) -> p c t", c=8),
                            in_=bass.AP(q_dram, qoff,
                                        [[8 * NT, 128], [NT, 8], [1, 1024]])
                            ).then_inc(sPL[ip % 2].h, 16)
                g.dma_start(out=kb[:].rearrange("p (c t) -> p c t", c=8),
                            in_=bass.AP(kv_in, qoff,
                                        [[NT, 128], [128 * NT, 8], [1, 1024]])
                            ).then_inc(sPL[ip % 2].h, 16)
                g.dma_start(out=vb[:].rearrange("p (tc v) -> p tc v", tc=8),
                            in_=bass.AP(kv_in, VB + qoff * C,
                                        [[C, 128], [128 * C, 8], [1, 1024]])
                            ).then_inc(sPL[ip % 2].h, 16)
            else:
                gi = pdesc[1]
                g.wait_ge(sCC.h, 2)
                g.wait_ge(sQW.h, LD_QW)
                base = gi * 20
                g.reg_load(rg1, schc[0:1, base:base + 1])
                koff = g.snap(rg1, min_val=0, max_val=7 * SHARD + 1024)
                g.reg_load(rg2, schc[0:1, base + 1:base + 2])
                voff = g.snap(rg2, min_val=0, max_val=7 * SHARD + VB + 1024 * C)
                g.reg_load(rg3, schc[0:1, base + 2:base + 3])
                qoff = g.snap(rg3, min_val=0, max_val=1024)
                g.dma_start(out=qcur[:].rearrange("p (c t) -> p c t", c=8),
                            in_=bass.AP(q_dram, qoff,
                                        [[8 * NT, 128], [NT, 8], [1, 1024]])
                            ).then_inc(sPL[ip % 2].h, 16)
                g.dma_start(out=kb[:].rearrange("p (c t) -> p c t", c=8),
                            in_=bass.AP(kv_all, koff,
                                        [[NT, 128], [128 * NT, 8], [1, 1024]])
                            ).then_inc(sPL[ip % 2].h, 16)
                g.dma_start(out=vb[:].rearrange("p (tc v) -> p tc v", tc=8),
                            in_=bass.AP(kv_all, voff,
                                        [[C, 128], [128 * C, 8], [1, 1024]])
                            ).then_inc(sPL[ip % 2].h, 16)
        sPL[ip % 2].n += 48
        pair_ld_gd.append(sPL[ip % 2].n)

        for qs in range(8):
            nkt = (qs + 1) if is_diag else 8
            W = nkt * 128
            psS = (psA, psB)[qs % 2]
            p_s = (p_sb0, p_sb1)[qs % 2]
            pT_s = (pT0, pT1)[qs % 2]
            fc = (fc0, fc1)[qs % 2]

            @blk.tensor
            def _(t, ip=ip, qs=qs, W=W, psS=psS, qcur=qcur, kb=kb):
                if qs == 0:
                    t.wait_ge(sPL[ip % 2].h, pair_ld_gd[ip])
                    if ip == 0:
                        t.wait_ge(sEV.h, EV_YZERO)
                key = qs % 2
                if key in last_exp_sc:
                    t.wait_ge(sSC.h, last_exp_sc[key])
                nhal = (W + 511) // 512
                last = None
                for cc in range(8):
                    for h in range(nhal):
                        w0 = h * 512
                        w1 = min(W, w0 + 512)
                        last = t.matmul(
                            psS[:, w0:w1],
                            qcur[:, cc * 1024 + qs * 128:cc * 1024 + (qs + 1) * 128],
                            kb[:, cc * 1024 + w0:cc * 1024 + w1],
                            start=(cc == 0), stop=(cc == 7), skip_group_check=True)
                last.then_inc(sMM.h, 1)
            sMM.n += 1
            mm_s = sMM.n

            @blk.vector
            def _(v, qs=qs, W=W, psS=psS, fc=fc, is_diag=is_diag, sbase=sbase,
                  pdesc=pdesc, mm_s=mm_s):
                v.wait_ge(sMM.h, mm_s)
                if is_diag:
                    v.tensor_add(psS[:, qs * 128:(qs + 1) * 128],
                                 psS[:, qs * 128:(qs + 1) * 128], trim[:])
                    sl = sbase + qs
                    v.tensor_copy(fc[:, 0:1], m_sb[:, sl:sl + 1])
                    v.reduce_max(fc[:, 1:2], psS[:, 0:W], AX)
                    v.tensor_copy(fc[:, 5:6], l_sb[:, sl:sl + 1])
                    v.tensor_copy(y_st[:], y_sb[:, sl * 1024:(sl + 1) * 1024])
                else:
                    base = pdesc[1] * 20
                    v.reduce_max(fc[:, 1:2], psS[:, 0:W], AX)
                    v.reg_load(rv1, schc[0:1, base + 12 + qs:base + 13 + qs])
                    mls = v.snap(rv1, min_val=0, max_val=15)
                    v.tensor_copy(fc[:, 0:1], m_sb[:, bass.ds(mls, 1)])
                    v.tensor_copy(fc[:, 5:6], l_sb[:, bass.ds(mls, 1)])
                    v.reg_load(rv1, schc[0:1, base + 4 + qs:base + 5 + qs])
                    ysl = v.snap(rv1, min_val=0, max_val=15 * 1024)
                    v.tensor_copy(y_st[:], y_sb[:, bass.ds(ysl, 1024)])
                v.tensor_scalar_mul(fc[:, 1:2], fc[:, 1:2], SCALE)
                gap(v)
                v.tensor_max(fc[:, 2:3], fc[:, 0:1], fc[:, 1:2])
                gap(v)
                v.tensor_scalar_mul(fc[:, 3:4], fc[:, 2:3], -1.0)
                v.tensor_sub(fc[:, 7:8], fc[:, 0:1], fc[:, 2:3])
                gap(v).then_inc(sEV.h, 1)
            sEV.n += 1
            ev_mn = sEV.n

            @blk.scalar
            def _(sc, qs=qs, W=W, psS=psS, p_s=p_s, fc=fc, ev_mn=ev_mn):
                sc.wait_ge(sEV.h, ev_mn)
                key = qs % 2
                if key in last_tr_mm:
                    sc.wait_ge(sMM.h, last_tr_mm[key])
                sc.activation(fc[:, 4:5], fc[:, 7:8], AF.Exp, bias=0.0, scale=1.0)
                sc.activation(p_s[:, 0:W], psS[:, 0:W], AF.Exp,
                              bias=fc[:, 3:4], scale=SCALE).then_inc(sSC.h, 1)
            sSC.n += 1
            last_exp_sc[qs % 2] = sSC.n
            sc_p = sSC.n

            @blk.tensor
            def _(t, qs=qs, nkt=nkt, p_s=p_s, sc_p=sc_p):
                t.wait_ge(sSC.h, sc_p)
                if last_pTe_ev[0] is not None:
                    t.wait_ge(sEV.h, last_pTe_ev[0])
                psD_bf = psD[:].bitcast(BF16)
                last = None
                for tc in range(nkt):
                    last = t.transpose(psD_bf[:, tc * 128:(tc + 1) * 128],
                                       p_s[:, tc * 128:(tc + 1) * 128], idb[:])
                last.then_inc(sMM.h, 1)
            sMM.n += 1
            last_tr_mm[qs % 2] = sMM.n
            mm_t = sMM.n

            @blk.vector
            def _(v, qs=qs, W=W, p_s=p_s, pT_s=pT_s, fc=fc, is_diag=is_diag,
                  sbase=sbase, pdesc=pdesc, sc_p=sc_p, mm_t=mm_t):
                v.wait_ge(sSC.h, sc_p)
                v.reduce_sum(fc[:, 6:7], p_s[:, 0:W], AX)
                if is_diag:
                    sl = sbase + qs
                    v.tensor_copy(m_sb[:, sl:sl + 1], fc[:, 2:3])
                    v.scalar_tensor_tensor(fc[:, 5:6], fc[:, 5:6], fc[:, 4:5],
                                           fc[:, 6:7], AL.mult, AL.add)
                    gap(v)
                    v.tensor_copy(l_sb[:, sl:sl + 1], fc[:, 5:6])
                else:
                    base = pdesc[1] * 20
                    v.reg_load(rv1, schc[0:1, base + 12 + qs:base + 13 + qs])
                    mls = v.snap(rv1, min_val=0, max_val=15)
                    v.tensor_copy(m_sb[:, bass.ds(mls, 1)], fc[:, 2:3])
                    v.scalar_tensor_tensor(fc[:, 5:6], fc[:, 5:6], fc[:, 4:5],
                                           fc[:, 6:7], AL.mult, AL.add)
                    gap(v)
                    v.tensor_copy(l_sb[:, bass.ds(mls, 1)], fc[:, 5:6])
                v.wait_ge(sMM.h, mm_t)
                v.tensor_copy(pT_s[:, 0:W], psD[:].bitcast(BF16)[:, 0:W]
                              ).then_inc(sEV.h, 1)
            sEV.n += 1
            last_pTe_ev[0] = sEV.n
            ev_pte = sEV.n

            @blk.tensor
            def _(t, qs=qs, nkt=nkt, pT_s=pT_s, vb=vb, ev_pte=ev_pte):
                t.wait_ge(sEV.h, ev_pte)
                if last_pve_sc[0] is not None:
                    t.wait_ge(sSC.h, last_pve_sc[0])
                for tc in range(nkt):
                    t.matmul(psC[:, 0:512], pT_s[:, tc * 128:(tc + 1) * 128],
                             vb[:, tc * 1024:tc * 1024 + 512],
                             start=(tc == 0), stop=(tc == nkt - 1),
                             skip_group_check=True)
                last = None
                for tc in range(nkt):
                    last = t.matmul(psC[:, 512:1024],
                                    pT_s[:, tc * 128:(tc + 1) * 128],
                                    vb[:, tc * 1024 + 512:(tc + 1) * 1024],
                                    start=(tc == 0), stop=(tc == nkt - 1),
                                    skip_group_check=True)
                last.then_inc(sMM.h, 1)
            sMM.n += 1
            mm_pv = sMM.n

            @blk.scalar
            def _(sc, mm_pv=mm_pv):
                sc.wait_ge(sMM.h, mm_pv)
                sc.activation(pvsb[:], psC[:, 0:1024], AF.Copy,
                              bias=0.0, scale=1.0).then_inc(sSC.h, 1)
            sSC.n += 1
            last_pve_sc[0] = sSC.n
            sc_pve = sSC.n

            @blk.vector
            def _(v, fc=fc, is_diag=is_diag, sbase=sbase, qs=qs, pdesc=pdesc,
                  sc_pve=sc_pve):
                v.wait_ge(sSC.h, sc_pve)
                v.scalar_tensor_tensor(y_st[:], y_st[:], fc[:, 4:5], pvsb[:],
                                       AL.mult, AL.add)
                if is_diag:
                    sl = sbase + qs
                    v.tensor_copy(y_sb[:, sl * 1024:(sl + 1) * 1024], y_st[:]
                                  ).then_inc(sEV.h, 1)
                else:
                    base = pdesc[1] * 20
                    v.reg_load(rv1, schc[0:1, base + 4 + qs:base + 5 + qs])
                    ysl = v.snap(rv1, min_val=0, max_val=15 * 1024)
                    v.tensor_copy(y_sb[:, bass.ds(ysl, 1024)], y_st[:]
                                  ).then_inc(sEV.h, 1)
            sEV.n += 1
        pair_done_mm.append(sMM.n)
    EV_ATT = sEV.n
    MM_ATT = sMM.n

    s2a.close()

    s2b = ExitStack()
    y_fm = s2b.enter_context(nc.sbuf_tensor("y_fm", [128, 8 * NT], BF16))
    y1 = s2b.enter_context(nc.sbuf_tensor("y1", [128, 8 * NT], BF16))
    wpTs = s2b.enter_context(nc.sbuf_tensor("wpTs", [128, 8 * 1024], BF16))
    wmpTs = s2b.enter_context(nc.sbuf_tensor("wmpTs", [128, 8 * 1024], BF16))
    wptmp = s2b.enter_context(nc.sbuf_tensor("wptmp", [128, 1024], F32))
    apTb = s2b.enter_context(nc.sbuf_tensor("apTb", [64, C], BF16))
    bpb = s2b.enter_context(nc.sbuf_tensor("bpb", [64, C], BF16))
    xtl0 = s2b.enter_context(nc.sbuf_tensor("xtl0", [128, TT], F32))
    xtl1 = s2b.enter_context(nc.sbuf_tensor("xtl1", [128, TT], F32))
    x2t0 = s2b.enter_context(nc.sbuf_tensor("x2t0", [128, TT], F32))
    x2t1 = s2b.enter_context(nc.sbuf_tensor("x2t1", [128, TT], F32))
    ytmp = s2b.enter_context(nc.sbuf_tensor("ytmp", [128, 1024], BF16))

    @blk.gpsimd
    def _(g):
        g.wait_ge(sMM.h, MM_ATT)
        g.dma_start(out=apTb[:], in_=apT.ap()).then_inc(sGA.h, 16)
        g.dma_start(out=bpb[:], in_=bp_in.ap()).then_inc(sGA.h, 16)
    sGA.n += 32
    GD_PL = sGA.n

    wp_evs = []
    for vcc in range(8):
        @blk.sync
        def _(s, vcc=vcc):
            s.wait_ge(sMM.h, MM_ATT)
            if vcc >= 1:
                s.wait_ge(sEV.h, wp_evs[-1])
            s.dma_start(out=wptmp[:], in_=wprojT[vcc * 128:(vcc + 1) * 128, :]
                        ).then_inc(sW.h, 16)
        sW.n += 16
        ld_wp = sW.n

        @blk.tensor
        def _(t, vcc=vcc, ld_wp=ld_wp):
            t.wait_ge(sW.h, ld_wp)
            last = None
            for dc in range(8):
                last = t.matmul(psD[:, 24 + dc:25 + dc],
                                wptmp[:, dc * 128:(dc + 1) * 128],
                                abv[:, 8 + vcc:9 + vcc],
                                start=(vcc == 0), stop=(vcc == 7),
                                skip_group_check=True)
            last.then_inc(sMM.h, 1)
        sMM.n += 1
        mm_pb = sMM.n

        @blk.vector
        def _(v, vcc=vcc, mm_pb=mm_pb):
            v.wait_ge(sMM.h, mm_pb)
            last = v.tensor_scalar_mul(wpTs[:, vcc * 1024:(vcc + 1) * 1024],
                                       wptmp[:], abv[:, vcc:vcc + 1])
            if vcc == 7:
                last = v.tensor_copy(pbc[:], psD[:, 24:32])
            last.then_inc(sEV.h, 1)
        sEV.n += 1
        wp_evs.append(sEV.n)

    wmp_evs = []
    for cc in range(8):
        @blk.sync
        def _(s, cc=cc):
            s.wait_ge(sEV.h, wp_evs[-1] if cc == 0 else wmp_evs[-1])
            s.dma_start(out=wptmp[:], in_=wprojT[cc * 128:(cc + 1) * 128, :]
                        ).then_inc(sW.h, 16)
        sW.n += 16
        ld_wp2 = sW.n

        @blk.tensor
        def _(t, cc=cc, ld_wp2=ld_wp2):
            t.wait_ge(sW.h, ld_wp2)
            if cc == 0:
                t.wait_ge(sGA.h, GD_PL)
            last = None
            for rb in range(2):
                last = t.matmul(psA[:, rb * 512:(rb + 1) * 512],
                                bpb[:, cc * 128:(cc + 1) * 128],
                                apTb[:, rb * 512:(rb + 1) * 512],
                                start=True, stop=True, skip_group_check=True)
            last.then_inc(sMM.h, 1)
        sMM.n += 1
        mm_mp = sMM.n

        @blk.vector
        def _(v, cc=cc, mm_mp=mm_mp):
            v.wait_ge(sMM.h, mm_mp)
            v.tensor_tensor(wmpTs[:, cc * 1024:(cc + 1) * 1024], wptmp[:],
                            psA[:, 0:1024], AL.add).then_inc(sEV.h, 1)
        sEV.n += 1
        wmp_evs.append(sEV.n)
    EV_WPM = sEV.n

    fin_evs = []
    for sl in range(16):
        @blk.vector
        def _(v, sl=sl):
            if sl == 0:
                v.wait_ge(sEV.h, EV_ATT)
            v.reciprocal(fc0[:, 7:8], l_sb[:, sl:sl + 1])
            gap(v)
            v.tensor_scalar_mul(ytmp[:], y_sb[:, sl * 1024:(sl + 1) * 1024],
                                fc0[:, 7:8]).then_inc(sEV.h, 1)
        sEV.n += 1
        ev_yt = sEV.n

        @blk.tensor
        def _(t, sl=sl, ev_yt=ev_yt):
            t.wait_ge(sEV.h, ev_yt)
            psB_bf = psB[:].bitcast(BF16)
            last = None
            for vcc in range(8):
                last = t.transpose(psB_bf[:, vcc * 128:(vcc + 1) * 128],
                                   ytmp[:, vcc * 128:(vcc + 1) * 128], idb[:])
            last.then_inc(sMM.h, 1)
        sMM.n += 1
        mm_ytr = sMM.n

        @blk.vector
        def _(v, sl=sl, mm_ytr=mm_ytr):
            v.wait_ge(sMM.h, mm_ytr)
            v.tensor_copy(bass.AP(y_fm, sl * 128, [[8 * NT, 128], [NT, 8], [1, 128]]),
                          psB[:].bitcast(BF16)[:, 0:1024
                          ].rearrange("p (c t) -> p c t", c=8)
                          ).then_inc(sEV.h, 1)
        sEV.n += 1
        fin_evs.append(sEV.n)
    EV_FIN = sEV.n

    p1_scs = []
    for dc in range(8):
        for tt4 in range(4):
            ps = (psA, psB)[(dc * 4 + tt4) % 2]

            @blk.tensor
            def _(t, dc=dc, tt4=tt4, ps=ps):
                if dc == 0 and tt4 == 0:
                    t.wait_ge(sEV.h, EV_FIN)
                    t.wait_ge(sEV.h, EV_WPM)
                if len(p1_scs) >= 2:
                    t.wait_ge(sSC.h, p1_scs[-2])
                last = None
                for vcc in range(8):
                    last = t.matmul(
                        ps[:, 0:512],
                        wpTs[:, vcc * 1024 + dc * 128:vcc * 1024 + (dc + 1) * 128],
                        y_fm[:, vcc * NT + tt4 * 512:vcc * NT + (tt4 + 1) * 512],
                        start=(vcc == 0), stop=(vcc == 7), skip_group_check=True)
                last.then_inc(sMM.h, 1)
            sMM.n += 1
            mm_p1 = sMM.n

            @blk.scalar
            def _(sc, dc=dc, tt4=tt4, ps=ps, mm_p1=mm_p1):
                sc.wait_ge(sMM.h, mm_p1)
                sc.activation(y1[:, dc * NT + tt4 * 512:dc * NT + (tt4 + 1) * 512],
                              ps[:, 0:512], AF.Identity,
                              bias=pbc[:, dc:dc + 1], scale=1.0).then_inc(sSC.h, 1)
            sSC.n += 1
            p1_scs.append(sSC.n)
    SC_P1 = sSC.n

    p2_evs = []
    x2st_lds = []
    for dc2 in range(8):
        for tt4 in range(4):
            idx = dc2 * 4 + tt4
            ps = (psC, psD)[idx % 2]
            xtl = (xtl0, xtl1)[idx % 2]
            x2t = (x2t0, x2t1)[idx % 2]

            sXT = (sXT0, sXT1)[idx % 2]

            @blk.sync
            def _(s, dc2=dc2, tt4=tt4, xtl=xtl, sXT=sXT):
                if len(p2_evs) >= 2:
                    s.wait_ge(sEV.h, p2_evs[-2])
                s.dma_start(out=xtl[:],
                            in_=bass.AP(x_fm, dc2 * 128 * NT + tt4 * 512,
                                        [[NT, 128], [1, 512]])
                            ).then_inc(sXT.h, 16)
            sXT.n += 16
            ld_xt = sXT.n

            @blk.tensor
            def _(t, dc2=dc2, tt4=tt4, ps=ps, idx=idx):
                if idx == 0:
                    t.wait_ge(sSC.h, SC_P1)
                if len(p2_evs) >= 2:
                    t.wait_ge(sEV.h, p2_evs[-2])
                last = None
                for cc in range(8):
                    last = t.matmul(
                        ps[:, 0:512],
                        wmpTs[:, cc * 1024 + dc2 * 128:cc * 1024 + (dc2 + 1) * 128],
                        y1[:, cc * NT + tt4 * 512:cc * NT + (tt4 + 1) * 512],
                        start=(cc == 0), stop=(cc == 7), skip_group_check=True)
                last.then_inc(sMM.h, 1)
            sMM.n += 1
            mm_p2 = sMM.n

            sX2 = (sX2a, sX2b)[idx % 2]

            @blk.vector
            def _(v, ps=ps, xtl=xtl, x2t=x2t, mm_p2=mm_p2, ld_xt=ld_xt,
                  sXT=sXT, sX2=sX2):
                v.wait_ge(sMM.h, mm_p2)
                v.wait_ge(sXT.h, ld_xt)
                if len(x2st_lds) >= 2:
                    v.wait_ge(sX2.h, x2st_lds[-2][1])
                v.tensor_add(x2t[:], ps[:, 0:512], xtl[:]).then_inc(sEV.h, 1)
            sEV.n += 1
            p2_evs.append(sEV.n)
            ev_x2 = sEV.n

            @blk.sync
            def _(s, dc2=dc2, tt4=tt4, x2t=x2t, ev_x2=ev_x2, sX2=sX2):
                s.wait_ge(sEV.h, ev_x2)
                s.dma_start(out=bass.AP(x2_sp, dc2 * 128 * NT + tt4 * 512,
                                        [[NT, 128], [1, 512]]),
                            in_=x2t[:]).then_inc(sX2.h, 16)
            sX2.n += 16
            x2st_lds.append((idx % 2, sX2.n))
    X2A_ALL = sX2a.n
    X2B_ALL = sX2b.n
    EV_P2 = sEV.n

    s2b.close()
    s2.close()

    s4 = ExitStack()
    ln1 = s4.enter_context(nc.sbuf_tensor("ln1", [128, 8 * NT], BF16))
    rowb4 = s4.enter_context(nc.sbuf_tensor("rowb4", [1, 4096], F32))
    wfm = s4.enter_context(nc.sbuf_tensor("wfm", [8, NT], BF16))
    s4b = ExitStack()
    xt4 = s4b.enter_context(nc.sbuf_tensor("xt4", [128, 8 * TT], F32))
    xb4 = s4b.enter_context(nc.sbuf_tensor("xb4", [128, 8 * TT], BF16))
    xsq4 = s4b.enter_context(nc.sbuf_tensor("xsq4", [128, 8 * TT], BF16))
    abc4 = s4b.enter_context(nc.sbuf_tensor("abc4", [128, 2 * TT], F32))
    xtmp4 = s4b.enter_context(nc.sbuf_tensor("xtmp4", [128, TT], F32))
    lgtm = s4b.enter_context(nc.sbuf_tensor("lgtm", [128, 128], F32))
    e8 = s4b.enter_context(nc.sbuf_tensor("e8", [128, 8], F32))
    eq8 = s4b.enter_context(nc.sbuf_tensor("eq8", [128, 8], F32))
    wtm = s4b.enter_context(nc.sbuf_tensor("wtm", [128, 128], F32))

    prev4 = None
    for ti in range(NTILE):
        fw_sync = [(sX2a, X2A_ALL), (sX2b, X2B_ALL)] if ti == 0 else []
        fw_mm = [(sEV, EV_P2)] if ti == 0 else []
        prev4 = emit_ln_tile(
            ti, x2_sp, xt4, xb4, xsq4, abc4, xtmp4, ln1,
            lambda cc, ti_: (cc * NT + ti_ * TT, cc * NT + (ti_ + 1) * TT),
            prev4, fw_sync, fw_mm, None, rowb4)
    EV_LN1 = prev4["ev_ln"]

    gate_evs = []
    for grp in range(2):
        @blk.tensor
        def _(t, grp=grp):
            t.wait_ge(sEV.h, EV_LN1)
            if gate_evs:
                t.wait_ge(sEV.h, gate_evs[-1])
            last = None
            for tc8 in range(8):
                tc = grp * 8 + tc8
                for cc in range(8):
                    last = t.matmul(
                        psD[:, tc8 * 8:(tc8 + 1) * 8],
                        ln1[:, cc * NT + tc * 128:cc * NT + (tc + 1) * 128],
                        gwb[:, cc * 8:(cc + 1) * 8],
                        start=(cc == 0), stop=(cc == 7), skip_group_check=True)
            last.then_inc(sMM.h, 1)
        sMM.n += 1
        mm_g = sMM.n

        @blk.vector
        def _(v, mm_g=mm_g):
            v.wait_ge(sMM.h, mm_g)
            gbb_b = bass.AP(gbb, 0, [[8, 128], [0, 8], [1, 8]])
            lg3 = lgtm[:, 0:64].rearrange("p (tc e) -> p tc e", tc=8)
            ps3 = psD[:, 0:64].rearrange("p (tc e) -> p tc e", tc=8)
            v.tensor_tensor(lg3, ps3, gbb_b, AL.add)
            gap(v).then_inc(sEV.h, 1)
        sEV.n += 1
        ev_lg = sEV.n

        for tc8 in range(8):
            @blk.vector
            def _(v, tc8=tc8, ev_lg=ev_lg):
                v.wait_ge(sEV.h, ev_lg)
                lg = lgtm[:, tc8 * 8:(tc8 + 1) * 8]
                v.reduce_max(sm[:, 0:1], lg, AX)
                v.tensor_copy(sm[:, 4:5], lg[:, 0:1])
                v.tensor_scalar_mul(sm[:, 3:4], sm[:, 0:1], -1.0)
                v.tensor_scalar(eq8[:], lg, sm[:, 0:1], None, AL.is_equal)
                gap(v)
                v.scalar_tensor_tensor(eq8[:], eq8[:], -1.0e30, lg, AL.mult, AL.add)
                gap(v)
                v.reduce_max(sm[:, 1:2], eq8[:], AX)
                gap(v).then_inc(sEV.h, 1)
            sEV.n += 1
            ev_m12 = sEV.n

            @blk.scalar
            def _(sc, tc8=tc8, ev_m12=ev_m12):
                sc.wait_ge(sEV.h, ev_m12)
                sc.activation(e8[:], lgtm[:, tc8 * 8:(tc8 + 1) * 8], AF.Exp,
                              bias=sm[:, 3:4], scale=1.0).then_inc(sSC.h, 1)
            sSC.n += 1
            sc_e8 = sSC.n

            @blk.vector
            def _(v, tc8=tc8, sc_e8=sc_e8):
                v.wait_ge(sSC.h, sc_e8)
                lg = lgtm[:, tc8 * 8:(tc8 + 1) * 8]
                v.reduce_sum(sm[:, 2:3], e8[:], AX)
                v.tensor_scalar(eq8[:], lg, sm[:, 1:2], None, AL.is_ge)
                v.reciprocal(sm[:, 2:3], sm[:, 2:3])
                v.tensor_mul(e8[:], e8[:], eq8[:])
                gap(v)
                v.tensor_scalar_mul(wtm[:, tc8 * 8:(tc8 + 1) * 8], e8[:],
                                    sm[:, 2:3])
                gap(v).then_inc(sEV.h, 1)
            sEV.n += 1
        ev_w = sEV.n

        @blk.tensor
        def _(t, grp=grp, ev_w=ev_w):
            t.wait_ge(sEV.h, ev_w)
            last = None
            for tc8 in range(8):
                last = t.transpose(psC[0:8, tc8 * 128:(tc8 + 1) * 128],
                                   wtm[:, tc8 * 8:(tc8 + 1) * 8], idf[:])
            last.then_inc(sMM.h, 1)
        sMM.n += 1
        mm_wt = sMM.n

        @blk.vector
        def _(v, grp=grp, mm_wt=mm_wt):
            v.wait_ge(sMM.h, mm_wt)
            v.tensor_copy(wfm[0:8, grp * 1024:(grp + 1) * 1024], psC[0:8, 0:1024]
                          ).then_inc(sEV.h, 1)
        sEV.n += 1
        gate_evs.append(sEV.n)
    EV_GATE = sEV.n

    s4b.close()

    s4c = ExitStack()
    hsl = s4c.enter_context(nc.sbuf_tensor("hsl", [128, 16 * 1024], BF16))
    w1p0 = s4c.enter_context(nc.sbuf_tensor("w1p0", [128, 8192], BF16))
    w1p1 = s4c.enter_context(nc.sbuf_tensor("w1p1", [128, 8192], BF16))
    w2p0 = s4c.enter_context(nc.sbuf_tensor("w2p0", [128, 8192], BF16))
    w2p1 = s4c.enter_context(nc.sbuf_tensor("w2p1", [128, 8192], BF16))
    oacc = s4c.enter_context(nc.sbuf_tensor("oacc", [128, 8 * 1024], F32))
    wbc0 = s4c.enter_context(nc.sbuf_tensor("wbc0", [128, 1024], BF16))
    wbc1 = s4c.enter_context(nc.sbuf_tensor("wbc1", [128, 1024], BF16))
    wr0 = s4c.enter_context(nc.sbuf_tensor("wr0", [1, 1024], BF16))
    b2b = s4c.enter_context(nc.sbuf_tensor("b2b", [8, 1024], BF16))

    ffn_e_mm = {}
    mm_f1 = {}
    mm_f2 = {}
    w_gd = {}
    last_mm_wb = [None]

    @blk.gpsimd
    def _(g):
        g.wait_ge(sEV.h, EV_GATE)
        g.dma_start(out=b2b[:], in_=b2_in.ap()).then_inc(sGA.h, 16)
    sGA.n += 16
    GD_B2 = sGA.n

    for th in range(2):
        @blk.sync
        def _(s, th=th):
            s.wait_ge(sEV.h, EV_GATE)
            if th == 1:
                s.wait_ge(sST.h, 16)
            s.dma_start(out=oacc[:].rearrange("p (o t) -> p o t", o=8),
                        in_=bass.AP(x2_sp, th * 1024,
                                    [[NT, 128], [128 * NT, 8], [1, 1024]])
                        ).then_inc(sOI.h, 16)
        sOI.n += 16
        ld_oi = sOI.n

        relu_scs = []
        o_evs = []
        for e in range(8):
            par = e % 2
            wbc = (wbc0, wbc1)[par]

            @blk.gpsimd
            def _(g, e=e, th=th):
                for jh in range(2):
                    if (th, e - 1, jh) in mm_f1:
                        g.wait_ge(sMM.h, mm_f1[(th, e - 1, jh)])
                    elif th == 1 and e == 0:
                        g.wait_ge(sMM.h, mm_f1[(0, 7, jh)])
                    elif th == 0 and e == 0:
                        g.wait_ge(sEV.h, EV_GATE)
                    sWx = (sW10, sW11)[jh]
                    g.dma_start(
                        out=(w1p0, w1p1)[jh][:].rearrange("p (c j) -> p c j", c=8),
                        in_=bass.AP(w1_in, e * C * 2048 + jh * 1024,
                                    [[2048, 128], [128 * 2048, 8], [1, 1024]])
                        ).then_inc(sWx.h, 16)
                    sWx.n += 16
                    w_gd[(th, e, "w1", jh)] = (sWx, sWx.n)
                for oh in range(2):
                    if (th, e - 1, oh) in mm_f2:
                        g.wait_ge(sMM.h, mm_f2[(th, e - 1, oh)])
                    elif th == 1 and e == 0:
                        g.wait_ge(sMM.h, mm_f2[(0, 7, oh)])
                    sWx = (sW20, sW21)[oh]
                    g.dma_start(
                        out=(w2p0, w2p1)[oh][:].rearrange("p (j o) -> p j o", j=16),
                        in_=bass.AP(w2_in, e * 2048 * C + oh * 512,
                                    [[C, 128], [128 * C, 16], [1, 512]])
                        ).then_inc(sWx.h, 16)
                    sWx.n += 16
                    w_gd[(th, e, "w2", oh)] = (sWx, sWx.n)

            @blk.sync
            def _(s, e=e, th=th):
                s.wait_ge(sEV.h, EV_GATE)
                if last_mm_wb[0] is not None:
                    s.wait_ge(sMM.h, last_mm_wb[0])
                s.dma_start(out=wr0[0:1, :],
                            in_=wfm[e:e + 1, th * 1024:(th + 1) * 1024]
                            ).then_inc(sWR.h, 16)
            sWR.n += 16
            ld_wr = sWR.n

            @blk.tensor
            def _(t, e=e, th=th, ld_wr=ld_wr):
                t.wait_ge(sWR.h, ld_wr)
                if relu_scs:
                    t.wait_ge(sSC.h, relu_scs[-1])
                t.matmul(psB[:, 0:512], onerb[:], wr0[0:1, 0:512],
                         start=True, stop=True, skip_group_check=True)
                t.matmul(psB[:, 512:1024], onerb[:], wr0[0:1, 512:1024],
                         start=True, stop=True, skip_group_check=True
                         ).then_inc(sMM.h, 1)
            sMM.n += 1
            mm_wb = sMM.n
            last_mm_wb[0] = mm_wb

            @blk.vector
            def _(v, wbc=wbc, mm_wb=mm_wb):
                v.wait_ge(sMM.h, mm_wb)
                v.tensor_copy(wbc[:], psB[:, 0:1024]).then_inc(sEV.h, 1)
            sEV.n += 1
            ev_wb = sEV.n

            for jc in range(16):
                psH = (psA, psB)[jc % 2]

                @blk.tensor
                def _(t, e=e, th=th, jc=jc, psH=psH, ev_wb=ev_wb):
                    jh = jc // 8
                    jj = jc % 8
                    if jj == 0:
                        wsem, wval = w_gd[(th, e, "w1", jh)]
                        t.wait_ge(wsem.h, wval)
                    if jc <= 1:
                        t.wait_ge(sEV.h, ev_wb)
                    if len(relu_scs) >= 2:
                        t.wait_ge(sSC.h, relu_scs[-2])
                    w1b = (w1p0, w1p1)[jh]
                    last = None
                    for cc in range(8):
                        lhs = w1b[:, cc * 1024 + jj * 128:
                                  cc * 1024 + (jj + 1) * 128]
                        t.matmul(psH[:, 0:512], lhs,
                                 ln1[:, cc * NT + th * 1024:
                                     cc * NT + th * 1024 + 512],
                                 start=(cc == 0), stop=(cc == 7),
                                 skip_group_check=True)
                        last = t.matmul(psH[:, 512:1024], lhs,
                                        ln1[:, cc * NT + th * 1024 + 512:
                                            cc * NT + (th + 1) * 1024],
                                        start=(cc == 0), stop=(cc == 7),
                                        skip_group_check=True)
                    last.then_inc(sMM.h, 1)
                sMM.n += 1
                mm_h = sMM.n
                if jc == 7:
                    mm_f1[(th, e, 0)] = mm_h
                elif jc == 15:
                    mm_f1[(th, e, 1)] = mm_h

                @blk.scalar
                def _(sc, e=e, jc=jc, psH=psH, mm_h=mm_h):
                    sc.wait_ge(sMM.h, mm_h)
                    sc.activation(hsl[:, jc * 1024:(jc + 1) * 1024], psH[:, 0:1024],
                                  AF.Relu, bias=b1c[:, e * 16 + jc:e * 16 + jc + 1],
                                  scale=1.0).then_inc(sSC.h, 1)
                sSC.n += 1
                relu_scs.append(sSC.n)
                sc_r = sSC.n

                @blk.vector
                def _(v, jc=jc, wbc=wbc, sc_r=sc_r):
                    v.wait_ge(sSC.h, sc_r)
                    v.tensor_mul(hsl[:, jc * 1024:(jc + 1) * 1024],
                                 hsl[:, jc * 1024:(jc + 1) * 1024], wbc[:]
                                 ).then_inc(sEV.h, 1)
                sEV.n += 1
            ev_hdone = sEV.n

            for oc in range(8):
                psO = (psC, psD)[oc % 2]

                @blk.tensor
                def _(t, e=e, th=th, oc=oc, psO=psO, ev_hdone=ev_hdone):
                    oh = oc // 4
                    oo = oc % 4
                    if oc == 0:
                        t.wait_ge(sEV.h, ev_hdone)
                    if oo == 0:
                        wsem, wval = w_gd[(th, e, "w2", oh)]
                        t.wait_ge(wsem.h, wval)
                    if len(o_evs) >= 2:
                        t.wait_ge(sEV.h, o_evs[-2])
                    w2b = (w2p0, w2p1)[oh]
                    last = None
                    for jc in range(16):
                        lhs = w2b[:, jc * 512 + oo * 128:
                                  jc * 512 + (oo + 1) * 128]
                        t.matmul(psO[:, 0:512], lhs,
                                 hsl[:, jc * 1024:jc * 1024 + 512],
                                 start=(jc == 0), stop=(jc == 15),
                                 skip_group_check=True)
                        last = t.matmul(psO[:, 512:1024], lhs,
                                        hsl[:, jc * 1024 + 512:(jc + 1) * 1024],
                                        start=(jc == 0), stop=(jc == 15),
                                        skip_group_check=True)
                    last.then_inc(sMM.h, 1)
                sMM.n += 1
                mm_o = sMM.n
                if oc == 3:
                    mm_f2[(th, e, 0)] = mm_o
                elif oc == 7:
                    mm_f2[(th, e, 1)] = mm_o

                @blk.vector
                def _(v, oc=oc, psO=psO, mm_o=mm_o, ld_oi=ld_oi):
                    v.wait_ge(sMM.h, mm_o)
                    v.wait_ge(sOI.h, ld_oi)
                    v.tensor_add(oacc[:, oc * 1024:(oc + 1) * 1024],
                                 oacc[:, oc * 1024:(oc + 1) * 1024],
                                 psO[:, 0:1024]).then_inc(sEV.h, 1)
                sEV.n += 1
                o_evs.append(sEV.n)
            ffn_e_mm[(th, e)] = sMM.n

        for ocp in range(4):
            @blk.tensor
            def _(t, th=th, ocp=ocp):
                if ocp == 0:
                    t.wait_ge(sEV.h, o_evs[-1])
                    t.wait_ge(sGA.h, GD_B2)
                else:
                    t.wait_ge(sEV.h, sEV.n)
                last = None
                for j in range(2):
                    oc = ocp * 2 + j
                    psO = (psC, psD)[j]
                    for hh in range(2):
                        last = t.matmul(
                            psO[:, hh * 512:(hh + 1) * 512],
                            b2b[:, oc * 128:(oc + 1) * 128],
                            wfm[0:8, th * 1024 + hh * 512:
                                th * 1024 + (hh + 1) * 512],
                            start=True, stop=True, skip_group_check=True)
                last.then_inc(sMM.h, 1)
            sMM.n += 1
            mm_b2 = sMM.n

            @blk.vector
            def _(v, ocp=ocp, mm_b2=mm_b2):
                v.wait_ge(sMM.h, mm_b2)
                last = None
                for j in range(2):
                    oc = ocp * 2 + j
                    psO = (psC, psD)[j]
                    last = v.tensor_add(oacc[:, oc * 1024:(oc + 1) * 1024],
                                        oacc[:, oc * 1024:(oc + 1) * 1024],
                                        psO[:, 0:1024])
                last.then_inc(sEV.h, 1)
            sEV.n += 1
        ev_b2 = sEV.n

        @blk.sync
        def _(s, th=th, ev_b2=ev_b2):
            s.wait_ge(sEV.h, ev_b2)
            s.dma_start(out=bass.AP(out_fm, th * 1024,
                                    [[NT, 128], [128 * NT, 8], [1, 1024]]),
                        in_=oacc[:].rearrange("p (o t) -> p o t", o=8)
                        ).then_inc(sST.h, 16)
        sST.n += 16

    @blk.sync
    def _(s):
        s.wait_ge(sST.h, sST.n)

    s4c.close()
    s4.close()
    es.close()
    nc.finalize()
    return nc


_NC_CACHE = {}


def _get_nc():
    if "nc" not in _NC_CACHE:
        _NC_CACHE["nc"] = build_nc()
    return _NC_CACHE["nc"]


def _make_sched(c):
    pA, pB = c, 7 - c
    rows = []
    for g in range(pA):
        sh = g
        rows.append((sh * SHARD + 0, sh * SHARD + VB + 0, 0, 0))
    for g in range(pB):
        sh = 7 - g
        rows.append((sh * SHARD + 1024, sh * SHARD + VB + 1024 * C, 1024, 8))
    assert len(rows) == 7
    tab = np.zeros((7, 20), dtype=np.int32)
    for gi, (ko, vo, qo, sb) in enumerate(rows):
        tab[gi, 0] = ko
        tab[gi, 1] = vo
        tab[gi, 2] = qo
        tab[gi, 3] = sb
        for qs in range(8):
            tab[gi, 4 + qs] = (sb + qs) * 1024
            tab[gi, 12 + qs] = sb + qs
    return tab.reshape(1, 140)


def kernel(L0_attn_x, L0_attn_W_attn, L0_attn_W_proj, L0_attn_lora_attn_A,
           L0_attn_lora_attn_B, L0_attn_lora_proj_A, L0_attn_lora_proj_B,
           L0_attn_bn_gamma, L0_attn_bn_beta, L0_ffn_gate_w, L0_ffn_gate_b,
           L0_ffn_w1, L0_ffn_b1, L0_ffn_w2, L0_ffn_b2):
    x = np.asarray(L0_attn_x, dtype=np.float32)
    cst = np.ascontiguousarray
    wattnT_h = cst(np.asarray(L0_attn_W_attn, np.float32).T)
    aaT_h = cst(np.asarray(L0_attn_lora_attn_A, np.float32).T)
    ba_h = cst(np.asarray(L0_attn_lora_attn_B, np.float32))
    wprojT_h = cst(np.asarray(L0_attn_W_proj, np.float32).T)
    apT_h = cst(np.asarray(L0_attn_lora_proj_A, np.float32).T)
    bp_h = cst(np.asarray(L0_attn_lora_proj_B, np.float32))
    gam = np.asarray(L0_attn_bn_gamma, np.float32)
    bet = np.asarray(L0_attn_bn_beta, np.float32)
    gam_qk_h = cst(gam[0:2048].reshape(16, 128).T)
    bet_qk_h = cst(bet[0:2048].reshape(16, 128).T)
    gam_vr_h = cst(gam[2048:3072].reshape(1, C))
    bet_vr_h = cst(bet[2048:3072].reshape(1, C))
    gwT_h = cst(np.asarray(L0_ffn_gate_w, np.float32).T)
    gb_row_h = cst(np.asarray(L0_ffn_gate_b, np.float32).reshape(1, 8))
    w1_h = cst(np.asarray(L0_ffn_w1, np.float32))
    b1_t_h = cst(np.asarray(L0_ffn_b1, np.float32).reshape(8, 16, 128)
                 .transpose(2, 0, 1).reshape(128, 128))
    w2_h = cst(np.asarray(L0_ffn_w2, np.float32))
    b2_h = cst(np.asarray(L0_ffn_b2, np.float32))

    nc = _get_nc()
    in_maps = []
    for c in range(NCORE):
        a0 = x[0, 1024 * c:1024 * (c + 1)]
        a1 = x[1, 1024 * (7 - c):1024 * (8 - c)]
        x_fm_h = cst(np.concatenate([a0, a1], axis=0).T)
        in_maps.append({
            "x_fm": x_fm_h, "wattnT": wattnT_h, "aaT": aaT_h, "ba_in": ba_h,
            "wprojT": wprojT_h, "apT": apT_h, "bp_in": bp_h,
            "gam_qk": gam_qk_h, "bet_qk": bet_qk_h,
            "gam_vr": gam_vr_h, "bet_vr": bet_vr_h,
            "gwT": gwT_h, "gb_row": gb_row_h,
            "w1_in": w1_h, "b1_t": b1_t_h, "w2_in": w2_h, "b2_in": b2_h,
            "sched": _make_sched(c),
        })
    res = run_bass_kernel_spmd(nc, in_maps, core_ids=list(range(NCORE)))
    out = np.zeros((2, 8192, 1024), dtype=np.float32)
    for c in range(NCORE):
        o = res.results[c]["out_fm"]
        out[0, 1024 * c:1024 * (c + 1)] = o[:, 0:1024].T
        out[1, 1024 * (7 - c):1024 * (8 - c)] = o[:, 1024:2048].T
    return out


if __name__ == "__main__":
    nc = build_nc()
    print("build OK; instructions:", nc.n_instructions())


# version 05
# speedup vs baseline: 1.0040x; 1.0040x over previous
import numpy as np
import ml_dtypes
from contextlib import ExitStack
import concourse.bass as bass
import concourse.bacc as bacc
import concourse.mybir as mybir
from concourse.bass_utils import run_bass_kernel_spmd

F32 = mybir.dt.float32
BF16 = mybir.dt.bfloat16
I32 = mybir.dt.int32
AF = mybir.ActivationFunctionType
AL = mybir.AluOpType
AX = mybir.AxisListType.X

NCORE = 8
NT = 2048
C = 1024
D3 = 3072
KSH = C * NT
VB = KSH
SHARD = KSH + NT * C
TT = 512
NTILE = NT // TT
HBLK = 128 * 8 * 1024
SCALE = 1.0 / 32.0
EPS = 1e-5
NEGBIG = -1.0e9
NTOK = 16384.0


def build_nc():
    nc = bacc.Bacc("TRN2", target_bir_lowering=False, num_devices=NCORE,
                   detect_race_conditions=False)

    x_fm = nc.dram_tensor("x_fm", [C, NT], F32, kind="ExternalInput")
    wattnT = nc.dram_tensor("wattnT", [C, D3], F32, kind="ExternalInput")
    aaT = nc.dram_tensor("aaT", [64, C], F32, kind="ExternalInput")
    ba_in = nc.dram_tensor("ba_in", [64, D3], F32, kind="ExternalInput")
    wprojT = nc.dram_tensor("wprojT", [C, C], F32, kind="ExternalInput")
    apT = nc.dram_tensor("apT", [64, C], F32, kind="ExternalInput")
    bp_in = nc.dram_tensor("bp_in", [64, C], F32, kind="ExternalInput")
    gam_qk = nc.dram_tensor("gam_qk", [128, 16], F32, kind="ExternalInput")
    bet_qk = nc.dram_tensor("bet_qk", [128, 16], F32, kind="ExternalInput")
    gam_vr = nc.dram_tensor("gam_vr", [1, C], F32, kind="ExternalInput")
    bet_vr = nc.dram_tensor("bet_vr", [1, C], F32, kind="ExternalInput")
    gwT = nc.dram_tensor("gwT", [C, 8], F32, kind="ExternalInput")
    gb_row = nc.dram_tensor("gb_row", [1, 8], F32, kind="ExternalInput")
    w1_in = nc.dram_tensor("w1_in", [8, C, 2048], F32, kind="ExternalInput")
    b1_t = nc.dram_tensor("b1_t", [128, 128], F32, kind="ExternalInput")
    w2_in = nc.dram_tensor("w2_in", [8, 2048, C], F32, kind="ExternalInput")
    b2_in = nc.dram_tensor("b2_in", [8, C], F32, kind="ExternalInput")
    sched = nc.dram_tensor("sched", [1, 7 * 20], I32, kind="ExternalInput")

    out_fm = nc.dram_tensor("out_fm", [C, NT], F32, kind="ExternalOutput")

    kv_in = nc.dram_tensor("kv_in", [1, SHARD], BF16)
    kv_all = nc.dram_tensor("kv_all", [NCORE, SHARD], BF16)
    st_in = nc.dram_tensor("st_in", [1, 6144], F32)
    st_out = nc.dram_tensor("st_out", [1, 6144], F32)
    q_dram = nc.dram_tensor("q_dram", [128, 8 * NT], BF16)
    x2_sp = nc.dram_tensor("x2_sp", [C, NT], F32)

    ident_f_d = nc.inline_tensor(np.eye(128, dtype=np.float32), name="identf")
    ident_b_d = nc.inline_tensor(np.eye(128, dtype=ml_dtypes.bfloat16), name="identb")
    tri_np = np.where(np.arange(128)[None, :] <= np.arange(128)[:, None],
                      0.0, NEGBIG).astype(np.float32)
    tri_d = nc.inline_tensor(tri_np, name="trim")
    onescol_b_d = nc.inline_tensor(np.ones((128, 1), dtype=ml_dtypes.bfloat16),
                                   name="onescolb")
    onesrow_f_d = nc.inline_tensor(np.ones((1, 128), dtype=np.float32),
                                   name="onesrowf")
    onesrow_b_d = nc.inline_tensor(np.ones((1, 128), dtype=ml_dtypes.bfloat16),
                                   name="onesrowb")

    psA = nc.alloc_psum_tensor("psA", [128, 1024], F32)
    psB = nc.alloc_psum_tensor("psB", [128, 1024], F32)
    psC = nc.alloc_psum_tensor("psC", [128, 1024], F32)
    psD = nc.alloc_psum_tensor("psD", [128, 1024], F32)

    A = nc.alloc_sbuf_tensor
    idb = A("idb", [128, 128], BF16)
    idf = A("idf", [128, 128], F32)
    trim = A("trim_sb", [128, 128], F32)
    onecb = A("onecb", [128, 1], BF16)
    onerf = A("onerf", [1, 128], F32)
    onerb = A("onerb", [1, 128], BF16)
    stc = A("stc", [128, 32], F32)
    abqk = A("abqk", [128, 32], F32)
    abv = A("abv", [128, 16], F32)
    qab = A("qab", [128, 16], F32)
    gamc = A("gamc", [128, 16], F32)
    betc = A("betc", [128, 16], F32)
    schc = A("schc", [1, 7 * 20], I32)
    m_sb = A("m_sb", [128, 16], F32)
    l_sb = A("l_sb", [128, 16], F32)
    b1c = A("b1c", [128, 128], F32)
    gwb = A("gwb", [128, 64], BF16)
    gbr = A("gbr", [1, 8], F32)
    gbb = A("gbb", [128, 8], F32)
    pbc = A("pbc", [128, 8], F32)
    sm = A("sm", [128, 16], F32)
    tcol = A("tcol", [128, 4], F32)
    dgap = A("dgap", [128, 64], F32)

    def gap(v):
        return v.tensor_copy(dgap[:, 0:1], dgap[:, 1:2])

    class Sem:
        def __init__(self, name):
            self.h = nc.alloc_semaphore(name)
            self.n = 0
    sMM = Sem("sMM")
    sEV = Sem("sEV")
    sSC = Sem("sSC")
    sCC = Sem("sCC")
    sST = Sem("sST")
    sC = Sem("sC")
    sW = Sem("sW")
    sX = Sem("sX")
    sKVk = Sem("sKVk")
    sKVv = Sem("sKVv")
    sVL = Sem("sVL")
    sSTT = Sem("sSTT")
    sSR = Sem("sSR")
    sGV = Sem("sGV")
    sQW = Sem("sQW")
    sGA = Sem("sGA")
    sPL0 = Sem("sPL0")
    sPL1 = Sem("sPL1")
    sPF0 = Sem("sPF0")
    sPF1 = Sem("sPF1")
    sXT0 = Sem("sXT0")
    sXT1 = Sem("sXT1")
    sX2a = Sem("sX2a")
    sX2b = Sem("sX2b")
    sOI = Sem("sOI")
    sWR = Sem("sWR")
    sW10 = Sem("sW10")
    sW11 = Sem("sW11")
    sW20 = Sem("sW20")
    sW21 = Sem("sW21")

    es = ExitStack()
    rg1 = es.enter_context(nc.gpsimd.register("rg1"))
    rg2 = es.enter_context(nc.gpsimd.register("rg2"))
    rg3 = es.enter_context(nc.gpsimd.register("rg3"))
    rv1 = es.enter_context(nc.vector.register("rv1"))
    rs1 = es.enter_context(nc.sync.register("rs1"))
    rc1 = es.enter_context(nc.scalar.register("rc1"))
    blk = es.enter_context(nc.Block("main"))

    s1 = ExitStack()
    wmT = s1.enter_context(nc.sbuf_tensor("wmT", [128, 8 * D3], BF16))
    wtmp = s1.enter_context(nc.sbuf_tensor("wtmp", [128, D3], F32))
    wtmpB = s1.enter_context(nc.sbuf_tensor("wtmpB", [128, D3], F32))
    aatb = s1.enter_context(nc.sbuf_tensor("aatb", [64, C], BF16))
    bab = s1.enter_context(nc.sbuf_tensor("bab", [64, D3], BF16))
    xt = s1.enter_context(nc.sbuf_tensor("xt", [128, 8 * TT], F32))
    xb = s1.enter_context(nc.sbuf_tensor("xb", [128, 8 * TT], BF16))
    xsq = s1.enter_context(nc.sbuf_tensor("xsq", [128, 8 * TT], BF16))
    ln0 = s1.enter_context(nc.sbuf_tensor("ln0", [128, 8 * TT], BF16))
    abc0 = s1.enter_context(nc.sbuf_tensor("abc0", [128, 2 * TT], F32))
    xtmp = s1.enter_context(nc.sbuf_tensor("xtmp", [128, TT], F32))
    qsb = s1.enter_context(nc.sbuf_tensor("qsb", [128, 8 * NT], BF16))
    kslab = s1.enter_context(nc.sbuf_tensor("kslab", [128, 8 * TT], BF16))
    vslab = s1.enter_context(nc.sbuf_tensor("vslab", [128, 4 * 1024], BF16))
    sqs = s1.enter_context(nc.sbuf_tensor("sqs", [128, 4 * TT], BF16))
    vrow = s1.enter_context(nc.sbuf_tensor("vrow", [1, 2 * C], F32))
    rowb = s1.enter_context(nc.sbuf_tensor("rowb", [1, 4096], F32))

    @blk.sync
    def _(s):
        for dst, src in [(idb, ident_b_d), (idf, ident_f_d), (trim, tri_d),
                         (onecb, onescol_b_d), (onerf, onesrow_f_d),
                         (onerb, onesrow_b_d)]:
            s.dma_start(out=dst[:], in_=src.ap()).then_inc(sC.h, 16)
        for dst, src in [(gamc, gam_qk), (betc, bet_qk), (schc, sched),
                         (b1c, b1_t), (gbr, gb_row)]:
            s.dma_start(out=dst[:], in_=src.ap()).then_inc(sC.h, 16)
    sC.n += 176
    LD_CONST = sC.n

    @blk.gpsimd
    def _(g):
        g.dma_start(out=aatb[:], in_=aaT.ap()).then_inc(sGA.h, 16)
        g.dma_start(out=bab[:], in_=ba_in.ap()).then_inc(sGA.h, 16)
        g.dma_start(out=gwb[:].rearrange("p (cc e) -> p cc e", cc=8),
                    in_=bass.AP(gwT, 0, [[8, 128], [1024, 8], [1, 8]])
                    ).then_inc(sGA.h, 16)
    sGA.n += 48
    GD_LORA = sGA.n

    wm_evs = []
    for cc in range(8):
        wt = (wtmp, wtmpB)[cc % 2]

        @blk.sync
        def _(s, cc=cc, wt=wt):
            if cc >= 2:
                s.wait_ge(sEV.h, wm_evs[cc - 2])
            s.dma_start(out=wt[:], in_=wattnT[cc * 128:(cc + 1) * 128, :]
                        ).then_inc(sW.h, 16)
        sW.n += 16
        ld_w = sW.n

        @blk.tensor
        def _(t, cc=cc, ld_w=ld_w):
            if cc == 0:
                t.wait_ge(sGA.h, GD_LORA)
            t.wait_ge(sW.h, ld_w)
            if cc >= 1:
                t.wait_ge(sEV.h, wm_evs[cc - 1])
            last = None
            for s3 in range(3):
                ps = (psA, psB, psC)[s3]
                for rb in range(2):
                    last = t.matmul(
                        ps[:, rb * 512:(rb + 1) * 512],
                        bab[:, s3 * 1024 + cc * 128: s3 * 1024 + (cc + 1) * 128],
                        aatb[:, rb * 512:(rb + 1) * 512],
                        start=True, stop=True, skip_group_check=True)
            last.then_inc(sMM.h, 1)
        sMM.n += 1
        mm_w = sMM.n

        @blk.vector
        def _(v, cc=cc, mm_w=mm_w, wt=wt):
            v.wait_ge(sMM.h, mm_w)
            last = None
            for s3 in range(3):
                ps = (psA, psB, psC)[s3]
                src_w = bass.AP(wt, s3, [[D3, 128], [3, 1024]])
                dst = bass.AP(wmT, cc * D3 + s3, [[8 * D3, 128], [3, 1024]])
                last = v.tensor_tensor(dst, src_w, ps[:, 0:1024], AL.add)
            last.then_inc(sEV.h, 1)
        sEV.n += 1
        wm_evs.append(sEV.n)
    MM_WM = sMM.n

    @blk.vector
    def _(v):
        v.memzero(stc[:])
        v.memzero(l_sb[:])
        v.memzero(dgap[:])
        v.memset(m_sb[:], -1.0e30).then_inc(sEV.h, 1)
    sEV.n += 1
    EV_INIT = sEV.n

    def emit_ln_tile(ti, src_dram, xt_, xb_, xsq_, abc_, xtmp_, ln_dst, dst_off,
                     prev, first_waits_sync, first_waits_mm, writeln_wait_mm,
                     rowb):
        t0 = ti * TT

        @blk.sync
        def _(s):
            for sem, val in first_waits_sync:
                s.wait_ge(sem.h, val)
            if prev is not None:
                s.wait_ge(sEV.h, prev["ev_ln"])
            s.dma_start(out=xt_[:].rearrange("p (c t) -> p c t", c=8),
                        in_=bass.AP(src_dram, t0, [[NT, 128], [128 * NT, 8],
                                                   [1, TT]])
                        ).then_inc(sX.h, 16)
        sX.n += 16
        ld_x = sX.n

        @blk.vector
        def _(v, ld_x=ld_x):
            v.wait_ge(sX.h, ld_x)
            if prev is not None:
                v.wait_ge(sMM.h, prev["mm_stats"])
            v.tensor_copy(xb_[:], xt_[:]).then_inc(sEV.h, 1)
        sEV.n += 1
        ev_xb = sEV.n

        @blk.scalar
        def _(sc, ev_xb=ev_xb):
            sc.wait_ge(sEV.h, ev_xb)
            sc.square(xsq_[:], xb_[:]).then_inc(sSC.h, 1)
        sSC.n += 1
        sc_sq = sSC.n

        @blk.tensor
        def _(t, sc_sq=sc_sq):
            t.wait_ge(sSC.h, sc_sq)
            for sem, val in first_waits_mm:
                t.wait_ge(sem.h, val)
            for cc in range(8):
                t.matmul(psD[0:1, 0:TT], onecb[:], xb_[:, cc * TT:(cc + 1) * TT],
                         start=(cc == 0), stop=(cc == 7), skip_group_check=True)
            last = None
            for cc in range(8):
                last = t.matmul(psD[0:1, 512:512 + TT], onecb[:],
                                xsq_[:, cc * TT:(cc + 1) * TT],
                                start=(cc == 0), stop=(cc == 7),
                                skip_group_check=True)
            last.then_inc(sMM.h, 1)
        sMM.n += 1
        mm_st = sMM.n

        MU, EX2, TMP, RR, NB = 0, TT, 2 * TT, 3 * TT, 2048 + 1024

        @blk.vector
        def _(v, mm_st=mm_st):
            v.wait_ge(sMM.h, mm_st)
            v.tensor_scalar_mul(rowb[0:1, MU:MU + TT], psD[0:1, 0:TT], 1.0 / C)
            v.tensor_scalar_mul(rowb[0:1, EX2:EX2 + TT], psD[0:1, 512:512 + TT],
                                1.0 / C).then_inc(sEV.h, 1)
        sEV.n += 1
        ev_mu = sEV.n

        @blk.scalar
        def _(sc, ev_mu=ev_mu):
            sc.wait_ge(sEV.h, ev_mu)
            sc.square(rowb[0:1, TMP:TMP + TT], rowb[0:1, MU:MU + TT]
                      ).then_inc(sSC.h, 1)
        sSC.n += 1
        sc_mu2 = sSC.n

        @blk.vector
        def _(v, sc_mu2=sc_mu2):
            v.wait_ge(sSC.h, sc_mu2)
            v.tensor_sub(rowb[0:1, TMP:TMP + TT], rowb[0:1, EX2:EX2 + TT],
                         rowb[0:1, TMP:TMP + TT])
            v.tensor_scalar_add(rowb[0:1, TMP:TMP + TT], rowb[0:1, TMP:TMP + TT],
                                EPS).then_inc(sEV.h, 1)
        sEV.n += 1
        ev_var = sEV.n

        @blk.scalar
        def _(sc, ev_var=ev_var):
            sc.wait_ge(sEV.h, ev_var)
            sc.sqrt(rowb[0:1, TMP:TMP + TT], rowb[0:1, TMP:TMP + TT]
                    ).then_inc(sSC.h, 1)
        sSC.n += 1
        sc_sd = sSC.n

        @blk.vector
        def _(v, sc_sd=sc_sd):
            v.wait_ge(sSC.h, sc_sd)
            v.reciprocal(rowb[0:1, RR:RR + TT], rowb[0:1, TMP:TMP + TT])
            v.tensor_mul(rowb[0:1, NB:NB + TT], rowb[0:1, MU:MU + TT],
                         rowb[0:1, RR:RR + TT])
            v.tensor_scalar_mul(rowb[0:1, NB:NB + TT], rowb[0:1, NB:NB + TT],
                                -1.0).then_inc(sEV.h, 1)
        sEV.n += 1
        ev_r = sEV.n

        @blk.tensor
        def _(t, ev_r=ev_r):
            t.wait_ge(sEV.h, ev_r)
            t.matmul(psD[:, 0:TT], onerf[:], rowb[0:1, RR:RR + TT],
                     start=True, stop=True, skip_group_check=True)
            t.matmul(psD[:, 512:512 + TT], onerf[:], rowb[0:1, NB:NB + TT],
                     start=True, stop=True, skip_group_check=True
                     ).then_inc(sMM.h, 1)
        sMM.n += 1
        mm_bc = sMM.n

        @blk.vector
        def _(v, mm_bc=mm_bc, writeln_wait_mm=writeln_wait_mm):
            v.wait_ge(sMM.h, mm_bc)
            if writeln_wait_mm is not None:
                v.wait_ge(sMM.h, writeln_wait_mm)
            v.tensor_copy(abc_[:, 0:TT], psD[:, 0:TT])
            v.tensor_copy(abc_[:, TT:2 * TT], psD[:, 512:512 + TT])
            last = None
            for cc in range(8):
                v.tensor_mul(xtmp_[:], xt_[:, cc * TT:(cc + 1) * TT], abc_[:, 0:TT])
                o0, o1 = dst_off(cc, ti)
                last = v.tensor_add(ln_dst[:, o0:o1], xtmp_[:], abc_[:, TT:2 * TT])
            last.then_inc(sEV.h, 1)
        sEV.n += 1
        return {"ev_ln": sEV.n, "mm_stats": mm_st}

    prev_ln = None
    qkv_evs = []
    qkv_scs = []
    kst_lds = []
    vst_lds = []
    last_vproj_mm = None
    last_vev_sc = None
    for ti in range(NTILE):
        t0 = ti * TT
        fw_sync = [(sC, LD_CONST)] if ti == 0 else []
        fw_mm = [(sMM, MM_WM)] if ti == 0 else []
        prev_ln = emit_ln_tile(
            ti, x_fm, xt, xb, xsq, abc0, xtmp, ln0,
            lambda cc, ti_: (cc * TT, (cc + 1) * TT),
            prev_ln, fw_sync, fw_mm,
            last_vproj_mm, rowb)
        ev_ln = prev_ln["ev_ln"]

        for half in range(4):
            dcs = [half * 4 + i for i in range(4)]

            @blk.tensor
            def _(t, dcs=dcs, ev_ln=ev_ln):
                t.wait_ge(sEV.h, ev_ln)
                if qkv_scs:
                    t.wait_ge(sSC.h, qkv_scs[-1])
                if qkv_evs:
                    t.wait_ge(sEV.h, qkv_evs[-1])
                last = None
                for i, dc in enumerate(dcs):
                    ps = (psA, psB)[i // 2]
                    off = (i % 2) * 512
                    for cc in range(8):
                        last = t.matmul(
                            ps[:, off:off + TT],
                            wmT[:, cc * D3 + dc * 128:cc * D3 + (dc + 1) * 128],
                            ln0[:, cc * TT:(cc + 1) * TT],
                            start=(cc == 0), stop=(cc == 7), skip_group_check=True)
                last.then_inc(sMM.h, 1)
            sMM.n += 1
            mm_qk = sMM.n

            @blk.scalar
            def _(sc, dcs=dcs, mm_qk=mm_qk, ti=ti, t0=t0):
                sc.wait_ge(sMM.h, mm_qk)
                if dcs[0] >= 8 and kst_lds:
                    sc.wait_ge(sKVk.h, kst_lds[-1])
                for i, dc in enumerate(dcs):
                    ps = (psA, psB)[i // 2]
                    off = (i % 2) * 512
                    if dc < 8:
                        dst = qsb[:, dc * NT + t0: dc * NT + t0 + TT]
                    else:
                        dst = kslab[:, (dc - 8) * TT:(dc - 7) * TT]
                    sc.activation(dst, ps[:, off:off + TT], AF.Copy,
                                  bias=0.0, scale=1.0)
                    sc.square(sqs[:, i * TT:(i + 1) * TT], ps[:, off:off + TT]
                              ).then_inc(sSC.h, 1)
            sSC.n += 4
            sc_e = sSC.n

            @blk.vector
            def _(v, dcs=dcs, sc_e=sc_e, mm_qk=mm_qk, ti=ti, t0=t0):
                last = None
                for i, dc in enumerate(dcs):
                    v.wait_ge(sSC.h, sc_e - 3 + i)
                    if dc < 8:
                        srcv = qsb[:, dc * NT + t0: dc * NT + t0 + TT]
                    else:
                        srcv = kslab[:, (dc - 8) * TT:(dc - 7) * TT]
                    v.reduce_sum(tcol[:, 0:1], srcv, AX)
                    v.reduce_sum(tcol[:, 1:2], sqs[:, i * TT:(i + 1) * TT], AX)
                    v.tensor_add(stc[:, dc:dc + 1], stc[:, dc:dc + 1], tcol[:, 0:1])
                    last = v.tensor_add(stc[:, 16 + dc:17 + dc],
                                        stc[:, 16 + dc:17 + dc], tcol[:, 1:2])
                last.then_inc(sEV.h, 1)
            sEV.n += 1
            qkv_evs.append(sEV.n)
            qkv_scs.append(sc_e)

        @blk.sync
        def _(s, ti=ti, t0=t0):
            s.wait_ge(sSC.h, qkv_scs[-1])
            s.dma_start(out=bass.AP(kv_in, (ti // 2) * HBLK + (ti % 2) * 512,
                                    [[8192, 128], [1024, 8], [1, 512]]),
                        in_=kslab[:].rearrange("p (c t) -> p c t", c=8)
                        ).then_inc(sKVk.h, 16)
        sKVk.n += 16
        kst_lds.append(sKVk.n)

        vev_scs = []
        for tc in range(4):
            @blk.tensor
            def _(t, tc=tc, ev_ln=ev_ln):
                if tc == 0:
                    t.wait_ge(sEV.h, ev_ln)
                    if last_vev_sc is not None:
                        t.wait_ge(sSC.h, last_vev_sc)
                else:
                    t.wait_ge(sSC.h, vev_scs[-1])
                last = None
                for cc in range(8):
                    t.matmul(psC[:, 0:512],
                             ln0[:, cc * TT + tc * 128:cc * TT + (tc + 1) * 128],
                             wmT[:, cc * D3 + 2048:cc * D3 + 2560],
                             start=(cc == 0), stop=(cc == 7), skip_group_check=True)
                for cc in range(8):
                    last = t.matmul(
                        psC[:, 512:1024],
                        ln0[:, cc * TT + tc * 128:cc * TT + (tc + 1) * 128],
                        wmT[:, cc * D3 + 2560:cc * D3 + 3072],
                        start=(cc == 0), stop=(cc == 7), skip_group_check=True)
                last.then_inc(sMM.h, 1)
            sMM.n += 1
            mm_v = sMM.n
            last_vproj_mm = mm_v

            @blk.scalar
            def _(sc, tc=tc, mm_v=mm_v):
                sc.wait_ge(sMM.h, mm_v)
                if tc == 0 and vst_lds:
                    sc.wait_ge(sKVv.h, vst_lds[-1])
                sc.activation(vslab[:, tc * 1024:(tc + 1) * 1024], psC[:, 0:1024],
                              AF.Copy, bias=0.0, scale=1.0).then_inc(sSC.h, 1)
            sSC.n += 1
            vev_scs.append(sSC.n)
        last_vev_sc = vev_scs[-1]

        @blk.sync
        def _(s, ti=ti, t0=t0, w=vev_scs[-1]):
            s.wait_ge(sSC.h, w)
            s.dma_start(out=bass.AP(kv_in, VB + (ti // 2) * HBLK + (ti % 2) * 4096,
                                    [[8192, 128], [1024, 4], [1, 1024]]),
                        in_=vslab[:].rearrange("p (tc v) -> p tc v", tc=4)
                        ).then_inc(sKVv.h, 16)
        sKVv.n += 16
        vst_lds.append(sKVv.n)
    KVK_ALL = sKVk.n
    KVV_ALL = sKVv.n

    vst_mms = []
    for ch in range(16):
        @blk.sync
        def _(s, ch=ch):
            if ch == 0:
                s.wait_ge(sKVk.h, KVK_ALL)
                s.wait_ge(sKVv.h, KVV_ALL)
                s.wait_ge(sEV.h, qkv_evs[-1])
            if len(vst_mms) >= 1:
                s.wait_ge(sMM.h, vst_mms[-1])
            s.dma_start(out=sqs[:, 0:1024],
                        in_=bass.AP(kv_in, VB + (ch // 8) * HBLK + (ch % 8) * 1024,
                                    [[8192, 128], [1, 1024]])
                        ).then_inc(sVL.h, 16)
        sVL.n += 16
        ld_v = sVL.n

        @blk.scalar
        def _(sc, ch=ch, ld_v=ld_v):
            sc.wait_ge(sVL.h, ld_v)
            sc.square(xsq[:, 0:1024], sqs[:, 0:1024]).then_inc(sSC.h, 1)
        sSC.n += 1
        sc_v2 = sSC.n

        @blk.tensor
        def _(t, ch=ch, sc_v2=sc_v2):
            t.wait_ge(sSC.h, sc_v2)
            last = None
            for hh in range(2):
                t.matmul(psD[0:1, hh * 512:(hh + 1) * 512], onecb[:],
                         sqs[:, hh * 512:(hh + 1) * 512],
                         start=(ch == 0), stop=(ch == 15), skip_group_check=True)
                last = t.matmul(psC[0:1, hh * 512:(hh + 1) * 512], onecb[:],
                                xsq[:, hh * 512:(hh + 1) * 512],
                                start=(ch == 0), stop=(ch == 15),
                                skip_group_check=True)
            last.then_inc(sMM.h, 1)
        sMM.n += 1
        vst_mms.append(sMM.n)
    MM_VST = sMM.n

    @blk.vector
    def _(v):
        v.wait_ge(sMM.h, MM_VST)
        v.tensor_copy(vrow[0:1, 0:C], psD[0:1, 0:C])
        v.tensor_copy(vrow[0:1, C:2 * C], psC[0:1, 0:C]).then_inc(sEV.h, 1)
    sEV.n += 1
    EV_VROW = sEV.n

    @blk.sync
    def _(s):
        s.wait_ge(sEV.h, EV_VROW)
        s.dma_start(out=bass.AP(st_in, 0, [[32, 128], [1, 32]]), in_=stc[:]
                    ).then_inc(sSTT.h, 16)
        s.dma_start(out=st_in[0:1, 4096:6144], in_=vrow[0:1, :]).then_inc(sSTT.h, 16)
    sSTT.n += 32
    LD_STW = sSTT.n

    @blk.gpsimd
    def _(g):
        g.wait_ge(sSTT.h, LD_STW)
        g.collective_compute("AllReduce", AL.add, replica_groups=[list(range(NCORE))],
                             ins=[st_in.ap().opt()], outs=[st_out.ap().opt()]
                             ).then_inc(sCC.h, 1)
        g.wait_ge(sKVk.h, KVK_ALL)
        g.wait_ge(sKVv.h, KVV_ALL)
        g.collective_compute("AllGather", AL.bypass,
                             replica_groups=[list(range(NCORE))],
                             ins=[kv_in.ap().opt()], outs=[kv_all.ap().opt()]
                             ).then_inc(sCC.h, 1)
    sCC.n += 2

    @blk.sync
    def _(s):
        s.wait_ge(sCC.h, 1)
        s.dma_start(out=stc[:], in_=bass.AP(st_out, 0, [[32, 128], [1, 32]])
                    ).then_inc(sSR.h, 16)
        s.dma_start(out=vrow[0:1, :], in_=st_out[0:1, 4096:6144]).then_inc(sSR.h, 16)
    sSR.n += 32
    GD_STR = sSR.n

    @blk.sync
    def _(s):
        s.wait_ge(sCC.h, 1)
        s.dma_start(out=rowb[0:1, 2048:3072], in_=gam_vr.ap()).then_inc(sGV.h, 16)
        s.dma_start(out=rowb[0:1, 3072:4096], in_=bet_vr.ap()).then_inc(sGV.h, 16)
    sGV.n += 32
    ld_gv = sGV.n

    @blk.vector
    def _(v):
        v.wait_ge(sSR.h, GD_STR)
        v.tensor_scalar_mul(stc[:, 0:16], stc[:, 0:16], 1.0 / NTOK)
        v.tensor_scalar_mul(stc[:, 16:32], stc[:, 16:32], 1.0 / NTOK)
        v.tensor_scalar_mul(vrow[0:1, 0:C], vrow[0:1, 0:C], 1.0 / NTOK)
        v.tensor_mul(sm[:, 0:16], stc[:, 0:16], stc[:, 0:16])
        v.tensor_scalar_mul(vrow[0:1, C:2 * C], vrow[0:1, C:2 * C], 1.0 / NTOK)
        v.tensor_sub(abqk[:, 0:16], stc[:, 16:32], sm[:, 0:16])
        gap(v)
        v.tensor_scalar_add(abqk[:, 0:16], abqk[:, 0:16], EPS)
        gap(v).then_inc(sEV.h, 1)
    sEV.n += 1
    ev1 = sEV.n

    @blk.scalar
    def _(sc, ev1=ev1):
        sc.wait_ge(sEV.h, ev1)
        sc.sqrt(abqk[:, 0:16], abqk[:, 0:16])
        sc.square(rowb[0:1, 0:C], vrow[0:1, 0:C]).then_inc(sSC.h, 1)
    sSC.n += 1
    sc1 = sSC.n

    @blk.vector
    def _(v, sc1=sc1):
        v.wait_ge(sSC.h, sc1)
        v.reciprocal(abqk[:, 0:16], abqk[:, 0:16])
        v.tensor_sub(rowb[0:1, 0:C], vrow[0:1, C:2 * C], rowb[0:1, 0:C])
        v.tensor_mul(abqk[:, 0:16], abqk[:, 0:16], gamc[:])
        v.tensor_scalar_add(rowb[0:1, 0:C], rowb[0:1, 0:C], EPS)
        v.tensor_mul(abqk[:, 16:32], stc[:, 0:16], abqk[:, 0:16])
        gap(v)
        v.tensor_sub(abqk[:, 16:32], betc[:], abqk[:, 16:32])
        gap(v).then_inc(sEV.h, 1)
    sEV.n += 1
    ev2 = sEV.n

    @blk.scalar
    def _(sc, ev2=ev2):
        sc.wait_ge(sEV.h, ev2)
        sc.sqrt(rowb[0:1, 0:C], rowb[0:1, 0:C]).then_inc(sSC.h, 1)
    sSC.n += 1
    sc2 = sSC.n

    @blk.vector
    def _(v, sc2=sc2, ld_gv=ld_gv):
        v.wait_ge(sSC.h, sc2)
        v.wait_ge(sGV.h, ld_gv)
        v.reciprocal(rowb[0:1, 0:C], rowb[0:1, 0:C])
        v.tensor_mul(rowb[0:1, 0:C], rowb[0:1, 0:C], rowb[0:1, 2048:3072])
        v.tensor_mul(rowb[0:1, 1024:2048], vrow[0:1, 0:C], rowb[0:1, 0:C])
        v.tensor_sub(rowb[0:1, 1024:2048], rowb[0:1, 3072:4096],
                     rowb[0:1, 1024:2048])
        v.tensor_mul(qab[:, 0:8], abqk[:, 0:8], abqk[:, 8:16])
        v.tensor_mul(qab[:, 8:16], abqk[:, 16:24], abqk[:, 8:16]
                     ).then_inc(sEV.h, 1)
    sEV.n += 1
    EV_STATS = sEV.n

    @blk.tensor
    def _(t):
        t.wait_ge(sEV.h, EV_STATS)
        last = None
        for vcc in range(8):
            t.matmul(psD[:, vcc:vcc + 1], rowb[0:1, vcc * 128:(vcc + 1) * 128],
                     onerf[0:1, 0:1], start=True, stop=True, skip_group_check=True)
            last = t.matmul(psD[:, 8 + vcc:9 + vcc],
                            rowb[0:1, 1024 + vcc * 128:1024 + (vcc + 1) * 128],
                            onerf[0:1, 0:1], start=True, stop=True,
                            skip_group_check=True)
        last = t.matmul(psD[:, 16:24], onerf[:], gbr[0:1, 0:8],
                        start=True, stop=True, skip_group_check=True)
        last.then_inc(sMM.h, 1)
    sMM.n += 1
    mm_col = sMM.n

    @blk.vector
    def _(v, mm_col=mm_col):
        v.wait_ge(sMM.h, mm_col)
        v.tensor_copy(abv[:], psD[:, 0:16])
        v.tensor_copy(gbb[:], psD[:, 16:24])
        last = None
        for cc in range(8):
            v.tensor_scalar_mul(qsb[:, cc * NT:(cc + 1) * NT],
                                qsb[:, cc * NT:(cc + 1) * NT], qab[:, cc:cc + 1])
            last = v.tensor_scalar_add(qsb[:, cc * NT:(cc + 1) * NT],
                                       qsb[:, cc * NT:(cc + 1) * NT],
                                       qab[:, 8 + cc:9 + cc])
        last.then_inc(sEV.h, 1)
    sEV.n += 1
    EV_QAFF = sEV.n

    @blk.sync
    def _(s):
        s.wait_ge(sEV.h, EV_QAFF)
        for h in range(2):
            s.dma_start(out=bass.AP(q_dram, h * HBLK,
                                    [[8192, 128], [1024, 8], [1, 1024]]),
                        in_=bass.AP(qsb, h * 1024,
                                    [[16384, 128], [2048, 8], [1, 1024]])
                        ).then_inc(sQW.h, 16)
    sQW.n += 32
    LD_QW = sQW.n

    s1.close()

    s2 = ExitStack()
    y_sb = s2.enter_context(nc.sbuf_tensor("y_sb", [128, 16 * 1024], F32))
    p_sb0 = s2.enter_context(nc.sbuf_tensor("p_sb0", [128, 1024], BF16))
    p_sb1 = s2.enter_context(nc.sbuf_tensor("p_sb1", [128, 1024], BF16))
    pT0 = s2.enter_context(nc.sbuf_tensor("pT0", [128, 1024], BF16))
    pT1 = s2.enter_context(nc.sbuf_tensor("pT1", [128, 1024], BF16))
    pvsb = s2.enter_context(nc.sbuf_tensor("pvsb", [128, 1024], F32))
    y_st = s2.enter_context(nc.sbuf_tensor("y_st", [128, 1024], F32))
    fc0 = s2.enter_context(nc.sbuf_tensor("fc0", [128, 8], F32))
    fc1 = s2.enter_context(nc.sbuf_tensor("fc1", [128, 8], F32))

    s2a = ExitStack()
    qcur0 = s2a.enter_context(nc.sbuf_tensor("qcur0", [128, 8192], BF16))
    qcur1 = s2a.enter_context(nc.sbuf_tensor("qcur1", [128, 8192], BF16))
    kb0 = s2a.enter_context(nc.sbuf_tensor("kb0", [128, 8192], BF16))
    kb1 = s2a.enter_context(nc.sbuf_tensor("kb1", [128, 8192], BF16))
    vb0 = s2a.enter_context(nc.sbuf_tensor("vb0", [128, 8192], BF16))
    vb1 = s2a.enter_context(nc.sbuf_tensor("vb1", [128, 8192], BF16))

    @blk.vector
    def _(v):
        v.wait_ge(sEV.h, EV_QAFF)
        v.memzero(y_sb[:]).then_inc(sEV.h, 1)
    sEV.n += 1
    EV_YZERO = sEV.n

    pairs = [("diagA",), ("diagB",)] + [("full", gi) for gi in range(7)]
    pair_done_mm = []
    pair_ld_gd = []
    sPL = (sPL0, sPL1)
    sPF = (sPF0, sPF1)
    last_exp_sc = {}
    last_tr_mm = {}
    last_pTe_ev = [None]
    last_pve_sc = [None]

    for ip, pdesc in enumerate(pairs):
        par = ip % 2
        qcur = (qcur0, qcur1)[par]
        kb = (kb0, kb1)[par]
        vb = (vb0, vb1)[par]
        is_diag = pdesc[0] in ("diagA", "diagB")
        sbase = {"diagA": 0, "diagB": 8}.get(pdesc[0])
        sLD_p = (sPL if is_diag else sPF)[ip % 2]

        if is_diag:
            h_d = 0 if pdesc[0] == "diagA" else 1

            @blk.sync
            def _(s, ip=ip, qcur=qcur, vb=vb, h_d=h_d):
                if ip >= 2:
                    s.wait_ge(sMM.h, pair_done_mm[ip - 2])
                s.wait_ge(sQW.h, LD_QW)
                s.dma_start(out=qcur[:],
                            in_=bass.AP(q_dram, h_d * HBLK,
                                        [[8192, 128], [1, 8192]])
                            ).then_inc(sLD_p.h, 16)
                s.wait_ge(sKVv.h, KVV_ALL)
                s.dma_start(out=vb[:],
                            in_=bass.AP(kv_in, VB + h_d * HBLK,
                                        [[8192, 128], [1, 8192]])
                            ).then_inc(sLD_p.h, 16)

            @blk.scalar
            def _(sc, ip=ip, kb=kb, h_d=h_d):
                if ip >= 2:
                    sc.wait_ge(sMM.h, pair_done_mm[ip - 2])
                sc.wait_ge(sKVk.h, KVK_ALL)
                sc.wait_ge(sQW.h, LD_QW)
                sc.dma_start(out=kb[:],
                             in_=bass.AP(kv_in, h_d * HBLK,
                                         [[8192, 128], [1, 8192]])
                             ).then_inc(sLD_p.h, 16)
        else:
            gi = pdesc[1]
            base = gi * 20

            @blk.gpsimd
            def _(g, ip=ip, qcur=qcur, kb=kb, vb=vb, base=base):
                if ip >= 2:
                    g.wait_ge(sMM.h, pair_done_mm[ip - 2])
                g.wait_ge(sCC.h, 2)
                g.wait_ge(sQW.h, LD_QW)
                g.reg_load(rg1, schc[0:1, base:base + 1])
                koff = g.snap(rg1, min_val=0, max_val=7 * SHARD + HBLK)
                g.reg_load(rg2, schc[0:1, base + 1:base + 2])
                voff = g.snap(rg2, min_val=0, max_val=7 * SHARD + VB + HBLK)
                g.reg_load(rg3, schc[0:1, base + 2:base + 3])
                qoff = g.snap(rg3, min_val=0, max_val=HBLK)
                g.dma_start(out=qcur[:],
                            in_=bass.AP(q_dram, qoff, [[8192, 128], [1, 8192]])
                            ).then_inc(sLD_p.h, 16)
                g.dma_start(out=kb[:],
                            in_=bass.AP(kv_all, koff, [[8192, 128], [1, 8192]])
                            ).then_inc(sLD_p.h, 16)
                g.dma_start(out=vb[:],
                            in_=bass.AP(kv_all, voff, [[8192, 128], [1, 8192]])
                            ).then_inc(sLD_p.h, 16)
        sLD_p.n += 48
        pair_ld_gd.append((sLD_p, sLD_p.n))

        for qs in range(8):
            nkt = (qs + 1) if is_diag else 8
            W = nkt * 128
            psS = (psA, psB)[qs % 2]
            p_s = (p_sb0, p_sb1)[qs % 2]
            pT_s = (pT0, pT1)[qs % 2]
            fc = (fc0, fc1)[qs % 2]

            @blk.tensor
            def _(t, ip=ip, qs=qs, W=W, psS=psS, qcur=qcur, kb=kb):
                if qs == 0:
                    ld_sem, ld_val = pair_ld_gd[ip]
                    t.wait_ge(ld_sem.h, ld_val)
                    if ip == 0:
                        t.wait_ge(sEV.h, EV_YZERO)
                key = qs % 2
                if key in last_exp_sc:
                    t.wait_ge(sSC.h, last_exp_sc[key])
                nhal = (W + 511) // 512
                last = None
                for cc in range(8):
                    for h in range(nhal):
                        w0 = h * 512
                        w1 = min(W, w0 + 512)
                        last = t.matmul(
                            psS[:, w0:w1],
                            qcur[:, cc * 1024 + qs * 128:cc * 1024 + (qs + 1) * 128],
                            kb[:, cc * 1024 + w0:cc * 1024 + w1],
                            start=(cc == 0), stop=(cc == 7), skip_group_check=True)
                last.then_inc(sMM.h, 1)
            sMM.n += 1
            mm_s = sMM.n

            @blk.vector
            def _(v, qs=qs, W=W, psS=psS, fc=fc, is_diag=is_diag, sbase=sbase,
                  pdesc=pdesc, mm_s=mm_s):
                v.wait_ge(sMM.h, mm_s)
                if is_diag:
                    v.tensor_add(psS[:, qs * 128:(qs + 1) * 128],
                                 psS[:, qs * 128:(qs + 1) * 128], trim[:])
                    sl = sbase + qs
                    v.tensor_copy(fc[:, 0:1], m_sb[:, sl:sl + 1])
                    v.reduce_max(fc[:, 1:2], psS[:, 0:W], AX)
                    v.tensor_copy(fc[:, 5:6], l_sb[:, sl:sl + 1])
                    v.tensor_copy(y_st[:], y_sb[:, sl * 1024:(sl + 1) * 1024])
                else:
                    base = pdesc[1] * 20
                    v.reduce_max(fc[:, 1:2], psS[:, 0:W], AX)
                    v.reg_load(rv1, schc[0:1, base + 12 + qs:base + 13 + qs])
                    mls = v.snap(rv1, min_val=0, max_val=15)
                    v.tensor_copy(fc[:, 0:1], m_sb[:, bass.ds(mls, 1)])
                    v.tensor_copy(fc[:, 5:6], l_sb[:, bass.ds(mls, 1)])
                    v.reg_load(rv1, schc[0:1, base + 4 + qs:base + 5 + qs])
                    ysl = v.snap(rv1, min_val=0, max_val=15 * 1024)
                    v.tensor_copy(y_st[:], y_sb[:, bass.ds(ysl, 1024)])
                v.tensor_scalar_mul(fc[:, 1:2], fc[:, 1:2], SCALE)
                gap(v)
                v.tensor_max(fc[:, 2:3], fc[:, 0:1], fc[:, 1:2])
                gap(v)
                v.tensor_scalar_mul(fc[:, 3:4], fc[:, 2:3], -1.0)
                v.tensor_sub(fc[:, 7:8], fc[:, 0:1], fc[:, 2:3])
                gap(v).then_inc(sEV.h, 1)
            sEV.n += 1
            ev_mn = sEV.n

            @blk.scalar
            def _(sc, qs=qs, W=W, psS=psS, p_s=p_s, fc=fc, ev_mn=ev_mn):
                sc.wait_ge(sEV.h, ev_mn)
                key = qs % 2
                if key in last_tr_mm:
                    sc.wait_ge(sMM.h, last_tr_mm[key])
                sc.activation(fc[:, 4:5], fc[:, 7:8], AF.Exp, bias=0.0, scale=1.0)
                sc.activation(p_s[:, 0:W], psS[:, 0:W], AF.Exp,
                              bias=fc[:, 3:4], scale=SCALE).then_inc(sSC.h, 1)
            sSC.n += 1
            last_exp_sc[qs % 2] = sSC.n
            sc_p = sSC.n

            @blk.tensor
            def _(t, qs=qs, nkt=nkt, p_s=p_s, sc_p=sc_p):
                t.wait_ge(sSC.h, sc_p)
                if last_pTe_ev[0] is not None:
                    t.wait_ge(sEV.h, last_pTe_ev[0])
                psD_bf = psD[:].bitcast(BF16)
                last = None
                for tc in range(nkt):
                    last = t.transpose(psD_bf[:, tc * 128:(tc + 1) * 128],
                                       p_s[:, tc * 128:(tc + 1) * 128], idb[:])
                last.then_inc(sMM.h, 1)
            sMM.n += 1
            last_tr_mm[qs % 2] = sMM.n
            mm_t = sMM.n

            @blk.vector
            def _(v, qs=qs, W=W, p_s=p_s, pT_s=pT_s, fc=fc, is_diag=is_diag,
                  sbase=sbase, pdesc=pdesc, sc_p=sc_p, mm_t=mm_t):
                v.wait_ge(sSC.h, sc_p)
                v.reduce_sum(fc[:, 6:7], p_s[:, 0:W], AX)
                if is_diag:
                    sl = sbase + qs
                    v.tensor_copy(m_sb[:, sl:sl + 1], fc[:, 2:3])
                    v.scalar_tensor_tensor(fc[:, 5:6], fc[:, 5:6], fc[:, 4:5],
                                           fc[:, 6:7], AL.mult, AL.add)
                    gap(v)
                    v.tensor_copy(l_sb[:, sl:sl + 1], fc[:, 5:6])
                else:
                    base = pdesc[1] * 20
                    v.reg_load(rv1, schc[0:1, base + 12 + qs:base + 13 + qs])
                    mls = v.snap(rv1, min_val=0, max_val=15)
                    v.tensor_copy(m_sb[:, bass.ds(mls, 1)], fc[:, 2:3])
                    v.scalar_tensor_tensor(fc[:, 5:6], fc[:, 5:6], fc[:, 4:5],
                                           fc[:, 6:7], AL.mult, AL.add)
                    gap(v)
                    v.tensor_copy(l_sb[:, bass.ds(mls, 1)], fc[:, 5:6])
                v.wait_ge(sMM.h, mm_t)
                v.tensor_copy(pT_s[:, 0:W], psD[:].bitcast(BF16)[:, 0:W]
                              ).then_inc(sEV.h, 1)
            sEV.n += 1
            last_pTe_ev[0] = sEV.n
            ev_pte = sEV.n

            @blk.tensor
            def _(t, qs=qs, nkt=nkt, pT_s=pT_s, vb=vb, ev_pte=ev_pte):
                t.wait_ge(sEV.h, ev_pte)
                if last_pve_sc[0] is not None:
                    t.wait_ge(sSC.h, last_pve_sc[0])
                for tc in range(nkt):
                    t.matmul(psC[:, 0:512], pT_s[:, tc * 128:(tc + 1) * 128],
                             vb[:, tc * 1024:tc * 1024 + 512],
                             start=(tc == 0), stop=(tc == nkt - 1),
                             skip_group_check=True)
                last = None
                for tc in range(nkt):
                    last = t.matmul(psC[:, 512:1024],
                                    pT_s[:, tc * 128:(tc + 1) * 128],
                                    vb[:, tc * 1024 + 512:(tc + 1) * 1024],
                                    start=(tc == 0), stop=(tc == nkt - 1),
                                    skip_group_check=True)
                last.then_inc(sMM.h, 1)
            sMM.n += 1
            mm_pv = sMM.n

            @blk.scalar
            def _(sc, mm_pv=mm_pv):
                sc.wait_ge(sMM.h, mm_pv)
                sc.activation(pvsb[:], psC[:, 0:1024], AF.Copy,
                              bias=0.0, scale=1.0).then_inc(sSC.h, 1)
            sSC.n += 1
            last_pve_sc[0] = sSC.n
            sc_pve = sSC.n

            @blk.vector
            def _(v, fc=fc, is_diag=is_diag, sbase=sbase, qs=qs, pdesc=pdesc,
                  sc_pve=sc_pve):
                v.wait_ge(sSC.h, sc_pve)
                v.scalar_tensor_tensor(y_st[:], y_st[:], fc[:, 4:5], pvsb[:],
                                       AL.mult, AL.add)
                if is_diag:
                    sl = sbase + qs
                    v.tensor_copy(y_sb[:, sl * 1024:(sl + 1) * 1024], y_st[:]
                                  ).then_inc(sEV.h, 1)
                else:
                    base = pdesc[1] * 20
                    v.reg_load(rv1, schc[0:1, base + 4 + qs:base + 5 + qs])
                    ysl = v.snap(rv1, min_val=0, max_val=15 * 1024)
                    v.tensor_copy(y_sb[:, bass.ds(ysl, 1024)], y_st[:]
                                  ).then_inc(sEV.h, 1)
            sEV.n += 1
        pair_done_mm.append(sMM.n)
    EV_ATT = sEV.n
    MM_ATT = sMM.n

    s2a.close()

    s2b = ExitStack()
    y_fm = s2b.enter_context(nc.sbuf_tensor("y_fm", [128, 8 * NT], BF16))
    y1 = s2b.enter_context(nc.sbuf_tensor("y1", [128, 8 * NT], BF16))
    wpTs = s2b.enter_context(nc.sbuf_tensor("wpTs", [128, 8 * 1024], BF16))
    wmpTs = s2b.enter_context(nc.sbuf_tensor("wmpTs", [128, 8 * 1024], BF16))
    wptmp = s2b.enter_context(nc.sbuf_tensor("wptmp", [128, 1024], F32))
    apTb = s2b.enter_context(nc.sbuf_tensor("apTb", [64, C], BF16))
    bpb = s2b.enter_context(nc.sbuf_tensor("bpb", [64, C], BF16))
    xtl0 = s2b.enter_context(nc.sbuf_tensor("xtl0", [128, TT], F32))
    xtl1 = s2b.enter_context(nc.sbuf_tensor("xtl1", [128, TT], F32))
    x2t0 = s2b.enter_context(nc.sbuf_tensor("x2t0", [128, TT], F32))
    x2t1 = s2b.enter_context(nc.sbuf_tensor("x2t1", [128, TT], F32))
    ytmp = s2b.enter_context(nc.sbuf_tensor("ytmp", [128, 1024], BF16))

    @blk.gpsimd
    def _(g):
        g.wait_ge(sMM.h, MM_ATT)
        g.dma_start(out=apTb[:], in_=apT.ap()).then_inc(sGA.h, 16)
        g.dma_start(out=bpb[:], in_=bp_in.ap()).then_inc(sGA.h, 16)
    sGA.n += 32
    GD_PL = sGA.n

    wp_evs = []
    for vcc in range(8):
        @blk.sync
        def _(s, vcc=vcc):
            s.wait_ge(sMM.h, MM_ATT)
            if vcc >= 1:
                s.wait_ge(sEV.h, wp_evs[-1])
            s.dma_start(out=wptmp[:], in_=wprojT[vcc * 128:(vcc + 1) * 128, :]
                        ).then_inc(sW.h, 16)
        sW.n += 16
        ld_wp = sW.n

        @blk.tensor
        def _(t, vcc=vcc, ld_wp=ld_wp):
            t.wait_ge(sW.h, ld_wp)
            last = None
            for dc in range(8):
                last = t.matmul(psD[:, 24 + dc:25 + dc],
                                wptmp[:, dc * 128:(dc + 1) * 128],
                                abv[:, 8 + vcc:9 + vcc],
                                start=(vcc == 0), stop=(vcc == 7),
                                skip_group_check=True)
            last.then_inc(sMM.h, 1)
        sMM.n += 1
        mm_pb = sMM.n

        @blk.vector
        def _(v, vcc=vcc, mm_pb=mm_pb):
            v.wait_ge(sMM.h, mm_pb)
            last = v.tensor_scalar_mul(wpTs[:, vcc * 1024:(vcc + 1) * 1024],
                                       wptmp[:], abv[:, vcc:vcc + 1])
            if vcc == 7:
                last = v.tensor_copy(pbc[:], psD[:, 24:32])
            last.then_inc(sEV.h, 1)
        sEV.n += 1
        wp_evs.append(sEV.n)

    wmp_evs = []
    for cc in range(8):
        @blk.sync
        def _(s, cc=cc):
            s.wait_ge(sEV.h, wp_evs[-1] if cc == 0 else wmp_evs[-1])
            s.dma_start(out=wptmp[:], in_=wprojT[cc * 128:(cc + 1) * 128, :]
                        ).then_inc(sW.h, 16)
        sW.n += 16
        ld_wp2 = sW.n

        @blk.tensor
        def _(t, cc=cc, ld_wp2=ld_wp2):
            t.wait_ge(sW.h, ld_wp2)
            if cc == 0:
                t.wait_ge(sGA.h, GD_PL)
            last = None
            for rb in range(2):
                last = t.matmul(psA[:, rb * 512:(rb + 1) * 512],
                                bpb[:, cc * 128:(cc + 1) * 128],
                                apTb[:, rb * 512:(rb + 1) * 512],
                                start=True, stop=True, skip_group_check=True)
            last.then_inc(sMM.h, 1)
        sMM.n += 1
        mm_mp = sMM.n

        @blk.vector
        def _(v, cc=cc, mm_mp=mm_mp):
            v.wait_ge(sMM.h, mm_mp)
            v.tensor_tensor(wmpTs[:, cc * 1024:(cc + 1) * 1024], wptmp[:],
                            psA[:, 0:1024], AL.add).then_inc(sEV.h, 1)
        sEV.n += 1
        wmp_evs.append(sEV.n)
    EV_WPM = sEV.n

    fin_evs = []
    for sl in range(16):
        @blk.vector
        def _(v, sl=sl):
            if sl == 0:
                v.wait_ge(sEV.h, EV_ATT)
            v.reciprocal(fc0[:, 7:8], l_sb[:, sl:sl + 1])
            gap(v)
            v.tensor_scalar_mul(ytmp[:], y_sb[:, sl * 1024:(sl + 1) * 1024],
                                fc0[:, 7:8]).then_inc(sEV.h, 1)
        sEV.n += 1
        ev_yt = sEV.n

        @blk.tensor
        def _(t, sl=sl, ev_yt=ev_yt):
            t.wait_ge(sEV.h, ev_yt)
            psB_bf = psB[:].bitcast(BF16)
            last = None
            for vcc in range(8):
                last = t.transpose(psB_bf[:, vcc * 128:(vcc + 1) * 128],
                                   ytmp[:, vcc * 128:(vcc + 1) * 128], idb[:])
            last.then_inc(sMM.h, 1)
        sMM.n += 1
        mm_ytr = sMM.n

        @blk.vector
        def _(v, sl=sl, mm_ytr=mm_ytr):
            v.wait_ge(sMM.h, mm_ytr)
            v.tensor_copy(bass.AP(y_fm, sl * 128, [[8 * NT, 128], [NT, 8], [1, 128]]),
                          psB[:].bitcast(BF16)[:, 0:1024
                          ].rearrange("p (c t) -> p c t", c=8)
                          ).then_inc(sEV.h, 1)
        sEV.n += 1
        fin_evs.append(sEV.n)
    EV_FIN = sEV.n

    p1_scs = []
    for dc in range(8):
        for tt4 in range(4):
            ps = (psA, psB)[(dc * 4 + tt4) % 2]

            @blk.tensor
            def _(t, dc=dc, tt4=tt4, ps=ps):
                if dc == 0 and tt4 == 0:
                    t.wait_ge(sEV.h, EV_FIN)
                    t.wait_ge(sEV.h, EV_WPM)
                if len(p1_scs) >= 2:
                    t.wait_ge(sSC.h, p1_scs[-2])
                last = None
                for vcc in range(8):
                    last = t.matmul(
                        ps[:, 0:512],
                        wpTs[:, vcc * 1024 + dc * 128:vcc * 1024 + (dc + 1) * 128],
                        y_fm[:, vcc * NT + tt4 * 512:vcc * NT + (tt4 + 1) * 512],
                        start=(vcc == 0), stop=(vcc == 7), skip_group_check=True)
                last.then_inc(sMM.h, 1)
            sMM.n += 1
            mm_p1 = sMM.n

            @blk.scalar
            def _(sc, dc=dc, tt4=tt4, ps=ps, mm_p1=mm_p1):
                sc.wait_ge(sMM.h, mm_p1)
                sc.activation(y1[:, dc * NT + tt4 * 512:dc * NT + (tt4 + 1) * 512],
                              ps[:, 0:512], AF.Identity,
                              bias=pbc[:, dc:dc + 1], scale=1.0).then_inc(sSC.h, 1)
            sSC.n += 1
            p1_scs.append(sSC.n)
    SC_P1 = sSC.n

    p2_evs = []
    x2st_lds = []
    for dc2 in range(8):
        for tt4 in range(4):
            idx = dc2 * 4 + tt4
            ps = (psC, psD)[idx % 2]
            xtl = (xtl0, xtl1)[idx % 2]
            x2t = (x2t0, x2t1)[idx % 2]

            sXT = (sXT0, sXT1)[idx % 2]

            @blk.sync
            def _(s, dc2=dc2, tt4=tt4, xtl=xtl, sXT=sXT):
                if len(p2_evs) >= 2:
                    s.wait_ge(sEV.h, p2_evs[-2])
                s.dma_start(out=xtl[:],
                            in_=bass.AP(x_fm, dc2 * 128 * NT + tt4 * 512,
                                        [[NT, 128], [1, 512]])
                            ).then_inc(sXT.h, 16)
            sXT.n += 16
            ld_xt = sXT.n

            @blk.tensor
            def _(t, dc2=dc2, tt4=tt4, ps=ps, idx=idx):
                if idx == 0:
                    t.wait_ge(sSC.h, SC_P1)
                if len(p2_evs) >= 2:
                    t.wait_ge(sEV.h, p2_evs[-2])
                last = None
                for cc in range(8):
                    last = t.matmul(
                        ps[:, 0:512],
                        wmpTs[:, cc * 1024 + dc2 * 128:cc * 1024 + (dc2 + 1) * 128],
                        y1[:, cc * NT + tt4 * 512:cc * NT + (tt4 + 1) * 512],
                        start=(cc == 0), stop=(cc == 7), skip_group_check=True)
                last.then_inc(sMM.h, 1)
            sMM.n += 1
            mm_p2 = sMM.n

            sX2 = (sX2a, sX2b)[idx % 2]

            @blk.vector
            def _(v, ps=ps, xtl=xtl, x2t=x2t, mm_p2=mm_p2, ld_xt=ld_xt,
                  sXT=sXT, sX2=sX2):
                v.wait_ge(sMM.h, mm_p2)
                v.wait_ge(sXT.h, ld_xt)
                if len(x2st_lds) >= 2:
                    v.wait_ge(sX2.h, x2st_lds[-2][1])
                v.tensor_add(x2t[:], ps[:, 0:512], xtl[:]).then_inc(sEV.h, 1)
            sEV.n += 1
            p2_evs.append(sEV.n)
            ev_x2 = sEV.n

            @blk.sync
            def _(s, dc2=dc2, tt4=tt4, x2t=x2t, ev_x2=ev_x2, sX2=sX2):
                s.wait_ge(sEV.h, ev_x2)
                s.dma_start(out=bass.AP(x2_sp, dc2 * 128 * NT + tt4 * 512,
                                        [[NT, 128], [1, 512]]),
                            in_=x2t[:]).then_inc(sX2.h, 16)
            sX2.n += 16
            x2st_lds.append((idx % 2, sX2.n))
    X2A_ALL = sX2a.n
    X2B_ALL = sX2b.n
    EV_P2 = sEV.n

    s2b.close()
    s2.close()

    s4 = ExitStack()
    ln1 = s4.enter_context(nc.sbuf_tensor("ln1", [128, 8 * NT], BF16))
    rowb4 = s4.enter_context(nc.sbuf_tensor("rowb4", [1, 4096], F32))
    wfm = s4.enter_context(nc.sbuf_tensor("wfm", [8, NT], BF16))
    s4b = ExitStack()
    xt4 = s4b.enter_context(nc.sbuf_tensor("xt4", [128, 8 * TT], F32))
    xb4 = s4b.enter_context(nc.sbuf_tensor("xb4", [128, 8 * TT], BF16))
    xsq4 = s4b.enter_context(nc.sbuf_tensor("xsq4", [128, 8 * TT], BF16))
    abc4 = s4b.enter_context(nc.sbuf_tensor("abc4", [128, 2 * TT], F32))
    xtmp4 = s4b.enter_context(nc.sbuf_tensor("xtmp4", [128, TT], F32))
    lgtm = s4b.enter_context(nc.sbuf_tensor("lgtm", [128, 128], F32))
    e8 = s4b.enter_context(nc.sbuf_tensor("e8", [128, 8], F32))
    eq8 = s4b.enter_context(nc.sbuf_tensor("eq8", [128, 8], F32))
    wtm = s4b.enter_context(nc.sbuf_tensor("wtm", [128, 128], F32))

    prev4 = None
    for ti in range(NTILE):
        fw_sync = [(sX2a, X2A_ALL), (sX2b, X2B_ALL)] if ti == 0 else []
        fw_mm = [(sEV, EV_P2)] if ti == 0 else []
        prev4 = emit_ln_tile(
            ti, x2_sp, xt4, xb4, xsq4, abc4, xtmp4, ln1,
            lambda cc, ti_: (cc * NT + ti_ * TT, cc * NT + (ti_ + 1) * TT),
            prev4, fw_sync, fw_mm, None, rowb4)
    EV_LN1 = prev4["ev_ln"]

    gate_evs = []
    for grp in range(2):
        @blk.tensor
        def _(t, grp=grp):
            t.wait_ge(sEV.h, EV_LN1)
            if gate_evs:
                t.wait_ge(sEV.h, gate_evs[-1])
            last = None
            for tc8 in range(8):
                tc = grp * 8 + tc8
                for cc in range(8):
                    last = t.matmul(
                        psD[:, tc8 * 8:(tc8 + 1) * 8],
                        ln1[:, cc * NT + tc * 128:cc * NT + (tc + 1) * 128],
                        gwb[:, cc * 8:(cc + 1) * 8],
                        start=(cc == 0), stop=(cc == 7), skip_group_check=True)
            last.then_inc(sMM.h, 1)
        sMM.n += 1
        mm_g = sMM.n

        @blk.vector
        def _(v, mm_g=mm_g):
            v.wait_ge(sMM.h, mm_g)
            gbb_b = bass.AP(gbb, 0, [[8, 128], [0, 8], [1, 8]])
            lg3 = lgtm[:, 0:64].rearrange("p (tc e) -> p tc e", tc=8)
            ps3 = psD[:, 0:64].rearrange("p (tc e) -> p tc e", tc=8)
            v.tensor_tensor(lg3, ps3, gbb_b, AL.add)
            gap(v).then_inc(sEV.h, 1)
        sEV.n += 1
        ev_lg = sEV.n

        for tc8 in range(8):
            @blk.vector
            def _(v, tc8=tc8, ev_lg=ev_lg):
                v.wait_ge(sEV.h, ev_lg)
                lg = lgtm[:, tc8 * 8:(tc8 + 1) * 8]
                v.reduce_max(sm[:, 0:1], lg, AX)
                v.tensor_copy(sm[:, 4:5], lg[:, 0:1])
                v.tensor_scalar_mul(sm[:, 3:4], sm[:, 0:1], -1.0)
                v.tensor_scalar(eq8[:], lg, sm[:, 0:1], None, AL.is_equal)
                gap(v)
                v.scalar_tensor_tensor(eq8[:], eq8[:], -1.0e30, lg, AL.mult, AL.add)
                gap(v)
                v.reduce_max(sm[:, 1:2], eq8[:], AX)
                gap(v).then_inc(sEV.h, 1)
            sEV.n += 1
            ev_m12 = sEV.n

            @blk.scalar
            def _(sc, tc8=tc8, ev_m12=ev_m12):
                sc.wait_ge(sEV.h, ev_m12)
                sc.activation(e8[:], lgtm[:, tc8 * 8:(tc8 + 1) * 8], AF.Exp,
                              bias=sm[:, 3:4], scale=1.0).then_inc(sSC.h, 1)
            sSC.n += 1
            sc_e8 = sSC.n

            @blk.vector
            def _(v, tc8=tc8, sc_e8=sc_e8):
                v.wait_ge(sSC.h, sc_e8)
                lg = lgtm[:, tc8 * 8:(tc8 + 1) * 8]
                v.reduce_sum(sm[:, 2:3], e8[:], AX)
                v.tensor_scalar(eq8[:], lg, sm[:, 1:2], None, AL.is_ge)
                v.reciprocal(sm[:, 2:3], sm[:, 2:3])
                v.tensor_mul(e8[:], e8[:], eq8[:])
                gap(v)
                v.tensor_scalar_mul(wtm[:, tc8 * 8:(tc8 + 1) * 8], e8[:],
                                    sm[:, 2:3])
                gap(v).then_inc(sEV.h, 1)
            sEV.n += 1
        ev_w = sEV.n

        @blk.tensor
        def _(t, grp=grp, ev_w=ev_w):
            t.wait_ge(sEV.h, ev_w)
            last = None
            for tc8 in range(8):
                last = t.transpose(psC[0:8, tc8 * 128:(tc8 + 1) * 128],
                                   wtm[:, tc8 * 8:(tc8 + 1) * 8], idf[:])
            last.then_inc(sMM.h, 1)
        sMM.n += 1
        mm_wt = sMM.n

        @blk.vector
        def _(v, grp=grp, mm_wt=mm_wt):
            v.wait_ge(sMM.h, mm_wt)
            v.tensor_copy(wfm[0:8, grp * 1024:(grp + 1) * 1024], psC[0:8, 0:1024]
                          ).then_inc(sEV.h, 1)
        sEV.n += 1
        gate_evs.append(sEV.n)
    EV_GATE = sEV.n

    s4b.close()

    s4c = ExitStack()
    hsl = s4c.enter_context(nc.sbuf_tensor("hsl", [128, 16 * 1024], BF16))
    w1p0 = s4c.enter_context(nc.sbuf_tensor("w1p0", [128, 8192], BF16))
    w1p1 = s4c.enter_context(nc.sbuf_tensor("w1p1", [128, 8192], BF16))
    w2p0 = s4c.enter_context(nc.sbuf_tensor("w2p0", [128, 8192], BF16))
    w2p1 = s4c.enter_context(nc.sbuf_tensor("w2p1", [128, 8192], BF16))
    oacc = s4c.enter_context(nc.sbuf_tensor("oacc", [128, 8 * 1024], F32))
    wbc0 = s4c.enter_context(nc.sbuf_tensor("wbc0", [128, 1024], BF16))
    wbc1 = s4c.enter_context(nc.sbuf_tensor("wbc1", [128, 1024], BF16))
    wr0 = s4c.enter_context(nc.sbuf_tensor("wr0", [1, 1024], BF16))
    b2b = s4c.enter_context(nc.sbuf_tensor("b2b", [8, 1024], BF16))

    ffn_e_mm = {}
    mm_f1 = {}
    mm_f2 = {}
    w_gd = {}
    last_mm_wb = [None]

    @blk.gpsimd
    def _(g):
        g.wait_ge(sEV.h, EV_GATE)
        g.dma_start(out=b2b[:], in_=b2_in.ap()).then_inc(sGA.h, 16)
    sGA.n += 16
    GD_B2 = sGA.n

    for th in range(2):
        @blk.sync
        def _(s, th=th):
            s.wait_ge(sEV.h, EV_GATE)
            if th == 1:
                s.wait_ge(sST.h, 16)
            s.dma_start(out=oacc[:].rearrange("p (o t) -> p o t", o=8),
                        in_=bass.AP(x2_sp, th * 1024,
                                    [[NT, 128], [128 * NT, 8], [1, 1024]])
                        ).then_inc(sOI.h, 16)
        sOI.n += 16
        ld_oi = sOI.n

        relu_scs = []
        o_evs = []
        for e in range(8):
            par = e % 2
            wbc = (wbc0, wbc1)[par]

            @blk.gpsimd
            def _(g, e=e, th=th):
                for jh in range(2):
                    if (th, e - 1, jh) in mm_f1:
                        g.wait_ge(sMM.h, mm_f1[(th, e - 1, jh)])
                    elif th == 1 and e == 0:
                        g.wait_ge(sMM.h, mm_f1[(0, 7, jh)])
                    elif th == 0 and e == 0:
                        g.wait_ge(sEV.h, EV_GATE)
                    sWx = (sW10, sW11)[jh]
                    g.dma_start(
                        out=(w1p0, w1p1)[jh][:].rearrange("p (c j) -> p c j", c=8),
                        in_=bass.AP(w1_in, e * C * 2048 + jh * 1024,
                                    [[2048, 128], [128 * 2048, 8], [1, 1024]])
                        ).then_inc(sWx.h, 16)
                    sWx.n += 16
                    w_gd[(th, e, "w1", jh)] = (sWx, sWx.n)
                for oh in range(2):
                    if (th, e - 1, oh) in mm_f2:
                        g.wait_ge(sMM.h, mm_f2[(th, e - 1, oh)])
                    elif th == 1 and e == 0:
                        g.wait_ge(sMM.h, mm_f2[(0, 7, oh)])
                    sWx = (sW20, sW21)[oh]
                    g.dma_start(
                        out=(w2p0, w2p1)[oh][:].rearrange("p (j o) -> p j o", j=16),
                        in_=bass.AP(w2_in, e * 2048 * C + oh * 512,
                                    [[C, 128], [128 * C, 16], [1, 512]])
                        ).then_inc(sWx.h, 16)
                    sWx.n += 16
                    w_gd[(th, e, "w2", oh)] = (sWx, sWx.n)

            @blk.sync
            def _(s, e=e, th=th):
                s.wait_ge(sEV.h, EV_GATE)
                if last_mm_wb[0] is not None:
                    s.wait_ge(sMM.h, last_mm_wb[0])
                s.dma_start(out=wr0[0:1, :],
                            in_=wfm[e:e + 1, th * 1024:(th + 1) * 1024]
                            ).then_inc(sWR.h, 16)
            sWR.n += 16
            ld_wr = sWR.n

            @blk.tensor
            def _(t, e=e, th=th, ld_wr=ld_wr):
                t.wait_ge(sWR.h, ld_wr)
                if relu_scs:
                    t.wait_ge(sSC.h, relu_scs[-1])
                t.matmul(psB[:, 0:512], onerb[:], wr0[0:1, 0:512],
                         start=True, stop=True, skip_group_check=True)
                t.matmul(psB[:, 512:1024], onerb[:], wr0[0:1, 512:1024],
                         start=True, stop=True, skip_group_check=True
                         ).then_inc(sMM.h, 1)
            sMM.n += 1
            mm_wb = sMM.n
            last_mm_wb[0] = mm_wb

            @blk.vector
            def _(v, wbc=wbc, mm_wb=mm_wb):
                v.wait_ge(sMM.h, mm_wb)
                v.tensor_copy(wbc[:], psB[:, 0:1024]).then_inc(sEV.h, 1)
            sEV.n += 1
            ev_wb = sEV.n

            for jc in range(16):
                psH = (psA, psB)[jc % 2]

                @blk.tensor
                def _(t, e=e, th=th, jc=jc, psH=psH, ev_wb=ev_wb):
                    jh = jc // 8
                    jj = jc % 8
                    if jj == 0:
                        wsem, wval = w_gd[(th, e, "w1", jh)]
                        t.wait_ge(wsem.h, wval)
                    if jc <= 1:
                        t.wait_ge(sEV.h, ev_wb)
                    if len(relu_scs) >= 2:
                        t.wait_ge(sSC.h, relu_scs[-2])
                    w1b = (w1p0, w1p1)[jh]
                    last = None
                    for cc in range(8):
                        lhs = w1b[:, cc * 1024 + jj * 128:
                                  cc * 1024 + (jj + 1) * 128]
                        t.matmul(psH[:, 0:512], lhs,
                                 ln1[:, cc * NT + th * 1024:
                                     cc * NT + th * 1024 + 512],
                                 start=(cc == 0), stop=(cc == 7),
                                 skip_group_check=True)
                        last = t.matmul(psH[:, 512:1024], lhs,
                                        ln1[:, cc * NT + th * 1024 + 512:
                                            cc * NT + (th + 1) * 1024],
                                        start=(cc == 0), stop=(cc == 7),
                                        skip_group_check=True)
                    last.then_inc(sMM.h, 1)
                sMM.n += 1
                mm_h = sMM.n
                if jc == 7:
                    mm_f1[(th, e, 0)] = mm_h
                elif jc == 15:
                    mm_f1[(th, e, 1)] = mm_h

                @blk.scalar
                def _(sc, e=e, jc=jc, psH=psH, mm_h=mm_h):
                    sc.wait_ge(sMM.h, mm_h)
                    sc.activation(hsl[:, jc * 1024:(jc + 1) * 1024], psH[:, 0:1024],
                                  AF.Relu, bias=b1c[:, e * 16 + jc:e * 16 + jc + 1],
                                  scale=1.0).then_inc(sSC.h, 1)
                sSC.n += 1
                relu_scs.append(sSC.n)
                sc_r = sSC.n

                @blk.vector
                def _(v, jc=jc, wbc=wbc, sc_r=sc_r):
                    v.wait_ge(sSC.h, sc_r)
                    v.tensor_mul(hsl[:, jc * 1024:(jc + 1) * 1024],
                                 hsl[:, jc * 1024:(jc + 1) * 1024], wbc[:]
                                 ).then_inc(sEV.h, 1)
                sEV.n += 1
            ev_hdone = sEV.n

            for oc in range(8):
                psO = (psC, psD)[oc % 2]

                @blk.tensor
                def _(t, e=e, th=th, oc=oc, psO=psO, ev_hdone=ev_hdone):
                    oh = oc // 4
                    oo = oc % 4
                    if oc == 0:
                        t.wait_ge(sEV.h, ev_hdone)
                    if oo == 0:
                        wsem, wval = w_gd[(th, e, "w2", oh)]
                        t.wait_ge(wsem.h, wval)
                    if len(o_evs) >= 2:
                        t.wait_ge(sEV.h, o_evs[-2])
                    w2b = (w2p0, w2p1)[oh]
                    last = None
                    for jc in range(16):
                        lhs = w2b[:, jc * 512 + oo * 128:
                                  jc * 512 + (oo + 1) * 128]
                        t.matmul(psO[:, 0:512], lhs,
                                 hsl[:, jc * 1024:jc * 1024 + 512],
                                 start=(jc == 0), stop=(jc == 15),
                                 skip_group_check=True)
                        last = t.matmul(psO[:, 512:1024], lhs,
                                        hsl[:, jc * 1024 + 512:(jc + 1) * 1024],
                                        start=(jc == 0), stop=(jc == 15),
                                        skip_group_check=True)
                    last.then_inc(sMM.h, 1)
                sMM.n += 1
                mm_o = sMM.n
                if oc == 3:
                    mm_f2[(th, e, 0)] = mm_o
                elif oc == 7:
                    mm_f2[(th, e, 1)] = mm_o

                @blk.vector
                def _(v, oc=oc, psO=psO, mm_o=mm_o, ld_oi=ld_oi):
                    v.wait_ge(sMM.h, mm_o)
                    v.wait_ge(sOI.h, ld_oi)
                    v.tensor_add(oacc[:, oc * 1024:(oc + 1) * 1024],
                                 oacc[:, oc * 1024:(oc + 1) * 1024],
                                 psO[:, 0:1024]).then_inc(sEV.h, 1)
                sEV.n += 1
                o_evs.append(sEV.n)
            ffn_e_mm[(th, e)] = sMM.n

        for ocp in range(4):
            @blk.tensor
            def _(t, th=th, ocp=ocp):
                if ocp == 0:
                    t.wait_ge(sEV.h, o_evs[-1])
                    t.wait_ge(sGA.h, GD_B2)
                else:
                    t.wait_ge(sEV.h, sEV.n)
                last = None
                for j in range(2):
                    oc = ocp * 2 + j
                    psO = (psC, psD)[j]
                    for hh in range(2):
                        last = t.matmul(
                            psO[:, hh * 512:(hh + 1) * 512],
                            b2b[:, oc * 128:(oc + 1) * 128],
                            wfm[0:8, th * 1024 + hh * 512:
                                th * 1024 + (hh + 1) * 512],
                            start=True, stop=True, skip_group_check=True)
                last.then_inc(sMM.h, 1)
            sMM.n += 1
            mm_b2 = sMM.n

            @blk.vector
            def _(v, ocp=ocp, mm_b2=mm_b2):
                v.wait_ge(sMM.h, mm_b2)
                last = None
                for j in range(2):
                    oc = ocp * 2 + j
                    psO = (psC, psD)[j]
                    last = v.tensor_add(oacc[:, oc * 1024:(oc + 1) * 1024],
                                        oacc[:, oc * 1024:(oc + 1) * 1024],
                                        psO[:, 0:1024])
                last.then_inc(sEV.h, 1)
            sEV.n += 1
        ev_b2 = sEV.n

        @blk.sync
        def _(s, th=th, ev_b2=ev_b2):
            s.wait_ge(sEV.h, ev_b2)
            s.dma_start(out=bass.AP(out_fm, th * 1024,
                                    [[NT, 128], [128 * NT, 8], [1, 1024]]),
                        in_=oacc[:].rearrange("p (o t) -> p o t", o=8)
                        ).then_inc(sST.h, 16)
        sST.n += 16

    @blk.sync
    def _(s):
        s.wait_ge(sST.h, sST.n)

    s4c.close()
    s4.close()
    es.close()
    nc.finalize()
    return nc


_NC_CACHE = {}


def _get_nc():
    if "nc" not in _NC_CACHE:
        _NC_CACHE["nc"] = build_nc()
    return _NC_CACHE["nc"]


def _make_sched(c):
    pA, pB = c, 7 - c
    rows = []
    for g in range(pA):
        sh = g
        rows.append((sh * SHARD + 0, sh * SHARD + VB + 0, 0, 0))
    for g in range(pB):
        sh = 7 - g
        rows.append((sh * SHARD + HBLK, sh * SHARD + VB + HBLK, HBLK, 8))
    assert len(rows) == 7
    tab = np.zeros((7, 20), dtype=np.int32)
    for gi, (ko, vo, qo, sb) in enumerate(rows):
        tab[gi, 0] = ko
        tab[gi, 1] = vo
        tab[gi, 2] = qo
        tab[gi, 3] = sb
        for qs in range(8):
            tab[gi, 4 + qs] = (sb + qs) * 1024
            tab[gi, 12 + qs] = sb + qs
    return tab.reshape(1, 140)


def kernel(L0_attn_x, L0_attn_W_attn, L0_attn_W_proj, L0_attn_lora_attn_A,
           L0_attn_lora_attn_B, L0_attn_lora_proj_A, L0_attn_lora_proj_B,
           L0_attn_bn_gamma, L0_attn_bn_beta, L0_ffn_gate_w, L0_ffn_gate_b,
           L0_ffn_w1, L0_ffn_b1, L0_ffn_w2, L0_ffn_b2):
    x = np.asarray(L0_attn_x, dtype=np.float32)
    cst = np.ascontiguousarray
    wattnT_h = cst(np.asarray(L0_attn_W_attn, np.float32).T)
    aaT_h = cst(np.asarray(L0_attn_lora_attn_A, np.float32).T)
    ba_h = cst(np.asarray(L0_attn_lora_attn_B, np.float32))
    wprojT_h = cst(np.asarray(L0_attn_W_proj, np.float32).T)
    apT_h = cst(np.asarray(L0_attn_lora_proj_A, np.float32).T)
    bp_h = cst(np.asarray(L0_attn_lora_proj_B, np.float32))
    gam = np.asarray(L0_attn_bn_gamma, np.float32)
    bet = np.asarray(L0_attn_bn_beta, np.float32)
    gam_qk_h = cst(gam[0:2048].reshape(16, 128).T)
    bet_qk_h = cst(bet[0:2048].reshape(16, 128).T)
    gam_vr_h = cst(gam[2048:3072].reshape(1, C))
    bet_vr_h = cst(bet[2048:3072].reshape(1, C))
    gwT_h = cst(np.asarray(L0_ffn_gate_w, np.float32).T)
    gb_row_h = cst(np.asarray(L0_ffn_gate_b, np.float32).reshape(1, 8))
    w1_h = cst(np.asarray(L0_ffn_w1, np.float32))
    b1_t_h = cst(np.asarray(L0_ffn_b1, np.float32).reshape(8, 16, 128)
                 .transpose(2, 0, 1).reshape(128, 128))
    w2_h = cst(np.asarray(L0_ffn_w2, np.float32))
    b2_h = cst(np.asarray(L0_ffn_b2, np.float32))

    nc = _get_nc()
    in_maps = []
    for c in range(NCORE):
        a0 = x[0, 1024 * c:1024 * (c + 1)]
        a1 = x[1, 1024 * (7 - c):1024 * (8 - c)]
        x_fm_h = cst(np.concatenate([a0, a1], axis=0).T)
        in_maps.append({
            "x_fm": x_fm_h, "wattnT": wattnT_h, "aaT": aaT_h, "ba_in": ba_h,
            "wprojT": wprojT_h, "apT": apT_h, "bp_in": bp_h,
            "gam_qk": gam_qk_h, "bet_qk": bet_qk_h,
            "gam_vr": gam_vr_h, "bet_vr": bet_vr_h,
            "gwT": gwT_h, "gb_row": gb_row_h,
            "w1_in": w1_h, "b1_t": b1_t_h, "w2_in": w2_h, "b2_in": b2_h,
            "sched": _make_sched(c),
        })
    res = run_bass_kernel_spmd(nc, in_maps, core_ids=list(range(NCORE)))
    out = np.zeros((2, 8192, 1024), dtype=np.float32)
    for c in range(NCORE):
        o = res.results[c]["out_fm"]
        out[0, 1024 * c:1024 * (c + 1)] = o[:, 0:1024].T
        out[1, 1024 * (7 - c):1024 * (8 - c)] = o[:, 1024:2048].T
    return out


if __name__ == "__main__":
    nc = build_nc()
    print("build OK; instructions:", nc.n_instructions())


# version 06
# speedup vs baseline: 1.0090x; 1.0049x over previous
import numpy as np
import ml_dtypes
from contextlib import ExitStack
import concourse.bass as bass
import concourse.bacc as bacc
import concourse.mybir as mybir
from concourse.bass_utils import run_bass_kernel_spmd

F32 = mybir.dt.float32
BF16 = mybir.dt.bfloat16
I32 = mybir.dt.int32
AF = mybir.ActivationFunctionType
AL = mybir.AluOpType
AX = mybir.AxisListType.X

NCORE = 8
NT = 2048
C = 1024
D3 = 3072
KSH = C * NT
VB = KSH
SHARD = KSH + NT * C
TT = 512
NTILE = NT // TT
HBLK = 128 * 8 * 1024
SCALE = 1.0 / 32.0
EPS = 1e-5
NEGBIG = -1.0e9
NTOK = 16384.0


def build_nc():
    nc = bacc.Bacc("TRN2", target_bir_lowering=False, num_devices=NCORE,
                   detect_race_conditions=False)

    x_fm = nc.dram_tensor("x_fm", [C, NT], F32, kind="ExternalInput")
    wattnT = nc.dram_tensor("wattnT", [C, D3], F32, kind="ExternalInput")
    aaT = nc.dram_tensor("aaT", [64, C], F32, kind="ExternalInput")
    ba_in = nc.dram_tensor("ba_in", [64, D3], F32, kind="ExternalInput")
    wprojT = nc.dram_tensor("wprojT", [C, C], F32, kind="ExternalInput")
    apT = nc.dram_tensor("apT", [64, C], F32, kind="ExternalInput")
    bp_in = nc.dram_tensor("bp_in", [64, C], F32, kind="ExternalInput")
    gam_qk = nc.dram_tensor("gam_qk", [128, 16], F32, kind="ExternalInput")
    bet_qk = nc.dram_tensor("bet_qk", [128, 16], F32, kind="ExternalInput")
    gam_vr = nc.dram_tensor("gam_vr", [1, C], F32, kind="ExternalInput")
    bet_vr = nc.dram_tensor("bet_vr", [1, C], F32, kind="ExternalInput")
    gwT = nc.dram_tensor("gwT", [C, 8], F32, kind="ExternalInput")
    gb_row = nc.dram_tensor("gb_row", [1, 8], F32, kind="ExternalInput")
    w1_in = nc.dram_tensor("w1_in", [8, C, 2048], F32, kind="ExternalInput")
    b1_t = nc.dram_tensor("b1_t", [128, 128], F32, kind="ExternalInput")
    w2_in = nc.dram_tensor("w2_in", [8, 2048, C], F32, kind="ExternalInput")
    b2_in = nc.dram_tensor("b2_in", [8, C], F32, kind="ExternalInput")
    sched = nc.dram_tensor("sched", [1, 7 * 20], I32, kind="ExternalInput")

    out_fm = nc.dram_tensor("out_fm", [C, NT], F32, kind="ExternalOutput")

    kv_in = nc.dram_tensor("kv_in", [1, SHARD], BF16)
    kv_all = nc.dram_tensor("kv_all", [NCORE, SHARD], BF16)
    st_in = nc.dram_tensor("st_in", [1, 6144], F32)
    st_out = nc.dram_tensor("st_out", [1, 6144], F32)
    q_dram = nc.dram_tensor("q_dram", [128, 8 * NT], BF16)
    x2_sp = nc.dram_tensor("x2_sp", [C, NT], F32)

    ident_f_d = nc.inline_tensor(np.eye(128, dtype=np.float32), name="identf")
    ident_b_d = nc.inline_tensor(np.eye(128, dtype=ml_dtypes.bfloat16), name="identb")
    tri_np = np.where(np.arange(128)[None, :] <= np.arange(128)[:, None],
                      0.0, NEGBIG).astype(np.float32)
    tri_d = nc.inline_tensor(tri_np, name="trim")
    onescol_b_d = nc.inline_tensor(np.ones((128, 1), dtype=ml_dtypes.bfloat16),
                                   name="onescolb")
    onesrow_f_d = nc.inline_tensor(np.ones((1, 128), dtype=np.float32),
                                   name="onesrowf")
    onesrow_b_d = nc.inline_tensor(np.ones((1, 128), dtype=ml_dtypes.bfloat16),
                                   name="onesrowb")

    psA = nc.alloc_psum_tensor("psA", [128, 1024], F32)
    psB = nc.alloc_psum_tensor("psB", [128, 1024], F32)
    psC = nc.alloc_psum_tensor("psC", [128, 1024], F32)
    psD = nc.alloc_psum_tensor("psD", [128, 1024], F32)

    A = nc.alloc_sbuf_tensor
    idb = A("idb", [128, 128], BF16)
    idf = A("idf", [128, 128], F32)
    trim = A("trim_sb", [128, 128], F32)
    onecb = A("onecb", [128, 1], BF16)
    onerf = A("onerf", [1, 128], F32)
    onerb = A("onerb", [1, 128], BF16)
    stc = A("stc", [128, 32], F32)
    abqk = A("abqk", [128, 32], F32)
    abv = A("abv", [128, 16], F32)
    qab = A("qab", [128, 16], F32)
    gamc = A("gamc", [128, 16], F32)
    betc = A("betc", [128, 16], F32)
    schc = A("schc", [1, 7 * 20], I32)
    m_sb = A("m_sb", [128, 16], F32)
    l_sb = A("l_sb", [128, 16], F32)
    b1c = A("b1c", [128, 128], F32)
    gwb = A("gwb", [128, 64], BF16)
    gbr = A("gbr", [1, 8], F32)
    gbb = A("gbb", [128, 8], F32)
    pbc = A("pbc", [128, 8], F32)
    sm = A("sm", [128, 16], F32)
    tcol = A("tcol", [128, 4], F32)
    dgap = A("dgap", [128, 64], F32)

    def gap(v):
        return v.tensor_copy(dgap[:, 0:1], dgap[:, 1:2])

    class Sem:
        def __init__(self, name):
            self.h = nc.alloc_semaphore(name)
            self.n = 0
    sMM = Sem("sMM")
    sEV = Sem("sEV")
    sSC = Sem("sSC")
    sCC = Sem("sCC")
    sST = Sem("sST")
    sC = Sem("sC")
    sW = Sem("sW")
    sX = Sem("sX")
    sKVk = Sem("sKVk")
    sKVv = Sem("sKVv")
    sVL = Sem("sVL")
    sSTT = Sem("sSTT")
    sSR = Sem("sSR")
    sGV = Sem("sGV")
    sQW = Sem("sQW")
    sGA = Sem("sGA")
    sPL0 = Sem("sPL0")
    sPL1 = Sem("sPL1")
    sQF = Sem("sQF")
    sKF = Sem("sKF")
    sVF = Sem("sVF")
    sXT0 = Sem("sXT0")
    sXT1 = Sem("sXT1")
    sX2a = Sem("sX2a")
    sX2b = Sem("sX2b")
    sOI = Sem("sOI")
    sWR = Sem("sWR")
    sW10 = Sem("sW10")
    sW11 = Sem("sW11")
    sW20 = Sem("sW20")
    sW21 = Sem("sW21")

    es = ExitStack()
    rg1 = es.enter_context(nc.gpsimd.register("rg1"))
    rg2 = es.enter_context(nc.gpsimd.register("rg2"))
    rg3 = es.enter_context(nc.gpsimd.register("rg3"))
    rv1 = es.enter_context(nc.vector.register("rv1"))
    rs1 = es.enter_context(nc.sync.register("rs1"))
    rc1 = es.enter_context(nc.scalar.register("rc1"))
    blk = es.enter_context(nc.Block("main"))

    s1 = ExitStack()
    wmT = s1.enter_context(nc.sbuf_tensor("wmT", [128, 8 * D3], BF16))
    wtmp = s1.enter_context(nc.sbuf_tensor("wtmp", [128, D3], F32))
    aatb = s1.enter_context(nc.sbuf_tensor("aatb", [64, C], BF16))
    bab = s1.enter_context(nc.sbuf_tensor("bab", [64, D3], BF16))
    xt = s1.enter_context(nc.sbuf_tensor("xt", [128, 8 * TT], F32))
    xb = s1.enter_context(nc.sbuf_tensor("xb", [128, 8 * TT], BF16))
    xsq = s1.enter_context(nc.sbuf_tensor("xsq", [128, 8 * TT], BF16))
    ln0 = s1.enter_context(nc.sbuf_tensor("ln0", [128, 8 * TT], BF16))
    abc0 = s1.enter_context(nc.sbuf_tensor("abc0", [128, 2 * TT], F32))
    xtmp = s1.enter_context(nc.sbuf_tensor("xtmp", [128, TT], F32))
    qstage = s1.enter_context(nc.sbuf_tensor("qstage", [128, 8 * TT], BF16))
    kslab = s1.enter_context(nc.sbuf_tensor("kslab", [128, 8 * TT], BF16))
    vslab = s1.enter_context(nc.sbuf_tensor("vslab", [128, 16 * 1024], BF16))
    sqs = s1.enter_context(nc.sbuf_tensor("sqs", [128, 4 * TT], BF16))
    vrow = s1.enter_context(nc.sbuf_tensor("vrow", [1, 2 * C], F32))
    rowb = s1.enter_context(nc.sbuf_tensor("rowb", [1, 4096], F32))

    @blk.sync
    def _(s):
        for dst, src in [(idb, ident_b_d), (idf, ident_f_d), (trim, tri_d),
                         (onecb, onescol_b_d), (onerf, onesrow_f_d),
                         (onerb, onesrow_b_d)]:
            s.dma_start(out=dst[:], in_=src.ap()).then_inc(sC.h, 16)
        for dst, src in [(gamc, gam_qk), (betc, bet_qk), (schc, sched),
                         (b1c, b1_t), (gbr, gb_row)]:
            s.dma_start(out=dst[:], in_=src.ap()).then_inc(sC.h, 16)
    sC.n += 176
    LD_CONST = sC.n

    @blk.gpsimd
    def _(g):
        g.dma_start(out=aatb[:], in_=aaT.ap()).then_inc(sGA.h, 16)
        g.dma_start(out=bab[:], in_=ba_in.ap()).then_inc(sGA.h, 16)
        g.dma_start(out=gwb[:].rearrange("p (cc e) -> p cc e", cc=8),
                    in_=bass.AP(gwT, 0, [[8, 128], [1024, 8], [1, 8]])
                    ).then_inc(sGA.h, 16)
    sGA.n += 48
    GD_LORA = sGA.n

    wm_evs = []
    for cc in range(8):
        wt = wtmp

        @blk.sync
        def _(s, cc=cc, wt=wt):
            if cc >= 1:
                s.wait_ge(sEV.h, wm_evs[cc - 1])
            s.dma_start(out=wt[:], in_=wattnT[cc * 128:(cc + 1) * 128, :]
                        ).then_inc(sW.h, 16)
        sW.n += 16
        ld_w = sW.n

        @blk.tensor
        def _(t, cc=cc, ld_w=ld_w):
            if cc == 0:
                t.wait_ge(sGA.h, GD_LORA)
            t.wait_ge(sW.h, ld_w)
            if cc >= 1:
                t.wait_ge(sEV.h, wm_evs[cc - 1])
            last = None
            for s3 in range(3):
                ps = (psA, psB, psC)[s3]
                for rb in range(2):
                    last = t.matmul(
                        ps[:, rb * 512:(rb + 1) * 512],
                        bab[:, s3 * 1024 + cc * 128: s3 * 1024 + (cc + 1) * 128],
                        aatb[:, rb * 512:(rb + 1) * 512],
                        start=True, stop=True, skip_group_check=True)
            last.then_inc(sMM.h, 1)
        sMM.n += 1
        mm_w = sMM.n

        @blk.vector
        def _(v, cc=cc, mm_w=mm_w, wt=wt):
            v.wait_ge(sMM.h, mm_w)
            last = None
            for s3 in range(3):
                ps = (psA, psB, psC)[s3]
                src_w = bass.AP(wt, s3, [[D3, 128], [3, 1024]])
                dst = bass.AP(wmT, cc * D3 + s3, [[8 * D3, 128], [3, 1024]])
                last = v.tensor_tensor(dst, src_w, ps[:, 0:1024], AL.add)
            last.then_inc(sEV.h, 1)
        sEV.n += 1
        wm_evs.append(sEV.n)
    MM_WM = sMM.n

    @blk.vector
    def _(v):
        v.memzero(stc[:])
        v.memzero(l_sb[:])
        v.memzero(dgap[:])
        v.memset(m_sb[:], -1.0e30).then_inc(sEV.h, 1)
    sEV.n += 1
    EV_INIT = sEV.n

    def emit_ln_tile(ti, src_dram, xt_, xb_, xsq_, abc_, xtmp_, ln_dst, dst_off,
                     prev, first_waits_sync, first_waits_mm, writeln_wait_mm,
                     rowb):
        t0 = ti * TT

        @blk.sync
        def _(s):
            for sem, val in first_waits_sync:
                s.wait_ge(sem.h, val)
            if prev is not None:
                s.wait_ge(sEV.h, prev["ev_ln"])
            s.dma_start(out=xt_[:].rearrange("p (c t) -> p c t", c=8),
                        in_=bass.AP(src_dram, t0, [[NT, 128], [128 * NT, 8],
                                                   [1, TT]])
                        ).then_inc(sX.h, 16)
        sX.n += 16
        ld_x = sX.n

        @blk.vector
        def _(v, ld_x=ld_x):
            v.wait_ge(sX.h, ld_x)
            if prev is not None:
                v.wait_ge(sMM.h, prev["mm_stats"])
            v.tensor_copy(xb_[:], xt_[:]).then_inc(sEV.h, 1)
        sEV.n += 1
        ev_xb = sEV.n

        @blk.scalar
        def _(sc, ev_xb=ev_xb):
            sc.wait_ge(sEV.h, ev_xb)
            sc.square(xsq_[:], xb_[:]).then_inc(sSC.h, 1)
        sSC.n += 1
        sc_sq = sSC.n

        @blk.tensor
        def _(t, sc_sq=sc_sq):
            t.wait_ge(sSC.h, sc_sq)
            for sem, val in first_waits_mm:
                t.wait_ge(sem.h, val)
            for cc in range(8):
                t.matmul(psD[0:1, 0:TT], onecb[:], xb_[:, cc * TT:(cc + 1) * TT],
                         start=(cc == 0), stop=(cc == 7), skip_group_check=True)
            last = None
            for cc in range(8):
                last = t.matmul(psD[0:1, 512:512 + TT], onecb[:],
                                xsq_[:, cc * TT:(cc + 1) * TT],
                                start=(cc == 0), stop=(cc == 7),
                                skip_group_check=True)
            last.then_inc(sMM.h, 1)
        sMM.n += 1
        mm_st = sMM.n

        MU, EX2, TMP, RR, NB = 0, TT, 2 * TT, 3 * TT, 2048 + 1024

        @blk.vector
        def _(v, mm_st=mm_st):
            v.wait_ge(sMM.h, mm_st)
            v.tensor_scalar_mul(rowb[0:1, MU:MU + TT], psD[0:1, 0:TT], 1.0 / C)
            v.tensor_scalar_mul(rowb[0:1, EX2:EX2 + TT], psD[0:1, 512:512 + TT],
                                1.0 / C).then_inc(sEV.h, 1)
        sEV.n += 1
        ev_mu = sEV.n

        @blk.scalar
        def _(sc, ev_mu=ev_mu):
            sc.wait_ge(sEV.h, ev_mu)
            sc.square(rowb[0:1, TMP:TMP + TT], rowb[0:1, MU:MU + TT]
                      ).then_inc(sSC.h, 1)
        sSC.n += 1
        sc_mu2 = sSC.n

        @blk.vector
        def _(v, sc_mu2=sc_mu2):
            v.wait_ge(sSC.h, sc_mu2)
            v.tensor_sub(rowb[0:1, TMP:TMP + TT], rowb[0:1, EX2:EX2 + TT],
                         rowb[0:1, TMP:TMP + TT])
            v.tensor_scalar_add(rowb[0:1, TMP:TMP + TT], rowb[0:1, TMP:TMP + TT],
                                EPS).then_inc(sEV.h, 1)
        sEV.n += 1
        ev_var = sEV.n

        @blk.scalar
        def _(sc, ev_var=ev_var):
            sc.wait_ge(sEV.h, ev_var)
            sc.sqrt(rowb[0:1, TMP:TMP + TT], rowb[0:1, TMP:TMP + TT]
                    ).then_inc(sSC.h, 1)
        sSC.n += 1
        sc_sd = sSC.n

        @blk.vector
        def _(v, sc_sd=sc_sd):
            v.wait_ge(sSC.h, sc_sd)
            v.reciprocal(rowb[0:1, RR:RR + TT], rowb[0:1, TMP:TMP + TT])
            v.tensor_mul(rowb[0:1, NB:NB + TT], rowb[0:1, MU:MU + TT],
                         rowb[0:1, RR:RR + TT])
            v.tensor_scalar_mul(rowb[0:1, NB:NB + TT], rowb[0:1, NB:NB + TT],
                                -1.0).then_inc(sEV.h, 1)
        sEV.n += 1
        ev_r = sEV.n

        @blk.tensor
        def _(t, ev_r=ev_r):
            t.wait_ge(sEV.h, ev_r)
            t.matmul(psD[:, 0:TT], onerf[:], rowb[0:1, RR:RR + TT],
                     start=True, stop=True, skip_group_check=True)
            t.matmul(psD[:, 512:512 + TT], onerf[:], rowb[0:1, NB:NB + TT],
                     start=True, stop=True, skip_group_check=True
                     ).then_inc(sMM.h, 1)
        sMM.n += 1
        mm_bc = sMM.n

        @blk.vector
        def _(v, mm_bc=mm_bc, writeln_wait_mm=writeln_wait_mm):
            v.wait_ge(sMM.h, mm_bc)
            if writeln_wait_mm is not None:
                v.wait_ge(sMM.h, writeln_wait_mm)
            v.tensor_copy(abc_[:, 0:TT], psD[:, 0:TT])
            v.tensor_copy(abc_[:, TT:2 * TT], psD[:, 512:512 + TT])
            last = None
            for cc in range(8):
                v.tensor_mul(xtmp_[:], xt_[:, cc * TT:(cc + 1) * TT], abc_[:, 0:TT])
                o0, o1 = dst_off(cc, ti)
                last = v.tensor_add(ln_dst[:, o0:o1], xtmp_[:], abc_[:, TT:2 * TT])
            last.then_inc(sEV.h, 1)
        sEV.n += 1
        return {"ev_ln": sEV.n, "mm_stats": mm_st}

    prev_ln = None
    qkv_evs = []
    qkv_scs = []
    kst_lds = []
    vst_lds = []
    last_vproj_mm = None
    last_vev_sc = None
    for ti in range(NTILE):
        t0 = ti * TT
        fw_sync = [(sC, LD_CONST)] if ti == 0 else []
        fw_mm = [(sMM, MM_WM)] if ti == 0 else []
        prev_ln = emit_ln_tile(
            ti, x_fm, xt, xb, xsq, abc0, xtmp, ln0,
            lambda cc, ti_: (cc * TT, (cc + 1) * TT),
            prev_ln, fw_sync, fw_mm,
            last_vproj_mm, rowb)
        ev_ln = prev_ln["ev_ln"]

        for half in range(4):
            dcs = [half * 4 + i for i in range(4)]

            @blk.tensor
            def _(t, dcs=dcs, ev_ln=ev_ln):
                t.wait_ge(sEV.h, ev_ln)
                if qkv_scs:
                    t.wait_ge(sSC.h, qkv_scs[-1])
                if qkv_evs:
                    t.wait_ge(sEV.h, qkv_evs[-1])
                last = None
                for i, dc in enumerate(dcs):
                    ps = (psA, psB)[i // 2]
                    off = (i % 2) * 512
                    for cc in range(8):
                        last = t.matmul(
                            ps[:, off:off + TT],
                            wmT[:, cc * D3 + dc * 128:cc * D3 + (dc + 1) * 128],
                            ln0[:, cc * TT:(cc + 1) * TT],
                            start=(cc == 0), stop=(cc == 7), skip_group_check=True)
                last.then_inc(sMM.h, 1)
            sMM.n += 1
            mm_qk = sMM.n

            @blk.scalar
            def _(sc, dcs=dcs, mm_qk=mm_qk, ti=ti, t0=t0):
                sc.wait_ge(sMM.h, mm_qk)
                if dcs[0] >= 8 and kst_lds:
                    sc.wait_ge(sKVk.h, kst_lds[-1])
                if dcs[0] < 8 and kst_lds:
                    sc.wait_ge(sQW.h, 16 * len(kst_lds))
                for i, dc in enumerate(dcs):
                    ps = (psA, psB)[i // 2]
                    off = (i % 2) * 512
                    if dc < 8:
                        dst = qstage[:, dc * TT:(dc + 1) * TT]
                    else:
                        dst = kslab[:, (dc - 8) * TT:(dc - 7) * TT]
                    sc.activation(dst, ps[:, off:off + TT], AF.Copy,
                                  bias=0.0, scale=1.0)
                    sc.square(sqs[:, i * TT:(i + 1) * TT], ps[:, off:off + TT]
                              ).then_inc(sSC.h, 1)
            sSC.n += 4
            sc_e = sSC.n

            @blk.vector
            def _(v, dcs=dcs, sc_e=sc_e, mm_qk=mm_qk, ti=ti, t0=t0):
                last = None
                for i, dc in enumerate(dcs):
                    v.wait_ge(sSC.h, sc_e - 3 + i)
                    if dc < 8:
                        srcv = qstage[:, dc * TT:(dc + 1) * TT]
                    else:
                        srcv = kslab[:, (dc - 8) * TT:(dc - 7) * TT]
                    v.reduce_sum(tcol[:, 0:1], srcv, AX)
                    v.reduce_sum(tcol[:, 1:2], sqs[:, i * TT:(i + 1) * TT], AX)
                    v.tensor_add(stc[:, dc:dc + 1], stc[:, dc:dc + 1], tcol[:, 0:1])
                    last = v.tensor_add(stc[:, 16 + dc:17 + dc],
                                        stc[:, 16 + dc:17 + dc], tcol[:, 1:2])
                last.then_inc(sEV.h, 1)
            sEV.n += 1
            qkv_evs.append(sEV.n)
            qkv_scs.append(sc_e)

        @blk.sync
        def _(s, ti=ti, t0=t0):
            s.wait_ge(sSC.h, qkv_scs[-1])
            s.dma_start(out=bass.AP(kv_in, (ti // 2) * HBLK + (ti % 2) * 512,
                                    [[8192, 128], [1024, 8], [1, 512]]),
                        in_=kslab[:].rearrange("p (c t) -> p c t", c=8)
                        ).then_inc(sKVk.h, 16)
            s.dma_start(out=bass.AP(q_dram, (ti // 2) * HBLK + (ti % 2) * 512,
                                    [[8192, 128], [1024, 8], [1, 512]]),
                        in_=qstage[:].rearrange("p (c t) -> p c t", c=8)
                        ).then_inc(sQW.h, 16)
        sKVk.n += 16
        kst_lds.append(sKVk.n)
        sQW.n += 16

        vev_scs = []
        for tc in range(4):
            @blk.tensor
            def _(t, tc=tc, ev_ln=ev_ln):
                if tc == 0:
                    t.wait_ge(sEV.h, ev_ln)
                    if last_vev_sc is not None:
                        t.wait_ge(sSC.h, last_vev_sc)
                else:
                    t.wait_ge(sSC.h, vev_scs[-1])
                last = None
                for cc in range(8):
                    t.matmul(psC[:, 0:512],
                             ln0[:, cc * TT + tc * 128:cc * TT + (tc + 1) * 128],
                             wmT[:, cc * D3 + 2048:cc * D3 + 2560],
                             start=(cc == 0), stop=(cc == 7), skip_group_check=True)
                for cc in range(8):
                    last = t.matmul(
                        psC[:, 512:1024],
                        ln0[:, cc * TT + tc * 128:cc * TT + (tc + 1) * 128],
                        wmT[:, cc * D3 + 2560:cc * D3 + 3072],
                        start=(cc == 0), stop=(cc == 7), skip_group_check=True)
                last.then_inc(sMM.h, 1)
            sMM.n += 1
            mm_v = sMM.n
            last_vproj_mm = mm_v

            @blk.scalar
            def _(sc, tc=tc, mm_v=mm_v, ti=ti):
                sc.wait_ge(sMM.h, mm_v)
                sc.activation(vslab[:, (ti * 4 + tc) * 1024:(ti * 4 + tc + 1) * 1024],
                              psC[:, 0:1024],
                              AF.Copy, bias=0.0, scale=1.0).then_inc(sSC.h, 1)
            sSC.n += 1
            vev_scs.append(sSC.n)
        last_vev_sc = vev_scs[-1]

        @blk.sync
        def _(s, ti=ti, t0=t0, w=vev_scs[-1]):
            s.wait_ge(sSC.h, w)
            s.dma_start(out=bass.AP(kv_in, VB + (ti // 2) * HBLK + (ti % 2) * 4096,
                                    [[8192, 128], [1024, 4], [1, 1024]]),
                        in_=vslab[:, ti * 4096:(ti + 1) * 4096
                                  ].rearrange("p (tc v) -> p tc v", tc=4)
                        ).then_inc(sKVv.h, 16)
        sKVv.n += 16
        vst_lds.append(sKVv.n)
    KVK_ALL = sKVk.n
    KVV_ALL = sKVv.n

    vst_mms = []
    LAST_LN_MM = prev_ln["mm_stats"]
    for ch in range(16):
        @blk.scalar
        def _(sc, ch=ch):
            if ch == 0:
                sc.wait_ge(sMM.h, LAST_LN_MM)
            sc.square(xsq[:, 0:1024], vslab[:, ch * 1024:(ch + 1) * 1024]
                      ).then_inc(sSC.h, 1)
        sSC.n += 1
        sc_v2 = sSC.n

        @blk.tensor
        def _(t, ch=ch, sc_v2=sc_v2):
            t.wait_ge(sSC.h, sc_v2)
            last = None
            for hh in range(2):
                t.matmul(psD[0:1, hh * 512:(hh + 1) * 512], onecb[:],
                         vslab[:, ch * 1024 + hh * 512:ch * 1024 + (hh + 1) * 512],
                         start=(ch == 0), stop=(ch == 15), skip_group_check=True)
                last = t.matmul(psC[0:1, hh * 512:(hh + 1) * 512], onecb[:],
                                xsq[:, hh * 512:(hh + 1) * 512],
                                start=(ch == 0), stop=(ch == 15),
                                skip_group_check=True)
            last.then_inc(sMM.h, 1)
        sMM.n += 1
        vst_mms.append(sMM.n)
    MM_VST = sMM.n

    @blk.vector
    def _(v):
        v.wait_ge(sMM.h, MM_VST)
        v.tensor_copy(vrow[0:1, 0:C], psD[0:1, 0:C])
        v.tensor_copy(vrow[0:1, C:2 * C], psC[0:1, 0:C]).then_inc(sEV.h, 1)
    sEV.n += 1
    EV_VROW = sEV.n

    @blk.sync
    def _(s):
        s.wait_ge(sEV.h, EV_VROW)
        s.dma_start(out=bass.AP(st_in, 0, [[32, 128], [1, 32]]), in_=stc[:]
                    ).then_inc(sSTT.h, 16)
        s.dma_start(out=st_in[0:1, 4096:6144], in_=vrow[0:1, :]).then_inc(sSTT.h, 16)
    sSTT.n += 32
    LD_STW = sSTT.n

    @blk.gpsimd
    def _(g):
        g.wait_ge(sSTT.h, LD_STW)
        g.collective_compute("AllReduce", AL.add, replica_groups=[list(range(NCORE))],
                             ins=[st_in.ap().opt()], outs=[st_out.ap().opt()]
                             ).then_inc(sCC.h, 1)
        g.wait_ge(sKVk.h, KVK_ALL)
        g.wait_ge(sKVv.h, KVV_ALL)
        g.collective_compute("AllGather", AL.bypass,
                             replica_groups=[list(range(NCORE))],
                             ins=[kv_in.ap().opt()], outs=[kv_all.ap().opt()]
                             ).then_inc(sCC.h, 1)
    sCC.n += 2

    @blk.sync
    def _(s):
        s.wait_ge(sCC.h, 1)
        s.dma_start(out=stc[:], in_=bass.AP(st_out, 0, [[32, 128], [1, 32]])
                    ).then_inc(sSR.h, 16)
        s.dma_start(out=vrow[0:1, :], in_=st_out[0:1, 4096:6144]).then_inc(sSR.h, 16)
    sSR.n += 32
    GD_STR = sSR.n

    @blk.sync
    def _(s):
        s.wait_ge(sCC.h, 1)
        s.dma_start(out=rowb[0:1, 2048:3072], in_=gam_vr.ap()).then_inc(sGV.h, 16)
        s.dma_start(out=rowb[0:1, 3072:4096], in_=bet_vr.ap()).then_inc(sGV.h, 16)
    sGV.n += 32
    ld_gv = sGV.n

    @blk.vector
    def _(v):
        v.wait_ge(sSR.h, GD_STR)
        v.tensor_scalar_mul(stc[:, 0:16], stc[:, 0:16], 1.0 / NTOK)
        v.tensor_scalar_mul(stc[:, 16:32], stc[:, 16:32], 1.0 / NTOK)
        v.tensor_scalar_mul(vrow[0:1, 0:C], vrow[0:1, 0:C], 1.0 / NTOK)
        v.tensor_mul(sm[:, 0:16], stc[:, 0:16], stc[:, 0:16])
        v.tensor_scalar_mul(vrow[0:1, C:2 * C], vrow[0:1, C:2 * C], 1.0 / NTOK)
        v.tensor_sub(abqk[:, 0:16], stc[:, 16:32], sm[:, 0:16])
        gap(v)
        v.tensor_scalar_add(abqk[:, 0:16], abqk[:, 0:16], EPS)
        gap(v).then_inc(sEV.h, 1)
    sEV.n += 1
    ev1 = sEV.n

    @blk.scalar
    def _(sc, ev1=ev1):
        sc.wait_ge(sEV.h, ev1)
        sc.sqrt(abqk[:, 0:16], abqk[:, 0:16])
        sc.square(rowb[0:1, 0:C], vrow[0:1, 0:C]).then_inc(sSC.h, 1)
    sSC.n += 1
    sc1 = sSC.n

    @blk.vector
    def _(v, sc1=sc1):
        v.wait_ge(sSC.h, sc1)
        v.reciprocal(abqk[:, 0:16], abqk[:, 0:16])
        v.tensor_sub(rowb[0:1, 0:C], vrow[0:1, C:2 * C], rowb[0:1, 0:C])
        v.tensor_mul(abqk[:, 0:16], abqk[:, 0:16], gamc[:])
        v.tensor_scalar_add(rowb[0:1, 0:C], rowb[0:1, 0:C], EPS)
        v.tensor_mul(abqk[:, 16:32], stc[:, 0:16], abqk[:, 0:16])
        gap(v)
        v.tensor_sub(abqk[:, 16:32], betc[:], abqk[:, 16:32])
        gap(v).then_inc(sEV.h, 1)
    sEV.n += 1
    ev2 = sEV.n

    @blk.scalar
    def _(sc, ev2=ev2):
        sc.wait_ge(sEV.h, ev2)
        sc.sqrt(rowb[0:1, 0:C], rowb[0:1, 0:C]).then_inc(sSC.h, 1)
    sSC.n += 1
    sc2 = sSC.n

    @blk.vector
    def _(v, sc2=sc2, ld_gv=ld_gv):
        v.wait_ge(sSC.h, sc2)
        v.wait_ge(sGV.h, ld_gv)
        v.reciprocal(rowb[0:1, 0:C], rowb[0:1, 0:C])
        v.tensor_mul(rowb[0:1, 0:C], rowb[0:1, 0:C], rowb[0:1, 2048:3072])
        v.tensor_mul(rowb[0:1, 1024:2048], vrow[0:1, 0:C], rowb[0:1, 0:C])
        v.tensor_sub(rowb[0:1, 1024:2048], rowb[0:1, 3072:4096],
                     rowb[0:1, 1024:2048])
        v.tensor_mul(qab[:, 0:8], abqk[:, 0:8], abqk[:, 8:16])
        v.tensor_mul(qab[:, 8:16], abqk[:, 16:24], abqk[:, 8:16]
                     ).then_inc(sEV.h, 1)
    sEV.n += 1
    EV_STATS = sEV.n

    @blk.tensor
    def _(t):
        t.wait_ge(sEV.h, EV_STATS)
        last = None
        for vcc in range(8):
            t.matmul(psD[:, vcc:vcc + 1], rowb[0:1, vcc * 128:(vcc + 1) * 128],
                     onerf[0:1, 0:1], start=True, stop=True, skip_group_check=True)
            last = t.matmul(psD[:, 8 + vcc:9 + vcc],
                            rowb[0:1, 1024 + vcc * 128:1024 + (vcc + 1) * 128],
                            onerf[0:1, 0:1], start=True, stop=True,
                            skip_group_check=True)
        last = t.matmul(psD[:, 16:24], onerf[:], gbr[0:1, 0:8],
                        start=True, stop=True, skip_group_check=True)
        last.then_inc(sMM.h, 1)
    sMM.n += 1
    mm_col = sMM.n

    @blk.vector
    def _(v, mm_col=mm_col):
        v.wait_ge(sMM.h, mm_col)
        v.tensor_copy(abv[:], psD[:, 0:16])
        v.tensor_copy(gbb[:], psD[:, 16:24]).then_inc(sEV.h, 1)
    sEV.n += 1
    EV_QAFF = sEV.n

    LD_QW = sQW.n

    s1.close()

    s2 = ExitStack()
    y_sb = s2.enter_context(nc.sbuf_tensor("y_sb", [128, 16 * 1024], F32))
    p_sb0 = s2.enter_context(nc.sbuf_tensor("p_sb0", [128, 1024], BF16))
    p_sb1 = s2.enter_context(nc.sbuf_tensor("p_sb1", [128, 1024], BF16))
    pT0 = s2.enter_context(nc.sbuf_tensor("pT0", [128, 1024], BF16))
    pT1 = s2.enter_context(nc.sbuf_tensor("pT1", [128, 1024], BF16))
    pvsb = s2.enter_context(nc.sbuf_tensor("pvsb", [128, 1024], F32))
    y_st = s2.enter_context(nc.sbuf_tensor("y_st", [128, 1024], F32))
    fc0 = s2.enter_context(nc.sbuf_tensor("fc0", [128, 8], F32))
    fc1 = s2.enter_context(nc.sbuf_tensor("fc1", [128, 8], F32))

    s2a = ExitStack()
    qcur0 = s2a.enter_context(nc.sbuf_tensor("qcur0", [128, 8192], BF16))
    qcur1 = s2a.enter_context(nc.sbuf_tensor("qcur1", [128, 8192], BF16))
    kb0 = s2a.enter_context(nc.sbuf_tensor("kb0", [128, 8192], BF16))
    kb1 = s2a.enter_context(nc.sbuf_tensor("kb1", [128, 8192], BF16))
    vb0 = s2a.enter_context(nc.sbuf_tensor("vb0", [128, 8192], BF16))
    vb1 = s2a.enter_context(nc.sbuf_tensor("vb1", [128, 8192], BF16))

    @blk.vector
    def _(v):
        v.wait_ge(sEV.h, EV_QAFF)
        v.memzero(y_sb[:]).then_inc(sEV.h, 1)
    sEV.n += 1
    EV_YZERO = sEV.n

    pairs = [("diagA",), ("diagB",)] + [("full", gi) for gi in range(7)]
    pair_done_mm = []
    pair_ld_gd = []
    qaff_evs = []
    sPL = (sPL0, sPL1)
    last_exp_sc = {}
    last_tr_mm = {}
    last_pTe_ev = [None]
    last_pve_sc = [None]

    for ip, pdesc in enumerate(pairs):
        par = ip % 2
        qcur = (qcur0, qcur1)[par]
        kb = (kb0, kb1)[par]
        vb = (vb0, vb1)[par]
        is_diag = pdesc[0] in ("diagA", "diagB")
        sbase = {"diagA": 0, "diagB": 8}.get(pdesc[0])
        sLD_p = sPL[ip % 2] if is_diag else None

        if is_diag:
            h_d = 0 if pdesc[0] == "diagA" else 1

            @blk.sync
            def _(s, ip=ip, qcur=qcur, vb=vb, h_d=h_d):
                if ip >= 2:
                    s.wait_ge(sMM.h, pair_done_mm[ip - 2])
                s.wait_ge(sMM.h, mm_col)
                s.wait_ge(sQW.h, LD_QW)
                s.dma_start(out=qcur[:],
                            in_=bass.AP(q_dram, h_d * HBLK,
                                        [[8192, 128], [1, 8192]])
                            ).then_inc(sLD_p.h, 16)
                s.wait_ge(sKVv.h, KVV_ALL)
                s.dma_start(out=vb[:],
                            in_=bass.AP(kv_in, VB + h_d * HBLK,
                                        [[8192, 128], [1, 8192]])
                            ).then_inc(sLD_p.h, 16)

            @blk.scalar
            def _(sc, ip=ip, kb=kb, h_d=h_d):
                if ip >= 2:
                    sc.wait_ge(sMM.h, pair_done_mm[ip - 2])
                sc.wait_ge(sKVk.h, KVK_ALL)
                sc.wait_ge(sMM.h, mm_col)
                sc.wait_ge(sQW.h, LD_QW)
                sc.dma_start(out=kb[:],
                             in_=bass.AP(kv_in, h_d * HBLK,
                                         [[8192, 128], [1, 8192]])
                             ).then_inc(sLD_p.h, 16)
        else:
            gi = pdesc[1]
            base = gi * 20

            @blk.gpsimd
            def _(g, ip=ip, qcur=qcur, base=base):
                if ip >= 2:
                    g.wait_ge(sMM.h, pair_done_mm[ip - 2])
                g.wait_ge(sCC.h, 2)
                g.wait_ge(sMM.h, mm_col)
                g.wait_ge(sQW.h, LD_QW)
                g.reg_load(rg3, schc[0:1, base + 2:base + 3])
                qoff = g.snap(rg3, min_val=0, max_val=HBLK)
                g.dma_start(out=qcur[:],
                            in_=bass.AP(q_dram, qoff, [[8192, 128], [1, 8192]])
                            ).then_inc(sQF.h, 16)
            sQF.n += 16

            @blk.sync
            def _(s, ip=ip, kb=kb, base=base):
                if ip >= 2:
                    s.wait_ge(sMM.h, pair_done_mm[ip - 2])
                s.wait_ge(sCC.h, 2)
                s.wait_ge(sMM.h, mm_col)
                s.wait_ge(sQW.h, LD_QW)
                s.reg_load(rs1, schc[0:1, base:base + 1])
                koff = s.snap(rs1, min_val=0, max_val=7 * SHARD + HBLK)
                s.dma_start(out=kb[:],
                            in_=bass.AP(kv_all, koff, [[8192, 128], [1, 8192]])
                            ).then_inc(sKF.h, 16)
            sKF.n += 16

            @blk.scalar
            def _(sc, ip=ip, vb=vb, base=base):
                if ip >= 2:
                    sc.wait_ge(sMM.h, pair_done_mm[ip - 2])
                sc.wait_ge(sCC.h, 2)
                sc.wait_ge(sMM.h, mm_col)
                sc.wait_ge(sQW.h, LD_QW)
                sc.reg_load(rc1, schc[0:1, base + 1:base + 2])
                voff = sc.snap(rc1, min_val=0, max_val=7 * SHARD + VB + HBLK)
                sc.dma_start(out=vb[:],
                             in_=bass.AP(kv_all, voff, [[8192, 128], [1, 8192]])
                             ).then_inc(sVF.h, 16)
            sVF.n += 16
        if is_diag:
            sLD_p.n += 48
            pair_ld_gd.append([(sLD_p, sLD_p.n)])
        else:
            pair_ld_gd.append([(sQF, sQF.n), (sKF, sKF.n), (sVF, sVF.n)])

        @blk.vector
        def _(v, ip=ip, qcur=qcur):
            for ld_sem, ld_val in pair_ld_gd[ip]:
                v.wait_ge(ld_sem.h, ld_val)
            v.wait_ge(sEV.h, EV_STATS)
            last = None
            for cc in range(8):
                v.tensor_scalar_mul(qcur[:, cc * 1024:(cc + 1) * 1024],
                                    qcur[:, cc * 1024:(cc + 1) * 1024],
                                    qab[:, cc:cc + 1])
                last = v.tensor_scalar_add(qcur[:, cc * 1024:(cc + 1) * 1024],
                                           qcur[:, cc * 1024:(cc + 1) * 1024],
                                           qab[:, 8 + cc:9 + cc])
            last.then_inc(sEV.h, 1)
        sEV.n += 1
        qaff_evs.append(sEV.n)

        for qs in range(8):
            nkt = (qs + 1) if is_diag else 8
            W = nkt * 128
            psS = (psA, psB)[qs % 2]
            p_s = (p_sb0, p_sb1)[qs % 2]
            pT_s = (pT0, pT1)[qs % 2]
            fc = (fc0, fc1)[qs % 2]

            @blk.tensor
            def _(t, ip=ip, qs=qs, W=W, psS=psS, qcur=qcur, kb=kb):
                if qs == 0:
                    for ld_sem, ld_val in pair_ld_gd[ip]:
                        t.wait_ge(ld_sem.h, ld_val)
                    t.wait_ge(sEV.h, qaff_evs[ip])
                    if ip == 0:
                        t.wait_ge(sEV.h, EV_YZERO)
                key = qs % 2
                if key in last_exp_sc:
                    t.wait_ge(sSC.h, last_exp_sc[key])
                nhal = (W + 511) // 512
                last = None
                for cc in range(8):
                    for h in range(nhal):
                        w0 = h * 512
                        w1 = min(W, w0 + 512)
                        last = t.matmul(
                            psS[:, w0:w1],
                            qcur[:, cc * 1024 + qs * 128:cc * 1024 + (qs + 1) * 128],
                            kb[:, cc * 1024 + w0:cc * 1024 + w1],
                            start=(cc == 0), stop=(cc == 7), skip_group_check=True)
                last.then_inc(sMM.h, 1)
            sMM.n += 1
            mm_s = sMM.n

            @blk.vector
            def _(v, qs=qs, W=W, psS=psS, fc=fc, is_diag=is_diag, sbase=sbase,
                  pdesc=pdesc, mm_s=mm_s):
                v.wait_ge(sMM.h, mm_s)
                if is_diag:
                    v.tensor_add(psS[:, qs * 128:(qs + 1) * 128],
                                 psS[:, qs * 128:(qs + 1) * 128], trim[:])
                    sl = sbase + qs
                    v.tensor_copy(fc[:, 0:1], m_sb[:, sl:sl + 1])
                    v.reduce_max(fc[:, 1:2], psS[:, 0:W], AX)
                    v.tensor_copy(fc[:, 5:6], l_sb[:, sl:sl + 1])
                    v.tensor_copy(y_st[:], y_sb[:, sl * 1024:(sl + 1) * 1024])
                else:
                    base = pdesc[1] * 20
                    v.reduce_max(fc[:, 1:2], psS[:, 0:W], AX)
                    v.reg_load(rv1, schc[0:1, base + 12 + qs:base + 13 + qs])
                    mls = v.snap(rv1, min_val=0, max_val=15)
                    v.tensor_copy(fc[:, 0:1], m_sb[:, bass.ds(mls, 1)])
                    v.tensor_copy(fc[:, 5:6], l_sb[:, bass.ds(mls, 1)])
                    v.reg_load(rv1, schc[0:1, base + 4 + qs:base + 5 + qs])
                    ysl = v.snap(rv1, min_val=0, max_val=15 * 1024)
                    v.tensor_copy(y_st[:], y_sb[:, bass.ds(ysl, 1024)])
                v.tensor_scalar_mul(fc[:, 1:2], fc[:, 1:2], SCALE)
                gap(v)
                v.tensor_max(fc[:, 2:3], fc[:, 0:1], fc[:, 1:2])
                gap(v)
                v.tensor_scalar_mul(fc[:, 3:4], fc[:, 2:3], -1.0)
                v.tensor_sub(fc[:, 7:8], fc[:, 0:1], fc[:, 2:3])
                gap(v).then_inc(sEV.h, 1)
            sEV.n += 1
            ev_mn = sEV.n

            @blk.scalar
            def _(sc, qs=qs, W=W, psS=psS, p_s=p_s, fc=fc, ev_mn=ev_mn):
                sc.wait_ge(sEV.h, ev_mn)
                key = qs % 2
                if key in last_tr_mm:
                    sc.wait_ge(sMM.h, last_tr_mm[key])
                sc.activation(fc[:, 4:5], fc[:, 7:8], AF.Exp, bias=0.0, scale=1.0)
                sc.activation(p_s[:, 0:W], psS[:, 0:W], AF.Exp,
                              bias=fc[:, 3:4], scale=SCALE).then_inc(sSC.h, 1)
            sSC.n += 1
            last_exp_sc[qs % 2] = sSC.n
            sc_p = sSC.n

            @blk.tensor
            def _(t, qs=qs, nkt=nkt, p_s=p_s, sc_p=sc_p):
                t.wait_ge(sSC.h, sc_p)
                if last_pTe_ev[0] is not None:
                    t.wait_ge(sEV.h, last_pTe_ev[0])
                psD_bf = psD[:].bitcast(BF16)
                last = None
                for tc in range(nkt):
                    last = t.transpose(psD_bf[:, tc * 128:(tc + 1) * 128],
                                       p_s[:, tc * 128:(tc + 1) * 128], idb[:])
                last.then_inc(sMM.h, 1)
            sMM.n += 1
            last_tr_mm[qs % 2] = sMM.n
            mm_t = sMM.n

            @blk.vector
            def _(v, qs=qs, W=W, p_s=p_s, pT_s=pT_s, fc=fc, is_diag=is_diag,
                  sbase=sbase, pdesc=pdesc, sc_p=sc_p, mm_t=mm_t):
                v.wait_ge(sSC.h, sc_p)
                v.reduce_sum(fc[:, 6:7], p_s[:, 0:W], AX)
                if is_diag:
                    sl = sbase + qs
                    v.tensor_copy(m_sb[:, sl:sl + 1], fc[:, 2:3])
                    v.scalar_tensor_tensor(fc[:, 5:6], fc[:, 5:6], fc[:, 4:5],
                                           fc[:, 6:7], AL.mult, AL.add)
                    gap(v)
                    v.tensor_copy(l_sb[:, sl:sl + 1], fc[:, 5:6])
                else:
                    base = pdesc[1] * 20
                    v.reg_load(rv1, schc[0:1, base + 12 + qs:base + 13 + qs])
                    mls = v.snap(rv1, min_val=0, max_val=15)
                    v.tensor_copy(m_sb[:, bass.ds(mls, 1)], fc[:, 2:3])
                    v.scalar_tensor_tensor(fc[:, 5:6], fc[:, 5:6], fc[:, 4:5],
                                           fc[:, 6:7], AL.mult, AL.add)
                    gap(v)
                    v.tensor_copy(l_sb[:, bass.ds(mls, 1)], fc[:, 5:6])
                v.wait_ge(sMM.h, mm_t)
                v.tensor_copy(pT_s[:, 0:W], psD[:].bitcast(BF16)[:, 0:W]
                              ).then_inc(sEV.h, 1)
            sEV.n += 1
            last_pTe_ev[0] = sEV.n
            ev_pte = sEV.n

            @blk.tensor
            def _(t, qs=qs, nkt=nkt, pT_s=pT_s, vb=vb, ev_pte=ev_pte):
                t.wait_ge(sEV.h, ev_pte)
                if last_pve_sc[0] is not None:
                    t.wait_ge(sSC.h, last_pve_sc[0])
                for tc in range(nkt):
                    t.matmul(psC[:, 0:512], pT_s[:, tc * 128:(tc + 1) * 128],
                             vb[:, tc * 1024:tc * 1024 + 512],
                             start=(tc == 0), stop=(tc == nkt - 1),
                             skip_group_check=True)
                last = None
                for tc in range(nkt):
                    last = t.matmul(psC[:, 512:1024],
                                    pT_s[:, tc * 128:(tc + 1) * 128],
                                    vb[:, tc * 1024 + 512:(tc + 1) * 1024],
                                    start=(tc == 0), stop=(tc == nkt - 1),
                                    skip_group_check=True)
                last.then_inc(sMM.h, 1)
            sMM.n += 1
            mm_pv = sMM.n

            @blk.scalar
            def _(sc, mm_pv=mm_pv):
                sc.wait_ge(sMM.h, mm_pv)
                sc.activation(pvsb[:], psC[:, 0:1024], AF.Copy,
                              bias=0.0, scale=1.0).then_inc(sSC.h, 1)
            sSC.n += 1
            last_pve_sc[0] = sSC.n
            sc_pve = sSC.n

            @blk.vector
            def _(v, fc=fc, is_diag=is_diag, sbase=sbase, qs=qs, pdesc=pdesc,
                  sc_pve=sc_pve):
                v.wait_ge(sSC.h, sc_pve)
                v.scalar_tensor_tensor(y_st[:], y_st[:], fc[:, 4:5], pvsb[:],
                                       AL.mult, AL.add)
                if is_diag:
                    sl = sbase + qs
                    v.tensor_copy(y_sb[:, sl * 1024:(sl + 1) * 1024], y_st[:]
                                  ).then_inc(sEV.h, 1)
                else:
                    base = pdesc[1] * 20
                    v.reg_load(rv1, schc[0:1, base + 4 + qs:base + 5 + qs])
                    ysl = v.snap(rv1, min_val=0, max_val=15 * 1024)
                    v.tensor_copy(y_sb[:, bass.ds(ysl, 1024)], y_st[:]
                                  ).then_inc(sEV.h, 1)
            sEV.n += 1
        pair_done_mm.append(sMM.n)
    EV_ATT = sEV.n
    MM_ATT = sMM.n

    s2a.close()

    s2b = ExitStack()
    y_fm = s2b.enter_context(nc.sbuf_tensor("y_fm", [128, 8 * NT], BF16))
    y1 = s2b.enter_context(nc.sbuf_tensor("y1", [128, 8 * NT], BF16))
    wpTs = s2b.enter_context(nc.sbuf_tensor("wpTs", [128, 8 * 1024], BF16))
    wmpTs = s2b.enter_context(nc.sbuf_tensor("wmpTs", [128, 8 * 1024], BF16))
    wptmp = s2b.enter_context(nc.sbuf_tensor("wptmp", [128, 1024], F32))
    apTb = s2b.enter_context(nc.sbuf_tensor("apTb", [64, C], BF16))
    bpb = s2b.enter_context(nc.sbuf_tensor("bpb", [64, C], BF16))
    xtl0 = s2b.enter_context(nc.sbuf_tensor("xtl0", [128, TT], F32))
    xtl1 = s2b.enter_context(nc.sbuf_tensor("xtl1", [128, TT], F32))
    x2t0 = s2b.enter_context(nc.sbuf_tensor("x2t0", [128, TT], F32))
    x2t1 = s2b.enter_context(nc.sbuf_tensor("x2t1", [128, TT], F32))
    ytmp = s2b.enter_context(nc.sbuf_tensor("ytmp", [128, 1024], BF16))

    @blk.gpsimd
    def _(g):
        g.wait_ge(sMM.h, MM_ATT)
        g.dma_start(out=apTb[:], in_=apT.ap()).then_inc(sGA.h, 16)
        g.dma_start(out=bpb[:], in_=bp_in.ap()).then_inc(sGA.h, 16)
    sGA.n += 32
    GD_PL = sGA.n

    wp_evs = []
    for vcc in range(8):
        @blk.sync
        def _(s, vcc=vcc):
            s.wait_ge(sMM.h, MM_ATT)
            if vcc >= 1:
                s.wait_ge(sEV.h, wp_evs[-1])
            s.dma_start(out=wptmp[:], in_=wprojT[vcc * 128:(vcc + 1) * 128, :]
                        ).then_inc(sW.h, 16)
        sW.n += 16
        ld_wp = sW.n

        @blk.tensor
        def _(t, vcc=vcc, ld_wp=ld_wp):
            t.wait_ge(sW.h, ld_wp)
            last = None
            for dc in range(8):
                last = t.matmul(psD[:, 24 + dc:25 + dc],
                                wptmp[:, dc * 128:(dc + 1) * 128],
                                abv[:, 8 + vcc:9 + vcc],
                                start=(vcc == 0), stop=(vcc == 7),
                                skip_group_check=True)
            last.then_inc(sMM.h, 1)
        sMM.n += 1
        mm_pb = sMM.n

        @blk.vector
        def _(v, vcc=vcc, mm_pb=mm_pb):
            v.wait_ge(sMM.h, mm_pb)
            last = v.tensor_scalar_mul(wpTs[:, vcc * 1024:(vcc + 1) * 1024],
                                       wptmp[:], abv[:, vcc:vcc + 1])
            if vcc == 7:
                last = v.tensor_copy(pbc[:], psD[:, 24:32])
            last.then_inc(sEV.h, 1)
        sEV.n += 1
        wp_evs.append(sEV.n)

    wmp_evs = []
    for cc in range(8):
        @blk.sync
        def _(s, cc=cc):
            s.wait_ge(sEV.h, wp_evs[-1] if cc == 0 else wmp_evs[-1])
            s.dma_start(out=wptmp[:], in_=wprojT[cc * 128:(cc + 1) * 128, :]
                        ).then_inc(sW.h, 16)
        sW.n += 16
        ld_wp2 = sW.n

        @blk.tensor
        def _(t, cc=cc, ld_wp2=ld_wp2):
            t.wait_ge(sW.h, ld_wp2)
            if cc == 0:
                t.wait_ge(sGA.h, GD_PL)
            last = None
            for rb in range(2):
                last = t.matmul(psA[:, rb * 512:(rb + 1) * 512],
                                bpb[:, cc * 128:(cc + 1) * 128],
                                apTb[:, rb * 512:(rb + 1) * 512],
                                start=True, stop=True, skip_group_check=True)
            last.then_inc(sMM.h, 1)
        sMM.n += 1
        mm_mp = sMM.n

        @blk.vector
        def _(v, cc=cc, mm_mp=mm_mp):
            v.wait_ge(sMM.h, mm_mp)
            v.tensor_tensor(wmpTs[:, cc * 1024:(cc + 1) * 1024], wptmp[:],
                            psA[:, 0:1024], AL.add).then_inc(sEV.h, 1)
        sEV.n += 1
        wmp_evs.append(sEV.n)
    EV_WPM = sEV.n

    fin_evs = []
    for sl in range(16):
        @blk.vector
        def _(v, sl=sl):
            if sl == 0:
                v.wait_ge(sEV.h, EV_ATT)
            v.reciprocal(fc0[:, 7:8], l_sb[:, sl:sl + 1])
            gap(v)
            v.tensor_scalar_mul(ytmp[:], y_sb[:, sl * 1024:(sl + 1) * 1024],
                                fc0[:, 7:8]).then_inc(sEV.h, 1)
        sEV.n += 1
        ev_yt = sEV.n

        @blk.tensor
        def _(t, sl=sl, ev_yt=ev_yt):
            t.wait_ge(sEV.h, ev_yt)
            psB_bf = psB[:].bitcast(BF16)
            last = None
            for vcc in range(8):
                last = t.transpose(psB_bf[:, vcc * 128:(vcc + 1) * 128],
                                   ytmp[:, vcc * 128:(vcc + 1) * 128], idb[:])
            last.then_inc(sMM.h, 1)
        sMM.n += 1
        mm_ytr = sMM.n

        @blk.vector
        def _(v, sl=sl, mm_ytr=mm_ytr):
            v.wait_ge(sMM.h, mm_ytr)
            v.tensor_copy(bass.AP(y_fm, sl * 128, [[8 * NT, 128], [NT, 8], [1, 128]]),
                          psB[:].bitcast(BF16)[:, 0:1024
                          ].rearrange("p (c t) -> p c t", c=8)
                          ).then_inc(sEV.h, 1)
        sEV.n += 1
        fin_evs.append(sEV.n)
    EV_FIN = sEV.n

    p1_scs = []
    for dc in range(8):
        for tt4 in range(4):
            ps = (psA, psB)[(dc * 4 + tt4) % 2]

            @blk.tensor
            def _(t, dc=dc, tt4=tt4, ps=ps):
                if dc == 0 and tt4 == 0:
                    t.wait_ge(sEV.h, EV_FIN)
                    t.wait_ge(sEV.h, EV_WPM)
                if len(p1_scs) >= 2:
                    t.wait_ge(sSC.h, p1_scs[-2])
                last = None
                for vcc in range(8):
                    last = t.matmul(
                        ps[:, 0:512],
                        wpTs[:, vcc * 1024 + dc * 128:vcc * 1024 + (dc + 1) * 128],
                        y_fm[:, vcc * NT + tt4 * 512:vcc * NT + (tt4 + 1) * 512],
                        start=(vcc == 0), stop=(vcc == 7), skip_group_check=True)
                last.then_inc(sMM.h, 1)
            sMM.n += 1
            mm_p1 = sMM.n

            @blk.scalar
            def _(sc, dc=dc, tt4=tt4, ps=ps, mm_p1=mm_p1):
                sc.wait_ge(sMM.h, mm_p1)
                sc.activation(y1[:, dc * NT + tt4 * 512:dc * NT + (tt4 + 1) * 512],
                              ps[:, 0:512], AF.Identity,
                              bias=pbc[:, dc:dc + 1], scale=1.0).then_inc(sSC.h, 1)
            sSC.n += 1
            p1_scs.append(sSC.n)
    SC_P1 = sSC.n

    p2_evs = []
    x2st_lds = []
    for dc2 in range(8):
        for tt4 in range(4):
            idx = dc2 * 4 + tt4
            ps = (psC, psD)[idx % 2]
            xtl = (xtl0, xtl1)[idx % 2]
            x2t = (x2t0, x2t1)[idx % 2]

            sXT = (sXT0, sXT1)[idx % 2]

            @blk.sync
            def _(s, dc2=dc2, tt4=tt4, xtl=xtl, sXT=sXT):
                if len(p2_evs) >= 2:
                    s.wait_ge(sEV.h, p2_evs[-2])
                s.dma_start(out=xtl[:],
                            in_=bass.AP(x_fm, dc2 * 128 * NT + tt4 * 512,
                                        [[NT, 128], [1, 512]])
                            ).then_inc(sXT.h, 16)
            sXT.n += 16
            ld_xt = sXT.n

            @blk.tensor
            def _(t, dc2=dc2, tt4=tt4, ps=ps, idx=idx):
                if idx == 0:
                    t.wait_ge(sSC.h, SC_P1)
                if len(p2_evs) >= 2:
                    t.wait_ge(sEV.h, p2_evs[-2])
                last = None
                for cc in range(8):
                    last = t.matmul(
                        ps[:, 0:512],
                        wmpTs[:, cc * 1024 + dc2 * 128:cc * 1024 + (dc2 + 1) * 128],
                        y1[:, cc * NT + tt4 * 512:cc * NT + (tt4 + 1) * 512],
                        start=(cc == 0), stop=(cc == 7), skip_group_check=True)
                last.then_inc(sMM.h, 1)
            sMM.n += 1
            mm_p2 = sMM.n

            sX2 = (sX2a, sX2b)[idx % 2]

            @blk.vector
            def _(v, ps=ps, xtl=xtl, x2t=x2t, mm_p2=mm_p2, ld_xt=ld_xt,
                  sXT=sXT, sX2=sX2):
                v.wait_ge(sMM.h, mm_p2)
                v.wait_ge(sXT.h, ld_xt)
                if len(x2st_lds) >= 2:
                    v.wait_ge(sX2.h, x2st_lds[-2][1])
                v.tensor_add(x2t[:], ps[:, 0:512], xtl[:]).then_inc(sEV.h, 1)
            sEV.n += 1
            p2_evs.append(sEV.n)
            ev_x2 = sEV.n

            @blk.sync
            def _(s, dc2=dc2, tt4=tt4, x2t=x2t, ev_x2=ev_x2, sX2=sX2):
                s.wait_ge(sEV.h, ev_x2)
                s.dma_start(out=bass.AP(x2_sp, dc2 * 128 * NT + tt4 * 512,
                                        [[NT, 128], [1, 512]]),
                            in_=x2t[:]).then_inc(sX2.h, 16)
            sX2.n += 16
            x2st_lds.append((idx % 2, sX2.n))
    X2A_ALL = sX2a.n
    X2B_ALL = sX2b.n
    EV_P2 = sEV.n

    s2b.close()
    s2.close()

    s4 = ExitStack()
    ln1 = s4.enter_context(nc.sbuf_tensor("ln1", [128, 8 * NT], BF16))
    rowb4 = s4.enter_context(nc.sbuf_tensor("rowb4", [1, 4096], F32))
    wfm = s4.enter_context(nc.sbuf_tensor("wfm", [8, NT], BF16))
    s4b = ExitStack()
    xt4 = s4b.enter_context(nc.sbuf_tensor("xt4", [128, 8 * TT], F32))
    xb4 = s4b.enter_context(nc.sbuf_tensor("xb4", [128, 8 * TT], BF16))
    xsq4 = s4b.enter_context(nc.sbuf_tensor("xsq4", [128, 8 * TT], BF16))
    abc4 = s4b.enter_context(nc.sbuf_tensor("abc4", [128, 2 * TT], F32))
    xtmp4 = s4b.enter_context(nc.sbuf_tensor("xtmp4", [128, TT], F32))
    lgtm = s4b.enter_context(nc.sbuf_tensor("lgtm", [128, 128], F32))
    e8 = s4b.enter_context(nc.sbuf_tensor("e8", [128, 8], F32))
    eq8 = s4b.enter_context(nc.sbuf_tensor("eq8", [128, 8], F32))
    wtm = s4b.enter_context(nc.sbuf_tensor("wtm", [128, 128], F32))

    prev4 = None
    for ti in range(NTILE):
        fw_sync = [(sX2a, X2A_ALL), (sX2b, X2B_ALL)] if ti == 0 else []
        fw_mm = [(sEV, EV_P2)] if ti == 0 else []
        prev4 = emit_ln_tile(
            ti, x2_sp, xt4, xb4, xsq4, abc4, xtmp4, ln1,
            lambda cc, ti_: (cc * NT + ti_ * TT, cc * NT + (ti_ + 1) * TT),
            prev4, fw_sync, fw_mm, None, rowb4)
    EV_LN1 = prev4["ev_ln"]

    gate_evs = []
    for grp in range(2):
        @blk.tensor
        def _(t, grp=grp):
            t.wait_ge(sEV.h, EV_LN1)
            if gate_evs:
                t.wait_ge(sEV.h, gate_evs[-1])
            last = None
            for tc8 in range(8):
                tc = grp * 8 + tc8
                for cc in range(8):
                    last = t.matmul(
                        psD[:, tc8 * 8:(tc8 + 1) * 8],
                        ln1[:, cc * NT + tc * 128:cc * NT + (tc + 1) * 128],
                        gwb[:, cc * 8:(cc + 1) * 8],
                        start=(cc == 0), stop=(cc == 7), skip_group_check=True)
            last.then_inc(sMM.h, 1)
        sMM.n += 1
        mm_g = sMM.n

        @blk.vector
        def _(v, mm_g=mm_g):
            v.wait_ge(sMM.h, mm_g)
            gbb_b = bass.AP(gbb, 0, [[8, 128], [0, 8], [1, 8]])
            lg3 = lgtm[:, 0:64].rearrange("p (tc e) -> p tc e", tc=8)
            ps3 = psD[:, 0:64].rearrange("p (tc e) -> p tc e", tc=8)
            v.tensor_tensor(lg3, ps3, gbb_b, AL.add)
            gap(v).then_inc(sEV.h, 1)
        sEV.n += 1
        ev_lg = sEV.n

        for tc8 in range(8):
            @blk.vector
            def _(v, tc8=tc8, ev_lg=ev_lg):
                v.wait_ge(sEV.h, ev_lg)
                lg = lgtm[:, tc8 * 8:(tc8 + 1) * 8]
                v.reduce_max(sm[:, 0:1], lg, AX)
                v.tensor_copy(sm[:, 4:5], lg[:, 0:1])
                v.tensor_scalar_mul(sm[:, 3:4], sm[:, 0:1], -1.0)
                v.tensor_scalar(eq8[:], lg, sm[:, 0:1], None, AL.is_equal)
                gap(v)
                v.scalar_tensor_tensor(eq8[:], eq8[:], -1.0e30, lg, AL.mult, AL.add)
                gap(v)
                v.reduce_max(sm[:, 1:2], eq8[:], AX)
                gap(v).then_inc(sEV.h, 1)
            sEV.n += 1
            ev_m12 = sEV.n

            @blk.scalar
            def _(sc, tc8=tc8, ev_m12=ev_m12):
                sc.wait_ge(sEV.h, ev_m12)
                sc.activation(e8[:], lgtm[:, tc8 * 8:(tc8 + 1) * 8], AF.Exp,
                              bias=sm[:, 3:4], scale=1.0).then_inc(sSC.h, 1)
            sSC.n += 1
            sc_e8 = sSC.n

            @blk.vector
            def _(v, tc8=tc8, sc_e8=sc_e8):
                v.wait_ge(sSC.h, sc_e8)
                lg = lgtm[:, tc8 * 8:(tc8 + 1) * 8]
                v.reduce_sum(sm[:, 2:3], e8[:], AX)
                v.tensor_scalar(eq8[:], lg, sm[:, 1:2], None, AL.is_ge)
                v.reciprocal(sm[:, 2:3], sm[:, 2:3])
                v.tensor_mul(e8[:], e8[:], eq8[:])
                gap(v)
                v.tensor_scalar_mul(wtm[:, tc8 * 8:(tc8 + 1) * 8], e8[:],
                                    sm[:, 2:3])
                gap(v).then_inc(sEV.h, 1)
            sEV.n += 1
        ev_w = sEV.n

        @blk.tensor
        def _(t, grp=grp, ev_w=ev_w):
            t.wait_ge(sEV.h, ev_w)
            last = None
            for tc8 in range(8):
                last = t.transpose(psC[0:8, tc8 * 128:(tc8 + 1) * 128],
                                   wtm[:, tc8 * 8:(tc8 + 1) * 8], idf[:])
            last.then_inc(sMM.h, 1)
        sMM.n += 1
        mm_wt = sMM.n

        @blk.vector
        def _(v, grp=grp, mm_wt=mm_wt):
            v.wait_ge(sMM.h, mm_wt)
            v.tensor_copy(wfm[0:8, grp * 1024:(grp + 1) * 1024], psC[0:8, 0:1024]
                          ).then_inc(sEV.h, 1)
        sEV.n += 1
        gate_evs.append(sEV.n)
    EV_GATE = sEV.n

    s4b.close()

    s4c = ExitStack()
    hsl = s4c.enter_context(nc.sbuf_tensor("hsl", [128, 16 * 1024], BF16))
    w1p0 = s4c.enter_context(nc.sbuf_tensor("w1p0", [128, 8192], BF16))
    w1p1 = s4c.enter_context(nc.sbuf_tensor("w1p1", [128, 8192], BF16))
    w2p0 = s4c.enter_context(nc.sbuf_tensor("w2p0", [128, 8192], BF16))
    w2p1 = s4c.enter_context(nc.sbuf_tensor("w2p1", [128, 8192], BF16))
    oacc = s4c.enter_context(nc.sbuf_tensor("oacc", [128, 8 * 1024], F32))
    wbc0 = s4c.enter_context(nc.sbuf_tensor("wbc0", [128, 1024], BF16))
    wbc1 = s4c.enter_context(nc.sbuf_tensor("wbc1", [128, 1024], BF16))
    wr0 = s4c.enter_context(nc.sbuf_tensor("wr0", [1, 1024], BF16))
    b2b = s4c.enter_context(nc.sbuf_tensor("b2b", [8, 1024], BF16))

    ffn_e_mm = {}
    mm_f1 = {}
    mm_f2 = {}
    w_gd = {}
    last_mm_wb = [None]

    @blk.gpsimd
    def _(g):
        g.wait_ge(sEV.h, EV_GATE)
        g.dma_start(out=b2b[:], in_=b2_in.ap()).then_inc(sGA.h, 16)
    sGA.n += 16
    GD_B2 = sGA.n

    for th in range(2):
        @blk.sync
        def _(s, th=th):
            s.wait_ge(sEV.h, EV_GATE)
            if th == 1:
                s.wait_ge(sST.h, 16)
            s.dma_start(out=oacc[:].rearrange("p (o t) -> p o t", o=8),
                        in_=bass.AP(x2_sp, th * 1024,
                                    [[NT, 128], [128 * NT, 8], [1, 1024]])
                        ).then_inc(sOI.h, 16)
        sOI.n += 16
        ld_oi = sOI.n

        relu_scs = []
        o_evs = []
        for e in range(8):
            par = e % 2
            wbc = (wbc0, wbc1)[par]

            @blk.gpsimd
            def _(g, e=e, th=th):
                for jh in range(2):
                    if (th, e - 1, jh) in mm_f1:
                        g.wait_ge(sMM.h, mm_f1[(th, e - 1, jh)])
                    elif th == 1 and e == 0:
                        g.wait_ge(sMM.h, mm_f1[(0, 7, jh)])
                    elif th == 0 and e == 0:
                        g.wait_ge(sEV.h, EV_GATE)
                    sWx = (sW10, sW11)[jh]
                    g.dma_start(
                        out=(w1p0, w1p1)[jh][:].rearrange("p (c j) -> p c j", c=8),
                        in_=bass.AP(w1_in, e * C * 2048 + jh * 1024,
                                    [[2048, 128], [128 * 2048, 8], [1, 1024]])
                        ).then_inc(sWx.h, 16)
                    sWx.n += 16
                    w_gd[(th, e, "w1", jh)] = (sWx, sWx.n)
                for oh in range(2):
                    if (th, e - 1, oh) in mm_f2:
                        g.wait_ge(sMM.h, mm_f2[(th, e - 1, oh)])
                    elif th == 1 and e == 0:
                        g.wait_ge(sMM.h, mm_f2[(0, 7, oh)])
                    sWx = (sW20, sW21)[oh]
                    g.dma_start(
                        out=(w2p0, w2p1)[oh][:].rearrange("p (j o) -> p j o", j=16),
                        in_=bass.AP(w2_in, e * 2048 * C + oh * 512,
                                    [[C, 128], [128 * C, 16], [1, 512]])
                        ).then_inc(sWx.h, 16)
                    sWx.n += 16
                    w_gd[(th, e, "w2", oh)] = (sWx, sWx.n)

            @blk.sync
            def _(s, e=e, th=th):
                s.wait_ge(sEV.h, EV_GATE)
                if last_mm_wb[0] is not None:
                    s.wait_ge(sMM.h, last_mm_wb[0])
                s.dma_start(out=wr0[0:1, :],
                            in_=wfm[e:e + 1, th * 1024:(th + 1) * 1024]
                            ).then_inc(sWR.h, 16)
            sWR.n += 16
            ld_wr = sWR.n

            @blk.tensor
            def _(t, e=e, th=th, ld_wr=ld_wr):
                t.wait_ge(sWR.h, ld_wr)
                if relu_scs:
                    t.wait_ge(sSC.h, relu_scs[-1])
                t.matmul(psB[:, 0:512], onerb[:], wr0[0:1, 0:512],
                         start=True, stop=True, skip_group_check=True)
                t.matmul(psB[:, 512:1024], onerb[:], wr0[0:1, 512:1024],
                         start=True, stop=True, skip_group_check=True
                         ).then_inc(sMM.h, 1)
            sMM.n += 1
            mm_wb = sMM.n
            last_mm_wb[0] = mm_wb

            @blk.vector
            def _(v, wbc=wbc, mm_wb=mm_wb):
                v.wait_ge(sMM.h, mm_wb)
                v.tensor_copy(wbc[:], psB[:, 0:1024]).then_inc(sEV.h, 1)
            sEV.n += 1
            ev_wb = sEV.n

            for jc in range(16):
                psH = (psA, psB)[jc % 2]

                @blk.tensor
                def _(t, e=e, th=th, jc=jc, psH=psH, ev_wb=ev_wb):
                    jh = jc // 8
                    jj = jc % 8
                    if jj == 0:
                        wsem, wval = w_gd[(th, e, "w1", jh)]
                        t.wait_ge(wsem.h, wval)
                    if jc <= 1:
                        t.wait_ge(sEV.h, ev_wb)
                    if len(relu_scs) >= 2:
                        t.wait_ge(sSC.h, relu_scs[-2])
                    w1b = (w1p0, w1p1)[jh]
                    last = None
                    for cc in range(8):
                        lhs = w1b[:, cc * 1024 + jj * 128:
                                  cc * 1024 + (jj + 1) * 128]
                        t.matmul(psH[:, 0:512], lhs,
                                 ln1[:, cc * NT + th * 1024:
                                     cc * NT + th * 1024 + 512],
                                 start=(cc == 0), stop=(cc == 7),
                                 skip_group_check=True)
                        last = t.matmul(psH[:, 512:1024], lhs,
                                        ln1[:, cc * NT + th * 1024 + 512:
                                            cc * NT + (th + 1) * 1024],
                                        start=(cc == 0), stop=(cc == 7),
                                        skip_group_check=True)
                    last.then_inc(sMM.h, 1)
                sMM.n += 1
                mm_h = sMM.n
                if jc == 7:
                    mm_f1[(th, e, 0)] = mm_h
                elif jc == 15:
                    mm_f1[(th, e, 1)] = mm_h

                @blk.scalar
                def _(sc, e=e, jc=jc, psH=psH, mm_h=mm_h):
                    sc.wait_ge(sMM.h, mm_h)
                    sc.activation(hsl[:, jc * 1024:(jc + 1) * 1024], psH[:, 0:1024],
                                  AF.Relu, bias=b1c[:, e * 16 + jc:e * 16 + jc + 1],
                                  scale=1.0).then_inc(sSC.h, 1)
                sSC.n += 1
                relu_scs.append(sSC.n)
                sc_r = sSC.n

                @blk.vector
                def _(v, jc=jc, wbc=wbc, sc_r=sc_r):
                    v.wait_ge(sSC.h, sc_r)
                    v.tensor_mul(hsl[:, jc * 1024:(jc + 1) * 1024],
                                 hsl[:, jc * 1024:(jc + 1) * 1024], wbc[:]
                                 ).then_inc(sEV.h, 1)
                sEV.n += 1
            ev_hdone = sEV.n

            for oc in range(8):
                psO = (psC, psD)[oc % 2]

                @blk.tensor
                def _(t, e=e, th=th, oc=oc, psO=psO, ev_hdone=ev_hdone):
                    oh = oc // 4
                    oo = oc % 4
                    if oc == 0:
                        t.wait_ge(sEV.h, ev_hdone)
                    if oo == 0:
                        wsem, wval = w_gd[(th, e, "w2", oh)]
                        t.wait_ge(wsem.h, wval)
                    if len(o_evs) >= 2:
                        t.wait_ge(sEV.h, o_evs[-2])
                    w2b = (w2p0, w2p1)[oh]
                    last = None
                    for jc in range(16):
                        lhs = w2b[:, jc * 512 + oo * 128:
                                  jc * 512 + (oo + 1) * 128]
                        t.matmul(psO[:, 0:512], lhs,
                                 hsl[:, jc * 1024:jc * 1024 + 512],
                                 start=(jc == 0), stop=(jc == 15),
                                 skip_group_check=True)
                        last = t.matmul(psO[:, 512:1024], lhs,
                                        hsl[:, jc * 1024 + 512:(jc + 1) * 1024],
                                        start=(jc == 0), stop=(jc == 15),
                                        skip_group_check=True)
                    last.then_inc(sMM.h, 1)
                sMM.n += 1
                mm_o = sMM.n
                if oc == 3:
                    mm_f2[(th, e, 0)] = mm_o
                elif oc == 7:
                    mm_f2[(th, e, 1)] = mm_o

                @blk.vector
                def _(v, oc=oc, psO=psO, mm_o=mm_o, ld_oi=ld_oi):
                    v.wait_ge(sMM.h, mm_o)
                    v.wait_ge(sOI.h, ld_oi)
                    v.tensor_add(oacc[:, oc * 1024:(oc + 1) * 1024],
                                 oacc[:, oc * 1024:(oc + 1) * 1024],
                                 psO[:, 0:1024]).then_inc(sEV.h, 1)
                sEV.n += 1
                o_evs.append(sEV.n)
            ffn_e_mm[(th, e)] = sMM.n

        for ocp in range(4):
            @blk.tensor
            def _(t, th=th, ocp=ocp):
                if ocp == 0:
                    t.wait_ge(sEV.h, o_evs[-1])
                    t.wait_ge(sGA.h, GD_B2)
                else:
                    t.wait_ge(sEV.h, sEV.n)
                last = None
                for j in range(2):
                    oc = ocp * 2 + j
                    psO = (psC, psD)[j]
                    for hh in range(2):
                        last = t.matmul(
                            psO[:, hh * 512:(hh + 1) * 512],
                            b2b[:, oc * 128:(oc + 1) * 128],
                            wfm[0:8, th * 1024 + hh * 512:
                                th * 1024 + (hh + 1) * 512],
                            start=True, stop=True, skip_group_check=True)
                last.then_inc(sMM.h, 1)
            sMM.n += 1
            mm_b2 = sMM.n

            @blk.vector
            def _(v, ocp=ocp, mm_b2=mm_b2):
                v.wait_ge(sMM.h, mm_b2)
                last = None
                for j in range(2):
                    oc = ocp * 2 + j
                    psO = (psC, psD)[j]
                    last = v.tensor_add(oacc[:, oc * 1024:(oc + 1) * 1024],
                                        oacc[:, oc * 1024:(oc + 1) * 1024],
                                        psO[:, 0:1024])
                last.then_inc(sEV.h, 1)
            sEV.n += 1
        ev_b2 = sEV.n

        @blk.sync
        def _(s, th=th, ev_b2=ev_b2):
            s.wait_ge(sEV.h, ev_b2)
            s.dma_start(out=bass.AP(out_fm, th * 1024,
                                    [[NT, 128], [128 * NT, 8], [1, 1024]]),
                        in_=oacc[:].rearrange("p (o t) -> p o t", o=8)
                        ).then_inc(sST.h, 16)
        sST.n += 16

    @blk.sync
    def _(s):
        s.wait_ge(sST.h, sST.n)

    s4c.close()
    s4.close()
    es.close()
    nc.finalize()
    return nc


_NC_CACHE = {}


def _get_nc():
    if "nc" not in _NC_CACHE:
        _NC_CACHE["nc"] = build_nc()
    return _NC_CACHE["nc"]


def _make_sched(c):
    pA, pB = c, 7 - c
    rows = []
    for g in range(pA):
        sh = g
        rows.append((sh * SHARD + 0, sh * SHARD + VB + 0, 0, 0))
    for g in range(pB):
        sh = 7 - g
        rows.append((sh * SHARD + HBLK, sh * SHARD + VB + HBLK, HBLK, 8))
    assert len(rows) == 7
    tab = np.zeros((7, 20), dtype=np.int32)
    for gi, (ko, vo, qo, sb) in enumerate(rows):
        tab[gi, 0] = ko
        tab[gi, 1] = vo
        tab[gi, 2] = qo
        tab[gi, 3] = sb
        for qs in range(8):
            tab[gi, 4 + qs] = (sb + qs) * 1024
            tab[gi, 12 + qs] = sb + qs
    return tab.reshape(1, 140)


def kernel(L0_attn_x, L0_attn_W_attn, L0_attn_W_proj, L0_attn_lora_attn_A,
           L0_attn_lora_attn_B, L0_attn_lora_proj_A, L0_attn_lora_proj_B,
           L0_attn_bn_gamma, L0_attn_bn_beta, L0_ffn_gate_w, L0_ffn_gate_b,
           L0_ffn_w1, L0_ffn_b1, L0_ffn_w2, L0_ffn_b2):
    x = np.asarray(L0_attn_x, dtype=np.float32)
    cst = np.ascontiguousarray
    wattnT_h = cst(np.asarray(L0_attn_W_attn, np.float32).T)
    aaT_h = cst(np.asarray(L0_attn_lora_attn_A, np.float32).T)
    ba_h = cst(np.asarray(L0_attn_lora_attn_B, np.float32))
    wprojT_h = cst(np.asarray(L0_attn_W_proj, np.float32).T)
    apT_h = cst(np.asarray(L0_attn_lora_proj_A, np.float32).T)
    bp_h = cst(np.asarray(L0_attn_lora_proj_B, np.float32))
    gam = np.asarray(L0_attn_bn_gamma, np.float32)
    bet = np.asarray(L0_attn_bn_beta, np.float32)
    gam_qk_h = cst(gam[0:2048].reshape(16, 128).T)
    bet_qk_h = cst(bet[0:2048].reshape(16, 128).T)
    gam_vr_h = cst(gam[2048:3072].reshape(1, C))
    bet_vr_h = cst(bet[2048:3072].reshape(1, C))
    gwT_h = cst(np.asarray(L0_ffn_gate_w, np.float32).T)
    gb_row_h = cst(np.asarray(L0_ffn_gate_b, np.float32).reshape(1, 8))
    w1_h = cst(np.asarray(L0_ffn_w1, np.float32))
    b1_t_h = cst(np.asarray(L0_ffn_b1, np.float32).reshape(8, 16, 128)
                 .transpose(2, 0, 1).reshape(128, 128))
    w2_h = cst(np.asarray(L0_ffn_w2, np.float32))
    b2_h = cst(np.asarray(L0_ffn_b2, np.float32))

    nc = _get_nc()
    in_maps = []
    for c in range(NCORE):
        a0 = x[0, 1024 * c:1024 * (c + 1)]
        a1 = x[1, 1024 * (7 - c):1024 * (8 - c)]
        x_fm_h = cst(np.concatenate([a0, a1], axis=0).T)
        in_maps.append({
            "x_fm": x_fm_h, "wattnT": wattnT_h, "aaT": aaT_h, "ba_in": ba_h,
            "wprojT": wprojT_h, "apT": apT_h, "bp_in": bp_h,
            "gam_qk": gam_qk_h, "bet_qk": bet_qk_h,
            "gam_vr": gam_vr_h, "bet_vr": bet_vr_h,
            "gwT": gwT_h, "gb_row": gb_row_h,
            "w1_in": w1_h, "b1_t": b1_t_h, "w2_in": w2_h, "b2_in": b2_h,
            "sched": _make_sched(c),
        })
    res = run_bass_kernel_spmd(nc, in_maps, core_ids=list(range(NCORE)))
    out = np.zeros((2, 8192, 1024), dtype=np.float32)
    for c in range(NCORE):
        o = res.results[c]["out_fm"]
        out[0, 1024 * c:1024 * (c + 1)] = o[:, 0:1024].T
        out[1, 1024 * (7 - c):1024 * (8 - c)] = o[:, 1024:2048].T
    return out


if __name__ == "__main__":
    nc = build_nc()
    print("build OK; instructions:", nc.n_instructions())


# version 07
# speedup vs baseline: 1.0440x; 1.0347x over previous
import numpy as np
import ml_dtypes
from contextlib import ExitStack
import concourse.bass as bass
import concourse.bacc as bacc
import concourse.mybir as mybir
from concourse.bass_utils import run_bass_kernel_spmd

F32 = mybir.dt.float32
BF16 = mybir.dt.bfloat16
I32 = mybir.dt.int32
AF = mybir.ActivationFunctionType
AL = mybir.AluOpType
AX = mybir.AxisListType.X

NCORE = 8
NT = 2048
C = 1024
D3 = 3072
KSH = C * NT
VB = KSH
SHARD = KSH + NT * C
TT = 512
NTILE = NT // TT
HBLK = 128 * 8 * 1024
SCALE = 1.0 / 32.0
EPS = 1e-5
NEGBIG = -1.0e9
NTOK = 16384.0


def build_nc():
    nc = bacc.Bacc("TRN2", target_bir_lowering=False, num_devices=NCORE,
                   detect_race_conditions=False)

    x_fm = nc.dram_tensor("x_fm", [C, NT], F32, kind="ExternalInput")
    wattnT = nc.dram_tensor("wattnT", [C, D3], F32, kind="ExternalInput")
    aaT = nc.dram_tensor("aaT", [64, C], F32, kind="ExternalInput")
    ba_in = nc.dram_tensor("ba_in", [64, D3], F32, kind="ExternalInput")
    wprojT = nc.dram_tensor("wprojT", [C, C], F32, kind="ExternalInput")
    apT = nc.dram_tensor("apT", [64, C], F32, kind="ExternalInput")
    bp_in = nc.dram_tensor("bp_in", [64, C], F32, kind="ExternalInput")
    gam_qk = nc.dram_tensor("gam_qk", [128, 16], F32, kind="ExternalInput")
    bet_qk = nc.dram_tensor("bet_qk", [128, 16], F32, kind="ExternalInput")
    gam_vr = nc.dram_tensor("gam_vr", [1, C], F32, kind="ExternalInput")
    bet_vr = nc.dram_tensor("bet_vr", [1, C], F32, kind="ExternalInput")
    gwT = nc.dram_tensor("gwT", [C, 8], F32, kind="ExternalInput")
    gb_row = nc.dram_tensor("gb_row", [1, 8], F32, kind="ExternalInput")
    w1_in = nc.dram_tensor("w1_in", [8, C, 2048], F32, kind="ExternalInput")
    b1_t = nc.dram_tensor("b1_t", [128, 128], F32, kind="ExternalInput")
    w2_in = nc.dram_tensor("w2_in", [8, 2048, C], F32, kind="ExternalInput")
    b2_in = nc.dram_tensor("b2_in", [8, C], F32, kind="ExternalInput")
    sched = nc.dram_tensor("sched", [1, 7 * 20], I32, kind="ExternalInput")

    out_fm = nc.dram_tensor("out_fm", [C, NT], F32, kind="ExternalOutput")

    kv_in = nc.dram_tensor("kv_in", [1, SHARD], BF16)
    kv_all = nc.dram_tensor("kv_all", [NCORE, SHARD], BF16)
    st_in = nc.dram_tensor("st_in", [1, 6144], F32)
    st_out = nc.dram_tensor("st_out", [1, 6144], F32)
    q_dram = nc.dram_tensor("q_dram", [128, 8 * NT], BF16)
    x2_sp = nc.dram_tensor("x2_sp", [C, NT], F32)

    ident_f_d = nc.inline_tensor(np.eye(128, dtype=np.float32), name="identf")
    ident_b_d = nc.inline_tensor(np.eye(128, dtype=ml_dtypes.bfloat16), name="identb")
    tri_np = np.where(np.arange(128)[None, :] <= np.arange(128)[:, None],
                      0.0, NEGBIG).astype(np.float32)
    tri_d = nc.inline_tensor(tri_np, name="trim")
    onescol_b_d = nc.inline_tensor(np.ones((128, 1), dtype=ml_dtypes.bfloat16),
                                   name="onescolb")
    onesrow_f_d = nc.inline_tensor(np.ones((1, 128), dtype=np.float32),
                                   name="onesrowf")
    onesrow_b_d = nc.inline_tensor(np.ones((1, 128), dtype=ml_dtypes.bfloat16),
                                   name="onesrowb")

    psA = nc.alloc_psum_tensor("psA", [128, 1024], F32)
    psB = nc.alloc_psum_tensor("psB", [128, 1024], F32)
    psC = nc.alloc_psum_tensor("psC", [128, 1024], F32)
    psD = nc.alloc_psum_tensor("psD", [128, 1024], F32)

    A = nc.alloc_sbuf_tensor
    idb = A("idb", [128, 128], BF16)
    idf = A("idf", [128, 128], F32)
    trim = A("trim_sb", [128, 128], F32)
    onecb = A("onecb", [128, 1], BF16)
    onerf = A("onerf", [1, 128], F32)
    onerb = A("onerb", [1, 128], BF16)
    stc = A("stc", [128, 32], F32)
    abqk = A("abqk", [128, 32], F32)
    abv = A("abv", [128, 16], F32)
    qab = A("qab", [128, 16], F32)
    gamc = A("gamc", [128, 16], F32)
    betc = A("betc", [128, 16], F32)
    schc = A("schc", [1, 7 * 20], I32)
    m_sb = A("m_sb", [128, 16], F32)
    l_sb = A("l_sb", [128, 16], F32)
    b1c = A("b1c", [128, 128], F32)
    gwb = A("gwb", [128, 64], BF16)
    gbr = A("gbr", [1, 8], F32)
    gbb = A("gbb", [128, 8], F32)
    pbc = A("pbc", [128, 8], F32)
    sm = A("sm", [128, 16], F32)
    tcol = A("tcol", [128, 4], F32)
    dgap = A("dgap", [128, 64], F32)

    def gap(v):
        return v.tensor_copy(dgap[:, 0:1], dgap[:, 1:2])

    class Sem:
        def __init__(self, name):
            self.h = nc.alloc_semaphore(name)
            self.n = 0
    sMM = Sem("sMM")
    sEV = Sem("sEV")
    sSC = Sem("sSC")
    sCC = Sem("sCC")
    sST = Sem("sST")
    sC = Sem("sC")
    sW = Sem("sW")
    sX = Sem("sX")
    sKVk = Sem("sKVk")
    sKVv = Sem("sKVv")
    sVL = Sem("sVL")
    sSTT = Sem("sSTT")
    sSR = Sem("sSR")
    sGV = Sem("sGV")
    sQW = Sem("sQW")
    sGA = Sem("sGA")
    sPL0 = Sem("sPL0")
    sPL1 = Sem("sPL1")
    sQF = Sem("sQF")
    sKF = Sem("sKF")
    sVF = Sem("sVF")
    sXT0 = Sem("sXT0")
    sXT1 = Sem("sXT1")
    sX2a = Sem("sX2a")
    sX2b = Sem("sX2b")
    sOI = Sem("sOI")
    sWR = Sem("sWR")
    sW10 = Sem("sW10")
    sW11 = Sem("sW11")
    sW20 = Sem("sW20")
    sW21 = Sem("sW21")

    es = ExitStack()
    rg1 = es.enter_context(nc.gpsimd.register("rg1"))
    rg2 = es.enter_context(nc.gpsimd.register("rg2"))
    rg3 = es.enter_context(nc.gpsimd.register("rg3"))
    rv1 = es.enter_context(nc.vector.register("rv1"))
    rs1 = es.enter_context(nc.sync.register("rs1"))
    rc1 = es.enter_context(nc.scalar.register("rc1"))
    blk = es.enter_context(nc.Block("main"))

    s1 = ExitStack()
    wmT = s1.enter_context(nc.sbuf_tensor("wmT", [128, 8 * D3], BF16))
    wtmp = s1.enter_context(nc.sbuf_tensor("wtmp", [128, D3], F32))
    wtmpB = s1.enter_context(nc.sbuf_tensor("wtmpB", [128, D3], F32))
    aatb = s1.enter_context(nc.sbuf_tensor("aatb", [64, C], BF16))
    bab = s1.enter_context(nc.sbuf_tensor("bab", [64, D3], BF16))
    xt = s1.enter_context(nc.sbuf_tensor("xt", [128, 8 * TT], F32))
    xb = s1.enter_context(nc.sbuf_tensor("xb", [128, 8 * TT], BF16))
    xsq = s1.enter_context(nc.sbuf_tensor("xsq", [128, 8 * TT], BF16))
    ln0 = s1.enter_context(nc.sbuf_tensor("ln0", [128, 8 * TT], BF16))
    abc0 = s1.enter_context(nc.sbuf_tensor("abc0", [128, 2 * TT], F32))
    xtmp = s1.enter_context(nc.sbuf_tensor("xtmp", [128, TT], F32))
    qstage = s1.enter_context(nc.sbuf_tensor("qstage", [128, 8 * TT], BF16))
    kslab = s1.enter_context(nc.sbuf_tensor("kslab", [128, 8 * TT], BF16))
    vslab = s1.enter_context(nc.sbuf_tensor("vslab", [128, 16 * 1024], BF16))
    sqs = s1.enter_context(nc.sbuf_tensor("sqs", [128, 4 * TT], BF16))
    vrow = s1.enter_context(nc.sbuf_tensor("vrow", [1, 2 * C], F32))
    rowb = s1.enter_context(nc.sbuf_tensor("rowb", [1, 4096], F32))

    @blk.sync
    def _(s):
        for dst, src in [(idb, ident_b_d), (idf, ident_f_d), (trim, tri_d),
                         (onecb, onescol_b_d), (onerf, onesrow_f_d),
                         (onerb, onesrow_b_d)]:
            s.dma_start(out=dst[:], in_=src.ap()).then_inc(sC.h, 16)
        for dst, src in [(gamc, gam_qk), (betc, bet_qk), (schc, sched),
                         (b1c, b1_t), (gbr, gb_row)]:
            s.dma_start(out=dst[:], in_=src.ap()).then_inc(sC.h, 16)
    sC.n += 176
    LD_CONST = sC.n

    @blk.gpsimd
    def _(g):
        g.dma_start(out=aatb[:], in_=aaT.ap()).then_inc(sGA.h, 16)
        g.dma_start(out=bab[:], in_=ba_in.ap()).then_inc(sGA.h, 16)
        g.dma_start(out=gwb[:].rearrange("p (cc e) -> p cc e", cc=8),
                    in_=bass.AP(gwT, 0, [[8, 128], [1024, 8], [1, 8]])
                    ).then_inc(sGA.h, 16)
    sGA.n += 48
    GD_LORA = sGA.n

    wm_evs = []
    for cc in range(8):
        wt = (wtmp, wtmpB)[cc % 2]

        @blk.sync
        def _(s, cc=cc, wt=wt):
            if cc >= 2:
                s.wait_ge(sEV.h, wm_evs[cc - 2])
            s.dma_start(out=wt[:], in_=wattnT[cc * 128:(cc + 1) * 128, :]
                        ).then_inc(sW.h, 16)
        sW.n += 16
        ld_w = sW.n

        @blk.tensor
        def _(t, cc=cc, ld_w=ld_w):
            if cc == 0:
                t.wait_ge(sGA.h, GD_LORA)
            t.wait_ge(sW.h, ld_w)
            if cc >= 1:
                t.wait_ge(sEV.h, wm_evs[cc - 1])
            last = None
            for s3 in range(3):
                ps = (psA, psB, psC)[s3]
                for rb in range(2):
                    last = t.matmul(
                        ps[:, rb * 512:(rb + 1) * 512],
                        bab[:, s3 * 1024 + cc * 128: s3 * 1024 + (cc + 1) * 128],
                        aatb[:, rb * 512:(rb + 1) * 512],
                        start=True, stop=True, skip_group_check=True)
            last.then_inc(sMM.h, 1)
        sMM.n += 1
        mm_w = sMM.n

        @blk.vector
        def _(v, cc=cc, mm_w=mm_w, wt=wt):
            v.wait_ge(sMM.h, mm_w)
            last = None
            for s3 in range(3):
                ps = (psA, psB, psC)[s3]
                src_w = bass.AP(wt, s3, [[D3, 128], [3, 1024]])
                dst = bass.AP(wmT, cc * D3 + s3, [[8 * D3, 128], [3, 1024]])
                last = v.tensor_tensor(dst, src_w, ps[:, 0:1024], AL.add)
            last.then_inc(sEV.h, 1)
        sEV.n += 1
        wm_evs.append(sEV.n)
    MM_WM = sMM.n

    @blk.vector
    def _(v):
        v.memzero(stc[:])
        v.memzero(l_sb[:])
        v.memzero(dgap[:])
        v.memset(m_sb[:], -1.0e30).then_inc(sEV.h, 1)
    sEV.n += 1
    EV_INIT = sEV.n

    def emit_ln_tile(ti, src_dram, xt_, xb_, xsq_, abc_, xtmp_, ln_dst, dst_off,
                     prev, first_waits_sync, first_waits_mm, writeln_wait_mm,
                     rowb):
        t0 = ti * TT

        @blk.sync
        def _(s):
            for sem, val in first_waits_sync:
                s.wait_ge(sem.h, val)
            if prev is not None:
                s.wait_ge(sEV.h, prev["ev_ln"])
            s.dma_start(out=xt_[:].rearrange("p (c t) -> p c t", c=8),
                        in_=bass.AP(src_dram, t0, [[NT, 128], [128 * NT, 8],
                                                   [1, TT]])
                        ).then_inc(sX.h, 16)
        sX.n += 16
        ld_x = sX.n

        @blk.vector
        def _(v, ld_x=ld_x):
            v.wait_ge(sX.h, ld_x)
            if prev is not None:
                v.wait_ge(sMM.h, prev["mm_stats"])
            v.tensor_copy(xb_[:], xt_[:]).then_inc(sEV.h, 1)
        sEV.n += 1
        ev_xb = sEV.n

        @blk.scalar
        def _(sc, ev_xb=ev_xb):
            sc.wait_ge(sEV.h, ev_xb)
            sc.square(xsq_[:], xb_[:]).then_inc(sSC.h, 1)
        sSC.n += 1
        sc_sq = sSC.n

        @blk.tensor
        def _(t, sc_sq=sc_sq):
            t.wait_ge(sSC.h, sc_sq)
            for sem, val in first_waits_mm:
                t.wait_ge(sem.h, val)
            for cc in range(8):
                t.matmul(psD[0:1, 0:TT], onecb[:], xb_[:, cc * TT:(cc + 1) * TT],
                         start=(cc == 0), stop=(cc == 7), skip_group_check=True)
            last = None
            for cc in range(8):
                last = t.matmul(psD[0:1, 512:512 + TT], onecb[:],
                                xsq_[:, cc * TT:(cc + 1) * TT],
                                start=(cc == 0), stop=(cc == 7),
                                skip_group_check=True)
            last.then_inc(sMM.h, 1)
        sMM.n += 1
        mm_st = sMM.n

        MU, EX2, TMP, RR, NB = 0, TT, 2 * TT, 3 * TT, 2048 + 1024

        @blk.vector
        def _(v, mm_st=mm_st):
            v.wait_ge(sMM.h, mm_st)
            v.tensor_scalar_mul(rowb[0:1, MU:MU + TT], psD[0:1, 0:TT], 1.0 / C)
            v.tensor_scalar_mul(rowb[0:1, EX2:EX2 + TT], psD[0:1, 512:512 + TT],
                                1.0 / C).then_inc(sEV.h, 1)
        sEV.n += 1
        ev_mu = sEV.n

        @blk.scalar
        def _(sc, ev_mu=ev_mu):
            sc.wait_ge(sEV.h, ev_mu)
            sc.square(rowb[0:1, TMP:TMP + TT], rowb[0:1, MU:MU + TT]
                      ).then_inc(sSC.h, 1)
        sSC.n += 1
        sc_mu2 = sSC.n

        @blk.vector
        def _(v, sc_mu2=sc_mu2):
            v.wait_ge(sSC.h, sc_mu2)
            v.tensor_sub(rowb[0:1, TMP:TMP + TT], rowb[0:1, EX2:EX2 + TT],
                         rowb[0:1, TMP:TMP + TT])
            v.tensor_scalar_add(rowb[0:1, TMP:TMP + TT], rowb[0:1, TMP:TMP + TT],
                                EPS).then_inc(sEV.h, 1)
        sEV.n += 1
        ev_var = sEV.n

        @blk.scalar
        def _(sc, ev_var=ev_var):
            sc.wait_ge(sEV.h, ev_var)
            sc.sqrt(rowb[0:1, TMP:TMP + TT], rowb[0:1, TMP:TMP + TT]
                    ).then_inc(sSC.h, 1)
        sSC.n += 1
        sc_sd = sSC.n

        @blk.vector
        def _(v, sc_sd=sc_sd):
            v.wait_ge(sSC.h, sc_sd)
            v.reciprocal(rowb[0:1, RR:RR + TT], rowb[0:1, TMP:TMP + TT])
            v.tensor_mul(rowb[0:1, NB:NB + TT], rowb[0:1, MU:MU + TT],
                         rowb[0:1, RR:RR + TT])
            v.tensor_scalar_mul(rowb[0:1, NB:NB + TT], rowb[0:1, NB:NB + TT],
                                -1.0).then_inc(sEV.h, 1)
        sEV.n += 1
        ev_r = sEV.n

        @blk.tensor
        def _(t, ev_r=ev_r):
            t.wait_ge(sEV.h, ev_r)
            t.matmul(psD[:, 0:TT], onerf[:], rowb[0:1, RR:RR + TT],
                     start=True, stop=True, skip_group_check=True)
            t.matmul(psD[:, 512:512 + TT], onerf[:], rowb[0:1, NB:NB + TT],
                     start=True, stop=True, skip_group_check=True
                     ).then_inc(sMM.h, 1)
        sMM.n += 1
        mm_bc = sMM.n

        @blk.vector
        def _(v, mm_bc=mm_bc, writeln_wait_mm=writeln_wait_mm):
            v.wait_ge(sMM.h, mm_bc)
            if writeln_wait_mm is not None:
                v.wait_ge(sMM.h, writeln_wait_mm)
            v.tensor_copy(abc_[:, 0:TT], psD[:, 0:TT])
            v.tensor_copy(abc_[:, TT:2 * TT], psD[:, 512:512 + TT])
            last = None
            for cc in range(8):
                v.tensor_mul(xtmp_[:], xt_[:, cc * TT:(cc + 1) * TT], abc_[:, 0:TT])
                o0, o1 = dst_off(cc, ti)
                last = v.tensor_add(ln_dst[:, o0:o1], xtmp_[:], abc_[:, TT:2 * TT])
            last.then_inc(sEV.h, 1)
        sEV.n += 1
        return {"ev_ln": sEV.n, "mm_stats": mm_st}

    prev_ln = None
    qkv_evs = []
    qkv_scs = []
    kst_lds = []
    vst_lds = []
    last_vproj_mm = None
    last_vev_sc = None
    for ti in range(NTILE):
        t0 = ti * TT
        fw_sync = [(sC, LD_CONST)] if ti == 0 else []
        fw_mm = [(sMM, MM_WM)] if ti == 0 else []
        prev_ln = emit_ln_tile(
            ti, x_fm, xt, xb, xsq, abc0, xtmp, ln0,
            lambda cc, ti_: (cc * TT, (cc + 1) * TT),
            prev_ln, fw_sync, fw_mm,
            last_vproj_mm, rowb)
        ev_ln = prev_ln["ev_ln"]

        for half in range(4):
            dcs = [half * 4 + i for i in range(4)]

            @blk.tensor
            def _(t, dcs=dcs, ev_ln=ev_ln):
                t.wait_ge(sEV.h, ev_ln)
                if qkv_scs:
                    t.wait_ge(sSC.h, qkv_scs[-1])
                if qkv_evs:
                    t.wait_ge(sEV.h, qkv_evs[-1])
                last = None
                for i, dc in enumerate(dcs):
                    ps = (psA, psB)[i // 2]
                    off = (i % 2) * 512
                    for cc in range(8):
                        last = t.matmul(
                            ps[:, off:off + TT],
                            wmT[:, cc * D3 + dc * 128:cc * D3 + (dc + 1) * 128],
                            ln0[:, cc * TT:(cc + 1) * TT],
                            start=(cc == 0), stop=(cc == 7), skip_group_check=True)
                last.then_inc(sMM.h, 1)
            sMM.n += 1
            mm_qk = sMM.n

            @blk.scalar
            def _(sc, dcs=dcs, mm_qk=mm_qk, ti=ti, t0=t0):
                sc.wait_ge(sMM.h, mm_qk)
                if dcs[0] >= 8 and kst_lds:
                    sc.wait_ge(sKVk.h, kst_lds[-1])
                if dcs[0] < 8 and kst_lds:
                    sc.wait_ge(sQW.h, 16 * len(kst_lds))
                for i, dc in enumerate(dcs):
                    ps = (psA, psB)[i // 2]
                    off = (i % 2) * 512
                    if dc < 8:
                        dst = qstage[:, dc * TT:(dc + 1) * TT]
                    else:
                        dst = kslab[:, (dc - 8) * TT:(dc - 7) * TT]
                    sc.activation(dst, ps[:, off:off + TT], AF.Copy,
                                  bias=0.0, scale=1.0)
                    sc.square(sqs[:, i * TT:(i + 1) * TT], ps[:, off:off + TT]
                              ).then_inc(sSC.h, 1)
            sSC.n += 4
            sc_e = sSC.n

            @blk.vector
            def _(v, dcs=dcs, sc_e=sc_e, mm_qk=mm_qk, ti=ti, t0=t0):
                last = None
                for i, dc in enumerate(dcs):
                    v.wait_ge(sSC.h, sc_e - 3 + i)
                    if dc < 8:
                        srcv = qstage[:, dc * TT:(dc + 1) * TT]
                    else:
                        srcv = kslab[:, (dc - 8) * TT:(dc - 7) * TT]
                    v.reduce_sum(tcol[:, 0:1], srcv, AX)
                    v.reduce_sum(tcol[:, 1:2], sqs[:, i * TT:(i + 1) * TT], AX)
                    v.tensor_add(stc[:, dc:dc + 1], stc[:, dc:dc + 1], tcol[:, 0:1])
                    last = v.tensor_add(stc[:, 16 + dc:17 + dc],
                                        stc[:, 16 + dc:17 + dc], tcol[:, 1:2])
                last.then_inc(sEV.h, 1)
            sEV.n += 1
            qkv_evs.append(sEV.n)
            qkv_scs.append(sc_e)

        @blk.sync
        def _(s, ti=ti, t0=t0):
            s.wait_ge(sSC.h, qkv_scs[-1])
            s.dma_start(out=bass.AP(kv_in, (ti // 2) * HBLK + (ti % 2) * 512,
                                    [[8192, 128], [1024, 8], [1, 512]]),
                        in_=kslab[:].rearrange("p (c t) -> p c t", c=8)
                        ).then_inc(sKVk.h, 16)
            s.dma_start(out=bass.AP(q_dram, (ti // 2) * HBLK + (ti % 2) * 512,
                                    [[8192, 128], [1024, 8], [1, 512]]),
                        in_=qstage[:].rearrange("p (c t) -> p c t", c=8)
                        ).then_inc(sQW.h, 16)
        sKVk.n += 16
        kst_lds.append(sKVk.n)
        sQW.n += 16

        vev_scs = []
        for tc in range(4):
            @blk.tensor
            def _(t, tc=tc, ev_ln=ev_ln):
                if tc == 0:
                    t.wait_ge(sEV.h, ev_ln)
                    if last_vev_sc is not None:
                        t.wait_ge(sSC.h, last_vev_sc)
                else:
                    t.wait_ge(sSC.h, vev_scs[-1])
                last = None
                for cc in range(8):
                    t.matmul(psC[:, 0:512],
                             ln0[:, cc * TT + tc * 128:cc * TT + (tc + 1) * 128],
                             wmT[:, cc * D3 + 2048:cc * D3 + 2560],
                             start=(cc == 0), stop=(cc == 7), skip_group_check=True)
                for cc in range(8):
                    last = t.matmul(
                        psC[:, 512:1024],
                        ln0[:, cc * TT + tc * 128:cc * TT + (tc + 1) * 128],
                        wmT[:, cc * D3 + 2560:cc * D3 + 3072],
                        start=(cc == 0), stop=(cc == 7), skip_group_check=True)
                last.then_inc(sMM.h, 1)
            sMM.n += 1
            mm_v = sMM.n
            last_vproj_mm = mm_v

            @blk.scalar
            def _(sc, tc=tc, mm_v=mm_v, ti=ti):
                sc.wait_ge(sMM.h, mm_v)
                sc.activation(vslab[:, (ti * 4 + tc) * 1024:(ti * 4 + tc + 1) * 1024],
                              psC[:, 0:1024],
                              AF.Copy, bias=0.0, scale=1.0).then_inc(sSC.h, 1)
            sSC.n += 1
            vev_scs.append(sSC.n)
        last_vev_sc = vev_scs[-1]

        @blk.sync
        def _(s, ti=ti, t0=t0, w=vev_scs[-1]):
            s.wait_ge(sSC.h, w)
            s.dma_start(out=bass.AP(kv_in, VB + (ti // 2) * HBLK + (ti % 2) * 4096,
                                    [[8192, 128], [1024, 4], [1, 1024]]),
                        in_=vslab[:, ti * 4096:(ti + 1) * 4096
                                  ].rearrange("p (tc v) -> p tc v", tc=4)
                        ).then_inc(sKVv.h, 16)
        sKVv.n += 16
        vst_lds.append(sKVv.n)
    KVK_ALL = sKVk.n
    KVV_ALL = sKVv.n

    vst_mms = []
    LAST_LN_MM = prev_ln["mm_stats"]
    for ch in range(16):
        @blk.scalar
        def _(sc, ch=ch):
            if ch == 0:
                sc.wait_ge(sMM.h, LAST_LN_MM)
            sc.square(xsq[:, 0:1024], vslab[:, ch * 1024:(ch + 1) * 1024]
                      ).then_inc(sSC.h, 1)
        sSC.n += 1
        sc_v2 = sSC.n

        @blk.tensor
        def _(t, ch=ch, sc_v2=sc_v2):
            t.wait_ge(sSC.h, sc_v2)
            last = None
            for hh in range(2):
                t.matmul(psD[0:1, hh * 512:(hh + 1) * 512], onecb[:],
                         vslab[:, ch * 1024 + hh * 512:ch * 1024 + (hh + 1) * 512],
                         start=(ch == 0), stop=(ch == 15), skip_group_check=True)
                last = t.matmul(psC[0:1, hh * 512:(hh + 1) * 512], onecb[:],
                                xsq[:, hh * 512:(hh + 1) * 512],
                                start=(ch == 0), stop=(ch == 15),
                                skip_group_check=True)
            last.then_inc(sMM.h, 1)
        sMM.n += 1
        vst_mms.append(sMM.n)
    MM_VST = sMM.n

    @blk.vector
    def _(v):
        v.wait_ge(sMM.h, MM_VST)
        v.tensor_copy(vrow[0:1, 0:C], psD[0:1, 0:C])
        v.tensor_copy(vrow[0:1, C:2 * C], psC[0:1, 0:C]).then_inc(sEV.h, 1)
    sEV.n += 1
    EV_VROW = sEV.n

    @blk.sync
    def _(s):
        s.wait_ge(sEV.h, EV_VROW)
        s.dma_start(out=bass.AP(st_in, 0, [[32, 128], [1, 32]]), in_=stc[:]
                    ).then_inc(sSTT.h, 16)
        s.dma_start(out=st_in[0:1, 4096:6144], in_=vrow[0:1, :]).then_inc(sSTT.h, 16)
    sSTT.n += 32
    LD_STW = sSTT.n

    @blk.gpsimd
    def _(g):
        g.wait_ge(sSTT.h, LD_STW)
        g.collective_compute("AllReduce", AL.add, replica_groups=[list(range(NCORE))],
                             ins=[st_in.ap().opt()], outs=[st_out.ap().opt()]
                             ).then_inc(sCC.h, 1)
        g.wait_ge(sKVk.h, KVK_ALL)
        g.wait_ge(sKVv.h, KVV_ALL)
        g.collective_compute("AllGather", AL.bypass,
                             replica_groups=[list(range(NCORE))],
                             ins=[kv_in.ap().opt()], outs=[kv_all.ap().opt()]
                             ).then_inc(sCC.h, 1)
    sCC.n += 2

    @blk.sync
    def _(s):
        s.wait_ge(sCC.h, 1)
        s.dma_start(out=stc[:], in_=bass.AP(st_out, 0, [[32, 128], [1, 32]])
                    ).then_inc(sSR.h, 16)
        s.dma_start(out=vrow[0:1, :], in_=st_out[0:1, 4096:6144]).then_inc(sSR.h, 16)
    sSR.n += 32
    GD_STR = sSR.n

    @blk.sync
    def _(s):
        s.wait_ge(sCC.h, 1)
        s.dma_start(out=rowb[0:1, 2048:3072], in_=gam_vr.ap()).then_inc(sGV.h, 16)
        s.dma_start(out=rowb[0:1, 3072:4096], in_=bet_vr.ap()).then_inc(sGV.h, 16)
    sGV.n += 32
    ld_gv = sGV.n

    @blk.vector
    def _(v):
        v.wait_ge(sSR.h, GD_STR)
        v.tensor_scalar_mul(stc[:, 0:16], stc[:, 0:16], 1.0 / NTOK)
        v.tensor_scalar_mul(stc[:, 16:32], stc[:, 16:32], 1.0 / NTOK)
        v.tensor_scalar_mul(vrow[0:1, 0:C], vrow[0:1, 0:C], 1.0 / NTOK)
        v.tensor_mul(sm[:, 0:16], stc[:, 0:16], stc[:, 0:16])
        v.tensor_scalar_mul(vrow[0:1, C:2 * C], vrow[0:1, C:2 * C], 1.0 / NTOK)
        v.tensor_sub(abqk[:, 0:16], stc[:, 16:32], sm[:, 0:16])
        gap(v)
        v.tensor_scalar_add(abqk[:, 0:16], abqk[:, 0:16], EPS)
        gap(v).then_inc(sEV.h, 1)
    sEV.n += 1
    ev1 = sEV.n

    @blk.scalar
    def _(sc, ev1=ev1):
        sc.wait_ge(sEV.h, ev1)
        sc.sqrt(abqk[:, 0:16], abqk[:, 0:16])
        sc.square(rowb[0:1, 0:C], vrow[0:1, 0:C]).then_inc(sSC.h, 1)
    sSC.n += 1
    sc1 = sSC.n

    @blk.vector
    def _(v, sc1=sc1):
        v.wait_ge(sSC.h, sc1)
        v.reciprocal(abqk[:, 0:16], abqk[:, 0:16])
        v.tensor_sub(rowb[0:1, 0:C], vrow[0:1, C:2 * C], rowb[0:1, 0:C])
        v.tensor_mul(abqk[:, 0:16], abqk[:, 0:16], gamc[:])
        v.tensor_scalar_add(rowb[0:1, 0:C], rowb[0:1, 0:C], EPS)
        v.tensor_mul(abqk[:, 16:32], stc[:, 0:16], abqk[:, 0:16])
        gap(v)
        v.tensor_sub(abqk[:, 16:32], betc[:], abqk[:, 16:32])
        gap(v).then_inc(sEV.h, 1)
    sEV.n += 1
    ev2 = sEV.n

    @blk.scalar
    def _(sc, ev2=ev2):
        sc.wait_ge(sEV.h, ev2)
        sc.sqrt(rowb[0:1, 0:C], rowb[0:1, 0:C]).then_inc(sSC.h, 1)
    sSC.n += 1
    sc2 = sSC.n

    @blk.vector
    def _(v, sc2=sc2, ld_gv=ld_gv):
        v.wait_ge(sSC.h, sc2)
        v.wait_ge(sGV.h, ld_gv)
        v.reciprocal(rowb[0:1, 0:C], rowb[0:1, 0:C])
        v.tensor_mul(rowb[0:1, 0:C], rowb[0:1, 0:C], rowb[0:1, 2048:3072])
        v.tensor_mul(rowb[0:1, 1024:2048], vrow[0:1, 0:C], rowb[0:1, 0:C])
        v.tensor_sub(rowb[0:1, 1024:2048], rowb[0:1, 3072:4096],
                     rowb[0:1, 1024:2048])
        v.tensor_mul(qab[:, 0:8], abqk[:, 0:8], abqk[:, 8:16])
        v.tensor_mul(qab[:, 8:16], abqk[:, 16:24], abqk[:, 8:16]
                     ).then_inc(sEV.h, 1)
    sEV.n += 1
    EV_STATS = sEV.n

    @blk.tensor
    def _(t):
        t.wait_ge(sEV.h, EV_STATS)
        last = None
        for vcc in range(8):
            t.matmul(psD[:, vcc:vcc + 1], rowb[0:1, vcc * 128:(vcc + 1) * 128],
                     onerf[0:1, 0:1], start=True, stop=True, skip_group_check=True)
            last = t.matmul(psD[:, 8 + vcc:9 + vcc],
                            rowb[0:1, 1024 + vcc * 128:1024 + (vcc + 1) * 128],
                            onerf[0:1, 0:1], start=True, stop=True,
                            skip_group_check=True)
        last = t.matmul(psD[:, 16:24], onerf[:], gbr[0:1, 0:8],
                        start=True, stop=True, skip_group_check=True)
        last.then_inc(sMM.h, 1)
    sMM.n += 1
    mm_col = sMM.n

    @blk.vector
    def _(v, mm_col=mm_col):
        v.wait_ge(sMM.h, mm_col)
        v.tensor_copy(abv[:], psD[:, 0:16])
        v.tensor_copy(gbb[:], psD[:, 16:24]).then_inc(sEV.h, 1)
    sEV.n += 1
    EV_QAFF = sEV.n

    LD_QW = sQW.n

    s1.close()

    s2 = ExitStack()
    y_sb = s2.enter_context(nc.sbuf_tensor("y_sb", [128, 16 * 1024], F32))
    p_sb0 = s2.enter_context(nc.sbuf_tensor("p_sb0", [128, 1024], BF16))
    p_sb1 = s2.enter_context(nc.sbuf_tensor("p_sb1", [128, 1024], BF16))
    pT0 = s2.enter_context(nc.sbuf_tensor("pT0", [128, 1024], BF16))
    pT1 = s2.enter_context(nc.sbuf_tensor("pT1", [128, 1024], BF16))
    pvsb0 = s2.enter_context(nc.sbuf_tensor("pvsb0", [128, 1024], F32))
    pvsb1 = s2.enter_context(nc.sbuf_tensor("pvsb1", [128, 1024], F32))
    y_st0 = s2.enter_context(nc.sbuf_tensor("y_st0", [128, 1024], F32))
    y_st1 = s2.enter_context(nc.sbuf_tensor("y_st1", [128, 1024], F32))
    fc0 = s2.enter_context(nc.sbuf_tensor("fc0", [128, 8], F32))
    fc1 = s2.enter_context(nc.sbuf_tensor("fc1", [128, 8], F32))

    s2a = ExitStack()
    qcur0 = s2a.enter_context(nc.sbuf_tensor("qcur0", [128, 8192], BF16))
    qcur1 = s2a.enter_context(nc.sbuf_tensor("qcur1", [128, 8192], BF16))
    kb0 = s2a.enter_context(nc.sbuf_tensor("kb0", [128, 8192], BF16))
    kb1 = s2a.enter_context(nc.sbuf_tensor("kb1", [128, 8192], BF16))
    vb0 = s2a.enter_context(nc.sbuf_tensor("vb0", [128, 8192], BF16))
    vb1 = s2a.enter_context(nc.sbuf_tensor("vb1", [128, 8192], BF16))

    @blk.vector
    def _(v):
        v.wait_ge(sEV.h, EV_QAFF)
        v.memzero(y_sb[:]).then_inc(sEV.h, 1)
    sEV.n += 1
    EV_YZERO = sEV.n

    pairs = [("diagA",), ("diagB",)] + [("full", gi) for gi in range(7)]
    pair_done_mm = []
    pair_ld_gd = []
    qaff_evs = []
    sPL = (sPL0, sPL1)
    last_exp_sc = {}
    last_s2v_ev = {}
    vy_ev = {}
    last_pve_sc = [None]

    for ip, pdesc in enumerate(pairs):
        par = ip % 2
        qcur = (qcur0, qcur1)[par]
        kb = (kb0, kb1)[par]
        vb = (vb0, vb1)[par]
        is_diag = pdesc[0] in ("diagA", "diagB")
        sbase = {"diagA": 0, "diagB": 8}.get(pdesc[0])
        sLD_p = sPL[ip % 2] if is_diag else None

        if is_diag:
            h_d = 0 if pdesc[0] == "diagA" else 1

            @blk.sync
            def _(s, ip=ip, qcur=qcur, vb=vb, h_d=h_d):
                if ip >= 2:
                    s.wait_ge(sMM.h, pair_done_mm[ip - 2])
                s.wait_ge(sMM.h, mm_col)
                s.wait_ge(sQW.h, LD_QW)
                s.dma_start(out=qcur[:],
                            in_=bass.AP(q_dram, h_d * HBLK,
                                        [[8192, 128], [1, 8192]])
                            ).then_inc(sLD_p.h, 16)
                s.wait_ge(sKVv.h, KVV_ALL)
                s.dma_start(out=vb[:],
                            in_=bass.AP(kv_in, VB + h_d * HBLK,
                                        [[8192, 128], [1, 8192]])
                            ).then_inc(sLD_p.h, 16)

            @blk.scalar
            def _(sc, ip=ip, kb=kb, h_d=h_d):
                if ip >= 2:
                    sc.wait_ge(sMM.h, pair_done_mm[ip - 2])
                sc.wait_ge(sKVk.h, KVK_ALL)
                sc.wait_ge(sMM.h, mm_col)
                sc.wait_ge(sQW.h, LD_QW)
                sc.dma_start(out=kb[:],
                             in_=bass.AP(kv_in, h_d * HBLK,
                                         [[8192, 128], [1, 8192]])
                             ).then_inc(sLD_p.h, 16)
        else:
            gi = pdesc[1]
            base = gi * 20

            @blk.gpsimd
            def _(g, ip=ip, qcur=qcur, base=base):
                if ip >= 2:
                    g.wait_ge(sMM.h, pair_done_mm[ip - 2])
                g.wait_ge(sCC.h, 2)
                g.wait_ge(sMM.h, mm_col)
                g.wait_ge(sQW.h, LD_QW)
                g.reg_load(rg3, schc[0:1, base + 2:base + 3])
                qoff = g.snap(rg3, min_val=0, max_val=HBLK)
                g.dma_start(out=qcur[:],
                            in_=bass.AP(q_dram, qoff, [[8192, 128], [1, 8192]])
                            ).then_inc(sQF.h, 16)
            sQF.n += 16

            @blk.sync
            def _(s, ip=ip, kb=kb, base=base):
                if ip >= 2:
                    s.wait_ge(sMM.h, pair_done_mm[ip - 2])
                s.wait_ge(sCC.h, 2)
                s.wait_ge(sMM.h, mm_col)
                s.wait_ge(sQW.h, LD_QW)
                s.reg_load(rs1, schc[0:1, base:base + 1])
                koff = s.snap(rs1, min_val=0, max_val=7 * SHARD + HBLK)
                s.dma_start(out=kb[:],
                            in_=bass.AP(kv_all, koff, [[8192, 128], [1, 8192]])
                            ).then_inc(sKF.h, 16)
            sKF.n += 16

            @blk.scalar
            def _(sc, ip=ip, vb=vb, base=base):
                if ip >= 2:
                    sc.wait_ge(sMM.h, pair_done_mm[ip - 2])
                sc.wait_ge(sCC.h, 2)
                sc.wait_ge(sMM.h, mm_col)
                sc.wait_ge(sQW.h, LD_QW)
                sc.reg_load(rc1, schc[0:1, base + 1:base + 2])
                voff = sc.snap(rc1, min_val=0, max_val=7 * SHARD + VB + HBLK)
                sc.dma_start(out=vb[:],
                             in_=bass.AP(kv_all, voff, [[8192, 128], [1, 8192]])
                             ).then_inc(sVF.h, 16)
            sVF.n += 16
        if is_diag:
            sLD_p.n += 48
            pair_ld_gd.append([(sLD_p, sLD_p.n)])
        else:
            pair_ld_gd.append([(sQF, sQF.n), (sKF, sKF.n), (sVF, sVF.n)])

        @blk.vector
        def _(v, ip=ip, qcur=qcur):
            for ld_sem, ld_val in pair_ld_gd[ip]:
                v.wait_ge(ld_sem.h, ld_val)
            v.wait_ge(sEV.h, EV_STATS)
            last = None
            for cc in range(8):
                v.tensor_scalar_mul(qcur[:, cc * 1024:(cc + 1) * 1024],
                                    qcur[:, cc * 1024:(cc + 1) * 1024],
                                    qab[:, cc:cc + 1])
                last = v.tensor_scalar_add(qcur[:, cc * 1024:(cc + 1) * 1024],
                                           qcur[:, cc * 1024:(cc + 1) * 1024],
                                           qab[:, 8 + cc:9 + cc])
            last.then_inc(sEV.h, 1)
        sEV.n += 1
        qaff_evs.append(sEV.n)

        def emit_head(qs):
            nkt = (qs + 1) if is_diag else 8
            W = nkt * 128
            psS = (psA, psB)[qs % 2]
            p_s = (p_sb0, p_sb1)[qs % 2]
            fc = (fc0, fc1)[qs % 2]
            y_stq = (y_st0, y_st1)[qs % 2]

            @blk.tensor
            def _(t, ip=ip, qs=qs, W=W, psS=psS, qcur=qcur, kb=kb):
                if qs == 0:
                    for ld_sem, ld_val in pair_ld_gd[ip]:
                        t.wait_ge(ld_sem.h, ld_val)
                    t.wait_ge(sEV.h, qaff_evs[ip])
                    if ip == 0:
                        t.wait_ge(sEV.h, EV_YZERO)
                key = qs % 2
                if key in last_exp_sc:
                    t.wait_ge(sSC.h, last_exp_sc[key])
                nhal = (W + 511) // 512
                last = None
                for cc in range(8):
                    for h in range(nhal):
                        w0 = h * 512
                        w1 = min(W, w0 + 512)
                        last = t.matmul(
                            psS[:, w0:w1],
                            qcur[:, cc * 1024 + qs * 128:cc * 1024 + (qs + 1) * 128],
                            kb[:, cc * 1024 + w0:cc * 1024 + w1],
                            start=(cc == 0), stop=(cc == 7), skip_group_check=True)
                last.then_inc(sMM.h, 1)
            sMM.n += 1
            mm_s = sMM.n

            @blk.vector
            def _(v, qs=qs, W=W, psS=psS, fc=fc, y_stq=y_stq, mm_s=mm_s):
                v.wait_ge(sMM.h, mm_s)
                if is_diag:
                    v.tensor_add(psS[:, qs * 128:(qs + 1) * 128],
                                 psS[:, qs * 128:(qs + 1) * 128], trim[:])
                    sl = sbase + qs
                    v.tensor_copy(fc[:, 0:1], m_sb[:, sl:sl + 1])
                    v.reduce_max(fc[:, 1:2], psS[:, 0:W], AX)
                    v.tensor_copy(fc[:, 5:6], l_sb[:, sl:sl + 1])
                    v.tensor_copy(y_stq[:], y_sb[:, sl * 1024:(sl + 1) * 1024])
                else:
                    base = pdesc[1] * 20
                    v.reduce_max(fc[:, 1:2], psS[:, 0:W], AX)
                    v.reg_load(rv1, schc[0:1, base + 12 + qs:base + 13 + qs])
                    mls = v.snap(rv1, min_val=0, max_val=15)
                    v.tensor_copy(fc[:, 0:1], m_sb[:, bass.ds(mls, 1)])
                    v.tensor_copy(fc[:, 5:6], l_sb[:, bass.ds(mls, 1)])
                    v.reg_load(rv1, schc[0:1, base + 4 + qs:base + 5 + qs])
                    ysl = v.snap(rv1, min_val=0, max_val=15 * 1024)
                    v.tensor_copy(y_stq[:], y_sb[:, bass.ds(ysl, 1024)])
                v.tensor_scalar_mul(fc[:, 1:2], fc[:, 1:2], SCALE)
                gap(v)
                v.tensor_max(fc[:, 2:3], fc[:, 0:1], fc[:, 1:2])
                gap(v)
                v.tensor_scalar_mul(fc[:, 3:4], fc[:, 2:3], -1.0)
                v.tensor_sub(fc[:, 7:8], fc[:, 0:1], fc[:, 2:3])
                gap(v).then_inc(sEV.h, 1)
            sEV.n += 1
            ev_mn = sEV.n

            @blk.scalar
            def _(sc, qs=qs, W=W, psS=psS, p_s=p_s, fc=fc, ev_mn=ev_mn):
                sc.wait_ge(sEV.h, ev_mn)
                key = qs % 2
                if key in last_s2v_ev:
                    sc.wait_ge(sEV.h, last_s2v_ev[key])
                if key in vy_ev:
                    sc.wait_ge(sEV.h, vy_ev[key])
                sc.activation(fc[:, 4:5], fc[:, 7:8], AF.Exp, bias=0.0, scale=1.0)
                sc.activation(p_s[:, 0:W], psS[:, 0:W], AF.Exp,
                              bias=fc[:, 3:4], scale=SCALE).then_inc(sSC.h, 1)
            sSC.n += 1
            last_exp_sc[qs % 2] = sSC.n
            sc_p = sSC.n
            return dict(qs=qs, nkt=nkt, W=W, p_s=p_s, fc=fc, y_stq=y_stq,
                        sc_p=sc_p)

        def emit_tail(st):
            qs = st["qs"]; nkt = st["nkt"]; W = st["W"]
            p_s = st["p_s"]; fc = st["fc"]; y_stq = st["y_stq"]
            sc_p = st["sc_p"]
            par = qs % 2
            pT_s = (pT0, pT1)[par]
            pvsbq = (pvsb0, pvsb1)[par]
            doff = par * 1024

            @blk.tensor
            def _(t, qs=qs, nkt=nkt, p_s=p_s, sc_p=sc_p, doff=doff):
                t.wait_ge(sSC.h, sc_p)
                key = qs % 2
                if key in last_s2v_ev:
                    t.wait_ge(sEV.h, last_s2v_ev[key])
                psD_bf = psD[:].bitcast(BF16)
                last = None
                for tc in range(nkt):
                    last = t.transpose(psD_bf[:, doff + tc * 128:doff + (tc + 1) * 128],
                                       p_s[:, tc * 128:(tc + 1) * 128], idb[:])
                last.then_inc(sMM.h, 1)
            sMM.n += 1
            mm_t = sMM.n

            @blk.vector
            def _(v, qs=qs, W=W, p_s=p_s, pT_s=pT_s, fc=fc, mm_t=mm_t, doff=doff):
                v.wait_ge(sMM.h, mm_t)
                v.reduce_sum(fc[:, 6:7], p_s[:, 0:W], AX)
                if is_diag:
                    sl = sbase + qs
                    v.tensor_copy(m_sb[:, sl:sl + 1], fc[:, 2:3])
                    v.scalar_tensor_tensor(fc[:, 5:6], fc[:, 5:6], fc[:, 4:5],
                                           fc[:, 6:7], AL.mult, AL.add)
                    gap(v)
                    v.tensor_copy(l_sb[:, sl:sl + 1], fc[:, 5:6])
                else:
                    base = pdesc[1] * 20
                    v.reg_load(rv1, schc[0:1, base + 12 + qs:base + 13 + qs])
                    mls = v.snap(rv1, min_val=0, max_val=15)
                    v.tensor_copy(m_sb[:, bass.ds(mls, 1)], fc[:, 2:3])
                    v.scalar_tensor_tensor(fc[:, 5:6], fc[:, 5:6], fc[:, 4:5],
                                           fc[:, 6:7], AL.mult, AL.add)
                    gap(v)
                    v.tensor_copy(l_sb[:, bass.ds(mls, 1)], fc[:, 5:6])
                v.tensor_copy(pT_s[:, 0:W],
                              psD[:].bitcast(BF16)[:, doff:doff + W]
                              ).then_inc(sEV.h, 1)
            sEV.n += 1
            last_s2v_ev[par] = sEV.n
            ev_pte = sEV.n

            @blk.tensor
            def _(t, qs=qs, nkt=nkt, pT_s=pT_s, vb=vb, ev_pte=ev_pte):
                t.wait_ge(sEV.h, ev_pte)
                if last_pve_sc[0] is not None:
                    t.wait_ge(sSC.h, last_pve_sc[0])
                last = None
                for tc in range(nkt):
                    t.matmul(psC[:, 0:512], pT_s[:, tc * 128:(tc + 1) * 128],
                             vb[:, tc * 1024:tc * 1024 + 512],
                             start=(tc == 0), stop=(tc == nkt - 1),
                             skip_group_check=True)
                    last = t.matmul(psC[:, 512:1024],
                                    pT_s[:, tc * 128:(tc + 1) * 128],
                                    vb[:, tc * 1024 + 512:(tc + 1) * 1024],
                                    start=(tc == 0), stop=(tc == nkt - 1),
                                    skip_group_check=True)
                last.then_inc(sMM.h, 1)
            sMM.n += 1
            mm_pv = sMM.n

            @blk.scalar
            def _(sc, mm_pv=mm_pv, pvsbq=pvsbq):
                sc.wait_ge(sMM.h, mm_pv)
                sc.activation(pvsbq[:], psC[:, 0:1024], AF.Copy,
                              bias=0.0, scale=1.0).then_inc(sSC.h, 1)
            sSC.n += 1
            last_pve_sc[0] = sSC.n
            sc_pve = sSC.n

            @blk.vector
            def _(v, fc=fc, qs=qs, sc_pve=sc_pve, pvsbq=pvsbq, y_stq=y_stq):
                v.wait_ge(sSC.h, sc_pve)
                v.scalar_tensor_tensor(y_stq[:], y_stq[:], fc[:, 4:5], pvsbq[:],
                                       AL.mult, AL.add)
                if is_diag:
                    sl = sbase + qs
                    v.tensor_copy(y_sb[:, sl * 1024:(sl + 1) * 1024], y_stq[:]
                                  ).then_inc(sEV.h, 1)
                else:
                    base = pdesc[1] * 20
                    v.reg_load(rv1, schc[0:1, base + 4 + qs:base + 5 + qs])
                    ysl = v.snap(rv1, min_val=0, max_val=15 * 1024)
                    v.tensor_copy(y_sb[:, bass.ds(ysl, 1024)], y_stq[:]
                                  ).then_inc(sEV.h, 1)
            sEV.n += 1
            vy_ev[par] = sEV.n

        pend = None
        for qs in range(8):
            cur = emit_head(qs)
            if pend is not None:
                emit_tail(pend)
            pend = cur
        emit_tail(pend)
        pair_done_mm.append(sMM.n)
    EV_ATT = sEV.n
    MM_ATT = sMM.n

    s2a.close()

    s2b = ExitStack()
    y_fm = s2b.enter_context(nc.sbuf_tensor("y_fm", [128, 8 * NT], BF16))
    y1 = s2b.enter_context(nc.sbuf_tensor("y1", [128, 8 * NT], BF16))
    wpTs = s2b.enter_context(nc.sbuf_tensor("wpTs", [128, 8 * 1024], BF16))
    wmpTs = s2b.enter_context(nc.sbuf_tensor("wmpTs", [128, 8 * 1024], BF16))
    wptmp = s2b.enter_context(nc.sbuf_tensor("wptmp", [128, 1024], F32))
    apTb = s2b.enter_context(nc.sbuf_tensor("apTb", [64, C], BF16))
    bpb = s2b.enter_context(nc.sbuf_tensor("bpb", [64, C], BF16))
    xtl0 = s2b.enter_context(nc.sbuf_tensor("xtl0", [128, TT], F32))
    xtl1 = s2b.enter_context(nc.sbuf_tensor("xtl1", [128, TT], F32))
    x2t0 = s2b.enter_context(nc.sbuf_tensor("x2t0", [128, TT], F32))
    x2t1 = s2b.enter_context(nc.sbuf_tensor("x2t1", [128, TT], F32))
    ytmp = s2b.enter_context(nc.sbuf_tensor("ytmp", [128, 1024], BF16))

    @blk.gpsimd
    def _(g):
        g.wait_ge(sMM.h, MM_ATT)
        g.dma_start(out=apTb[:], in_=apT.ap()).then_inc(sGA.h, 16)
        g.dma_start(out=bpb[:], in_=bp_in.ap()).then_inc(sGA.h, 16)
    sGA.n += 32
    GD_PL = sGA.n

    wp_evs = []
    for vcc in range(8):
        @blk.sync
        def _(s, vcc=vcc):
            s.wait_ge(sMM.h, MM_ATT)
            if vcc >= 1:
                s.wait_ge(sEV.h, wp_evs[-1])
            s.dma_start(out=wptmp[:], in_=wprojT[vcc * 128:(vcc + 1) * 128, :]
                        ).then_inc(sW.h, 16)
        sW.n += 16
        ld_wp = sW.n

        @blk.tensor
        def _(t, vcc=vcc, ld_wp=ld_wp):
            t.wait_ge(sW.h, ld_wp)
            last = None
            for dc in range(8):
                last = t.matmul(psD[:, 24 + dc:25 + dc],
                                wptmp[:, dc * 128:(dc + 1) * 128],
                                abv[:, 8 + vcc:9 + vcc],
                                start=(vcc == 0), stop=(vcc == 7),
                                skip_group_check=True)
            last.then_inc(sMM.h, 1)
        sMM.n += 1
        mm_pb = sMM.n

        @blk.vector
        def _(v, vcc=vcc, mm_pb=mm_pb):
            v.wait_ge(sMM.h, mm_pb)
            last = v.tensor_scalar_mul(wpTs[:, vcc * 1024:(vcc + 1) * 1024],
                                       wptmp[:], abv[:, vcc:vcc + 1])
            if vcc == 7:
                last = v.tensor_copy(pbc[:], psD[:, 24:32])
            last.then_inc(sEV.h, 1)
        sEV.n += 1
        wp_evs.append(sEV.n)

    wmp_evs = []
    for cc in range(8):
        @blk.sync
        def _(s, cc=cc):
            s.wait_ge(sEV.h, wp_evs[-1] if cc == 0 else wmp_evs[-1])
            s.dma_start(out=wptmp[:], in_=wprojT[cc * 128:(cc + 1) * 128, :]
                        ).then_inc(sW.h, 16)
        sW.n += 16
        ld_wp2 = sW.n

        @blk.tensor
        def _(t, cc=cc, ld_wp2=ld_wp2):
            t.wait_ge(sW.h, ld_wp2)
            if cc == 0:
                t.wait_ge(sGA.h, GD_PL)
            last = None
            for rb in range(2):
                last = t.matmul(psA[:, rb * 512:(rb + 1) * 512],
                                bpb[:, cc * 128:(cc + 1) * 128],
                                apTb[:, rb * 512:(rb + 1) * 512],
                                start=True, stop=True, skip_group_check=True)
            last.then_inc(sMM.h, 1)
        sMM.n += 1
        mm_mp = sMM.n

        @blk.vector
        def _(v, cc=cc, mm_mp=mm_mp):
            v.wait_ge(sMM.h, mm_mp)
            v.tensor_tensor(wmpTs[:, cc * 1024:(cc + 1) * 1024], wptmp[:],
                            psA[:, 0:1024], AL.add).then_inc(sEV.h, 1)
        sEV.n += 1
        wmp_evs.append(sEV.n)
    EV_WPM = sEV.n

    fin_evs = []
    for sl in range(16):
        @blk.vector
        def _(v, sl=sl):
            if sl == 0:
                v.wait_ge(sEV.h, EV_ATT)
            v.reciprocal(fc0[:, 7:8], l_sb[:, sl:sl + 1])
            gap(v)
            v.tensor_scalar_mul(ytmp[:], y_sb[:, sl * 1024:(sl + 1) * 1024],
                                fc0[:, 7:8]).then_inc(sEV.h, 1)
        sEV.n += 1
        ev_yt = sEV.n

        @blk.tensor
        def _(t, sl=sl, ev_yt=ev_yt):
            t.wait_ge(sEV.h, ev_yt)
            psB_bf = psB[:].bitcast(BF16)
            last = None
            for vcc in range(8):
                last = t.transpose(psB_bf[:, vcc * 128:(vcc + 1) * 128],
                                   ytmp[:, vcc * 128:(vcc + 1) * 128], idb[:])
            last.then_inc(sMM.h, 1)
        sMM.n += 1
        mm_ytr = sMM.n

        @blk.vector
        def _(v, sl=sl, mm_ytr=mm_ytr):
            v.wait_ge(sMM.h, mm_ytr)
            v.tensor_copy(bass.AP(y_fm, sl * 128, [[8 * NT, 128], [NT, 8], [1, 128]]),
                          psB[:].bitcast(BF16)[:, 0:1024
                          ].rearrange("p (c t) -> p c t", c=8)
                          ).then_inc(sEV.h, 1)
        sEV.n += 1
        fin_evs.append(sEV.n)
    EV_FIN = sEV.n

    p1_scs = []
    for dc in range(8):
        for tt4 in range(4):
            ps = (psA, psB)[(dc * 4 + tt4) % 2]

            @blk.tensor
            def _(t, dc=dc, tt4=tt4, ps=ps):
                if dc == 0 and tt4 == 0:
                    t.wait_ge(sEV.h, EV_FIN)
                    t.wait_ge(sEV.h, EV_WPM)
                if len(p1_scs) >= 2:
                    t.wait_ge(sSC.h, p1_scs[-2])
                last = None
                for vcc in range(8):
                    last = t.matmul(
                        ps[:, 0:512],
                        wpTs[:, vcc * 1024 + dc * 128:vcc * 1024 + (dc + 1) * 128],
                        y_fm[:, vcc * NT + tt4 * 512:vcc * NT + (tt4 + 1) * 512],
                        start=(vcc == 0), stop=(vcc == 7), skip_group_check=True)
                last.then_inc(sMM.h, 1)
            sMM.n += 1
            mm_p1 = sMM.n

            @blk.scalar
            def _(sc, dc=dc, tt4=tt4, ps=ps, mm_p1=mm_p1):
                sc.wait_ge(sMM.h, mm_p1)
                sc.activation(y1[:, dc * NT + tt4 * 512:dc * NT + (tt4 + 1) * 512],
                              ps[:, 0:512], AF.Identity,
                              bias=pbc[:, dc:dc + 1], scale=1.0).then_inc(sSC.h, 1)
            sSC.n += 1
            p1_scs.append(sSC.n)
    SC_P1 = sSC.n

    p2_evs = []
    x2st_lds = []
    for dc2 in range(8):
        for tt4 in range(4):
            idx = dc2 * 4 + tt4
            ps = (psC, psD)[idx % 2]
            xtl = (xtl0, xtl1)[idx % 2]
            x2t = (x2t0, x2t1)[idx % 2]

            sXT = (sXT0, sXT1)[idx % 2]

            @blk.sync
            def _(s, dc2=dc2, tt4=tt4, xtl=xtl, sXT=sXT):
                if len(p2_evs) >= 2:
                    s.wait_ge(sEV.h, p2_evs[-2])
                s.dma_start(out=xtl[:],
                            in_=bass.AP(x_fm, dc2 * 128 * NT + tt4 * 512,
                                        [[NT, 128], [1, 512]])
                            ).then_inc(sXT.h, 16)
            sXT.n += 16
            ld_xt = sXT.n

            @blk.tensor
            def _(t, dc2=dc2, tt4=tt4, ps=ps, idx=idx):
                if idx == 0:
                    t.wait_ge(sSC.h, SC_P1)
                if len(p2_evs) >= 2:
                    t.wait_ge(sEV.h, p2_evs[-2])
                last = None
                for cc in range(8):
                    last = t.matmul(
                        ps[:, 0:512],
                        wmpTs[:, cc * 1024 + dc2 * 128:cc * 1024 + (dc2 + 1) * 128],
                        y1[:, cc * NT + tt4 * 512:cc * NT + (tt4 + 1) * 512],
                        start=(cc == 0), stop=(cc == 7), skip_group_check=True)
                last.then_inc(sMM.h, 1)
            sMM.n += 1
            mm_p2 = sMM.n

            sX2 = (sX2a, sX2b)[idx % 2]

            @blk.vector
            def _(v, ps=ps, xtl=xtl, x2t=x2t, mm_p2=mm_p2, ld_xt=ld_xt,
                  sXT=sXT, sX2=sX2):
                v.wait_ge(sMM.h, mm_p2)
                v.wait_ge(sXT.h, ld_xt)
                if len(x2st_lds) >= 2:
                    v.wait_ge(sX2.h, x2st_lds[-2][1])
                v.tensor_add(x2t[:], ps[:, 0:512], xtl[:]).then_inc(sEV.h, 1)
            sEV.n += 1
            p2_evs.append(sEV.n)
            ev_x2 = sEV.n

            @blk.sync
            def _(s, dc2=dc2, tt4=tt4, x2t=x2t, ev_x2=ev_x2, sX2=sX2):
                s.wait_ge(sEV.h, ev_x2)
                s.dma_start(out=bass.AP(x2_sp, dc2 * 128 * NT + tt4 * 512,
                                        [[NT, 128], [1, 512]]),
                            in_=x2t[:]).then_inc(sX2.h, 16)
            sX2.n += 16
            x2st_lds.append((idx % 2, sX2.n))
    X2A_ALL = sX2a.n
    X2B_ALL = sX2b.n
    EV_P2 = sEV.n

    s2b.close()
    s2.close()

    s4 = ExitStack()
    ln1 = s4.enter_context(nc.sbuf_tensor("ln1", [128, 8 * NT], BF16))
    rowb4 = s4.enter_context(nc.sbuf_tensor("rowb4", [1, 4096], F32))
    wfm = s4.enter_context(nc.sbuf_tensor("wfm", [8, NT], BF16))
    s4b = ExitStack()
    xt4 = s4b.enter_context(nc.sbuf_tensor("xt4", [128, 8 * TT], F32))
    xb4 = s4b.enter_context(nc.sbuf_tensor("xb4", [128, 8 * TT], BF16))
    xsq4 = s4b.enter_context(nc.sbuf_tensor("xsq4", [128, 8 * TT], BF16))
    abc4 = s4b.enter_context(nc.sbuf_tensor("abc4", [128, 2 * TT], F32))
    xtmp4 = s4b.enter_context(nc.sbuf_tensor("xtmp4", [128, TT], F32))
    lgtm = s4b.enter_context(nc.sbuf_tensor("lgtm", [128, 128], F32))
    e8 = s4b.enter_context(nc.sbuf_tensor("e8", [128, 8], F32))
    eq8 = s4b.enter_context(nc.sbuf_tensor("eq8", [128, 8], F32))
    wtm = s4b.enter_context(nc.sbuf_tensor("wtm", [128, 128], F32))

    prev4 = None
    for ti in range(NTILE):
        fw_sync = [(sX2a, X2A_ALL), (sX2b, X2B_ALL)] if ti == 0 else []
        fw_mm = [(sEV, EV_P2)] if ti == 0 else []
        prev4 = emit_ln_tile(
            ti, x2_sp, xt4, xb4, xsq4, abc4, xtmp4, ln1,
            lambda cc, ti_: (cc * NT + ti_ * TT, cc * NT + (ti_ + 1) * TT),
            prev4, fw_sync, fw_mm, None, rowb4)
    EV_LN1 = prev4["ev_ln"]

    gate_evs = []
    for grp in range(2):
        @blk.tensor
        def _(t, grp=grp):
            t.wait_ge(sEV.h, EV_LN1)
            if gate_evs:
                t.wait_ge(sEV.h, gate_evs[-1])
            last = None
            for tc8 in range(8):
                tc = grp * 8 + tc8
                for cc in range(8):
                    last = t.matmul(
                        psD[:, tc8 * 8:(tc8 + 1) * 8],
                        ln1[:, cc * NT + tc * 128:cc * NT + (tc + 1) * 128],
                        gwb[:, cc * 8:(cc + 1) * 8],
                        start=(cc == 0), stop=(cc == 7), skip_group_check=True)
            last.then_inc(sMM.h, 1)
        sMM.n += 1
        mm_g = sMM.n

        @blk.vector
        def _(v, mm_g=mm_g):
            v.wait_ge(sMM.h, mm_g)
            gbb_b = bass.AP(gbb, 0, [[8, 128], [0, 8], [1, 8]])
            lg3 = lgtm[:, 0:64].rearrange("p (tc e) -> p tc e", tc=8)
            ps3 = psD[:, 0:64].rearrange("p (tc e) -> p tc e", tc=8)
            v.tensor_tensor(lg3, ps3, gbb_b, AL.add)
            gap(v).then_inc(sEV.h, 1)
        sEV.n += 1
        ev_lg = sEV.n

        for tc8 in range(8):
            @blk.vector
            def _(v, tc8=tc8, ev_lg=ev_lg):
                v.wait_ge(sEV.h, ev_lg)
                lg = lgtm[:, tc8 * 8:(tc8 + 1) * 8]
                v.reduce_max(sm[:, 0:1], lg, AX)
                v.tensor_copy(sm[:, 4:5], lg[:, 0:1])
                v.tensor_scalar_mul(sm[:, 3:4], sm[:, 0:1], -1.0)
                v.tensor_scalar(eq8[:], lg, sm[:, 0:1], None, AL.is_equal)
                gap(v)
                v.scalar_tensor_tensor(eq8[:], eq8[:], -1.0e30, lg, AL.mult, AL.add)
                gap(v)
                v.reduce_max(sm[:, 1:2], eq8[:], AX)
                gap(v).then_inc(sEV.h, 1)
            sEV.n += 1
            ev_m12 = sEV.n

            @blk.scalar
            def _(sc, tc8=tc8, ev_m12=ev_m12):
                sc.wait_ge(sEV.h, ev_m12)
                sc.activation(e8[:], lgtm[:, tc8 * 8:(tc8 + 1) * 8], AF.Exp,
                              bias=sm[:, 3:4], scale=1.0).then_inc(sSC.h, 1)
            sSC.n += 1
            sc_e8 = sSC.n

            @blk.vector
            def _(v, tc8=tc8, sc_e8=sc_e8):
                v.wait_ge(sSC.h, sc_e8)
                lg = lgtm[:, tc8 * 8:(tc8 + 1) * 8]
                v.reduce_sum(sm[:, 2:3], e8[:], AX)
                v.tensor_scalar(eq8[:], lg, sm[:, 1:2], None, AL.is_ge)
                v.reciprocal(sm[:, 2:3], sm[:, 2:3])
                v.tensor_mul(e8[:], e8[:], eq8[:])
                gap(v)
                v.tensor_scalar_mul(wtm[:, tc8 * 8:(tc8 + 1) * 8], e8[:],
                                    sm[:, 2:3])
                gap(v).then_inc(sEV.h, 1)
            sEV.n += 1
        ev_w = sEV.n

        @blk.tensor
        def _(t, grp=grp, ev_w=ev_w):
            t.wait_ge(sEV.h, ev_w)
            last = None
            for tc8 in range(8):
                last = t.transpose(psC[0:8, tc8 * 128:(tc8 + 1) * 128],
                                   wtm[:, tc8 * 8:(tc8 + 1) * 8], idf[:])
            last.then_inc(sMM.h, 1)
        sMM.n += 1
        mm_wt = sMM.n

        @blk.vector
        def _(v, grp=grp, mm_wt=mm_wt):
            v.wait_ge(sMM.h, mm_wt)
            v.tensor_copy(wfm[0:8, grp * 1024:(grp + 1) * 1024], psC[0:8, 0:1024]
                          ).then_inc(sEV.h, 1)
        sEV.n += 1
        gate_evs.append(sEV.n)
    EV_GATE = sEV.n

    s4b.close()

    s4c = ExitStack()
    hsl = s4c.enter_context(nc.sbuf_tensor("hsl", [128, 16 * 1024], BF16))
    w1p0 = s4c.enter_context(nc.sbuf_tensor("w1p0", [128, 8192], BF16))
    w1p1 = s4c.enter_context(nc.sbuf_tensor("w1p1", [128, 8192], BF16))
    w2p0 = s4c.enter_context(nc.sbuf_tensor("w2p0", [128, 8192], BF16))
    w2p1 = s4c.enter_context(nc.sbuf_tensor("w2p1", [128, 8192], BF16))
    oacc = s4c.enter_context(nc.sbuf_tensor("oacc", [128, 8 * 1024], F32))
    wbc0 = s4c.enter_context(nc.sbuf_tensor("wbc0", [128, 1024], BF16))
    wbc1 = s4c.enter_context(nc.sbuf_tensor("wbc1", [128, 1024], BF16))
    wr0 = s4c.enter_context(nc.sbuf_tensor("wr0", [1, 1024], BF16))
    b2b = s4c.enter_context(nc.sbuf_tensor("b2b", [8, 1024], BF16))

    ffn_e_mm = {}
    mm_f1 = {}
    mm_f2 = {}
    w_gd = {}
    last_mm_wb = [None]

    @blk.gpsimd
    def _(g):
        g.wait_ge(sEV.h, EV_GATE)
        g.dma_start(out=b2b[:], in_=b2_in.ap()).then_inc(sGA.h, 16)
    sGA.n += 16
    GD_B2 = sGA.n

    for th in range(2):
        @blk.sync
        def _(s, th=th):
            s.wait_ge(sEV.h, EV_GATE)
            if th == 1:
                s.wait_ge(sST.h, 16)
            s.dma_start(out=oacc[:].rearrange("p (o t) -> p o t", o=8),
                        in_=bass.AP(x2_sp, th * 1024,
                                    [[NT, 128], [128 * NT, 8], [1, 1024]])
                        ).then_inc(sOI.h, 16)
        sOI.n += 16
        ld_oi = sOI.n

        relu_scs = []
        o_evs = []
        for e in range(8):
            par = e % 2
            wbc = (wbc0, wbc1)[par]

            @blk.gpsimd
            def _(g, e=e, th=th):
                for jh in range(2):
                    if (th, e - 1, jh) in mm_f1:
                        g.wait_ge(sMM.h, mm_f1[(th, e - 1, jh)])
                    elif th == 1 and e == 0:
                        g.wait_ge(sMM.h, mm_f1[(0, 7, jh)])
                    elif th == 0 and e == 0:
                        g.wait_ge(sEV.h, EV_GATE)
                    sWx = (sW10, sW11)[jh]
                    g.dma_start(
                        out=(w1p0, w1p1)[jh][:].rearrange("p (c j) -> p c j", c=8),
                        in_=bass.AP(w1_in, e * C * 2048 + jh * 1024,
                                    [[2048, 128], [128 * 2048, 8], [1, 1024]])
                        ).then_inc(sWx.h, 16)
                    sWx.n += 16
                    w_gd[(th, e, "w1", jh)] = (sWx, sWx.n)
                for oh in range(2):
                    if (th, e - 1, oh) in mm_f2:
                        g.wait_ge(sMM.h, mm_f2[(th, e - 1, oh)])
                    elif th == 1 and e == 0:
                        g.wait_ge(sMM.h, mm_f2[(0, 7, oh)])
                    sWx = (sW20, sW21)[oh]
                    g.dma_start(
                        out=(w2p0, w2p1)[oh][:].rearrange("p (j o) -> p j o", j=16),
                        in_=bass.AP(w2_in, e * 2048 * C + oh * 512,
                                    [[C, 128], [128 * C, 16], [1, 512]])
                        ).then_inc(sWx.h, 16)
                    sWx.n += 16
                    w_gd[(th, e, "w2", oh)] = (sWx, sWx.n)

            @blk.sync
            def _(s, e=e, th=th):
                s.wait_ge(sEV.h, EV_GATE)
                if last_mm_wb[0] is not None:
                    s.wait_ge(sMM.h, last_mm_wb[0])
                s.dma_start(out=wr0[0:1, :],
                            in_=wfm[e:e + 1, th * 1024:(th + 1) * 1024]
                            ).then_inc(sWR.h, 16)
            sWR.n += 16
            ld_wr = sWR.n

            @blk.tensor
            def _(t, e=e, th=th, ld_wr=ld_wr):
                t.wait_ge(sWR.h, ld_wr)
                if relu_scs:
                    t.wait_ge(sSC.h, relu_scs[-1])
                t.matmul(psB[:, 0:512], onerb[:], wr0[0:1, 0:512],
                         start=True, stop=True, skip_group_check=True)
                t.matmul(psB[:, 512:1024], onerb[:], wr0[0:1, 512:1024],
                         start=True, stop=True, skip_group_check=True
                         ).then_inc(sMM.h, 1)
            sMM.n += 1
            mm_wb = sMM.n
            last_mm_wb[0] = mm_wb

            @blk.vector
            def _(v, wbc=wbc, mm_wb=mm_wb):
                v.wait_ge(sMM.h, mm_wb)
                v.tensor_copy(wbc[:], psB[:, 0:1024]).then_inc(sEV.h, 1)
            sEV.n += 1
            ev_wb = sEV.n

            for jc in range(16):
                psH = (psA, psB)[jc % 2]

                @blk.tensor
                def _(t, e=e, th=th, jc=jc, psH=psH, ev_wb=ev_wb):
                    jh = jc // 8
                    jj = jc % 8
                    if jj == 0:
                        wsem, wval = w_gd[(th, e, "w1", jh)]
                        t.wait_ge(wsem.h, wval)
                    if jc <= 1:
                        t.wait_ge(sEV.h, ev_wb)
                    if len(relu_scs) >= 2:
                        t.wait_ge(sSC.h, relu_scs[-2])
                    w1b = (w1p0, w1p1)[jh]
                    last = None
                    for cc in range(8):
                        lhs = w1b[:, cc * 1024 + jj * 128:
                                  cc * 1024 + (jj + 1) * 128]
                        t.matmul(psH[:, 0:512], lhs,
                                 ln1[:, cc * NT + th * 1024:
                                     cc * NT + th * 1024 + 512],
                                 start=(cc == 0), stop=(cc == 7),
                                 skip_group_check=True)
                        last = t.matmul(psH[:, 512:1024], lhs,
                                        ln1[:, cc * NT + th * 1024 + 512:
                                            cc * NT + (th + 1) * 1024],
                                        start=(cc == 0), stop=(cc == 7),
                                        skip_group_check=True)
                    last.then_inc(sMM.h, 1)
                sMM.n += 1
                mm_h = sMM.n
                if jc == 7:
                    mm_f1[(th, e, 0)] = mm_h
                elif jc == 15:
                    mm_f1[(th, e, 1)] = mm_h

                @blk.scalar
                def _(sc, e=e, jc=jc, psH=psH, mm_h=mm_h):
                    sc.wait_ge(sMM.h, mm_h)
                    sc.activation(hsl[:, jc * 1024:(jc + 1) * 1024], psH[:, 0:1024],
                                  AF.Relu, bias=b1c[:, e * 16 + jc:e * 16 + jc + 1],
                                  scale=1.0).then_inc(sSC.h, 1)
                sSC.n += 1
                relu_scs.append(sSC.n)
                sc_r = sSC.n

                @blk.vector
                def _(v, jc=jc, wbc=wbc, sc_r=sc_r):
                    v.wait_ge(sSC.h, sc_r)
                    v.tensor_mul(hsl[:, jc * 1024:(jc + 1) * 1024],
                                 hsl[:, jc * 1024:(jc + 1) * 1024], wbc[:]
                                 ).then_inc(sEV.h, 1)
                sEV.n += 1
            ev_hdone = sEV.n

            for oc in range(8):
                psO = (psC, psD)[oc % 2]

                @blk.tensor
                def _(t, e=e, th=th, oc=oc, psO=psO, ev_hdone=ev_hdone):
                    oh = oc // 4
                    oo = oc % 4
                    if oc == 0:
                        t.wait_ge(sEV.h, ev_hdone)
                    if oo == 0:
                        wsem, wval = w_gd[(th, e, "w2", oh)]
                        t.wait_ge(wsem.h, wval)
                    if len(o_evs) >= 2:
                        t.wait_ge(sEV.h, o_evs[-2])
                    w2b = (w2p0, w2p1)[oh]
                    last = None
                    for jc in range(16):
                        lhs = w2b[:, jc * 512 + oo * 128:
                                  jc * 512 + (oo + 1) * 128]
                        t.matmul(psO[:, 0:512], lhs,
                                 hsl[:, jc * 1024:jc * 1024 + 512],
                                 start=(jc == 0), stop=(jc == 15),
                                 skip_group_check=True)
                        last = t.matmul(psO[:, 512:1024], lhs,
                                        hsl[:, jc * 1024 + 512:(jc + 1) * 1024],
                                        start=(jc == 0), stop=(jc == 15),
                                        skip_group_check=True)
                    last.then_inc(sMM.h, 1)
                sMM.n += 1
                mm_o = sMM.n
                if oc == 3:
                    mm_f2[(th, e, 0)] = mm_o
                elif oc == 7:
                    mm_f2[(th, e, 1)] = mm_o

                @blk.vector
                def _(v, oc=oc, psO=psO, mm_o=mm_o, ld_oi=ld_oi):
                    v.wait_ge(sMM.h, mm_o)
                    v.wait_ge(sOI.h, ld_oi)
                    v.tensor_add(oacc[:, oc * 1024:(oc + 1) * 1024],
                                 oacc[:, oc * 1024:(oc + 1) * 1024],
                                 psO[:, 0:1024]).then_inc(sEV.h, 1)
                sEV.n += 1
                o_evs.append(sEV.n)
            ffn_e_mm[(th, e)] = sMM.n

        for ocp in range(4):
            @blk.tensor
            def _(t, th=th, ocp=ocp):
                if ocp == 0:
                    t.wait_ge(sEV.h, o_evs[-1])
                    t.wait_ge(sGA.h, GD_B2)
                else:
                    t.wait_ge(sEV.h, sEV.n)
                last = None
                for j in range(2):
                    oc = ocp * 2 + j
                    psO = (psC, psD)[j]
                    for hh in range(2):
                        last = t.matmul(
                            psO[:, hh * 512:(hh + 1) * 512],
                            b2b[:, oc * 128:(oc + 1) * 128],
                            wfm[0:8, th * 1024 + hh * 512:
                                th * 1024 + (hh + 1) * 512],
                            start=True, stop=True, skip_group_check=True)
                last.then_inc(sMM.h, 1)
            sMM.n += 1
            mm_b2 = sMM.n

            @blk.vector
            def _(v, ocp=ocp, mm_b2=mm_b2):
                v.wait_ge(sMM.h, mm_b2)
                last = None
                for j in range(2):
                    oc = ocp * 2 + j
                    psO = (psC, psD)[j]
                    last = v.tensor_add(oacc[:, oc * 1024:(oc + 1) * 1024],
                                        oacc[:, oc * 1024:(oc + 1) * 1024],
                                        psO[:, 0:1024])
                last.then_inc(sEV.h, 1)
            sEV.n += 1
        ev_b2 = sEV.n

        @blk.sync
        def _(s, th=th, ev_b2=ev_b2):
            s.wait_ge(sEV.h, ev_b2)
            s.dma_start(out=bass.AP(out_fm, th * 1024,
                                    [[NT, 128], [128 * NT, 8], [1, 1024]]),
                        in_=oacc[:].rearrange("p (o t) -> p o t", o=8)
                        ).then_inc(sST.h, 16)
        sST.n += 16

    @blk.sync
    def _(s):
        s.wait_ge(sST.h, sST.n)

    s4c.close()
    s4.close()
    es.close()
    nc.finalize()
    return nc


_NC_CACHE = {}


def _get_nc():
    if "nc" not in _NC_CACHE:
        _NC_CACHE["nc"] = build_nc()
    return _NC_CACHE["nc"]


def _make_sched(c):
    pA, pB = c, 7 - c
    rows = []
    for g in range(pA):
        sh = g
        rows.append((sh * SHARD + 0, sh * SHARD + VB + 0, 0, 0))
    for g in range(pB):
        sh = 7 - g
        rows.append((sh * SHARD + HBLK, sh * SHARD + VB + HBLK, HBLK, 8))
    assert len(rows) == 7
    tab = np.zeros((7, 20), dtype=np.int32)
    for gi, (ko, vo, qo, sb) in enumerate(rows):
        tab[gi, 0] = ko
        tab[gi, 1] = vo
        tab[gi, 2] = qo
        tab[gi, 3] = sb
        for qs in range(8):
            tab[gi, 4 + qs] = (sb + qs) * 1024
            tab[gi, 12 + qs] = sb + qs
    return tab.reshape(1, 140)


def kernel(L0_attn_x, L0_attn_W_attn, L0_attn_W_proj, L0_attn_lora_attn_A,
           L0_attn_lora_attn_B, L0_attn_lora_proj_A, L0_attn_lora_proj_B,
           L0_attn_bn_gamma, L0_attn_bn_beta, L0_ffn_gate_w, L0_ffn_gate_b,
           L0_ffn_w1, L0_ffn_b1, L0_ffn_w2, L0_ffn_b2):
    x = np.asarray(L0_attn_x, dtype=np.float32)
    cst = np.ascontiguousarray
    wattnT_h = cst(np.asarray(L0_attn_W_attn, np.float32).T)
    aaT_h = cst(np.asarray(L0_attn_lora_attn_A, np.float32).T)
    ba_h = cst(np.asarray(L0_attn_lora_attn_B, np.float32))
    wprojT_h = cst(np.asarray(L0_attn_W_proj, np.float32).T)
    apT_h = cst(np.asarray(L0_attn_lora_proj_A, np.float32).T)
    bp_h = cst(np.asarray(L0_attn_lora_proj_B, np.float32))
    gam = np.asarray(L0_attn_bn_gamma, np.float32)
    bet = np.asarray(L0_attn_bn_beta, np.float32)
    gam_qk_h = cst(gam[0:2048].reshape(16, 128).T)
    bet_qk_h = cst(bet[0:2048].reshape(16, 128).T)
    gam_vr_h = cst(gam[2048:3072].reshape(1, C))
    bet_vr_h = cst(bet[2048:3072].reshape(1, C))
    gwT_h = cst(np.asarray(L0_ffn_gate_w, np.float32).T)
    gb_row_h = cst(np.asarray(L0_ffn_gate_b, np.float32).reshape(1, 8))
    w1_h = cst(np.asarray(L0_ffn_w1, np.float32))
    b1_t_h = cst(np.asarray(L0_ffn_b1, np.float32).reshape(8, 16, 128)
                 .transpose(2, 0, 1).reshape(128, 128))
    w2_h = cst(np.asarray(L0_ffn_w2, np.float32))
    b2_h = cst(np.asarray(L0_ffn_b2, np.float32))

    nc = _get_nc()
    in_maps = []
    for c in range(NCORE):
        a0 = x[0, 1024 * c:1024 * (c + 1)]
        a1 = x[1, 1024 * (7 - c):1024 * (8 - c)]
        x_fm_h = cst(np.concatenate([a0, a1], axis=0).T)
        in_maps.append({
            "x_fm": x_fm_h, "wattnT": wattnT_h, "aaT": aaT_h, "ba_in": ba_h,
            "wprojT": wprojT_h, "apT": apT_h, "bp_in": bp_h,
            "gam_qk": gam_qk_h, "bet_qk": bet_qk_h,
            "gam_vr": gam_vr_h, "bet_vr": bet_vr_h,
            "gwT": gwT_h, "gb_row": gb_row_h,
            "w1_in": w1_h, "b1_t": b1_t_h, "w2_in": w2_h, "b2_in": b2_h,
            "sched": _make_sched(c),
        })
    res = run_bass_kernel_spmd(nc, in_maps, core_ids=list(range(NCORE)))
    out = np.zeros((2, 8192, 1024), dtype=np.float32)
    for c in range(NCORE):
        o = res.results[c]["out_fm"]
        out[0, 1024 * c:1024 * (c + 1)] = o[:, 0:1024].T
        out[1, 1024 * (7 - c):1024 * (8 - c)] = o[:, 1024:2048].T
    return out


if __name__ == "__main__":
    nc = build_nc()
    print("build OK; instructions:", nc.n_instructions())


# version 08
# speedup vs baseline: 1.0793x; 1.0338x over previous
import numpy as np
import ml_dtypes
from contextlib import ExitStack
import concourse.bass as bass
import concourse.bacc as bacc
import concourse.mybir as mybir
from concourse.bass_utils import run_bass_kernel_spmd

F32 = mybir.dt.float32
BF16 = mybir.dt.bfloat16
I32 = mybir.dt.int32
AF = mybir.ActivationFunctionType
AL = mybir.AluOpType
AX = mybir.AxisListType.X

NCORE = 8
NT = 2048
C = 1024
D3 = 3072
KSH = C * NT
VB = KSH
SHARD = KSH + NT * C
TT = 512
NTILE = NT // TT
HBLK = 128 * 8 * 1024
SCALE = 1.0 / 32.0
EPS = 1e-5
NEGBIG = -1.0e9
NTOK = 16384.0


def build_nc():
    nc = bacc.Bacc("TRN2", target_bir_lowering=False, num_devices=NCORE,
                   detect_race_conditions=False)

    x_fm = nc.dram_tensor("x_fm", [C, NT], F32, kind="ExternalInput")
    wattnT = nc.dram_tensor("wattnT", [C, D3], F32, kind="ExternalInput")
    aaT = nc.dram_tensor("aaT", [64, C], F32, kind="ExternalInput")
    ba_in = nc.dram_tensor("ba_in", [64, D3], F32, kind="ExternalInput")
    wprojT = nc.dram_tensor("wprojT", [C, C], F32, kind="ExternalInput")
    apT = nc.dram_tensor("apT", [64, C], F32, kind="ExternalInput")
    bp_in = nc.dram_tensor("bp_in", [64, C], F32, kind="ExternalInput")
    gam_qk = nc.dram_tensor("gam_qk", [128, 16], F32, kind="ExternalInput")
    bet_qk = nc.dram_tensor("bet_qk", [128, 16], F32, kind="ExternalInput")
    gam_vr = nc.dram_tensor("gam_vr", [1, C], F32, kind="ExternalInput")
    bet_vr = nc.dram_tensor("bet_vr", [1, C], F32, kind="ExternalInput")
    gwT = nc.dram_tensor("gwT", [C, 8], F32, kind="ExternalInput")
    gb_row = nc.dram_tensor("gb_row", [1, 8], F32, kind="ExternalInput")
    w1_in = nc.dram_tensor("w1_in", [8, C, 2048], F32, kind="ExternalInput")
    b1_t = nc.dram_tensor("b1_t", [128, 128], F32, kind="ExternalInput")
    w2_in = nc.dram_tensor("w2_in", [8, 2048, C], F32, kind="ExternalInput")
    b2_in = nc.dram_tensor("b2_in", [8, C], F32, kind="ExternalInput")
    sched = nc.dram_tensor("sched", [1, 7 * 20], I32, kind="ExternalInput")

    out_fm = nc.dram_tensor("out_fm", [C, NT], F32, kind="ExternalOutput")

    kv_in = nc.dram_tensor("kv_in", [1, SHARD], BF16)
    kv_all = nc.dram_tensor("kv_all", [NCORE, SHARD], BF16)
    st_in = nc.dram_tensor("st_in", [1, 6144], F32)
    st_out = nc.dram_tensor("st_out", [1, 6144], F32)
    q_dram = nc.dram_tensor("q_dram", [128, 8 * NT], BF16)
    x2_sp = nc.dram_tensor("x2_sp", [C, NT], F32)

    ident_f_d = nc.inline_tensor(np.eye(128, dtype=np.float32), name="identf")
    ident_b_d = nc.inline_tensor(np.eye(128, dtype=ml_dtypes.bfloat16), name="identb")
    tri_np = np.where(np.arange(128)[None, :] <= np.arange(128)[:, None],
                      0.0, NEGBIG).astype(np.float32)
    tri_d = nc.inline_tensor(tri_np, name="trim")
    onescol_b_d = nc.inline_tensor(np.ones((128, 1), dtype=ml_dtypes.bfloat16),
                                   name="onescolb")
    onesrow_f_d = nc.inline_tensor(np.ones((1, 128), dtype=np.float32),
                                   name="onesrowf")
    onesrow_b_d = nc.inline_tensor(np.ones((1, 128), dtype=ml_dtypes.bfloat16),
                                   name="onesrowb")

    psA = nc.alloc_psum_tensor("psA", [128, 1024], F32)
    psB = nc.alloc_psum_tensor("psB", [128, 1024], F32)
    psC = nc.alloc_psum_tensor("psC", [128, 1024], F32)
    psD = nc.alloc_psum_tensor("psD", [128, 1024], F32)

    A = nc.alloc_sbuf_tensor
    idb = A("idb", [128, 128], BF16)
    idf = A("idf", [128, 128], F32)
    trim = A("trim_sb", [128, 128], F32)
    onecb = A("onecb", [128, 1], BF16)
    onerf = A("onerf", [1, 128], F32)
    onerb = A("onerb", [1, 128], BF16)
    stc = A("stc", [128, 32], F32)
    abqk = A("abqk", [128, 32], F32)
    abv = A("abv", [128, 16], F32)
    qab = A("qab", [128, 16], F32)
    gamc = A("gamc", [128, 16], F32)
    betc = A("betc", [128, 16], F32)
    schc = A("schc", [1, 7 * 20], I32)
    m_sb = A("m_sb", [128, 16], F32)
    l_sb = A("l_sb", [128, 16], F32)
    b1c = A("b1c", [128, 128], F32)
    gwb = A("gwb", [128, 64], BF16)
    gbr = A("gbr", [1, 8], F32)
    gbb = A("gbb", [128, 8], F32)
    pbc = A("pbc", [128, 8], F32)
    sm = A("sm", [128, 16], F32)
    tcol = A("tcol", [128, 4], F32)
    dgap = A("dgap", [128, 64], F32)

    def gap(v):
        return v.tensor_copy(dgap[:, 0:1], dgap[:, 1:2])

    class Sem:
        def __init__(self, name):
            self.h = nc.alloc_semaphore(name)
            self.n = 0
    sMM = Sem("sMM")
    sEV = Sem("sEV")
    sSC = Sem("sSC")
    sCC = Sem("sCC")
    sST = Sem("sST")
    sC = Sem("sC")
    sW = Sem("sW")
    sX = Sem("sX")
    sKVk = Sem("sKVk")
    sKVv = Sem("sKVv")
    sVL = Sem("sVL")
    sSTT = Sem("sSTT")
    sSR = Sem("sSR")
    sGV = Sem("sGV")
    sQW = Sem("sQW")
    sGA = Sem("sGA")
    sPL0 = Sem("sPL0")
    sPL1 = Sem("sPL1")
    sQF = Sem("sQF")
    sKF = Sem("sKF")
    sVF = Sem("sVF")
    sXT0 = Sem("sXT0")
    sXT1 = Sem("sXT1")
    sX2a = Sem("sX2a")
    sX2b = Sem("sX2b")
    sOI = Sem("sOI")
    sWR = Sem("sWR")
    sW10 = Sem("sW10")
    sW11 = Sem("sW11")
    sW20 = Sem("sW20")
    sW21 = Sem("sW21")

    es = ExitStack()
    rg1 = es.enter_context(nc.gpsimd.register("rg1"))
    rg2 = es.enter_context(nc.gpsimd.register("rg2"))
    rg3 = es.enter_context(nc.gpsimd.register("rg3"))
    rv1 = es.enter_context(nc.vector.register("rv1"))
    rs1 = es.enter_context(nc.sync.register("rs1"))
    rc1 = es.enter_context(nc.scalar.register("rc1"))
    blk = es.enter_context(nc.Block("main"))

    s1 = ExitStack()
    wmT = s1.enter_context(nc.sbuf_tensor("wmT", [128, 8 * D3], BF16))
    wtmp = s1.enter_context(nc.sbuf_tensor("wtmp", [128, D3], F32))
    wtmpB = s1.enter_context(nc.sbuf_tensor("wtmpB", [128, D3], F32))
    aatb = s1.enter_context(nc.sbuf_tensor("aatb", [64, C], BF16))
    bab = s1.enter_context(nc.sbuf_tensor("bab", [64, D3], BF16))
    xt = s1.enter_context(nc.sbuf_tensor("xt", [128, 8 * TT], F32))
    xb = s1.enter_context(nc.sbuf_tensor("xb", [128, 8 * TT], BF16))
    xsq = s1.enter_context(nc.sbuf_tensor("xsq", [128, 8 * TT], BF16))
    ln0 = s1.enter_context(nc.sbuf_tensor("ln0", [128, 8 * TT], BF16))
    abc0 = s1.enter_context(nc.sbuf_tensor("abc0", [128, 2 * TT], F32))
    xtmp = s1.enter_context(nc.sbuf_tensor("xtmp", [128, TT], F32))
    qstage = s1.enter_context(nc.sbuf_tensor("qstage", [128, 8 * TT], BF16))
    kslab = s1.enter_context(nc.sbuf_tensor("kslab", [128, 8 * TT], BF16))
    vslab = s1.enter_context(nc.sbuf_tensor("vslab", [128, 16 * 1024], BF16))
    sqs = s1.enter_context(nc.sbuf_tensor("sqs", [128, 4 * TT], BF16))
    vrow = s1.enter_context(nc.sbuf_tensor("vrow", [1, 2 * C], F32))
    rowb = s1.enter_context(nc.sbuf_tensor("rowb", [1, 4096], F32))

    @blk.sync
    def _(s):
        for dst, src in [(idb, ident_b_d), (idf, ident_f_d), (trim, tri_d),
                         (onecb, onescol_b_d), (onerf, onesrow_f_d),
                         (onerb, onesrow_b_d)]:
            s.dma_start(out=dst[:], in_=src.ap()).then_inc(sC.h, 16)
        for dst, src in [(gamc, gam_qk), (betc, bet_qk), (schc, sched),
                         (b1c, b1_t), (gbr, gb_row)]:
            s.dma_start(out=dst[:], in_=src.ap()).then_inc(sC.h, 16)
    sC.n += 176
    LD_CONST = sC.n

    @blk.gpsimd
    def _(g):
        g.dma_start(out=aatb[:], in_=aaT.ap()).then_inc(sGA.h, 16)
        g.dma_start(out=bab[:], in_=ba_in.ap()).then_inc(sGA.h, 16)
        g.dma_start(out=gwb[:].rearrange("p (cc e) -> p cc e", cc=8),
                    in_=bass.AP(gwT, 0, [[8, 128], [1024, 8], [1, 8]])
                    ).then_inc(sGA.h, 16)
    sGA.n += 48
    GD_LORA = sGA.n

    wm_evs = []
    for cc in range(8):
        wt = (wtmp, wtmpB)[cc % 2]

        @blk.sync
        def _(s, cc=cc, wt=wt):
            if cc >= 2:
                s.wait_ge(sEV.h, wm_evs[cc - 2])
            s.dma_start(out=wt[:], in_=wattnT[cc * 128:(cc + 1) * 128, :]
                        ).then_inc(sW.h, 16)
        sW.n += 16
        ld_w = sW.n

        @blk.tensor
        def _(t, cc=cc, ld_w=ld_w):
            if cc == 0:
                t.wait_ge(sGA.h, GD_LORA)
            t.wait_ge(sW.h, ld_w)
            if cc >= 1:
                t.wait_ge(sEV.h, wm_evs[cc - 1])
            last = None
            for s3 in range(3):
                ps = (psA, psB, psC)[s3]
                for rb in range(2):
                    last = t.matmul(
                        ps[:, rb * 512:(rb + 1) * 512],
                        bab[:, s3 * 1024 + cc * 128: s3 * 1024 + (cc + 1) * 128],
                        aatb[:, rb * 512:(rb + 1) * 512],
                        start=True, stop=True, skip_group_check=True)
            last.then_inc(sMM.h, 1)
        sMM.n += 1
        mm_w = sMM.n

        @blk.vector
        def _(v, cc=cc, mm_w=mm_w, wt=wt):
            v.wait_ge(sMM.h, mm_w)
            last = None
            for s3 in range(3):
                ps = (psA, psB, psC)[s3]
                src_w = bass.AP(wt, s3, [[D3, 128], [3, 1024]])
                dst = bass.AP(wmT, cc * D3 + s3, [[8 * D3, 128], [3, 1024]])
                last = v.tensor_tensor(dst, src_w, ps[:, 0:1024], AL.add)
            last.then_inc(sEV.h, 1)
        sEV.n += 1
        wm_evs.append(sEV.n)
    MM_WM = sMM.n

    @blk.vector
    def _(v):
        v.memzero(stc[:])
        v.memzero(l_sb[:])
        v.memzero(dgap[:])
        v.memset(m_sb[:], -1.0e30).then_inc(sEV.h, 1)
    sEV.n += 1
    EV_INIT = sEV.n

    def emit_ln_tile(ti, src_dram, xt_, xb_, xsq_, abc_, xtmp_, ln_dst, dst_off,
                     prev, first_waits_sync, first_waits_mm, writeln_wait_mm,
                     rowb):
        t0 = ti * TT

        @blk.sync
        def _(s):
            for sem, val in first_waits_sync:
                s.wait_ge(sem.h, val)
            if prev is not None:
                s.wait_ge(sEV.h, prev["ev_ln"])
            s.dma_start(out=xt_[:].rearrange("p (c t) -> p c t", c=8),
                        in_=bass.AP(src_dram, t0, [[NT, 128], [128 * NT, 8],
                                                   [1, TT]])
                        ).then_inc(sX.h, 16)
        sX.n += 16
        ld_x = sX.n

        @blk.vector
        def _(v, ld_x=ld_x):
            v.wait_ge(sX.h, ld_x)
            if prev is not None:
                v.wait_ge(sMM.h, prev["mm_stats"])
            v.tensor_copy(xb_[:], xt_[:]).then_inc(sEV.h, 1)
        sEV.n += 1
        ev_xb = sEV.n

        @blk.scalar
        def _(sc, ev_xb=ev_xb):
            sc.wait_ge(sEV.h, ev_xb)
            sc.square(xsq_[:], xb_[:]).then_inc(sSC.h, 1)
        sSC.n += 1
        sc_sq = sSC.n

        @blk.tensor
        def _(t, sc_sq=sc_sq):
            t.wait_ge(sSC.h, sc_sq)
            for sem, val in first_waits_mm:
                t.wait_ge(sem.h, val)
            for cc in range(8):
                t.matmul(psD[0:1, 0:TT], onecb[:], xb_[:, cc * TT:(cc + 1) * TT],
                         start=(cc == 0), stop=(cc == 7), skip_group_check=True)
            last = None
            for cc in range(8):
                last = t.matmul(psD[0:1, 512:512 + TT], onecb[:],
                                xsq_[:, cc * TT:(cc + 1) * TT],
                                start=(cc == 0), stop=(cc == 7),
                                skip_group_check=True)
            last.then_inc(sMM.h, 1)
        sMM.n += 1
        mm_st = sMM.n

        MU, EX2, TMP, RR, NB = 0, TT, 2 * TT, 3 * TT, 2048 + 1024

        @blk.vector
        def _(v, mm_st=mm_st):
            v.wait_ge(sMM.h, mm_st)
            v.tensor_scalar_mul(rowb[0:1, MU:MU + TT], psD[0:1, 0:TT], 1.0 / C)
            v.tensor_scalar_mul(rowb[0:1, EX2:EX2 + TT], psD[0:1, 512:512 + TT],
                                1.0 / C).then_inc(sEV.h, 1)
        sEV.n += 1
        ev_mu = sEV.n

        @blk.scalar
        def _(sc, ev_mu=ev_mu):
            sc.wait_ge(sEV.h, ev_mu)
            sc.square(rowb[0:1, TMP:TMP + TT], rowb[0:1, MU:MU + TT]
                      ).then_inc(sSC.h, 1)
        sSC.n += 1
        sc_mu2 = sSC.n

        @blk.vector
        def _(v, sc_mu2=sc_mu2):
            v.wait_ge(sSC.h, sc_mu2)
            v.tensor_sub(rowb[0:1, TMP:TMP + TT], rowb[0:1, EX2:EX2 + TT],
                         rowb[0:1, TMP:TMP + TT])
            v.tensor_scalar_add(rowb[0:1, TMP:TMP + TT], rowb[0:1, TMP:TMP + TT],
                                EPS).then_inc(sEV.h, 1)
        sEV.n += 1
        ev_var = sEV.n

        @blk.scalar
        def _(sc, ev_var=ev_var):
            sc.wait_ge(sEV.h, ev_var)
            sc.sqrt(rowb[0:1, TMP:TMP + TT], rowb[0:1, TMP:TMP + TT]
                    ).then_inc(sSC.h, 1)
        sSC.n += 1
        sc_sd = sSC.n

        @blk.vector
        def _(v, sc_sd=sc_sd):
            v.wait_ge(sSC.h, sc_sd)
            v.reciprocal(rowb[0:1, RR:RR + TT], rowb[0:1, TMP:TMP + TT])
            v.tensor_mul(rowb[0:1, NB:NB + TT], rowb[0:1, MU:MU + TT],
                         rowb[0:1, RR:RR + TT])
            v.tensor_scalar_mul(rowb[0:1, NB:NB + TT], rowb[0:1, NB:NB + TT],
                                -1.0).then_inc(sEV.h, 1)
        sEV.n += 1
        ev_r = sEV.n

        @blk.tensor
        def _(t, ev_r=ev_r):
            t.wait_ge(sEV.h, ev_r)
            t.matmul(psD[:, 0:TT], onerf[:], rowb[0:1, RR:RR + TT],
                     start=True, stop=True, skip_group_check=True)
            t.matmul(psD[:, 512:512 + TT], onerf[:], rowb[0:1, NB:NB + TT],
                     start=True, stop=True, skip_group_check=True
                     ).then_inc(sMM.h, 1)
        sMM.n += 1
        mm_bc = sMM.n

        @blk.vector
        def _(v, mm_bc=mm_bc, writeln_wait_mm=writeln_wait_mm):
            v.wait_ge(sMM.h, mm_bc)
            if writeln_wait_mm is not None:
                v.wait_ge(sMM.h, writeln_wait_mm)
            v.tensor_copy(abc_[:, 0:TT], psD[:, 0:TT])
            v.tensor_copy(abc_[:, TT:2 * TT], psD[:, 512:512 + TT])
            last = None
            for cc in range(8):
                v.tensor_mul(xtmp_[:], xt_[:, cc * TT:(cc + 1) * TT], abc_[:, 0:TT])
                o0, o1 = dst_off(cc, ti)
                last = v.tensor_add(ln_dst[:, o0:o1], xtmp_[:], abc_[:, TT:2 * TT])
            last.then_inc(sEV.h, 1)
        sEV.n += 1
        return {"ev_ln": sEV.n, "mm_stats": mm_st}

    prev_ln = None
    qkv_evs = []
    qkv_scs = []
    kst_lds = []
    vst_lds = []
    last_vproj_mm = None
    last_vev_sc = None
    for ti in range(NTILE):
        t0 = ti * TT
        fw_sync = [(sC, LD_CONST)] if ti == 0 else []
        fw_mm = [(sMM, MM_WM)] if ti == 0 else []
        prev_ln = emit_ln_tile(
            ti, x_fm, xt, xb, xsq, abc0, xtmp, ln0,
            lambda cc, ti_: (cc * TT, (cc + 1) * TT),
            prev_ln, fw_sync, fw_mm,
            last_vproj_mm, rowb)
        ev_ln = prev_ln["ev_ln"]

        for half in range(4):
            dcs = [half * 4 + i for i in range(4)]

            @blk.tensor
            def _(t, dcs=dcs, ev_ln=ev_ln):
                t.wait_ge(sEV.h, ev_ln)
                if qkv_scs:
                    t.wait_ge(sSC.h, qkv_scs[-1])
                if qkv_evs:
                    t.wait_ge(sEV.h, qkv_evs[-1])
                last = None
                for i, dc in enumerate(dcs):
                    ps = (psA, psB)[i // 2]
                    off = (i % 2) * 512
                    for cc in range(8):
                        last = t.matmul(
                            ps[:, off:off + TT],
                            wmT[:, cc * D3 + dc * 128:cc * D3 + (dc + 1) * 128],
                            ln0[:, cc * TT:(cc + 1) * TT],
                            start=(cc == 0), stop=(cc == 7), skip_group_check=True)
                last.then_inc(sMM.h, 1)
            sMM.n += 1
            mm_qk = sMM.n

            @blk.scalar
            def _(sc, dcs=dcs, mm_qk=mm_qk, ti=ti, t0=t0):
                sc.wait_ge(sMM.h, mm_qk)
                if dcs[0] >= 8 and kst_lds:
                    sc.wait_ge(sKVk.h, kst_lds[-1])
                if dcs[0] < 8 and kst_lds:
                    sc.wait_ge(sQW.h, 16 * len(kst_lds))
                for i, dc in enumerate(dcs):
                    ps = (psA, psB)[i // 2]
                    off = (i % 2) * 512
                    if dc < 8:
                        dst = qstage[:, dc * TT:(dc + 1) * TT]
                    else:
                        dst = kslab[:, (dc - 8) * TT:(dc - 7) * TT]
                    sc.activation(dst, ps[:, off:off + TT], AF.Copy,
                                  bias=0.0, scale=1.0)
                    sc.square(sqs[:, i * TT:(i + 1) * TT], ps[:, off:off + TT]
                              ).then_inc(sSC.h, 1)
            sSC.n += 4
            sc_e = sSC.n

            @blk.vector
            def _(v, dcs=dcs, sc_e=sc_e, mm_qk=mm_qk, ti=ti, t0=t0):
                last = None
                for i, dc in enumerate(dcs):
                    v.wait_ge(sSC.h, sc_e - 3 + i)
                    if dc < 8:
                        srcv = qstage[:, dc * TT:(dc + 1) * TT]
                    else:
                        srcv = kslab[:, (dc - 8) * TT:(dc - 7) * TT]
                    v.reduce_sum(tcol[:, 0:1], srcv, AX)
                    v.reduce_sum(tcol[:, 1:2], sqs[:, i * TT:(i + 1) * TT], AX)
                    v.tensor_add(stc[:, dc:dc + 1], stc[:, dc:dc + 1], tcol[:, 0:1])
                    last = v.tensor_add(stc[:, 16 + dc:17 + dc],
                                        stc[:, 16 + dc:17 + dc], tcol[:, 1:2])
                last.then_inc(sEV.h, 1)
            sEV.n += 1
            qkv_evs.append(sEV.n)
            qkv_scs.append(sc_e)

        @blk.sync
        def _(s, ti=ti, t0=t0):
            s.wait_ge(sSC.h, qkv_scs[-1])
            s.dma_start(out=bass.AP(kv_in, (ti // 2) * HBLK + (ti % 2) * 512,
                                    [[8192, 128], [1024, 8], [1, 512]]),
                        in_=kslab[:].rearrange("p (c t) -> p c t", c=8)
                        ).then_inc(sKVk.h, 16)
            s.dma_start(out=bass.AP(q_dram, (ti // 2) * HBLK + (ti % 2) * 512,
                                    [[8192, 128], [1024, 8], [1, 512]]),
                        in_=qstage[:].rearrange("p (c t) -> p c t", c=8)
                        ).then_inc(sQW.h, 16)
        sKVk.n += 16
        kst_lds.append(sKVk.n)
        sQW.n += 16

        vev_scs = []
        for tc in range(4):
            @blk.tensor
            def _(t, tc=tc, ev_ln=ev_ln):
                if tc == 0:
                    t.wait_ge(sEV.h, ev_ln)
                    if last_vev_sc is not None:
                        t.wait_ge(sSC.h, last_vev_sc)
                else:
                    t.wait_ge(sSC.h, vev_scs[-1])
                last = None
                for cc in range(8):
                    t.matmul(psC[:, 0:512],
                             ln0[:, cc * TT + tc * 128:cc * TT + (tc + 1) * 128],
                             wmT[:, cc * D3 + 2048:cc * D3 + 2560],
                             start=(cc == 0), stop=(cc == 7), skip_group_check=True)
                for cc in range(8):
                    last = t.matmul(
                        psC[:, 512:1024],
                        ln0[:, cc * TT + tc * 128:cc * TT + (tc + 1) * 128],
                        wmT[:, cc * D3 + 2560:cc * D3 + 3072],
                        start=(cc == 0), stop=(cc == 7), skip_group_check=True)
                last.then_inc(sMM.h, 1)
            sMM.n += 1
            mm_v = sMM.n
            last_vproj_mm = mm_v

            @blk.scalar
            def _(sc, tc=tc, mm_v=mm_v, ti=ti):
                sc.wait_ge(sMM.h, mm_v)
                sc.activation(vslab[:, (ti * 4 + tc) * 1024:(ti * 4 + tc + 1) * 1024],
                              psC[:, 0:1024],
                              AF.Copy, bias=0.0, scale=1.0).then_inc(sSC.h, 1)
            sSC.n += 1
            vev_scs.append(sSC.n)
        last_vev_sc = vev_scs[-1]

        @blk.sync
        def _(s, ti=ti, t0=t0, w=vev_scs[-1]):
            s.wait_ge(sSC.h, w)
            s.dma_start(out=bass.AP(kv_in, VB + (ti // 2) * HBLK + (ti % 2) * 4096,
                                    [[8192, 128], [1024, 4], [1, 1024]]),
                        in_=vslab[:, ti * 4096:(ti + 1) * 4096
                                  ].rearrange("p (tc v) -> p tc v", tc=4)
                        ).then_inc(sKVv.h, 16)
        sKVv.n += 16
        vst_lds.append(sKVv.n)
    KVK_ALL = sKVk.n
    KVV_ALL = sKVv.n

    vst_mms = []
    LAST_LN_MM = prev_ln["mm_stats"]
    for ch in range(16):
        @blk.scalar
        def _(sc, ch=ch):
            if ch == 0:
                sc.wait_ge(sMM.h, LAST_LN_MM)
            sc.square(xsq[:, 0:1024], vslab[:, ch * 1024:(ch + 1) * 1024]
                      ).then_inc(sSC.h, 1)
        sSC.n += 1
        sc_v2 = sSC.n

        @blk.tensor
        def _(t, ch=ch, sc_v2=sc_v2):
            t.wait_ge(sSC.h, sc_v2)
            last = None
            for hh in range(2):
                t.matmul(psD[0:1, hh * 512:(hh + 1) * 512], onecb[:],
                         vslab[:, ch * 1024 + hh * 512:ch * 1024 + (hh + 1) * 512],
                         start=(ch == 0), stop=(ch == 15), skip_group_check=True)
                last = t.matmul(psC[0:1, hh * 512:(hh + 1) * 512], onecb[:],
                                xsq[:, hh * 512:(hh + 1) * 512],
                                start=(ch == 0), stop=(ch == 15),
                                skip_group_check=True)
            last.then_inc(sMM.h, 1)
        sMM.n += 1
        vst_mms.append(sMM.n)
    MM_VST = sMM.n

    @blk.vector
    def _(v):
        v.wait_ge(sMM.h, MM_VST)
        v.tensor_copy(vrow[0:1, 0:C], psD[0:1, 0:C])
        v.tensor_copy(vrow[0:1, C:2 * C], psC[0:1, 0:C]).then_inc(sEV.h, 1)
    sEV.n += 1
    EV_VROW = sEV.n

    @blk.sync
    def _(s):
        s.wait_ge(sEV.h, EV_VROW)
        s.dma_start(out=bass.AP(st_in, 0, [[32, 128], [1, 32]]), in_=stc[:]
                    ).then_inc(sSTT.h, 16)
        s.dma_start(out=st_in[0:1, 4096:6144], in_=vrow[0:1, :]).then_inc(sSTT.h, 16)
    sSTT.n += 32
    LD_STW = sSTT.n

    @blk.gpsimd
    def _(g):
        g.wait_ge(sSTT.h, LD_STW)
        g.collective_compute("AllReduce", AL.add, replica_groups=[list(range(NCORE))],
                             ins=[st_in.ap().opt()], outs=[st_out.ap().opt()]
                             ).then_inc(sCC.h, 1)
        g.wait_ge(sKVk.h, KVK_ALL)
        g.wait_ge(sKVv.h, KVV_ALL)
        g.collective_compute("AllGather", AL.bypass,
                             replica_groups=[list(range(NCORE))],
                             ins=[kv_in.ap().opt()], outs=[kv_all.ap().opt()]
                             ).then_inc(sCC.h, 1)
    sCC.n += 2

    @blk.sync
    def _(s):
        s.wait_ge(sCC.h, 1)
        s.dma_start(out=stc[:], in_=bass.AP(st_out, 0, [[32, 128], [1, 32]])
                    ).then_inc(sSR.h, 16)
        s.dma_start(out=vrow[0:1, :], in_=st_out[0:1, 4096:6144]).then_inc(sSR.h, 16)
    sSR.n += 32
    GD_STR = sSR.n

    @blk.sync
    def _(s):
        s.wait_ge(sCC.h, 1)
        s.dma_start(out=rowb[0:1, 2048:3072], in_=gam_vr.ap()).then_inc(sGV.h, 16)
        s.dma_start(out=rowb[0:1, 3072:4096], in_=bet_vr.ap()).then_inc(sGV.h, 16)
    sGV.n += 32
    ld_gv = sGV.n

    @blk.vector
    def _(v):
        v.wait_ge(sSR.h, GD_STR)
        v.tensor_scalar_mul(stc[:, 0:16], stc[:, 0:16], 1.0 / NTOK)
        v.tensor_scalar_mul(stc[:, 16:32], stc[:, 16:32], 1.0 / NTOK)
        v.tensor_scalar_mul(vrow[0:1, 0:C], vrow[0:1, 0:C], 1.0 / NTOK)
        v.tensor_mul(sm[:, 0:16], stc[:, 0:16], stc[:, 0:16])
        v.tensor_scalar_mul(vrow[0:1, C:2 * C], vrow[0:1, C:2 * C], 1.0 / NTOK)
        v.tensor_sub(abqk[:, 0:16], stc[:, 16:32], sm[:, 0:16])
        gap(v)
        v.tensor_scalar_add(abqk[:, 0:16], abqk[:, 0:16], EPS)
        gap(v).then_inc(sEV.h, 1)
    sEV.n += 1
    ev1 = sEV.n

    @blk.scalar
    def _(sc, ev1=ev1):
        sc.wait_ge(sEV.h, ev1)
        sc.sqrt(abqk[:, 0:16], abqk[:, 0:16])
        sc.square(rowb[0:1, 0:C], vrow[0:1, 0:C]).then_inc(sSC.h, 1)
    sSC.n += 1
    sc1 = sSC.n

    @blk.vector
    def _(v, sc1=sc1):
        v.wait_ge(sSC.h, sc1)
        v.reciprocal(abqk[:, 0:16], abqk[:, 0:16])
        v.tensor_sub(rowb[0:1, 0:C], vrow[0:1, C:2 * C], rowb[0:1, 0:C])
        v.tensor_mul(abqk[:, 0:16], abqk[:, 0:16], gamc[:])
        v.tensor_scalar_add(rowb[0:1, 0:C], rowb[0:1, 0:C], EPS)
        v.tensor_mul(abqk[:, 16:32], stc[:, 0:16], abqk[:, 0:16])
        gap(v)
        v.tensor_sub(abqk[:, 16:32], betc[:], abqk[:, 16:32])
        gap(v).then_inc(sEV.h, 1)
    sEV.n += 1
    ev2 = sEV.n

    @blk.scalar
    def _(sc, ev2=ev2):
        sc.wait_ge(sEV.h, ev2)
        sc.sqrt(rowb[0:1, 0:C], rowb[0:1, 0:C]).then_inc(sSC.h, 1)
    sSC.n += 1
    sc2 = sSC.n

    @blk.vector
    def _(v, sc2=sc2, ld_gv=ld_gv):
        v.wait_ge(sSC.h, sc2)
        v.wait_ge(sGV.h, ld_gv)
        v.reciprocal(rowb[0:1, 0:C], rowb[0:1, 0:C])
        v.tensor_mul(rowb[0:1, 0:C], rowb[0:1, 0:C], rowb[0:1, 2048:3072])
        v.tensor_mul(rowb[0:1, 1024:2048], vrow[0:1, 0:C], rowb[0:1, 0:C])
        v.tensor_sub(rowb[0:1, 1024:2048], rowb[0:1, 3072:4096],
                     rowb[0:1, 1024:2048])
        v.tensor_mul(qab[:, 0:8], abqk[:, 0:8], abqk[:, 8:16])
        v.tensor_mul(qab[:, 8:16], abqk[:, 16:24], abqk[:, 8:16]
                     ).then_inc(sEV.h, 1)
    sEV.n += 1
    EV_STATS = sEV.n

    @blk.tensor
    def _(t):
        t.wait_ge(sEV.h, EV_STATS)
        last = None
        for vcc in range(8):
            t.matmul(psD[:, vcc:vcc + 1], rowb[0:1, vcc * 128:(vcc + 1) * 128],
                     onerf[0:1, 0:1], start=True, stop=True, skip_group_check=True)
            last = t.matmul(psD[:, 8 + vcc:9 + vcc],
                            rowb[0:1, 1024 + vcc * 128:1024 + (vcc + 1) * 128],
                            onerf[0:1, 0:1], start=True, stop=True,
                            skip_group_check=True)
        last = t.matmul(psD[:, 16:24], onerf[:], gbr[0:1, 0:8],
                        start=True, stop=True, skip_group_check=True)
        last.then_inc(sMM.h, 1)
    sMM.n += 1
    mm_col = sMM.n

    @blk.vector
    def _(v, mm_col=mm_col):
        v.wait_ge(sMM.h, mm_col)
        v.tensor_copy(abv[:], psD[:, 0:16])
        v.tensor_copy(gbb[:], psD[:, 16:24]).then_inc(sEV.h, 1)
    sEV.n += 1
    EV_QAFF = sEV.n

    LD_QW = sQW.n

    s1.close()

    s2 = ExitStack()
    y_sb = s2.enter_context(nc.sbuf_tensor("y_sb", [128, 16 * 1024], F32))
    p_sb0 = s2.enter_context(nc.sbuf_tensor("p_sb0", [128, 1024], BF16))
    p_sb1 = s2.enter_context(nc.sbuf_tensor("p_sb1", [128, 1024], BF16))
    pT0 = s2.enter_context(nc.sbuf_tensor("pT0", [128, 1024], BF16))
    pT1 = s2.enter_context(nc.sbuf_tensor("pT1", [128, 1024], BF16))
    pvsb0 = s2.enter_context(nc.sbuf_tensor("pvsb0", [128, 1024], F32))
    pvsb1 = s2.enter_context(nc.sbuf_tensor("pvsb1", [128, 1024], F32))
    y_st0 = s2.enter_context(nc.sbuf_tensor("y_st0", [128, 1024], F32))
    y_st1 = s2.enter_context(nc.sbuf_tensor("y_st1", [128, 1024], F32))
    fc0 = s2.enter_context(nc.sbuf_tensor("fc0", [128, 8], F32))
    fc1 = s2.enter_context(nc.sbuf_tensor("fc1", [128, 8], F32))

    s2a = ExitStack()
    qcur0 = s2a.enter_context(nc.sbuf_tensor("qcur0", [128, 8192], BF16))
    qcur1 = s2a.enter_context(nc.sbuf_tensor("qcur1", [128, 8192], BF16))
    kb0 = s2a.enter_context(nc.sbuf_tensor("kb0", [128, 8192], BF16))
    kb1 = s2a.enter_context(nc.sbuf_tensor("kb1", [128, 8192], BF16))
    vb0 = s2a.enter_context(nc.sbuf_tensor("vb0", [128, 8192], BF16))
    vb1 = s2a.enter_context(nc.sbuf_tensor("vb1", [128, 8192], BF16))

    @blk.vector
    def _(v):
        v.wait_ge(sEV.h, EV_QAFF)
        v.memzero(y_sb[:]).then_inc(sEV.h, 1)
    sEV.n += 1
    EV_YZERO = sEV.n

    pairs = [("diagA",), ("diagB",)] + [("full", gi) for gi in range(7)]
    pair_done_mm = []
    pair_ld_gd = []
    qaff_evs = []
    sPL = (sPL0, sPL1)
    last_exp_sc = {}
    last_s2v_ev = {}
    vy_ev = {}
    last_pve_sc = [None]
    pend_st = [None]
    emit_tail_fn = [None]

    for ip, pdesc in enumerate(pairs):
        par = ip % 2
        qcur = (qcur0, qcur1)[par]
        kb = (kb0, kb1)[par]
        vb = (vb0, vb1)[par]
        is_diag = pdesc[0] in ("diagA", "diagB")
        sbase = {"diagA": 0, "diagB": 8}.get(pdesc[0])
        sLD_p = sPL[ip % 2] if is_diag else None

        if is_diag:
            h_d = 0 if pdesc[0] == "diagA" else 1

            @blk.sync
            def _(s, ip=ip, qcur=qcur, vb=vb, h_d=h_d):
                if ip >= 2:
                    s.wait_ge(sMM.h, pair_done_mm[ip - 2])
                s.wait_ge(sMM.h, mm_col)
                s.wait_ge(sQW.h, LD_QW)
                s.dma_start(out=qcur[:],
                            in_=bass.AP(q_dram, h_d * HBLK,
                                        [[8192, 128], [1, 8192]])
                            ).then_inc(sLD_p.h, 16)
                s.wait_ge(sKVv.h, KVV_ALL)
                s.dma_start(out=vb[:],
                            in_=bass.AP(kv_in, VB + h_d * HBLK,
                                        [[8192, 128], [1, 8192]])
                            ).then_inc(sLD_p.h, 16)

            @blk.scalar
            def _(sc, ip=ip, kb=kb, h_d=h_d):
                if ip >= 2:
                    sc.wait_ge(sMM.h, pair_done_mm[ip - 2])
                sc.wait_ge(sKVk.h, KVK_ALL)
                sc.wait_ge(sMM.h, mm_col)
                sc.wait_ge(sQW.h, LD_QW)
                sc.dma_start(out=kb[:],
                             in_=bass.AP(kv_in, h_d * HBLK,
                                         [[8192, 128], [1, 8192]])
                             ).then_inc(sLD_p.h, 16)
        else:
            gi = pdesc[1]
            base = gi * 20

            @blk.gpsimd
            def _(g, ip=ip, qcur=qcur, base=base):
                if ip >= 2:
                    g.wait_ge(sMM.h, pair_done_mm[ip - 2])
                g.wait_ge(sCC.h, 2)
                g.wait_ge(sMM.h, mm_col)
                g.wait_ge(sQW.h, LD_QW)
                g.reg_load(rg3, schc[0:1, base + 2:base + 3])
                qoff = g.snap(rg3, min_val=0, max_val=HBLK)
                g.dma_start(out=qcur[:],
                            in_=bass.AP(q_dram, qoff, [[8192, 128], [1, 8192]])
                            ).then_inc(sQF.h, 16)
            sQF.n += 16

            @blk.sync
            def _(s, ip=ip, kb=kb, base=base):
                if ip >= 2:
                    s.wait_ge(sMM.h, pair_done_mm[ip - 2])
                s.wait_ge(sCC.h, 2)
                s.wait_ge(sMM.h, mm_col)
                s.wait_ge(sQW.h, LD_QW)
                s.reg_load(rs1, schc[0:1, base:base + 1])
                koff = s.snap(rs1, min_val=0, max_val=7 * SHARD + HBLK)
                s.dma_start(out=kb[:],
                            in_=bass.AP(kv_all, koff, [[8192, 128], [1, 8192]])
                            ).then_inc(sKF.h, 16)
            sKF.n += 16

            @blk.scalar
            def _(sc, ip=ip, vb=vb, base=base):
                if ip >= 2:
                    sc.wait_ge(sMM.h, pair_done_mm[ip - 2])
                sc.wait_ge(sCC.h, 2)
                sc.wait_ge(sMM.h, mm_col)
                sc.wait_ge(sQW.h, LD_QW)
                sc.reg_load(rc1, schc[0:1, base + 1:base + 2])
                voff = sc.snap(rc1, min_val=0, max_val=7 * SHARD + VB + HBLK)
                sc.dma_start(out=vb[:],
                             in_=bass.AP(kv_all, voff, [[8192, 128], [1, 8192]])
                             ).then_inc(sVF.h, 16)
            sVF.n += 16
        if is_diag:
            sLD_p.n += 48
            pair_ld_gd.append([(sLD_p, sLD_p.n)])
        else:
            pair_ld_gd.append([(sQF, sQF.n), (sKF, sKF.n), (sVF, sVF.n)])

        @blk.vector
        def _(v, ip=ip, qcur=qcur):
            for ld_sem, ld_val in pair_ld_gd[ip]:
                v.wait_ge(ld_sem.h, ld_val)
            v.wait_ge(sEV.h, EV_STATS)
            last = None
            for cc in range(8):
                v.tensor_scalar_mul(qcur[:, cc * 1024:(cc + 1) * 1024],
                                    qcur[:, cc * 1024:(cc + 1) * 1024],
                                    qab[:, cc:cc + 1])
                last = v.tensor_scalar_add(qcur[:, cc * 1024:(cc + 1) * 1024],
                                           qcur[:, cc * 1024:(cc + 1) * 1024],
                                           qab[:, 8 + cc:9 + cc])
            last.then_inc(sEV.h, 1)
        sEV.n += 1
        qaff_evs.append(sEV.n)

        def emit_head(qs):
            nkt = (qs + 1) if is_diag else 8
            W = nkt * 128
            psS = (psA, psB)[qs % 2]
            p_s = (p_sb0, p_sb1)[qs % 2]
            fc = (fc0, fc1)[qs % 2]
            y_stq = (y_st0, y_st1)[qs % 2]

            @blk.tensor
            def _(t, ip=ip, qs=qs, W=W, psS=psS, qcur=qcur, kb=kb):
                if qs == 0:
                    for ld_sem, ld_val in pair_ld_gd[ip]:
                        t.wait_ge(ld_sem.h, ld_val)
                    t.wait_ge(sEV.h, qaff_evs[ip])
                    if ip == 0:
                        t.wait_ge(sEV.h, EV_YZERO)
                key = qs % 2
                if key in last_exp_sc:
                    t.wait_ge(sSC.h, last_exp_sc[key])
                nhal = (W + 511) // 512
                last = None
                for cc in range(8):
                    for h in range(nhal):
                        w0 = h * 512
                        w1 = min(W, w0 + 512)
                        last = t.matmul(
                            psS[:, w0:w1],
                            qcur[:, cc * 1024 + qs * 128:cc * 1024 + (qs + 1) * 128],
                            kb[:, cc * 1024 + w0:cc * 1024 + w1],
                            start=(cc == 0), stop=(cc == 7), skip_group_check=True)
                last.then_inc(sMM.h, 1)
            sMM.n += 1
            mm_s = sMM.n

            @blk.vector
            def _(v, qs=qs, W=W, psS=psS, fc=fc, y_stq=y_stq, mm_s=mm_s):
                v.wait_ge(sMM.h, mm_s)
                if is_diag:
                    v.tensor_add(psS[:, qs * 128:(qs + 1) * 128],
                                 psS[:, qs * 128:(qs + 1) * 128], trim[:])
                    sl = sbase + qs
                    v.tensor_copy(fc[:, 0:1], m_sb[:, sl:sl + 1])
                    v.reduce_max(fc[:, 1:2], psS[:, 0:W], AX)
                    v.tensor_copy(fc[:, 5:6], l_sb[:, sl:sl + 1])
                    v.tensor_copy(y_stq[:], y_sb[:, sl * 1024:(sl + 1) * 1024])
                else:
                    base = pdesc[1] * 20
                    v.reduce_max(fc[:, 1:2], psS[:, 0:W], AX)
                    v.reg_load(rv1, schc[0:1, base + 12 + qs:base + 13 + qs])
                    mls = v.snap(rv1, min_val=0, max_val=15)
                    v.tensor_copy(fc[:, 0:1], m_sb[:, bass.ds(mls, 1)])
                    v.tensor_copy(fc[:, 5:6], l_sb[:, bass.ds(mls, 1)])
                    v.reg_load(rv1, schc[0:1, base + 4 + qs:base + 5 + qs])
                    ysl = v.snap(rv1, min_val=0, max_val=15 * 1024)
                    v.tensor_copy(y_stq[:], y_sb[:, bass.ds(ysl, 1024)])
                v.tensor_scalar_mul(fc[:, 1:2], fc[:, 1:2], SCALE)
                gap(v)
                v.tensor_max(fc[:, 2:3], fc[:, 0:1], fc[:, 1:2])
                gap(v)
                v.tensor_scalar_mul(fc[:, 3:4], fc[:, 2:3], -1.0)
                v.tensor_sub(fc[:, 7:8], fc[:, 0:1], fc[:, 2:3])
                gap(v).then_inc(sEV.h, 1)
            sEV.n += 1
            ev_mn = sEV.n

            @blk.scalar
            def _(sc, qs=qs, W=W, psS=psS, p_s=p_s, fc=fc, ev_mn=ev_mn):
                sc.wait_ge(sEV.h, ev_mn)
                key = qs % 2
                if key in last_s2v_ev:
                    sc.wait_ge(sEV.h, last_s2v_ev[key])
                if key in vy_ev:
                    sc.wait_ge(sEV.h, vy_ev[key])
                sc.activation(fc[:, 4:5], fc[:, 7:8], AF.Exp, bias=0.0, scale=1.0)
                sc.activation(p_s[:, 0:W], psS[:, 0:W], AF.Exp,
                              bias=fc[:, 3:4], scale=SCALE).then_inc(sSC.h, 1)
            sSC.n += 1
            last_exp_sc[qs % 2] = sSC.n
            sc_p = sSC.n
            return dict(qs=qs, nkt=nkt, W=W, p_s=p_s, fc=fc, y_stq=y_stq,
                        sc_p=sc_p, vb=vb, is_diag=is_diag, sbase=sbase,
                        pdesc=pdesc)

        def emit_tail(st):
            qs = st["qs"]; nkt = st["nkt"]; W = st["W"]
            p_s = st["p_s"]; fc = st["fc"]; y_stq = st["y_stq"]
            sc_p = st["sc_p"]
            vb = st["vb"]; is_diag = st["is_diag"]
            sbase = st["sbase"]; pdesc = st["pdesc"]
            par = qs % 2
            pT_s = (pT0, pT1)[par]
            pvsbq = (pvsb0, pvsb1)[par]
            doff = par * 1024

            @blk.tensor
            def _(t, qs=qs, nkt=nkt, p_s=p_s, sc_p=sc_p, doff=doff):
                t.wait_ge(sSC.h, sc_p)
                key = qs % 2
                if key in last_s2v_ev:
                    t.wait_ge(sEV.h, last_s2v_ev[key])
                psD_bf = psD[:].bitcast(BF16)
                last = None
                for tc in range(nkt):
                    last = t.transpose(psD_bf[:, doff + tc * 128:doff + (tc + 1) * 128],
                                       p_s[:, tc * 128:(tc + 1) * 128], idb[:])
                last.then_inc(sMM.h, 1)
            sMM.n += 1
            mm_t = sMM.n

            @blk.vector
            def _(v, qs=qs, W=W, pT_s=pT_s, mm_t=mm_t, doff=doff):
                v.wait_ge(sMM.h, mm_t)
                v.tensor_copy(pT_s[:, 0:W],
                              psD[:].bitcast(BF16)[:, doff:doff + W]
                              ).then_inc(sEV.h, 1)
            sEV.n += 1
            ev_pte = sEV.n

            @blk.vector
            def _(v, qs=qs, W=W, p_s=p_s, fc=fc, is_diag=is_diag, sbase=sbase,
                  pdesc=pdesc):
                v.reduce_sum(fc[:, 6:7], p_s[:, 0:W], AX)
                if is_diag:
                    sl = sbase + qs
                    v.tensor_copy(m_sb[:, sl:sl + 1], fc[:, 2:3])
                    v.scalar_tensor_tensor(fc[:, 5:6], fc[:, 5:6], fc[:, 4:5],
                                           fc[:, 6:7], AL.mult, AL.add)
                    gap(v)
                    v.tensor_copy(l_sb[:, sl:sl + 1], fc[:, 5:6])
                else:
                    base = pdesc[1] * 20
                    v.reg_load(rv1, schc[0:1, base + 12 + qs:base + 13 + qs])
                    mls = v.snap(rv1, min_val=0, max_val=15)
                    v.tensor_copy(m_sb[:, bass.ds(mls, 1)], fc[:, 2:3])
                    v.scalar_tensor_tensor(fc[:, 5:6], fc[:, 5:6], fc[:, 4:5],
                                           fc[:, 6:7], AL.mult, AL.add)
                    gap(v)
                    v.tensor_copy(l_sb[:, bass.ds(mls, 1)], fc[:, 5:6])
                gap(v).then_inc(sEV.h, 1)
            sEV.n += 1
            last_s2v_ev[par] = sEV.n

            @blk.tensor
            def _(t, qs=qs, nkt=nkt, pT_s=pT_s, vb=vb, ev_pte=ev_pte):
                t.wait_ge(sEV.h, ev_pte)
                if last_pve_sc[0] is not None:
                    t.wait_ge(sSC.h, last_pve_sc[0])
                last = None
                for tc in range(nkt):
                    t.matmul(psC[:, 0:512], pT_s[:, tc * 128:(tc + 1) * 128],
                             vb[:, tc * 1024:tc * 1024 + 512],
                             start=(tc == 0), stop=(tc == nkt - 1),
                             skip_group_check=True)
                    last = t.matmul(psC[:, 512:1024],
                                    pT_s[:, tc * 128:(tc + 1) * 128],
                                    vb[:, tc * 1024 + 512:(tc + 1) * 1024],
                                    start=(tc == 0), stop=(tc == nkt - 1),
                                    skip_group_check=True)
                last.then_inc(sMM.h, 1)
            sMM.n += 1
            mm_pv = sMM.n

            @blk.scalar
            def _(sc, mm_pv=mm_pv, pvsbq=pvsbq):
                sc.wait_ge(sMM.h, mm_pv)
                sc.activation(pvsbq[:], psC[:, 0:1024], AF.Copy,
                              bias=0.0, scale=1.0).then_inc(sSC.h, 1)
            sSC.n += 1
            last_pve_sc[0] = sSC.n
            sc_pve = sSC.n

            @blk.vector
            def _(v, fc=fc, qs=qs, sc_pve=sc_pve, pvsbq=pvsbq, y_stq=y_stq):
                v.wait_ge(sSC.h, sc_pve)
                v.scalar_tensor_tensor(y_stq[:], y_stq[:], fc[:, 4:5], pvsbq[:],
                                       AL.mult, AL.add)
                if is_diag:
                    sl = sbase + qs
                    v.tensor_copy(y_sb[:, sl * 1024:(sl + 1) * 1024], y_stq[:]
                                  ).then_inc(sEV.h, 1)
                else:
                    base = pdesc[1] * 20
                    v.reg_load(rv1, schc[0:1, base + 4 + qs:base + 5 + qs])
                    ysl = v.snap(rv1, min_val=0, max_val=15 * 1024)
                    v.tensor_copy(y_sb[:, bass.ds(ysl, 1024)], y_stq[:]
                                  ).then_inc(sEV.h, 1)
            sEV.n += 1
            vy_ev[par] = sEV.n
            if qs == 7:
                pair_done_mm.append(sMM.n)

        emit_tail_fn[0] = emit_tail
        for qs in range(8):
            cur = emit_head(qs)
            if pend_st[0] is not None:
                emit_tail(pend_st[0])
            pend_st[0] = cur
    emit_tail_fn[0](pend_st[0])
    EV_ATT = sEV.n
    MM_ATT = sMM.n

    s2a.close()

    s2b = ExitStack()
    y_fm = s2b.enter_context(nc.sbuf_tensor("y_fm", [128, 8 * NT], BF16))
    y1 = s2b.enter_context(nc.sbuf_tensor("y1", [128, 8 * NT], BF16))
    wpTs = s2b.enter_context(nc.sbuf_tensor("wpTs", [128, 8 * 1024], BF16))
    wmpTs = s2b.enter_context(nc.sbuf_tensor("wmpTs", [128, 8 * 1024], BF16))
    wptmp = s2b.enter_context(nc.sbuf_tensor("wptmp", [128, 1024], F32))
    wptmpB = s2b.enter_context(nc.sbuf_tensor("wptmpB", [128, 1024], F32))
    apTb = s2b.enter_context(nc.sbuf_tensor("apTb", [64, C], BF16))
    bpb = s2b.enter_context(nc.sbuf_tensor("bpb", [64, C], BF16))
    xtl0 = s2b.enter_context(nc.sbuf_tensor("xtl0", [128, TT], F32))
    xtl1 = s2b.enter_context(nc.sbuf_tensor("xtl1", [128, TT], F32))


    @blk.gpsimd
    def _(g):
        g.wait_ge(sMM.h, MM_ATT)
        g.dma_start(out=apTb[:], in_=apT.ap()).then_inc(sGA.h, 16)
        g.dma_start(out=bpb[:], in_=bp_in.ap()).then_inc(sGA.h, 16)
    sGA.n += 32
    GD_PL = sGA.n

    wp_evs = []
    for vcc in range(8):
        wpt = (wptmp, wptmpB)[vcc % 2]

        @blk.sync
        def _(s, vcc=vcc, wpt=wpt):
            s.wait_ge(sMM.h, MM_ATT)
            if vcc >= 2:
                s.wait_ge(sEV.h, wp_evs[-2])
            s.dma_start(out=wpt[:], in_=wprojT[vcc * 128:(vcc + 1) * 128, :]
                        ).then_inc(sW.h, 16)
        sW.n += 16
        ld_wp = sW.n

        @blk.tensor
        def _(t, vcc=vcc, ld_wp=ld_wp, wpt=wpt):
            t.wait_ge(sW.h, ld_wp)
            last = None
            for dc in range(8):
                last = t.matmul(psD[:, 24 + dc:25 + dc],
                                wpt[:, dc * 128:(dc + 1) * 128],
                                abv[:, 8 + vcc:9 + vcc],
                                start=(vcc == 0), stop=(vcc == 7),
                                skip_group_check=True)
            last.then_inc(sMM.h, 1)
        sMM.n += 1
        mm_pb = sMM.n

        @blk.vector
        def _(v, vcc=vcc, mm_pb=mm_pb, wpt=wpt):
            v.wait_ge(sMM.h, mm_pb)
            last = v.tensor_scalar_mul(wpTs[:, vcc * 1024:(vcc + 1) * 1024],
                                       wpt[:], abv[:, vcc:vcc + 1])
            if vcc == 7:
                last = v.tensor_copy(pbc[:], psD[:, 24:32])
            last.then_inc(sEV.h, 1)
        sEV.n += 1
        wp_evs.append(sEV.n)

    wmp_evs = []
    for cc in range(8):
        wpt = (wptmp, wptmpB)[cc % 2]

        @blk.sync
        def _(s, cc=cc, wpt=wpt):
            if cc == 0:
                s.wait_ge(sEV.h, wp_evs[-2])
            elif cc == 1:
                s.wait_ge(sEV.h, wp_evs[-1])
            else:
                s.wait_ge(sEV.h, wmp_evs[-2])
            s.dma_start(out=wpt[:], in_=wprojT[cc * 128:(cc + 1) * 128, :]
                        ).then_inc(sW.h, 16)
        sW.n += 16
        ld_wp2 = sW.n

        @blk.tensor
        def _(t, cc=cc, ld_wp2=ld_wp2, wpt=wpt):
            t.wait_ge(sW.h, ld_wp2)
            if cc == 0:
                t.wait_ge(sGA.h, GD_PL)
            last = None
            for rb in range(2):
                last = t.matmul(psA[:, rb * 512:(rb + 1) * 512],
                                bpb[:, cc * 128:(cc + 1) * 128],
                                apTb[:, rb * 512:(rb + 1) * 512],
                                start=True, stop=True, skip_group_check=True)
            last.then_inc(sMM.h, 1)
        sMM.n += 1
        mm_mp = sMM.n

        @blk.vector
        def _(v, cc=cc, mm_mp=mm_mp, wpt=wpt):
            v.wait_ge(sMM.h, mm_mp)
            v.tensor_tensor(wmpTs[:, cc * 1024:(cc + 1) * 1024], wpt[:],
                            psA[:, 0:1024], AL.add).then_inc(sEV.h, 1)
        sEV.n += 1
        wmp_evs.append(sEV.n)
    EV_WPM = sEV.n

    fin_evs = []
    for sl in range(16):
        @blk.vector
        def _(v, sl=sl):
            if sl == 0:
                v.wait_ge(sEV.h, EV_ATT)
            v.reciprocal(fc0[:, 7:8], l_sb[:, sl:sl + 1])
            gap(v)
            v.tensor_scalar_mul(pT0[:], y_sb[:, sl * 1024:(sl + 1) * 1024],
                                fc0[:, 7:8]).then_inc(sEV.h, 1)
        sEV.n += 1
        ev_yt = sEV.n

        @blk.tensor
        def _(t, sl=sl, ev_yt=ev_yt):
            t.wait_ge(sEV.h, ev_yt)
            psB_bf = psB[:].bitcast(BF16)
            last = None
            for vcc in range(8):
                last = t.transpose(psB_bf[:, vcc * 128:(vcc + 1) * 128],
                                   pT0[:, vcc * 128:(vcc + 1) * 128], idb[:])
            last.then_inc(sMM.h, 1)
        sMM.n += 1
        mm_ytr = sMM.n

        @blk.vector
        def _(v, sl=sl, mm_ytr=mm_ytr):
            v.wait_ge(sMM.h, mm_ytr)
            v.tensor_copy(bass.AP(y_fm, sl * 128, [[8 * NT, 128], [NT, 8], [1, 128]]),
                          psB[:].bitcast(BF16)[:, 0:1024
                          ].rearrange("p (c t) -> p c t", c=8)
                          ).then_inc(sEV.h, 1)
        sEV.n += 1
        fin_evs.append(sEV.n)
    EV_FIN = sEV.n

    p1_scs = []
    for dc in range(8):
        for tt4 in range(4):
            ps = (psA, psB)[(dc * 4 + tt4) % 2]

            @blk.tensor
            def _(t, dc=dc, tt4=tt4, ps=ps):
                if dc == 0 and tt4 == 0:
                    t.wait_ge(sEV.h, EV_FIN)
                    t.wait_ge(sEV.h, EV_WPM)
                if len(p1_scs) >= 2:
                    t.wait_ge(sSC.h, p1_scs[-2])
                last = None
                for vcc in range(8):
                    last = t.matmul(
                        ps[:, 0:512],
                        wpTs[:, vcc * 1024 + dc * 128:vcc * 1024 + (dc + 1) * 128],
                        y_fm[:, vcc * NT + tt4 * 512:vcc * NT + (tt4 + 1) * 512],
                        start=(vcc == 0), stop=(vcc == 7), skip_group_check=True)
                last.then_inc(sMM.h, 1)
            sMM.n += 1
            mm_p1 = sMM.n

            @blk.scalar
            def _(sc, dc=dc, tt4=tt4, ps=ps, mm_p1=mm_p1):
                sc.wait_ge(sMM.h, mm_p1)
                sc.activation(y1[:, dc * NT + tt4 * 512:dc * NT + (tt4 + 1) * 512],
                              ps[:, 0:512], AF.Identity,
                              bias=pbc[:, dc:dc + 1], scale=1.0).then_inc(sSC.h, 1)
            sSC.n += 1
            p1_scs.append(sSC.n)
    SC_P1 = sSC.n

    p2_evs = []
    x2st_lds = []
    for dc2 in range(8):
        for tt4 in range(4):
            idx = dc2 * 4 + tt4
            ps = (psC, psD)[idx % 2]
            xtl = (xtl0, xtl1)[idx % 2]
            x2t = xtl

            sXT = (sXT0, sXT1)[idx % 2]

            @blk.sync
            def _(s, dc2=dc2, tt4=tt4, xtl=xtl, sXT=sXT):
                if len(x2st_lds) >= 2:
                    sX2p, sx2v = x2st_lds[-2]
                    s.wait_ge(((sX2a, sX2b)[sX2p]).h, sx2v)
                s.dma_start(out=xtl[:],
                            in_=bass.AP(x_fm, dc2 * 128 * NT + tt4 * 512,
                                        [[NT, 128], [1, 512]])
                            ).then_inc(sXT.h, 16)
            sXT.n += 16
            ld_xt = sXT.n

            @blk.tensor
            def _(t, dc2=dc2, tt4=tt4, ps=ps, idx=idx):
                if idx == 0:
                    t.wait_ge(sSC.h, SC_P1)
                if len(p2_evs) >= 2:
                    t.wait_ge(sEV.h, p2_evs[-2])
                last = None
                for cc in range(8):
                    last = t.matmul(
                        ps[:, 0:512],
                        wmpTs[:, cc * 1024 + dc2 * 128:cc * 1024 + (dc2 + 1) * 128],
                        y1[:, cc * NT + tt4 * 512:cc * NT + (tt4 + 1) * 512],
                        start=(cc == 0), stop=(cc == 7), skip_group_check=True)
                last.then_inc(sMM.h, 1)
            sMM.n += 1
            mm_p2 = sMM.n

            sX2 = (sX2a, sX2b)[idx % 2]

            @blk.vector
            def _(v, ps=ps, xtl=xtl, x2t=x2t, mm_p2=mm_p2, ld_xt=ld_xt,
                  sXT=sXT, sX2=sX2):
                v.wait_ge(sMM.h, mm_p2)
                v.wait_ge(sXT.h, ld_xt)
                v.tensor_add(x2t[:], ps[:, 0:512], xtl[:]).then_inc(sEV.h, 1)
            sEV.n += 1
            p2_evs.append(sEV.n)
            ev_x2 = sEV.n

            @blk.sync
            def _(s, dc2=dc2, tt4=tt4, x2t=x2t, ev_x2=ev_x2, sX2=sX2):
                s.wait_ge(sEV.h, ev_x2)
                s.dma_start(out=bass.AP(x2_sp, dc2 * 128 * NT + tt4 * 512,
                                        [[NT, 128], [1, 512]]),
                            in_=x2t[:]).then_inc(sX2.h, 16)
            sX2.n += 16
            x2st_lds.append((idx % 2, sX2.n))
    X2A_ALL = sX2a.n
    X2B_ALL = sX2b.n
    EV_P2 = sEV.n

    s2b.close()
    s2.close()

    s4 = ExitStack()
    ln1 = s4.enter_context(nc.sbuf_tensor("ln1", [128, 8 * NT], BF16))
    rowb4 = s4.enter_context(nc.sbuf_tensor("rowb4", [1, 4096], F32))
    wfm = s4.enter_context(nc.sbuf_tensor("wfm", [8, NT], BF16))
    s4b = ExitStack()
    xt4 = s4b.enter_context(nc.sbuf_tensor("xt4", [128, 8 * TT], F32))
    xb4 = s4b.enter_context(nc.sbuf_tensor("xb4", [128, 8 * TT], BF16))
    xsq4 = s4b.enter_context(nc.sbuf_tensor("xsq4", [128, 8 * TT], BF16))
    abc4 = s4b.enter_context(nc.sbuf_tensor("abc4", [128, 2 * TT], F32))
    xtmp4 = s4b.enter_context(nc.sbuf_tensor("xtmp4", [128, TT], F32))
    lgtm = s4b.enter_context(nc.sbuf_tensor("lgtm", [128, 128], F32))
    e8 = s4b.enter_context(nc.sbuf_tensor("e8", [128, 8], F32))
    eq8 = s4b.enter_context(nc.sbuf_tensor("eq8", [128, 8], F32))
    wtm = s4b.enter_context(nc.sbuf_tensor("wtm", [128, 128], F32))

    prev4 = None
    for ti in range(NTILE):
        fw_sync = [(sX2a, X2A_ALL), (sX2b, X2B_ALL)] if ti == 0 else []
        fw_mm = [(sEV, EV_P2)] if ti == 0 else []
        prev4 = emit_ln_tile(
            ti, x2_sp, xt4, xb4, xsq4, abc4, xtmp4, ln1,
            lambda cc, ti_: (cc * NT + ti_ * TT, cc * NT + (ti_ + 1) * TT),
            prev4, fw_sync, fw_mm, None, rowb4)
    EV_LN1 = prev4["ev_ln"]

    gate_evs = []
    for grp in range(2):
        @blk.tensor
        def _(t, grp=grp):
            t.wait_ge(sEV.h, EV_LN1)
            if gate_evs:
                t.wait_ge(sEV.h, gate_evs[-1])
            last = None
            for tc8 in range(8):
                tc = grp * 8 + tc8
                for cc in range(8):
                    last = t.matmul(
                        psD[:, tc8 * 8:(tc8 + 1) * 8],
                        ln1[:, cc * NT + tc * 128:cc * NT + (tc + 1) * 128],
                        gwb[:, cc * 8:(cc + 1) * 8],
                        start=(cc == 0), stop=(cc == 7), skip_group_check=True)
            last.then_inc(sMM.h, 1)
        sMM.n += 1
        mm_g = sMM.n

        @blk.vector
        def _(v, mm_g=mm_g):
            v.wait_ge(sMM.h, mm_g)
            gbb_b = bass.AP(gbb, 0, [[8, 128], [0, 8], [1, 8]])
            lg3 = lgtm[:, 0:64].rearrange("p (tc e) -> p tc e", tc=8)
            ps3 = psD[:, 0:64].rearrange("p (tc e) -> p tc e", tc=8)
            v.tensor_tensor(lg3, ps3, gbb_b, AL.add)
            gap(v).then_inc(sEV.h, 1)
        sEV.n += 1
        ev_lg = sEV.n

        for tc8 in range(8):
            @blk.vector
            def _(v, tc8=tc8, ev_lg=ev_lg):
                v.wait_ge(sEV.h, ev_lg)
                lg = lgtm[:, tc8 * 8:(tc8 + 1) * 8]
                v.reduce_max(sm[:, 0:1], lg, AX)
                v.tensor_copy(sm[:, 4:5], lg[:, 0:1])
                v.tensor_scalar_mul(sm[:, 3:4], sm[:, 0:1], -1.0)
                v.tensor_scalar(eq8[:], lg, sm[:, 0:1], None, AL.is_equal)
                gap(v)
                v.scalar_tensor_tensor(eq8[:], eq8[:], -1.0e30, lg, AL.mult, AL.add)
                gap(v)
                v.reduce_max(sm[:, 1:2], eq8[:], AX)
                gap(v).then_inc(sEV.h, 1)
            sEV.n += 1
            ev_m12 = sEV.n

            @blk.scalar
            def _(sc, tc8=tc8, ev_m12=ev_m12):
                sc.wait_ge(sEV.h, ev_m12)
                sc.activation(e8[:], lgtm[:, tc8 * 8:(tc8 + 1) * 8], AF.Exp,
                              bias=sm[:, 3:4], scale=1.0).then_inc(sSC.h, 1)
            sSC.n += 1
            sc_e8 = sSC.n

            @blk.vector
            def _(v, tc8=tc8, sc_e8=sc_e8):
                v.wait_ge(sSC.h, sc_e8)
                lg = lgtm[:, tc8 * 8:(tc8 + 1) * 8]
                v.reduce_sum(sm[:, 2:3], e8[:], AX)
                v.tensor_scalar(eq8[:], lg, sm[:, 1:2], None, AL.is_ge)
                v.reciprocal(sm[:, 2:3], sm[:, 2:3])
                v.tensor_mul(e8[:], e8[:], eq8[:])
                gap(v)
                v.tensor_scalar_mul(wtm[:, tc8 * 8:(tc8 + 1) * 8], e8[:],
                                    sm[:, 2:3])
                gap(v).then_inc(sEV.h, 1)
            sEV.n += 1
        ev_w = sEV.n

        @blk.tensor
        def _(t, grp=grp, ev_w=ev_w):
            t.wait_ge(sEV.h, ev_w)
            last = None
            for tc8 in range(8):
                last = t.transpose(psC[0:8, tc8 * 128:(tc8 + 1) * 128],
                                   wtm[:, tc8 * 8:(tc8 + 1) * 8], idf[:])
            last.then_inc(sMM.h, 1)
        sMM.n += 1
        mm_wt = sMM.n

        @blk.vector
        def _(v, grp=grp, mm_wt=mm_wt):
            v.wait_ge(sMM.h, mm_wt)
            v.tensor_copy(wfm[0:8, grp * 1024:(grp + 1) * 1024], psC[0:8, 0:1024]
                          ).then_inc(sEV.h, 1)
        sEV.n += 1
        gate_evs.append(sEV.n)
    EV_GATE = sEV.n

    s4b.close()

    s4c = ExitStack()
    hsl = s4c.enter_context(nc.sbuf_tensor("hsl", [128, 16 * 1024], BF16))
    w1p0 = s4c.enter_context(nc.sbuf_tensor("w1p0", [128, 8192], BF16))
    w1p1 = s4c.enter_context(nc.sbuf_tensor("w1p1", [128, 8192], BF16))
    w2p0 = s4c.enter_context(nc.sbuf_tensor("w2p0", [128, 8192], BF16))
    w2p1 = s4c.enter_context(nc.sbuf_tensor("w2p1", [128, 8192], BF16))
    oacc = s4c.enter_context(nc.sbuf_tensor("oacc", [128, 8 * 1024], F32))
    wbc0 = s4c.enter_context(nc.sbuf_tensor("wbc0", [128, 1024], BF16))
    wbc1 = s4c.enter_context(nc.sbuf_tensor("wbc1", [128, 1024], BF16))
    wr0 = s4c.enter_context(nc.sbuf_tensor("wr0", [1, 1024], BF16))
    b2b = s4c.enter_context(nc.sbuf_tensor("b2b", [8, 1024], BF16))

    ffn_e_mm = {}
    mm_f1 = {}
    mm_f2 = {}
    w_gd = {}
    last_mm_wb = [None]

    @blk.gpsimd
    def _(g):
        g.wait_ge(sEV.h, EV_GATE)
        g.dma_start(out=b2b[:], in_=b2_in.ap()).then_inc(sGA.h, 16)
    sGA.n += 16
    GD_B2 = sGA.n

    for th in range(2):
        @blk.sync
        def _(s, th=th):
            s.wait_ge(sEV.h, EV_GATE)
            if th == 1:
                s.wait_ge(sST.h, 16)
            s.dma_start(out=oacc[:].rearrange("p (o t) -> p o t", o=8),
                        in_=bass.AP(x2_sp, th * 1024,
                                    [[NT, 128], [128 * NT, 8], [1, 1024]])
                        ).then_inc(sOI.h, 16)
        sOI.n += 16
        ld_oi = sOI.n

        relu_scs = []
        o_evs = []
        for e in range(8):
            par = e % 2
            wbc = (wbc0, wbc1)[par]

            @blk.gpsimd
            def _(g, e=e, th=th):
                for jh in range(2):
                    if (th, e - 1, jh) in mm_f1:
                        g.wait_ge(sMM.h, mm_f1[(th, e - 1, jh)])
                    elif th == 1 and e == 0:
                        g.wait_ge(sMM.h, mm_f1[(0, 7, jh)])
                    elif th == 0 and e == 0:
                        g.wait_ge(sEV.h, EV_GATE)
                    sWx = (sW10, sW11)[jh]
                    g.dma_start(
                        out=(w1p0, w1p1)[jh][:].rearrange("p (c j) -> p c j", c=8),
                        in_=bass.AP(w1_in, e * C * 2048 + jh * 1024,
                                    [[2048, 128], [128 * 2048, 8], [1, 1024]])
                        ).then_inc(sWx.h, 16)
                    sWx.n += 16
                    w_gd[(th, e, "w1", jh)] = (sWx, sWx.n)
                for oh in range(2):
                    if (th, e - 1, oh) in mm_f2:
                        g.wait_ge(sMM.h, mm_f2[(th, e - 1, oh)])
                    elif th == 1 and e == 0:
                        g.wait_ge(sMM.h, mm_f2[(0, 7, oh)])
                    sWx = (sW20, sW21)[oh]
                    g.dma_start(
                        out=(w2p0, w2p1)[oh][:].rearrange("p (j o) -> p j o", j=16),
                        in_=bass.AP(w2_in, e * 2048 * C + oh * 512,
                                    [[C, 128], [128 * C, 16], [1, 512]])
                        ).then_inc(sWx.h, 16)
                    sWx.n += 16
                    w_gd[(th, e, "w2", oh)] = (sWx, sWx.n)

            @blk.sync
            def _(s, e=e, th=th):
                s.wait_ge(sEV.h, EV_GATE)
                if last_mm_wb[0] is not None:
                    s.wait_ge(sMM.h, last_mm_wb[0])
                s.dma_start(out=wr0[0:1, :],
                            in_=wfm[e:e + 1, th * 1024:(th + 1) * 1024]
                            ).then_inc(sWR.h, 16)
            sWR.n += 16
            ld_wr = sWR.n

            @blk.tensor
            def _(t, e=e, th=th, ld_wr=ld_wr):
                t.wait_ge(sWR.h, ld_wr)
                if relu_scs:
                    t.wait_ge(sSC.h, relu_scs[-1])
                t.matmul(psB[:, 0:512], onerb[:], wr0[0:1, 0:512],
                         start=True, stop=True, skip_group_check=True)
                t.matmul(psB[:, 512:1024], onerb[:], wr0[0:1, 512:1024],
                         start=True, stop=True, skip_group_check=True
                         ).then_inc(sMM.h, 1)
            sMM.n += 1
            mm_wb = sMM.n
            last_mm_wb[0] = mm_wb

            @blk.vector
            def _(v, wbc=wbc, mm_wb=mm_wb):
                v.wait_ge(sMM.h, mm_wb)
                v.tensor_copy(wbc[:], psB[:, 0:1024]).then_inc(sEV.h, 1)
            sEV.n += 1
            ev_wb = sEV.n

            for jc in range(16):
                psH = (psA, psB)[jc % 2]

                @blk.tensor
                def _(t, e=e, th=th, jc=jc, psH=psH, ev_wb=ev_wb):
                    jh = jc // 8
                    jj = jc % 8
                    if jj == 0:
                        wsem, wval = w_gd[(th, e, "w1", jh)]
                        t.wait_ge(wsem.h, wval)
                    if jc <= 1:
                        t.wait_ge(sEV.h, ev_wb)
                    if len(relu_scs) >= 2:
                        t.wait_ge(sSC.h, relu_scs[-2])
                    w1b = (w1p0, w1p1)[jh]
                    last = None
                    for cc in range(8):
                        lhs = w1b[:, cc * 1024 + jj * 128:
                                  cc * 1024 + (jj + 1) * 128]
                        t.matmul(psH[:, 0:512], lhs,
                                 ln1[:, cc * NT + th * 1024:
                                     cc * NT + th * 1024 + 512],
                                 start=(cc == 0), stop=(cc == 7),
                                 skip_group_check=True)
                        last = t.matmul(psH[:, 512:1024], lhs,
                                        ln1[:, cc * NT + th * 1024 + 512:
                                            cc * NT + (th + 1) * 1024],
                                        start=(cc == 0), stop=(cc == 7),
                                        skip_group_check=True)
                    last.then_inc(sMM.h, 1)
                sMM.n += 1
                mm_h = sMM.n
                if jc == 7:
                    mm_f1[(th, e, 0)] = mm_h
                elif jc == 15:
                    mm_f1[(th, e, 1)] = mm_h

                @blk.scalar
                def _(sc, e=e, jc=jc, psH=psH, mm_h=mm_h):
                    sc.wait_ge(sMM.h, mm_h)
                    sc.activation(hsl[:, jc * 1024:(jc + 1) * 1024], psH[:, 0:1024],
                                  AF.Relu, bias=b1c[:, e * 16 + jc:e * 16 + jc + 1],
                                  scale=1.0).then_inc(sSC.h, 1)
                sSC.n += 1
                relu_scs.append(sSC.n)
                sc_r = sSC.n

                @blk.vector
                def _(v, jc=jc, wbc=wbc, sc_r=sc_r):
                    v.wait_ge(sSC.h, sc_r)
                    v.tensor_mul(hsl[:, jc * 1024:(jc + 1) * 1024],
                                 hsl[:, jc * 1024:(jc + 1) * 1024], wbc[:]
                                 ).then_inc(sEV.h, 1)
                sEV.n += 1
            ev_hdone = sEV.n

            for oc in range(8):
                psO = (psC, psD)[oc % 2]

                @blk.tensor
                def _(t, e=e, th=th, oc=oc, psO=psO, ev_hdone=ev_hdone):
                    oh = oc // 4
                    oo = oc % 4
                    if oc == 0:
                        t.wait_ge(sEV.h, ev_hdone)
                    if oo == 0:
                        wsem, wval = w_gd[(th, e, "w2", oh)]
                        t.wait_ge(wsem.h, wval)
                    if len(o_evs) >= 2:
                        t.wait_ge(sEV.h, o_evs[-2])
                    w2b = (w2p0, w2p1)[oh]
                    last = None
                    for jc in range(16):
                        lhs = w2b[:, jc * 512 + oo * 128:
                                  jc * 512 + (oo + 1) * 128]
                        t.matmul(psO[:, 0:512], lhs,
                                 hsl[:, jc * 1024:jc * 1024 + 512],
                                 start=(jc == 0), stop=(jc == 15),
                                 skip_group_check=True)
                        last = t.matmul(psO[:, 512:1024], lhs,
                                        hsl[:, jc * 1024 + 512:(jc + 1) * 1024],
                                        start=(jc == 0), stop=(jc == 15),
                                        skip_group_check=True)
                    last.then_inc(sMM.h, 1)
                sMM.n += 1
                mm_o = sMM.n
                if oc == 3:
                    mm_f2[(th, e, 0)] = mm_o
                elif oc == 7:
                    mm_f2[(th, e, 1)] = mm_o

                @blk.vector
                def _(v, oc=oc, psO=psO, mm_o=mm_o, ld_oi=ld_oi):
                    v.wait_ge(sMM.h, mm_o)
                    v.wait_ge(sOI.h, ld_oi)
                    v.tensor_add(oacc[:, oc * 1024:(oc + 1) * 1024],
                                 oacc[:, oc * 1024:(oc + 1) * 1024],
                                 psO[:, 0:1024]).then_inc(sEV.h, 1)
                sEV.n += 1
                o_evs.append(sEV.n)
            ffn_e_mm[(th, e)] = sMM.n

        for ocp in range(4):
            @blk.tensor
            def _(t, th=th, ocp=ocp):
                if ocp == 0:
                    t.wait_ge(sEV.h, o_evs[-1])
                    t.wait_ge(sGA.h, GD_B2)
                else:
                    t.wait_ge(sEV.h, sEV.n)
                last = None
                for j in range(2):
                    oc = ocp * 2 + j
                    psO = (psC, psD)[j]
                    for hh in range(2):
                        last = t.matmul(
                            psO[:, hh * 512:(hh + 1) * 512],
                            b2b[:, oc * 128:(oc + 1) * 128],
                            wfm[0:8, th * 1024 + hh * 512:
                                th * 1024 + (hh + 1) * 512],
                            start=True, stop=True, skip_group_check=True)
                last.then_inc(sMM.h, 1)
            sMM.n += 1
            mm_b2 = sMM.n

            @blk.vector
            def _(v, ocp=ocp, mm_b2=mm_b2):
                v.wait_ge(sMM.h, mm_b2)
                last = None
                for j in range(2):
                    oc = ocp * 2 + j
                    psO = (psC, psD)[j]
                    last = v.tensor_add(oacc[:, oc * 1024:(oc + 1) * 1024],
                                        oacc[:, oc * 1024:(oc + 1) * 1024],
                                        psO[:, 0:1024])
                last.then_inc(sEV.h, 1)
            sEV.n += 1
        ev_b2 = sEV.n

        @blk.sync
        def _(s, th=th, ev_b2=ev_b2):
            s.wait_ge(sEV.h, ev_b2)
            s.dma_start(out=bass.AP(out_fm, th * 1024,
                                    [[NT, 128], [128 * NT, 8], [1, 1024]]),
                        in_=oacc[:].rearrange("p (o t) -> p o t", o=8)
                        ).then_inc(sST.h, 16)
        sST.n += 16

    @blk.sync
    def _(s):
        s.wait_ge(sST.h, sST.n)

    s4c.close()
    s4.close()
    es.close()
    nc.finalize()
    return nc


_NC_CACHE = {}


def _get_nc():
    if "nc" not in _NC_CACHE:
        _NC_CACHE["nc"] = build_nc()
    return _NC_CACHE["nc"]


def _make_sched(c):
    pA, pB = c, 7 - c
    rows = []
    for g in range(pA):
        sh = g
        rows.append((sh * SHARD + 0, sh * SHARD + VB + 0, 0, 0))
    for g in range(pB):
        sh = 7 - g
        rows.append((sh * SHARD + HBLK, sh * SHARD + VB + HBLK, HBLK, 8))
    assert len(rows) == 7
    tab = np.zeros((7, 20), dtype=np.int32)
    for gi, (ko, vo, qo, sb) in enumerate(rows):
        tab[gi, 0] = ko
        tab[gi, 1] = vo
        tab[gi, 2] = qo
        tab[gi, 3] = sb
        for qs in range(8):
            tab[gi, 4 + qs] = (sb + qs) * 1024
            tab[gi, 12 + qs] = sb + qs
    return tab.reshape(1, 140)


def kernel(L0_attn_x, L0_attn_W_attn, L0_attn_W_proj, L0_attn_lora_attn_A,
           L0_attn_lora_attn_B, L0_attn_lora_proj_A, L0_attn_lora_proj_B,
           L0_attn_bn_gamma, L0_attn_bn_beta, L0_ffn_gate_w, L0_ffn_gate_b,
           L0_ffn_w1, L0_ffn_b1, L0_ffn_w2, L0_ffn_b2):
    x = np.asarray(L0_attn_x, dtype=np.float32)
    cst = np.ascontiguousarray
    wattnT_h = cst(np.asarray(L0_attn_W_attn, np.float32).T)
    aaT_h = cst(np.asarray(L0_attn_lora_attn_A, np.float32).T)
    ba_h = cst(np.asarray(L0_attn_lora_attn_B, np.float32))
    wprojT_h = cst(np.asarray(L0_attn_W_proj, np.float32).T)
    apT_h = cst(np.asarray(L0_attn_lora_proj_A, np.float32).T)
    bp_h = cst(np.asarray(L0_attn_lora_proj_B, np.float32))
    gam = np.asarray(L0_attn_bn_gamma, np.float32)
    bet = np.asarray(L0_attn_bn_beta, np.float32)
    gam_qk_h = cst(gam[0:2048].reshape(16, 128).T)
    bet_qk_h = cst(bet[0:2048].reshape(16, 128).T)
    gam_vr_h = cst(gam[2048:3072].reshape(1, C))
    bet_vr_h = cst(bet[2048:3072].reshape(1, C))
    gwT_h = cst(np.asarray(L0_ffn_gate_w, np.float32).T)
    gb_row_h = cst(np.asarray(L0_ffn_gate_b, np.float32).reshape(1, 8))
    w1_h = cst(np.asarray(L0_ffn_w1, np.float32))
    b1_t_h = cst(np.asarray(L0_ffn_b1, np.float32).reshape(8, 16, 128)
                 .transpose(2, 0, 1).reshape(128, 128))
    w2_h = cst(np.asarray(L0_ffn_w2, np.float32))
    b2_h = cst(np.asarray(L0_ffn_b2, np.float32))

    nc = _get_nc()
    in_maps = []
    for c in range(NCORE):
        a0 = x[0, 1024 * c:1024 * (c + 1)]
        a1 = x[1, 1024 * (7 - c):1024 * (8 - c)]
        x_fm_h = cst(np.concatenate([a0, a1], axis=0).T)
        in_maps.append({
            "x_fm": x_fm_h, "wattnT": wattnT_h, "aaT": aaT_h, "ba_in": ba_h,
            "wprojT": wprojT_h, "apT": apT_h, "bp_in": bp_h,
            "gam_qk": gam_qk_h, "bet_qk": bet_qk_h,
            "gam_vr": gam_vr_h, "bet_vr": bet_vr_h,
            "gwT": gwT_h, "gb_row": gb_row_h,
            "w1_in": w1_h, "b1_t": b1_t_h, "w2_in": w2_h, "b2_in": b2_h,
            "sched": _make_sched(c),
        })
    res = run_bass_kernel_spmd(nc, in_maps, core_ids=list(range(NCORE)))
    out = np.zeros((2, 8192, 1024), dtype=np.float32)
    for c in range(NCORE):
        o = res.results[c]["out_fm"]
        out[0, 1024 * c:1024 * (c + 1)] = o[:, 0:1024].T
        out[1, 1024 * (7 - c):1024 * (8 - c)] = o[:, 1024:2048].T
    return out


if __name__ == "__main__":
    nc = build_nc()
    print("build OK; instructions:", nc.n_instructions())
